# version 14
# baseline (speedup 1.0000x reference)
import math
from contextlib import ExitStack

import numpy as np
import ml_dtypes

import concourse.bass as bass
import concourse.mybir as mybir
from concourse.bass_utils import run_bass_kernel_spmd

F32 = mybir.dt.float32
BF16 = mybir.dt.bfloat16
I32 = mybir.dt.int32
AF = mybir.ActivationFunctionType
ALU = mybir.AluOpType
AX = mybir.AxisListType
AP = bass.AP

T = 2048
D = 1024
FCW = 512
NFC = T // FCW

FUSED = True
LN_HARD = False
GN_HARD = False
MERGE_ACC = False
YG_REUSE = True


def _t(a):
    return np.ascontiguousarray(np.asarray(a).T)


def _cols(vec, n128):
    return np.ascontiguousarray(np.asarray(vec).reshape(n128, 128).T)


def _acols(alog):
    a = np.asarray(alog).reshape(16, 128, 16).transpose(1, 0, 2).reshape(128, 256)
    return np.ascontiguousarray(a)


def _bucket_onehot():
    r = np.arange(4095) - 2047
    nb = 16
    rb = (r > 0).astype(np.int64) * nb
    ra = np.abs(r)
    me = 8
    rp_f = np.maximum(ra, 1).astype(np.float32)
    ril = me + (np.log(rp_f / me) / math.log(128 / me) * (nb - me)).astype(np.int32)
    ril = np.minimum(ril, nb - 1)
    b = rb + np.where(ra < me, ra, ril)
    oh = np.zeros((32, 4095), np.float32)
    oh[b, np.arange(4095)] = 1.0
    return oh


COLS = {}
_nxt = [0]
for _n, _w in [
    ("L0_b_in", 32), ("L0_b_xproj", 1), ("L0_b_dt", 16), ("L0_b_out", 8),
    ("L0_D", 16), ("L0_w1s", 32), ("L0_w2s", 32), ("L0_h1s", 32),
    ("L0_h2s", 32), ("L0_w3s", 8),
    ("L1_bdt", 16), ("L1_dtb", 16), ("L1_gamma", 16), ("L1_beta", 16),
    ("L2_w1s", 32), ("L2_w2s", 32), ("L2_h1s", 32), ("L2_h2s", 32),
    ("L2_w3s", 8),
    ("L3_rb", 1), ("L3_b1", 128), ("L3_sb1", 32),
    ("L1_dtsum", 16),
]:
    COLS[_n] = (_nxt[0], _w)
    _nxt[0] += _w
NCOLS = _nxt[0]


def _build_cols(inp):
    c = np.zeros((128, NCOLS), np.float32)

    def put(name, vec):
        o, w = COLS[name]
        v = np.asarray(vec).astype(np.float32).reshape(-1)
        if w == 1:
            c[: v.shape[0], o] = v
        else:
            c[:, o:o + w] = _cols(v, w)

    put("L0_b_in", inp["L0_attn_b_in"]); put("L0_b_xproj", inp["L0_attn_b_xproj"])
    put("L0_b_dt", inp["L0_attn_b_dt"]); put("L0_b_out", inp["L0_attn_b_out"])
    put("L0_D", inp["L0_attn_D_param"])
    put("L0_w1s", inp["L0_ffn_w1_s"]); put("L0_w2s", inp["L0_ffn_w2_s"])
    put("L0_h1s", inp["L0_ffn_h1_s"]); put("L0_h2s", inp["L0_ffn_h2_s"])
    put("L0_w3s", inp["L0_ffn_w3_s"])
    put("L1_bdt", inp["L1_attn_bdt"]); put("L1_dtb", inp["L1_attn_dt_bias"])
    put("L1_gamma", inp["L1_attn_gamma"]); put("L1_beta", inp["L1_attn_beta"])
    put("L2_w1s", inp["L2_ffn_w1_s"]); put("L2_w2s", inp["L2_ffn_w2_s"])
    put("L2_h1s", inp["L2_ffn_h1_s"]); put("L2_h2s", inp["L2_ffn_h2_s"])
    put("L2_w3s", inp["L2_ffn_w3_s"])
    put("L3_rb", inp["L3_ffn_router_b"]); put("L3_b1", inp["L3_ffn_b1"])
    put("L3_sb1", inp["L3_ffn_sb1"])
    return c


def _wspec(layer):
    if layer == 0:
        return [
            ("w0_in", lambda i: i["L0_attn_W_in"]),
            ("w0_xp", lambda i: i["L0_attn_W_xproj"]),
            ("w0_dt", lambda i: i["L0_attn_W_dt"]),
            ("w0_out", lambda i: i["L0_attn_W_out"]),
            ("a0", lambda i: _acols(i["L0_attn_A_log"])),
            ("f0_w1", lambda i: _t(i["L0_ffn_w1_w"])),
            ("f0_w2", lambda i: _t(i["L0_ffn_w2_w"])),
            ("f0_h1", lambda i: _t(i["L0_ffn_h1_w"])),
            ("f0_h2", lambda i: _t(i["L0_ffn_h2_w"])),
            ("f0_w3", lambda i: _t(i["L0_ffn_w3_w"])),
        ]
    if layer == 1:
        return [
            ("w1_in", lambda i: _t(i["L1_attn_Win"])),
            ("w1_bc", lambda i: np.concatenate(
                [_t(i["L1_attn_WB"]), _t(i["L1_attn_WC"])], axis=1)),
            ("w1_dt", lambda i: _t(i["L1_attn_Wdt"])),
            ("w1_out", lambda i: _t(i["L1_attn_Wout"])),
            ("a1", lambda i: _acols(i["L1_attn_A_log"])),
            ("f1_wg", lambda i: _t(i["L1_ffn_w_gate"])),
            ("f1_wu", lambda i: _t(i["L1_ffn_w_up"])),
            ("f1_wd", lambda i: _t(i["L1_ffn_w_down"])),
        ]
    if layer == 2:
        return [
            ("w2_q", lambda i: _t(i["L2_attn_Wq"])),
            ("w2_k", lambda i: _t(i["L2_attn_Wk"])),
            ("w2_v", lambda i: _t(i["L2_attn_Wv"])),
            ("w2_o", lambda i: _t(i["L2_attn_Wo"])),
            ("rel_emb", lambda i: i["L2_attn_rel_emb"]),
            ("onehot", lambda i: _bucket_onehot()),
            ("f2_w1", lambda i: _t(i["L2_ffn_w1_w"])),
            ("f2_w2", lambda i: _t(i["L2_ffn_w2_w"])),
            ("f2_h1", lambda i: _t(i["L2_ffn_h1_w"])),
            ("f2_h2", lambda i: _t(i["L2_ffn_h2_w"])),
            ("f2_w3", lambda i: _t(i["L2_ffn_w3_w"])),
        ]
    return [
        ("w3_qkv", lambda i: _t(i["L3_attn_W_qkv"])),
        ("r3_w", lambda i: i["L3_ffn_router_w"]),
        ("m3_w1", lambda i: i["L3_ffn_w1"]),
        ("m3_w2", lambda i: i["L3_ffn_w2"]),
        ("m3_sw1", lambda i: i["L3_ffn_sw1"]),
        ("m3_sw2", lambda i: i["L3_ffn_sw2"]),
        ("b2ext", lambda i: np.concatenate(
            [i["L3_ffn_b2"], i["L3_ffn_sb2"]], axis=0)),
    ]



class Buf:
    __slots__ = ("w", "r")

    def __init__(self):
        self.w = None
        self.r = []


_NULL = Buf()


class P:
    ENG = ("t", "v", "s", "g", "y")

    def __init__(self, nc):
        self.nc = nc
        self.root = ExitStack()
        self.stacks = [self.root]
        self.streams = {k: [] for k in self.ENG}
        self.cnt = dict.fromkeys(self.ENG, 0)
        self.seen = {k: dict.fromkeys(self.ENG, 0) for k in self.ENG}
        self.sems = {k: self.root.enter_context(nc.semaphore("sem_" + k))
                     for k in self.ENG}
        self.dbufs = {}
        self._uid = 0
        self.banks = []
        for i in range(8):
            h = self.root.enter_context(
                nc.psum_tensor(f"bank{i}", [128, FCW], F32))
            self.banks.append((h, Buf()))

    def sb(self, shape, dt, name=None):
        self._uid += 1
        h = self.stacks[-1].enter_context(
            self.nc.sbuf_tensor(name or f"sb{self._uid}", list(shape), dt))
        return (h, Buf())

    def db(self, key):
        b = self.dbufs.get(key)
        if b is None:
            b = self.dbufs[key] = Buf()
        return b

    paranoid = False
    PARANOID_FULL = {k: ("v", "s", "y", "g")
                     for k in ("t", "v", "s", "g", "y")}
    PARANOID_FAST = {"v": ("y", "g"), "s": ("v", "y", "g"),
                     "t": (), "y": (), "g": ()}

    def op(self, ek, emit, reads=(), writes=(), tail=False):
        need = {}
        if self.paranoid:
            pset = self.paranoid if isinstance(self.paranoid, dict) else                 P.PARANOID_FULL
            for sk in pset.get(ek, ()):
                if self.cnt[sk] > need.get(sk, 0):
                    need[sk] = self.cnt[sk]
        for b in reads:
            if b.w is not None:
                sk, v = b.w
                if (sk != ek or tail) and v > need.get(sk, 0):
                    need[sk] = v
        for b in writes:
            deps = ([b.w] if b.w is not None else []) + b.r
            for sk, v in deps:
                if sk != ek and v > need.get(sk, 0):
                    need[sk] = v
        st = self.streams[ek]
        for sk in self.ENG:
            v = need.get(sk, 0)
            if v > self.seen[ek].get(sk, 0):
                self.seen[ek][sk] = v
                st.append(("w", sk, v))
        n = 16 if ek in ("g", "y") else 1
        self.cnt[ek] += n
        tok = (ek, self.cnt[ek])
        st.append(("e", emit, ek, n))
        for b in reads:
            b.r.append(tok)
        for b in writes:
            b.w = tok
            b.r = []
        return tok

    def raw_wait(self, ek, sk, v):
        if v > self.seen[ek][sk]:
            self.seen[ek][sk] = v
            self.streams[ek].append(("w", sk, v))

    def fence(self):
        for ek in self.ENG:
            for sk in self.ENG:
                if sk != ek:
                    self.raw_wait(ek, sk, self.cnt[sk])

    class _Phase:
        def __init__(self, p):
            self.p = p

        def __enter__(self):
            self.p.stacks.append(ExitStack())
            return self

        def __exit__(self, *a):
            st = self.p.stacks.pop()
            st.close()
            self.p.fence()
            return False

    def phase(self):
        return P._Phase(self)

    def finalize(self, block):
        sems = self.sems
        names = {"t": "tensor", "v": "vector", "s": "scalar",
                 "g": "gpsimd", "y": "sync"}
        for k in self.ENG:
            items = self.streams[k]

            def make(items=items):
                def f(q):
                    for it in items:
                        if it[0] == "w":
                            q.wait_ge(sems[it[1]], it[2])
                        else:
                            it[1](q).then_inc(sems[it[2]], it[3])
                return f

            getattr(block, names[k])(make())



class B:
    def __init__(self, p, dram):
        self.p = p
        self.dram = dram

    def col(self, name, blk=0, rows=128):
        o, w = COLS[name]
        h, b = self.cols
        return (h[0:rows, o + blk:o + blk + 1], b)

    def launder(self, tiles, out_tiles):
        p = self.p
        for (sh, sb2), (dh, db2) in zip(tiles, out_tiles):
            p.op("y", (lambda q, s=sh, d=dh: q.dma_start(out=d[:, :],
                                                         in_=s[:, :])),
                 reads=[sb2], writes=[db2])
        return out_tiles

    def wload(self, stage, dst_ap, dst_buf, src_ap, cols, view=None):
        p = self.p
        dview = dst_ap if view is None else dst_ap.rearrange(
            "p (c k) -> p c k", c=int(view))
        p.op("g", (lambda q, d=dview, s=src_ap: q.dma_start(out=d, in_=s)),
             reads=[], writes=[dst_buf])

    def linear(self, wname, in_tiles, out_cb, in_dim, out_dim, banks=(0, 1),
               woff=0, wrowlen=None):
        p = self.p
        nci = in_dim // 128
        ncj = out_dim // 128
        wrl = wrowlen if wrowlen is not None else out_dim
        w_d = self.dram[wname]
        wb = [p.sb([128, nci * 128], BF16) for _ in range(2)]
        for j in range(ncj):
            wt, wbuf = wb[j % 2]
            for cgi, cg0 in enumerate(range(0, nci, 4)):
                cgn = min(4, nci - cg0)
                src = AP(tensor=w_d.tensor, offset=woff + cg0 * 128 * wrl + j * 128,
                         ap=[[wrl, 128], [128 * wrl, cgn], [1, 128]])
                stage = None
                sview = cgn
                dst = wt[:, cg0 * 128:(cg0 + cgn) * 128]
                self.wload(stage, dst, wbuf, src, cgn * 128, view=sview)
            for fc in range(NFC):
                bh, bb = self.p.banks[banks[(j * NFC + fc) % len(banks)]]

                def emit(q, wt=wt, bh=bh, fc=fc, nci=nci, its=in_tiles):
                    ins = None
                    for c in range(nci):
                        ins = q.matmul(bh[:, :], wt[:, c * 128:(c + 1) * 128],
                                       its[c][0](fc), start=(c == 0),
                                       stop=(c == nci - 1))
                    return ins

                p.op("t", emit, reads=[wbuf] + [t[1] for t in in_tiles],
                     writes=[bb])
                out_cb(j, fc, bh, bb)

    def linear_stream(self, wname, rhs_dram, out_cb, in_dim, out_dim,
                      banks=(0, 1), jg=8, rhs_bf=True):
        p = self.p
        nci = in_dim // 128
        ncj = out_dim // 128
        w_d = self.dram[wname]
        with p.phase():
            panels = [p.sb([128, nci * 128], BF16) for _ in range(jg)]
            rts = [p.sb([128, FCW], BF16 if rhs_bf else F32)
                   for _ in range(nci)]
            nst = 0
            for jg0 in range(0, ncj, jg):
                jn = min(jg, ncj - jg0)
                for ji in range(jn):
                    j = jg0 + ji
                    wt, wbuf = panels[ji]
                    for cg0 in range(0, nci, 4):
                        cgn = min(4, nci - cg0)
                        src = AP(tensor=w_d.tensor,
                                 offset=cg0 * 128 * out_dim + j * 128,
                                 ap=[[out_dim, 128], [128 * out_dim, cgn],
                                     [1, 128]])
                        stage = None
                        nst += 1
                        sview = cgn
                        dst = wt[:, cg0 * 128:(cg0 + cgn) * 128]
                        self.wload(stage, dst, wbuf, src, cgn * 128,
                                   view=sview)
                for fc in range(NFC):
                    for c in range(nci):
                        h, bb = rts[c]
                        src = self.dram[rhs_dram][c * 128:(c + 1) * 128,
                                                  fc * FCW:(fc + 1) * FCW]
                        p.op("y", (lambda q, d=h, s=src:
                                   q.dma_start(out=d[:, :], in_=s)),
                             reads=[p.db((rhs_dram, c, fc))], writes=[bb])
                    for ji in range(jn):
                        j = jg0 + ji
                        wt, wbuf = panels[ji]
                        bh, bb = p.banks[banks[(j * NFC + fc) % len(banks)]]

                        def emit(q, wt=wt, bh=bh, nci=nci):
                            ins = None
                            for c in range(nci):
                                ins = q.matmul(bh[:, :],
                                               wt[:, c * 128:(c + 1) * 128],
                                               rts[c][0][:, :], start=(c == 0),
                                               stop=(c == nci - 1))
                            return ins

                        p.op("t", emit, reads=[wbuf] + [t[1] for t in rts],
                             writes=[bb])
                        out_cb(j, fc, bh, bb)

    def merge(self, j, fc, ps_h, ps_b, scalar=1.0, scalar_buf=None,
              op0=ALU.mult, src_key="xres", dst_key="xres"):
        p = self.p
        if not MERGE_ACC:
            return self.merge_v0(j, fc, ps_h, ps_b, scalar, scalar_buf, op0,
                                 src_key, dst_key)
        h, b = self.mt[self.mti]
        h2, b2 = self.mt2[self.mti]
        self.mti = (self.mti + 1) % len(self.mt)
        dbuf = p.db(("xres", j, fc))
        dst = self.dram["xres"][j * 128:(j + 1) * 128, fc * FCW:(fc + 1) * FCW]
        sb_extra = [scalar_buf] if scalar_buf is not None else []
        p.op("v", (lambda q, d=h, ps=ps_h, sc=scalar, op0=op0:
                   q.tensor_scalar(d[:, :], ps[:, :], sc, None, op0)),
             reads=[ps_b] + sb_extra, writes=[b])
        p.op("y", (lambda q, s=h, d=h2: q.dma_start(out=d[:, :], in_=s[:, :])),
             reads=[b], writes=[b2])
        p.op("g", (lambda q, s=h2, d=dst: q.dma_start(
            out=d, in_=s[:, :], accum_op=ALU.add)),
             reads=[b2, dbuf], writes=[dbuf], tail=True)

    def merge_sb(self, j, fc, src_ap, src_buf, src_key="xres", dst_key="xres"):
        p = self.p
        if not MERGE_ACC:
            return self.merge_sb_v0(j, fc, src_ap, src_buf, src_key, dst_key)
        h2, b2 = self.mt2[self.mti]
        self.mti = (self.mti + 1) % len(self.mt)
        dbuf = p.db(("xres", j, fc))
        dst = self.dram["xres"][j * 128:(j + 1) * 128, fc * FCW:(fc + 1) * FCW]
        p.op("y", (lambda q, s=src_ap, d=h2: q.dma_start(out=d[:, :], in_=s)),
             reads=[src_buf], writes=[b2])
        p.op("g", (lambda q, s=h2, d=dst: q.dma_start(
            out=d, in_=s[:, :], accum_op=ALU.add)),
             reads=[b2, dbuf], writes=[dbuf], tail=True)

    def merge_v0(self, j, fc, ps_h, ps_b, scalar=1.0, scalar_buf=None,
                 op0=ALU.mult, src_key="xres", dst_key="xres"):
        p = self.p
        h, b = self.mt[(j + fc) % 2]
        sbuf = p.db((src_key, j, fc))
        dbuf = p.db((dst_key, j, fc))
        src = self.dram[src_key][j * 128:(j + 1) * 128, fc * FCW:(fc + 1) * FCW]
        dst = self.dram[dst_key][j * 128:(j + 1) * 128, fc * FCW:(fc + 1) * FCW]
        p.op("y", (lambda q, d=h, s=src: q.dma_start(out=d[:, :], in_=s)),
             reads=[sbuf], writes=[b])
        sb_extra = [scalar_buf] if scalar_buf is not None else []
        p.op("v", (lambda q, d=h, ps=ps_h, sc=scalar, op0=op0:
                   q.scalar_tensor_tensor(d[:, :], ps[:, :], sc, d[:, :],
                                          op0, ALU.add)),
             reads=[ps_b, b] + sb_extra, writes=[b])
        p.op("y", (lambda q, s=h, d=dst: q.dma_start(out=d, in_=s[:, :])),
             reads=[b], writes=[dbuf])

    def merge_sb_v0(self, j, fc, src_ap, src_buf, src_key="xres",
                    dst_key="xres"):
        p = self.p
        h, b = self.mt[(j + fc) % 2]
        sbuf = p.db((src_key, j, fc))
        dbuf = p.db((dst_key, j, fc))
        src = self.dram[src_key][j * 128:(j + 1) * 128, fc * FCW:(fc + 1) * FCW]
        dst = self.dram[dst_key][j * 128:(j + 1) * 128, fc * FCW:(fc + 1) * FCW]
        p.op("y", (lambda q, d=h, s=src: q.dma_start(out=d[:, :], in_=s)),
             reads=[sbuf], writes=[b])
        p.op("v", (lambda q, d=h, sa=src_ap: q.tensor_tensor(
            d[:, :], d[:, :], sa, ALU.add)), reads=[src_buf, b], writes=[b])
        p.op("y", (lambda q, s=h, d=dst: q.dma_start(out=d, in_=s[:, :])),
             reads=[b], writes=[dbuf])

    def xn_in(self):
        return [((lambda fc, h=h: h[:, fc * FCW:(fc + 1) * FCW]), bb)
                for (h, bb) in self.xn]



def emit_layernorm(b, src_key):
    p = b.p
    x_d = b.dram[src_key]
    ones_c, ones_cb = b.ones_col
    ones_r, ones_rb = b.ones_row
    with p.phase():
        xt = [p.sb([128, FCW], F32) for _ in range(8)]
        sq = [p.sb([128, FCW], F32) for _ in range(2)]
        sq2 = [p.sb([128, FCW], F32) for _ in range(2)]
        rws = [p.sb([1, FCW], F32) for _ in range(7)]
        (rmu, rmub), (rex, rexb), (rm2, rm2b), (rvar, rvarb), (rstd, rstdb), \
            (rbn, rbnb), (pad, padb) = rws
        rvL, rvLb = rm2, rm2b
        lnv, lnvb = rex, rexb
        rsL, rsLb = rvar, rvarb
        bsx = p.banks[4]
        bsq = p.banks[5]
        ba = p.banks[6]
        bbc = p.banks[7]
        for fc in range(NFC):
            for i in range(8):
                h, bb = xt[i]
                src = x_d[i * 128:(i + 1) * 128, fc * FCW:(fc + 1) * FCW]
                p.op("y", (lambda q, d=h, s=src: q.dma_start(out=d[:, :], in_=s)),
                     reads=[p.db((src_key, i, fc))], writes=[bb])
            for i in range(8):
                h, bb = xt[i]
                sh, sb2 = sq[i % 2]
                s2h, s2b = sq2[i % 2]
                p.op("s", (lambda q, d=sh, s=h: q.activation(
                    d[:, :], s[:, :], AF.Square)), reads=[bb], writes=[sb2])
                p.op("y", (lambda q, s=sh, d=s2h: q.dma_start(
                    out=d[:, :], in_=s[:, :])), reads=[sb2], writes=[s2b])
                p.op("t", (lambda q, s=h, i=i: q.matmul(
                    bsx[0][0:1, :], ones_c[:, :], s[:, :], start=(i == 0),
                    stop=(i == 7))), reads=[bb, ones_cb], writes=[bsx[1]])
                p.op("t", (lambda q, s=s2h, i=i: q.matmul(
                    bsq[0][0:1, :], ones_c[:, :], s[:, :], start=(i == 0),
                    stop=(i == 7))), reads=[s2b, ones_cb], writes=[bsq[1]])
            p.op("v", (lambda q: q.tensor_scalar(rmu[0:1, :], bsx[0][0:1, :],
                                                 1.0 / D, None, ALU.mult)),
                 reads=[bsx[1]], writes=[rmub])
            p.op("v", (lambda q: q.tensor_scalar(rex[0:1, :], bsq[0][0:1, :],
                                                 1.0 / D, None, ALU.mult)),
                 reads=[bsq[1]], writes=[rexb])
            p.op("v", (lambda q: q.tensor_tensor(rm2[0:1, :], rmu[0:1, :],
                                                 rmu[0:1, :], ALU.mult)),
                 reads=[rmub], writes=[rm2b])
            p.op("v", (lambda q: q.memset(pad[0:1, :], 0.0)), writes=[padb])
            p.op("v", (lambda q: q.tensor_tensor(rvar[0:1, :], rex[0:1, :],
                                                 rm2[0:1, :], ALU.subtract)),
                 reads=[rexb, rm2b], writes=[rvarb])
            p.op("y", (lambda q: q.dma_start(out=rvL[0:1, :], in_=rvar[0:1, :])),
                 reads=[rvarb], writes=[rvLb])
            p.op("s", (lambda q: q.activation(lnv[0:1, :], rvL[0:1, :], AF.Ln,
                                              bias=1e-5)), reads=[rvLb], writes=[lnvb])
            p.op("s", (lambda q: q.activation(pad[0:1, :], pad[0:1, :],
                                              AF.Identity)), reads=[padb],
                 writes=[padb])
            p.op("s", (lambda q: q.activation(rstd[0:1, :], lnv[0:1, :], AF.Exp,
                                              scale=-0.5)), reads=[lnvb], writes=[rstdb])
            p.op("y", (lambda q: q.dma_start(out=rsL[0:1, :], in_=rstd[0:1, :])),
                 reads=[rstdb], writes=[rsLb])
            p.op("v", (lambda q: q.scalar_tensor_tensor(
                rbn[0:1, :], rmu[0:1, :], -1.0, rsL[0:1, :], ALU.mult, ALU.mult)),
                reads=[rmub, rsLb], writes=[rbnb])
            p.op("y", (lambda q: q.dma_start(out=lnv[0:1, :], in_=rbn[0:1, :])),
                 reads=[rbnb], writes=[lnvb])
            p.op("t", (lambda q: q.matmul(ba[0][:, :], ones_r[:, :], rsL[0:1, :],
                                          start=True, stop=True)),
                 reads=[rsLb, ones_rb], writes=[ba[1]])
            p.op("t", (lambda q: q.matmul(bbc[0][:, :], ones_r[:, :], lnv[0:1, :],
                                          start=True, stop=True)),
                 reads=[lnvb, ones_rb], writes=[bbc[1]])
            for i in range(8):
                h, bb = xt[i]
                sh, sb2 = sq[i % 2]
                xh, xb = b.xn[i]
                p.op("v", (lambda q, s=h, d=sh: q.tensor_tensor(
                    d[:, :], s[:, :], ba[0][:, :], ALU.mult)),
                    reads=[bb, ba[1]], writes=[sb2])
                p.op("v", (lambda q, s=sh, d=xh, fc=fc: q.tensor_tensor(
                    d[:, fc * FCW:(fc + 1) * FCW], s[:, :], bbc[0][:, :],
                    ALU.add)), reads=[sb2, bbc[1]], writes=[xb])



def _softplus_cb(b, tiles, bias_name):
    p = b.p

    def cb(j, fc, ps_h, ps_b):
        (v, vb), (a, ab), (l, lb), (r, rb2) = tiles
        bias = b.col(bias_name, j)
        p.op("s", (lambda q, ps=ps_h, bc=bias[0]: q.activation(
            v[:, :], ps[:, :], AF.Identity, bias=bc)),
            reads=[ps_b, bias[1]], writes=[vb])
        p.op("s", (lambda q: q.activation(a[:, :], v[:, :], AF.Abs)),
             reads=[vb], writes=[ab])
        p.op("s", (lambda q: q.activation(a[:, :], a[:, :], AF.Exp, scale=-1.0)),
             reads=[ab], writes=[ab])
        p.op("s", (lambda q: q.activation(l[:, :], a[:, :], AF.Ln, bias=1.0)),
             reads=[ab], writes=[lb])
        p.op("v", (lambda q: q.tensor_scalar_max(r[:, :], v[:, :], 0.0)),
             reads=[vb], writes=[rb2])
        p.op("v", (lambda q: q.tensor_tensor(r[:, :], r[:, :], l[:, :],
                                             ALU.add)), reads=[lb, rb2],
             writes=[rb2])
        dst = b.dram["sp_dt"][j * 128:(j + 1) * 128, fc * FCW:(fc + 1) * FCW]
        p.op("y", (lambda q, s=r, d=dst: q.dma_start(out=d, in_=s[:, :])),
             reads=[rb2], writes=[p.db(("sp_dt", j, fc))])

    return cb



def emit_mamba0(b, src_key):
    p = b.p
    with p.phase():
        ev = [p.sb([128, FCW], BF16) for _ in range(2)]

        def cb_win(j, fc, ps_h, ps_b):
            h, bb = ev[(j * NFC + fc) % 2]
            bias = b.col("L0_b_in", j)
            p.op("s", (lambda q, d=h, ps=ps_h, bc=bias[0]: q.activation(
                d[:, :], ps[:, :], AF.Silu, bias=bc)),
                reads=[ps_b, bias[1]], writes=[bb])
            tgt = "sp_u" if j < 16 else "sp_z"
            jj = j if j < 16 else j - 16
            dst = b.dram[tgt][jj * 128:(jj + 1) * 128, fc * FCW:(fc + 1) * FCW]
            p.op("y", (lambda q, s=h, d=dst: q.dma_start(out=d, in_=s[:, :])),
                 reads=[bb], writes=[p.db((tgt, jj, fc))])

        b.linear("w0_in", b.xn_in(), cb_win, 1024, 4096)

    with p.phase():
        u_t = [p.sb([128, T], BF16) for _ in range(16)]
        for c in range(16):
            h, bb = u_t[c]
            src = b.dram["sp_u"][c * 128:(c + 1) * 128, :]
            p.op("y", (lambda q, d=h, s=src: q.dma_start(out=d[:, :], in_=s)),
                 reads=[p.db(("sp_u", c, fc2)) for fc2 in range(NFC)],
                 writes=[bb])
        u_in = [((lambda fc, h=h: h[:, fc * FCW:(fc + 1) * FCW]), bb)
                for (h, bb) in u_t]

        wxp = p.sb([128, 16 * 32], BF16)
        w_d = b.dram["w0_xp"]
        for gi, cg0 in enumerate((0, 8)):
            src = AP(tensor=w_d.tensor, offset=cg0 * 128 * 32,
                     ap=[[32, 128], [128 * 32, 8], [1, 32]])
            stage = None
            sview = 8
            b.wload(stage, wxp[0][:, cg0 * 32:(cg0 + 8) * 32], wxp[1], src,
                    8 * 32, view=sview)
        xd_ev = p.sb([32, FCW], BF16)
        bxp = p.banks[4]
        for fc in range(NFC):
            def emit(q, fc=fc):
                ins = None
                for c in range(16):
                    ins = q.matmul(bxp[0][0:32, :], wxp[0][:, c * 32:(c + 1) * 32],
                                   u_t[c][0][:, fc * FCW:(fc + 1) * FCW],
                                   start=(c == 0), stop=(c == 15))
                return ins
            p.op("t", emit, reads=[wxp[1]] + [bb for (_, bb) in u_t],
                 writes=[bxp[1]])
            h, bb = xd_ev
            bias = b.col("L0_b_xproj", rows=32)
            p.op("s", (lambda q, d=h, bc=bias[0]: q.activation(
                d[:, :], bxp[0][0:32, :], AF.Identity, bias=bc)),
                reads=[bxp[1], bias[1]], writes=[bb])
            dst = b.dram["sp_bc"][:, fc * FCW:(fc + 1) * FCW]
            p.op("y", (lambda q, s=h, d=dst: q.dma_start(out=d, in_=s[:, :])),
                 reads=[bb], writes=[p.db(("sp_bc", fc))])

        sp_tiles = [p.sb([128, FCW], F32) for _ in range(4)]
        b.linear("w0_dt", u_in, _softplus_cb(b, sp_tiles, "L0_b_dt"),
                 2048, 2048)

    emit_scan(b, mode=0, aname="a0", src_key=src_key)


def emit_mamba1(b, src_key):
    p = b.p
    with p.phase():
        ev = [p.sb([128, FCW], BF16) for _ in range(2)]

        def cb_win(j, fc, ps_h, ps_b):
            h, bb = ev[(j * NFC + fc) % 2]
            if j < 16:
                p.op("s", (lambda q, d=h, ps=ps_h: q.activation(
                    d[:, :], ps[:, :], AF.Copy)), reads=[ps_b], writes=[bb])
                tgt, jj = "sp_u", j
            else:
                p.op("s", (lambda q, d=h, ps=ps_h: q.activation(
                    d[:, :], ps[:, :], AF.Silu)), reads=[ps_b], writes=[bb])
                tgt, jj = "sp_z", j - 16
            dst = b.dram[tgt][jj * 128:(jj + 1) * 128, fc * FCW:(fc + 1) * FCW]
            p.op("y", (lambda q, s=h, d=dst: q.dma_start(out=d, in_=s[:, :])),
                 reads=[bb], writes=[p.db((tgt, jj, fc))])

        b.linear("w1_in", b.xn_in(), cb_win, 1024, 4096)

    with p.phase():
        wbc = p.sb([128, 8 * 32], BF16)
        w_d = b.dram["w1_bc"]
        src = AP(tensor=w_d.tensor, offset=0, ap=[[32, 128], [128 * 32, 8], [1, 32]])
        b.wload(None, wbc[0][:, :], wbc[1], src, 8 * 32,
                view=8)
        xd_ev = p.sb([32, FCW], BF16)
        bxp = p.banks[4]
        for fc in range(NFC):
            def emit(q, fc=fc):
                ins = None
                for c in range(8):
                    ins = q.matmul(bxp[0][0:32, :], wbc[0][:, c * 32:(c + 1) * 32],
                                   b.xn[c][0][:, fc * FCW:(fc + 1) * FCW],
                                   start=(c == 0), stop=(c == 7))
                return ins
            p.op("t", emit, reads=[wbc[1]] + [bb for (_, bb) in b.xn],
                 writes=[bxp[1]])
            h, bb = xd_ev
            p.op("s", (lambda q, d=h: q.activation(d[:, :], bxp[0][0:32, :],
                                                   AF.Copy)),
                 reads=[bxp[1]], writes=[bb])
            dst = b.dram["sp_bc"][:, fc * FCW:(fc + 1) * FCW]
            p.op("y", (lambda q, s=h, d=dst: q.dma_start(out=d, in_=s[:, :])),
                 reads=[bb], writes=[p.db(("sp_bc", fc))])

        sp_tiles = [p.sb([128, FCW], F32) for _ in range(4)]
        b.linear("w1_dt", b.xn_in(), _softplus_cb(b, sp_tiles, "L1_dtsum"),
                 1024, 2048)

    emit_scan(b, mode=1, aname="a1", src_key=src_key)


def emit_scan(b, mode, aname, src_key):
    p = b.p
    p.paranoid = True
    with p.phase():
        a_sb = p.sb([128, 256], F32)
        p.op("y", (lambda q: q.dma_start(out=a_sb[0][:, :], in_=b.dram[aname])),
             writes=[a_sb[1]])
        p.op("s", (lambda q: q.activation(a_sb[0][:, :], a_sb[0][:, :], AF.Exp)),
             reads=[a_sb[1]], writes=[a_sb[1]])
        p.op("s", (lambda q, m=(-1.0 if mode == 0 else 0.5):
                   q.activation(a_sb[0][:, :], a_sb[0][:, :], AF.Identity,
                                scale=m)),
             reads=[a_sb[1]], writes=[a_sb[1]])

        wname = "w0_out" if mode == 0 else "w1_out"
        w_d = b.dram[wname]
        wout = [p.sb([128, 16 * 128], BF16) for _ in range(8)]
        for j in range(8):
            for cg0 in (0, 4, 8, 12):
                wsrc = AP(tensor=w_d.tensor,
                          offset=cg0 * 128 * 1024 + j * 128,
                          ap=[[1024, 128], [128 * 1024, 4], [1, 128]])
                b.wload(None, wout[j][0][:, cg0 * 128:(cg0 + 4) * 128],
                        wout[j][1], wsrc, 4 * 128, view=4)

        hstate = [p.sb([128, 16], F32) for _ in range(16)]
        for i in range(16):
            p.op("v", (lambda q, h=hstate[i][0]: q.memset(h[:, :], 0.0)),
                 writes=[hstate[i][1]])

        u_c = [p.sb([128, FCW], BF16) for _ in range(2)]
        dt_c = [p.sb([128, FCW], F32) for _ in range(2)]
        z_c = [p.sb([128, FCW], BF16) for _ in range(2)]
        bbc = [p.sb([128, FCW], BF16) for _ in range(16)]
        cbc = [p.sb([128, FCW], BF16) for _ in range(16)]
        dA = [[p.sb([128, FCW], F32) for _ in range(2)] for _ in range(2)]
        bt = [[p.sb([128, FCW], F32) for _ in range(2)] for _ in range(2)]
        ht = [[p.sb([128, FCW], F32) for _ in range(2)] for _ in range(2)]
        if mode == 0:
            dA2 = [[p.sb([128, FCW], F32) for _ in range(2)]
                   for _ in range(2)]
            rr = None
        else:
            dA2 = None
            rr = [[p.sb([128, FCW], F32) for _ in range(2)] for _ in range(2)]
        tmp = [p.sb([128, FCW], F32) for _ in range(2)]
        du = [p.sb([128, FCW], F32) for _ in range(2)]
        yac = [p.sb([128, FCW], F32) for _ in range(2)]
        yg = [p.sb([128, FCW], BF16) for _ in range(16)]
        yg2 = [p.sb([128, FCW], BF16) for _ in range(16)] if mode == 0 \
            else None

        gn = None
        if mode == 1:
            gn = dict(
                sq=[p.sb([128, FCW], BF16) for _ in range(2 if GN_HARD else 1)],
                sq2=([p.sb([128, FCW], BF16) for _ in range(2)]
                     if GN_HARD else None),
                rws=[p.sb([1, FCW], F32) for _ in range(7 if GN_HARD else 6)],
                nt=[p.sb([128, FCW], BF16) for _ in range(16)],
                t1=p.sb([128, FCW], F32),
            )

        for tc in range(NFC):
            fsl = slice(tc * FCW, (tc + 1) * FCW)
            for s_i in range(16):
                for (tiles, row) in ((bbc, s_i), (cbc, 16 + s_i)):
                    h, bb = tiles[s_i]
                    src = b.dram["sp_bc"][row:row + 1, fsl].broadcast_to(
                        (128, FCW))
                    p.op("y", (lambda q, d=h, s=src: q.dma_start(
                        out=d[:, :], in_=s)),
                        reads=[p.db(("sp_bc", tc))], writes=[bb])
            for ip in range(8):
                ii = (2 * ip, 2 * ip + 1)
                for k in range(2):
                    i = ii[k]
                    for (tt, dname) in ((u_c[k], "sp_u"), (dt_c[k], "sp_dt"),
                                        (z_c[k], "sp_z")):
                        h2, bb2 = tt
                        src = b.dram[dname][i * 128:(i + 1) * 128, fsl]
                        p.op("y", (lambda q, d=h2, s=src: q.dma_start(
                            out=d[:, :], in_=s)),
                            reads=[p.db((dname, i, tc))], writes=[bb2])
                for k in range(2):
                    p.op("v", (lambda q, k=k: q.tensor_tensor(
                        du[k][0][:, :], dt_c[k][0][:, :], u_c[k][0][:, :],
                        ALU.mult)),
                        reads=[u_c[k][1], dt_c[k][1]], writes=[du[k][1]])
                for k in range(2):
                    p.op("v", (lambda q, k=k: q.memset(yac[k][0][:, :], 0.0)),
                         writes=[yac[k][1]])
                if mode == 0:
                    for k in range(2):
                        a2 = a_sb[0][:, ii[k] * 16:ii[k] * 16 + 1]
                        p.op("s", (lambda q, k=k, a2=a2: q.activation(
                            dA[k][0][0][:, :], dt_c[k][0][:, :], AF.Exp,
                            scale=a2)),
                            reads=[dt_c[k][1], a_sb[1]], writes=[dA[k][0][1]])
                    for k in range(2):
                        p.op("y", (lambda q, k=k: q.dma_start(
                            out=dA2[k][0][0][:, :], in_=dA[k][0][0][:, :])),
                            reads=[dA[k][0][1]], writes=[dA2[k][0][1]])
                for s_i in range(16):
                    sp = s_i % 2
                    ac = [a_sb[0][:, ii[k] * 16 + s_i:ii[k] * 16 + s_i + 1]
                          for k in range(2)]
                    if mode == 0:
                        if s_i + 1 < 16:
                            spn = (s_i + 1) % 2
                            for k in range(2):
                                a2 = a_sb[0][:, ii[k] * 16 + s_i + 1:
                                             ii[k] * 16 + s_i + 2]
                                p.op("s", (lambda q, k=k, spn=spn, a2=a2:
                                           q.activation(dA[k][spn][0][:, :],
                                                        dt_c[k][0][:, :],
                                                        AF.Exp, scale=a2)),
                                     reads=[dt_c[k][1], a_sb[1]],
                                     writes=[dA[k][spn][1]])
                            for k in range(2):
                                p.op("y", (lambda q, k=k, spn=spn: q.dma_start(
                                    out=dA2[k][spn][0][:, :],
                                    in_=dA[k][spn][0][:, :])),
                                    reads=[dA[k][spn][1]],
                                    writes=[dA2[k][spn][1]])
                        for k in range(2):
                            p.op("v", (lambda q, k=k, sp=sp, s1=bbc[s_i][0]:
                                       q.tensor_tensor(bt[k][sp][0][:, :],
                                                       du[k][0][:, :],
                                                       s1[:, :], ALU.mult)),
                                 reads=[du[k][1], bbc[s_i][1]],
                                 writes=[bt[k][sp][1]])
                    else:
                        for k in range(2):
                            p.op("s", (lambda q, k=k, sp=sp, a2=ac[k]:
                                       q.activation(rr[k][sp][0][:, :],
                                                    dt_c[k][0][:, :], AF.Ln,
                                                    scale=a2, bias=1.0)),
                                 reads=[dt_c[k][1], a_sb[1]],
                                 writes=[rr[k][sp][1]])
                        for k in range(2):
                            p.op("s", (lambda q, k=k, sp=sp: q.activation(
                                rr[k][sp][0][:, :], rr[k][sp][0][:, :],
                                AF.Exp, scale=-1.0)),
                                reads=[rr[k][sp][1]], writes=[rr[k][sp][1]])
                        for k in range(2):
                            p.op("v", (lambda q, k=k, sp=sp: q.tensor_scalar(
                                dA[k][sp][0][:, :], rr[k][sp][0][:, :], 2.0,
                                -1.0, ALU.mult, ALU.add)),
                                reads=[rr[k][sp][1]], writes=[dA[k][sp][1]])
                        for k in range(2):
                            p.op("v", (lambda q, k=k, sp=sp, s1=bbc[s_i][0]:
                                       q.tensor_tensor(bt[k][sp][0][:, :],
                                                       du[k][0][:, :],
                                                       s1[:, :], ALU.mult)),
                                 reads=[du[k][1], bbc[s_i][1]],
                                 writes=[bt[k][sp][1]])
                        for k in range(2):
                            p.op("v", (lambda q, k=k, sp=sp: q.tensor_tensor(
                                bt[k][sp][0][:, :], bt[k][sp][0][:, :],
                                rr[k][sp][0][:, :], ALU.mult)),
                                reads=[rr[k][sp][1], bt[k][sp][1]],
                                writes=[bt[k][sp][1]])
                    for k in range(2):
                        hs_h, hs_b = hstate[ii[k]]
                        da_use = dA2 if mode == 0 else dA
                        p.op("v", (lambda q, k=k, sp=sp, hs=hs_h, s_i=s_i,
                                   du2=da_use: q.tensor_tensor_scan(
                                       ht[k][sp][0][:, :], du2[k][sp][0][:, :],
                                       bt[k][sp][0][:, :],
                                       hs[:, s_i:s_i + 1], ALU.mult, ALU.add)),
                             reads=[da_use[k][sp][1], bt[k][sp][1], hs_b],
                             writes=[ht[k][sp][1]])
                    for k in range(2):
                        hs_h, hs_b = hstate[ii[k]]
                        p.op("s", (lambda q, k=k, sp=sp, hs=hs_h, s_i=s_i:
                                   q.copy(hs[:, s_i:s_i + 1],
                                          ht[k][sp][0][:, FCW - 1:FCW])),
                             reads=[ht[k][sp][1]], writes=[hs_b])
                    for k in range(2):
                        p.op("v", (lambda q, k=k, sp=sp, c1=cbc[s_i][0]:
                                   q.tensor_tensor(tmp[k][0][:, :],
                                                   ht[k][sp][0][:, :],
                                                   c1[:, :], ALU.mult)),
                             reads=[ht[k][sp][1], cbc[s_i][1]],
                             writes=[tmp[k][1]])
                    for k in range(2):
                        p.op("v", (lambda q, k=k: q.tensor_tensor(
                            yac[k][0][:, :], yac[k][0][:, :], tmp[k][0][:, :],
                            ALU.add)),
                            reads=[tmp[k][1], yac[k][1]], writes=[yac[k][1]])
                if mode == 0:
                    for k in range(2):
                        dcol = b.col("L0_D", ii[k])
                        p.op("v", (lambda q, k=k, dc=dcol[0]:
                                   q.scalar_tensor_tensor(
                                       yac[k][0][:, :], u_c[k][0][:, :], dc,
                                       yac[k][0][:, :], ALU.mult, ALU.add)),
                             reads=[u_c[k][1], dcol[1], yac[k][1]],
                             writes=[yac[k][1]])
                for k in range(2):
                    gh, gb = yg[ii[k]]
                    p.op("v", (lambda q, k=k, gh=gh: q.tensor_tensor(
                        gh[:, :], yac[k][0][:, :], z_c[k][0][:, :], ALU.mult)),
                        reads=[yac[k][1], z_c[k][1]], writes=[gb])
            if mode == 1:
                (_gn_chunk if GN_HARD else _gn_chunk_v0)(b, gn, yg)
            if mode == 0:
                src_tiles = b.launder(yg, yg2)
            else:
                src_tiles = b.launder(gn["nt"], yg)
            for j in range(8):
                wt, wbuf = wout[j]
                bh, bb = p.banks[6 + j % 2]

                def emit(q, j=j, bh=bh, wt=wt, st=src_tiles):
                    ins = None
                    for c in range(16):
                        ins = q.matmul(bh[:, :],
                                       wt[:, c * 128:(c + 1) * 128],
                                       st[c][0][:, :], start=(c == 0),
                                       stop=(c == 15))
                    return ins
                p.op("t", emit, reads=[wbuf] + [t[1] for t in src_tiles],
                     writes=[bb])
                if mode == 0:
                    bias = b.col("L0_b_out", j)
                    b.merge(j, tc, bh, bb, scalar=bias[0], scalar_buf=bias[1],
                            op0=ALU.add, src_key=src_key)
                else:
                    b.merge(j, tc, bh, bb, src_key=src_key)
        p.paranoid = False


def _gn_chunk(b, gn, yg):
    p = b.p
    old_par = p.paranoid
    p.paranoid = True
    ones_cb_h, ones_cb_b = b.ones_col_bf
    ones_r, ones_rb = b.ones_row
    (rmu, rmub), (rex, rexb), (rm2, rm2b), (rvar, rvarb), (rstd, rstdb), \
        (rbnr, rbnrb), (pad, padb) = gn["rws"]
    rvL, rvLb = rm2, rm2b
    lnv, lnvb = rex, rexb
    rsL, rsLb = rvar, rvarb
    bsx = p.banks[4]; bsq = p.banks[5]; ba = p.banks[6]; bbn = p.banks[7]
    for i in range(16):
        gh, gb = yg[i]
        sh, sb2 = gn["sq"][i % 2]
        s2h, s2b = gn["sq2"][i % 2]
        p.op("s", (lambda q, d=sh, s=gh: q.activation(d[:, :], s[:, :],
                                                      AF.Square)),
             reads=[gb], writes=[sb2])
        p.op("y", (lambda q, s=sh, d=s2h: q.dma_start(out=d[:, :], in_=s[:, :])),
             reads=[sb2], writes=[s2b])
        p.op("t", (lambda q, s=gh, i=i: q.matmul(
            bsx[0][0:1, :], ones_cb_h[:, :], s[:, :], start=(i == 0),
            stop=(i == 15))), reads=[gb, ones_cb_b], writes=[bsx[1]])
        p.op("t", (lambda q, s=s2h, i=i: q.matmul(
            bsq[0][0:1, :], ones_cb_h[:, :], s[:, :], start=(i == 0),
            stop=(i == 15))), reads=[s2b, ones_cb_b], writes=[bsq[1]])
    p.op("v", (lambda q: q.tensor_scalar(rmu[0:1, :], bsx[0][0:1, :], 1.0 / 2048,
                                         None, ALU.mult)),
         reads=[bsx[1]], writes=[rmub])
    p.op("v", (lambda q: q.tensor_scalar(rex[0:1, :], bsq[0][0:1, :], 1.0 / 2048,
                                         None, ALU.mult)),
         reads=[bsq[1]], writes=[rexb])
    p.op("v", (lambda q: q.tensor_tensor(rm2[0:1, :], rmu[0:1, :], rmu[0:1, :],
                                         ALU.mult)), reads=[rmub], writes=[rm2b])
    p.op("v", (lambda q: q.memset(pad[0:1, :], 0.0)), writes=[padb])
    p.op("v", (lambda q: q.tensor_tensor(rvar[0:1, :], rex[0:1, :], rm2[0:1, :],
                                         ALU.subtract)), reads=[rexb, rm2b], writes=[rvarb])
    p.op("y", (lambda q: q.dma_start(out=rvL[0:1, :], in_=rvar[0:1, :])),
         reads=[rvarb], writes=[rvLb])
    p.op("s", (lambda q: q.activation(lnv[0:1, :], rvL[0:1, :], AF.Ln, bias=1e-5)),
         reads=[rvLb], writes=[lnvb])
    p.op("s", (lambda q: q.activation(pad[0:1, :], pad[0:1, :], AF.Identity)),
         reads=[padb], writes=[padb])
    p.op("s", (lambda q: q.activation(rstd[0:1, :], lnv[0:1, :], AF.Exp, scale=-0.5)),
         reads=[lnvb], writes=[rstdb])
    p.op("y", (lambda q: q.dma_start(out=rsL[0:1, :], in_=rstd[0:1, :])),
         reads=[rstdb], writes=[rsLb])
    p.op("v", (lambda q: q.scalar_tensor_tensor(rbnr[0:1, :], rmu[0:1, :], -1.0,
                                                rsL[0:1, :], ALU.mult, ALU.mult)),
         reads=[rmub, rsLb], writes=[rbnrb])
    p.op("y", (lambda q: q.dma_start(out=lnv[0:1, :], in_=rbnr[0:1, :])),
         reads=[rbnrb], writes=[lnvb])
    p.op("t", (lambda q: q.matmul(ba[0][:, :], ones_r[:, :], rsL[0:1, :],
                                  start=True, stop=True)),
         reads=[rsLb, ones_rb], writes=[ba[1]])
    p.op("t", (lambda q: q.matmul(bbn[0][:, :], ones_r[:, :], lnv[0:1, :],
                                  start=True, stop=True)),
         reads=[lnvb, ones_rb], writes=[bbn[1]])
    th, tb = gn["t1"]
    for i in range(16):
        gh, gb = yg[i]
        nh, nb = gn["nt"][i]
        p.op("v", (lambda q, s=gh: q.tensor_tensor(
            th[:, :], s[:, :], ba[0][:, :], ALU.mult)),
            reads=[gb, ba[1]], writes=[tb])
        p.op("v", (lambda q: q.tensor_tensor(th[:, :], th[:, :], bbn[0][:, :],
                                             ALU.add)),
             reads=[bbn[1], tb], writes=[tb])
        gc = b.col("L1_gamma", i)
        bc = b.col("L1_beta", i)
        p.op("s", (lambda q, d=nh, g2=gc[0], b2=bc[0]: q.activation(
            d[:, :], th[:, :], AF.Identity, scale=g2, bias=b2)),
            reads=[tb, gc[1], bc[1]], writes=[nb])
    p.paranoid = old_par




def emit_layernorm_v0(b, src_key):
    p = b.p
    x_d = b.dram[src_key]
    ones_c, ones_cb = b.ones_col
    ones_r, ones_rb = b.ones_row
    with p.phase():
        xt = [p.sb([128, FCW], F32) for _ in range(8)]
        sq = [p.sb([128, FCW], F32) for _ in range(2)]
        rws = [p.sb([1, FCW], F32) for _ in range(6)]
        (rmu, rmub), (rex, rexb), (rm2, rm2b), (rvar, rvarb), (rstd, rstdb), (rbn, rbnb) = rws
        bsx = p.banks[4]
        bsq = p.banks[5]
        ba = p.banks[6]
        bbc = p.banks[7]
        for fc in range(NFC):
            for i in range(8):
                h, bb = xt[i]
                src = x_d[i * 128:(i + 1) * 128, fc * FCW:(fc + 1) * FCW]
                p.op("y", (lambda q, d=h, s=src: q.dma_start(out=d[:, :], in_=s)),
                     reads=[p.db((src_key, i, fc))], writes=[bb])
            for i in range(8):
                h, bb = xt[i]
                sh, sb2 = sq[i % 2]
                p.op("s", (lambda q, d=sh, s=h: q.activation(
                    d[:, :], s[:, :], AF.Square)), reads=[bb], writes=[sb2])
                p.op("t", (lambda q, s=h, i=i: q.matmul(
                    bsx[0][0:1, :], ones_c[:, :], s[:, :], start=(i == 0),
                    stop=(i == 7))), reads=[bb, ones_cb], writes=[bsx[1]])
                p.op("t", (lambda q, s=sh, i=i: q.matmul(
                    bsq[0][0:1, :], ones_c[:, :], s[:, :], start=(i == 0),
                    stop=(i == 7))), reads=[sb2, ones_cb], writes=[bsq[1]])
            p.op("v", (lambda q: q.tensor_scalar(rmu[0:1, :], bsx[0][0:1, :],
                                                 1.0 / D, None, ALU.mult)),
                 reads=[bsx[1]], writes=[rmub])
            p.op("v", (lambda q: q.tensor_scalar(rex[0:1, :], bsq[0][0:1, :],
                                                 1.0 / D, None, ALU.mult)),
                 reads=[bsq[1]], writes=[rexb])
            p.op("s", (lambda q: q.activation(rm2[0:1, :], rmu[0:1, :], AF.Square)),
                 reads=[rmub], writes=[rm2b])
            p.op("v", (lambda q: q.tensor_tensor(rvar[0:1, :], rex[0:1, :],
                                                 rm2[0:1, :], ALU.subtract)),
                 reads=[rexb, rm2b], writes=[rvarb])
            p.op("s", (lambda q: q.activation(rvar[0:1, :], rvar[0:1, :], AF.Ln,
                                              bias=1e-5)), reads=[rvarb], writes=[rvarb])
            p.op("s", (lambda q: q.activation(rstd[0:1, :], rvar[0:1, :], AF.Exp,
                                              scale=-0.5)), reads=[rvarb], writes=[rstdb])
            p.op("v", (lambda q: q.scalar_tensor_tensor(
                rbn[0:1, :], rmu[0:1, :], -1.0, rstd[0:1, :], ALU.mult, ALU.mult)),
                reads=[rmub, rstdb], writes=[rbnb])
            p.op("t", (lambda q: q.matmul(ba[0][:, :], ones_r[:, :], rstd[0:1, :],
                                          start=True, stop=True)),
                 reads=[rstdb, ones_rb], writes=[ba[1]])
            p.op("t", (lambda q: q.matmul(bbc[0][:, :], ones_r[:, :], rbn[0:1, :],
                                          start=True, stop=True)),
                 reads=[rbnb, ones_rb], writes=[bbc[1]])
            for i in range(8):
                h, bb = xt[i]
                sh, sb2 = sq[i % 2]
                xh, xb = b.xn[i]
                p.op("v", (lambda q, s=h, d=sh: q.tensor_tensor(
                    d[:, :], s[:, :], ba[0][:, :], ALU.mult)),
                    reads=[bb, ba[1]], writes=[sb2])
                p.op("v", (lambda q, s=sh, d=xh, fc=fc: q.tensor_tensor(
                    d[:, fc * FCW:(fc + 1) * FCW], s[:, :], bbc[0][:, :],
                    ALU.add)), reads=[sb2, bbc[1]], writes=[xb])




def _gn_chunk_v0(b, gn, yg):
    p = b.p
    old_par = p.paranoid
    p.paranoid = True
    ones_cb_h, ones_cb_b = b.ones_col_bf
    ones_r, ones_rb = b.ones_row
    (rmu, rmub), (rex, rexb), (rm2, rm2b), (rvar, rvarb), (rstd, rstdb), (rbnr, rbnrb) = gn["rws"]
    bsx = p.banks[4]; bsq = p.banks[5]; ba = p.banks[6]; bbn = p.banks[7]
    sh, sb2 = gn["sq"][0]
    for i in range(16):
        gh, gb = yg[i]
        p.op("s", (lambda q, s=gh: q.activation(sh[:, :], s[:, :], AF.Square)),
             reads=[gb], writes=[sb2])
        p.op("t", (lambda q, s=gh, i=i: q.matmul(
            bsx[0][0:1, :], ones_cb_h[:, :], s[:, :], start=(i == 0),
            stop=(i == 15))), reads=[gb, ones_cb_b], writes=[bsx[1]])
        p.op("t", (lambda q, i=i: q.matmul(
            bsq[0][0:1, :], ones_cb_h[:, :], sh[:, :], start=(i == 0),
            stop=(i == 15))), reads=[sb2, ones_cb_b], writes=[bsq[1]])
    p.op("v", (lambda q: q.tensor_scalar(rmu[0:1, :], bsx[0][0:1, :], 1.0 / 2048,
                                         None, ALU.mult)),
         reads=[bsx[1]], writes=[rmub])
    p.op("v", (lambda q: q.tensor_scalar(rex[0:1, :], bsq[0][0:1, :], 1.0 / 2048,
                                         None, ALU.mult)),
         reads=[bsq[1]], writes=[rexb])
    p.op("s", (lambda q: q.activation(rm2[0:1, :], rmu[0:1, :], AF.Square)),
         reads=[rmub], writes=[rm2b])
    p.op("v", (lambda q: q.tensor_tensor(rvar[0:1, :], rex[0:1, :], rm2[0:1, :],
                                         ALU.subtract)), reads=[rexb, rm2b], writes=[rvarb])
    p.op("s", (lambda q: q.activation(rvar[0:1, :], rvar[0:1, :], AF.Ln, bias=1e-5)),
         reads=[rvarb], writes=[rvarb])
    p.op("s", (lambda q: q.activation(rstd[0:1, :], rvar[0:1, :], AF.Exp, scale=-0.5)),
         reads=[rvarb], writes=[rstdb])
    p.op("v", (lambda q: q.scalar_tensor_tensor(rbnr[0:1, :], rmu[0:1, :], -1.0,
                                                rstd[0:1, :], ALU.mult, ALU.mult)),
         reads=[rmub, rstdb], writes=[rbnrb])
    p.op("t", (lambda q: q.matmul(ba[0][:, :], ones_r[:, :], rstd[0:1, :],
                                  start=True, stop=True)),
         reads=[rstdb, ones_rb], writes=[ba[1]])
    p.op("t", (lambda q: q.matmul(bbn[0][:, :], ones_r[:, :], rbnr[0:1, :],
                                  start=True, stop=True)),
         reads=[rbnrb, ones_rb], writes=[bbn[1]])
    th, tb = gn["t1"]
    for i in range(16):
        gh, gb = yg[i]
        nh, nb = gn["nt"][i]
        p.op("v", (lambda q, s=gh: q.tensor_tensor(
            th[:, :], s[:, :], ba[0][:, :], ALU.mult)),
            reads=[gb, ba[1]], writes=[tb])
        p.op("v", (lambda q: q.tensor_tensor(th[:, :], th[:, :], bbn[0][:, :],
                                             ALU.add)),
             reads=[bbn[1], tb], writes=[tb])
        gc = b.col("L1_gamma", i)
        bc = b.col("L1_beta", i)
        p.op("s", (lambda q, d=nh, g2=gc[0], b2=bc[0]: q.activation(
            d[:, :], th[:, :], AF.Identity, scale=g2, bias=b2)),
            reads=[tb, gc[1], bc[1]], writes=[nb])
    p.paranoid = old_par




def emit_ffn_ternary(b, pref, src_key, dst_key):
    p = b.p
    lp = "f0" if pref == "L0" else "f2"
    with p.phase():
        ha = [p.sb([128, T], BF16) for _ in range(32)]
        t1e = [p.sb([128, FCW], BF16) for _ in range(2)]
        w1d, w2d = b.dram[lp + "_w1"], b.dram[lp + "_w2"]
        wt1 = [p.sb([128, 8 * 128], BF16) for _ in range(2)]
        wt2 = [p.sb([128, 8 * 128], BF16) for _ in range(2)]
        for j in range(32):
            b1h, b1b = wt1[j % 2]
            b2h, b2b = wt2[j % 2]
            for wi2, (wd, wh, wb2) in enumerate(((w1d, b1h, b1b),
                                                 (w2d, b2h, b2b))):
                for hf in range(2):
                    src = AP(tensor=wd.tensor,
                             offset=hf * 4 * 128 * 4096 + j * 128,
                             ap=[[4096, 128], [128 * 4096, 4], [1, 128]])
                    stage = None
                    sview = 4
                    b.wload(stage, wh[:, hf * 512:(hf + 1) * 512], wb2, src,
                            4 * 128, view=sview)
            for fc in range(NFC):
                p1h, p1b = p.banks[fc % 2]
                p2h, p2b = p.banks[2 + fc % 2]
                fsl = slice(fc * FCW, (fc + 1) * FCW)

                def emit1(q, wh=b1h, ph=p1h, fsl=fsl):
                    ins = None
                    for c in range(8):
                        ins = q.matmul(ph[:, :], wh[:, c * 128:(c + 1) * 128],
                                       b.xn[c][0][:, fsl], start=(c == 0),
                                       stop=(c == 7))
                    return ins

                def emit2(q, wh=b2h, ph=p2h, fsl=fsl):
                    ins = None
                    for c in range(8):
                        ins = q.matmul(ph[:, :], wh[:, c * 128:(c + 1) * 128],
                                       b.xn[c][0][:, fsl], start=(c == 0),
                                       stop=(c == 7))
                    return ins

                p.op("t", emit1, reads=[b1b] + [bb for (_, bb) in b.xn],
                     writes=[p1b])
                p.op("t", emit2, reads=[b2b] + [bb for (_, bb) in b.xn],
                     writes=[p2b])
                th, tb = t1e[fc % 2]
                s1 = b.col(pref + "_w1s", j)
                s2 = b.col(pref + "_w2s", j)
                p.op("s", (lambda q, d=th, ph=p1h, sc=s1[0]: q.activation(
                    d[:, :], ph[:, :], AF.Silu, scale=sc)),
                    reads=[p1b, s1[1]], writes=[tb])
                hh, hb = ha[j]
                p.op("v", (lambda q, d=hh, ph=p2h, s=th, sc=s2[0], fsl=fsl:
                           q.scalar_tensor_tensor(d[:, fsl], ph[:, :], sc,
                                                  s[:, :], ALU.mult, ALU.mult)),
                     reads=[p2b, s2[1], tb], writes=[hb])

        ha_in = [((lambda fc, h=h: h[:, fc * FCW:(fc + 1) * FCW]), bb)
                 for (h, bb) in ha]
        hb_ev = [p.sb([128, FCW], BF16) for _ in range(2)]

        def cb_h1(j, fc, ps_h, ps_b):
            h, bb = hb_ev[(j * NFC + fc) % 2]
            sc = b.col(pref + "_h1s", j)
            p.op("s", (lambda q, d=h, ps=ps_h, sc2=sc[0]: q.activation(
                d[:, :], ps[:, :], AF.Gelu, scale=sc2)),
                reads=[ps_b, sc[1]], writes=[bb])
            dst = b.dram["sp_hb"][j * 128:(j + 1) * 128,
                                  fc * FCW:(fc + 1) * FCW]
            p.op("y", (lambda q, s=h, d=dst: q.dma_start(out=d, in_=s[:, :])),
                 reads=[bb], writes=[p.db(("sp_hb", j, fc))])

        b.linear(lp + "_h1", ha_in, cb_h1, 4096, 4096, banks=(4, 5))

    hc_ev = [p.sb([128, FCW], BF16) for _ in range(2)]

    def cb_h2(j, fc, ps_h, ps_b):
        h, bb = hc_ev[(j * NFC + fc) % 2]
        sc = b.col(pref + "_h2s", j)
        b.p.op("s", (lambda q, d=h, ps=ps_h, sc2=sc[0]: q.activation(
            d[:, :], ps[:, :], AF.Gelu, scale=sc2)),
            reads=[ps_b, sc[1]], writes=[bb])
        dst = b.dram["sp_hc"][j * 128:(j + 1) * 128, fc * FCW:(fc + 1) * FCW]
        b.p.op("y", (lambda q, s=h, d=dst: q.dma_start(out=d, in_=s[:, :])),
               reads=[bb], writes=[b.p.db(("sp_hc", j, fc))])

    b.linear_stream(lp + "_h2", "sp_hb", cb_h2, 4096, 4096)

    def cb_w3(j, fc, ps_h, ps_b):
        sc = b.col(pref + "_w3s", j)
        b.merge(j, fc, ps_h, ps_b, scalar=sc[0], scalar_buf=sc[1],
                op0=ALU.mult, src_key=src_key, dst_key=dst_key)

    b.linear_stream(lp + "_w3", "sp_hc", cb_w3, 4096, 1024)


def emit_ffn_quant(b, src_key, dst_key):
    p = b.p
    with p.phase():
        hq = [p.sb([128, T], BF16) for _ in range(24)]
        t1e = [p.sb([128, FCW], BF16) for _ in range(2)]
        wgd, wud = b.dram["f1_wg"], b.dram["f1_wu"]
        wt1 = [p.sb([128, 8 * 128], BF16) for _ in range(2)]
        wt2 = [p.sb([128, 8 * 128], BF16) for _ in range(2)]
        for j in range(24):
            b1h, b1b = wt1[j % 2]
            b2h, b2b = wt2[j % 2]
            for wi2, (wd, wh, wb2) in enumerate(((wgd, b1h, b1b),
                                                 (wud, b2h, b2b))):
                for hf in range(2):
                    src = AP(tensor=wd.tensor,
                             offset=hf * 4 * 128 * 3072 + j * 128,
                             ap=[[3072, 128], [128 * 3072, 4], [1, 128]])
                    stage = None
                    sview = 4
                    b.wload(stage, wh[:, hf * 512:(hf + 1) * 512], wb2, src,
                            4 * 128, view=sview)
            for fc in range(NFC):
                p1h, p1b = p.banks[fc % 2]
                p2h, p2b = p.banks[2 + fc % 2]
                fsl = slice(fc * FCW, (fc + 1) * FCW)

                def emit1(q, wh=b1h, ph=p1h, fsl=fsl):
                    ins = None
                    for c in range(8):
                        ins = q.matmul(ph[:, :], wh[:, c * 128:(c + 1) * 128],
                                       b.xn[c][0][:, fsl], start=(c == 0),
                                       stop=(c == 7))
                    return ins

                def emit2(q, wh=b2h, ph=p2h, fsl=fsl):
                    ins = None
                    for c in range(8):
                        ins = q.matmul(ph[:, :], wh[:, c * 128:(c + 1) * 128],
                                       b.xn[c][0][:, fsl], start=(c == 0),
                                       stop=(c == 7))
                    return ins

                p.op("t", emit1, reads=[b1b] + [bb for (_, bb) in b.xn],
                     writes=[p1b])
                p.op("t", emit2, reads=[b2b] + [bb for (_, bb) in b.xn],
                     writes=[p2b])
                th, tb = t1e[fc % 2]
                p.op("s", (lambda q, d=th, ph=p1h: q.activation(
                    d[:, :], ph[:, :], AF.Silu, scale=1.0 / 127.0)),
                    reads=[p1b], writes=[tb])
                hh, hb = hq[j]
                p.op("v", (lambda q, d=hh, ph=p2h, s=th, fsl=fsl:
                           q.scalar_tensor_tensor(d[:, fsl], ph[:, :],
                                                  1.0 / 127.0, s[:, :],
                                                  ALU.mult, ALU.mult)),
                     reads=[p2b, tb], writes=[hb])

        hq_in = [((lambda fc, h=h: h[:, fc * FCW:(fc + 1) * FCW]), bb)
                 for (h, bb) in hq]

        def cb_wd(j, fc, ps_h, ps_b):
            b.merge(j, fc, ps_h, ps_b, scalar=1.0 / 127.0, op0=ALU.mult,
                    src_key=src_key, dst_key=dst_key)

        b.linear("f1_wd", hq_in, cb_wd, 3072, 1024, banks=(4, 5))



def _emit_attention(b, src_key, layer):
    p = b.p
    NH = 16
    with p.phase():
        qf = [p.sb([128, T], BF16) for _ in range(8)]
        kf = [p.sb([128, T], BF16) for _ in range(8)]
        vtok = [p.sb([128, NH * 65], BF16) for _ in range(16)]
        negc = p.sb([128, NH], F32)

        with p.phase():
            if layer == 2:
                ps_e = p.banks[4]
                re_sb = p.sb([32, 16], F32)
                oh_sb = p.sb([32, 4095], F32)
                eb_ev = p.sb([16, FCW], BF16)
                p.op("y", (lambda q: q.dma_start(out=re_sb[0][:, :],
                                                 in_=b.dram["rel_emb"])),
                     writes=[re_sb[1]])
                p.op("y", (lambda q: q.dma_start(out=oh_sb[0][:, :],
                                                 in_=b.dram["onehot"])),
                     writes=[oh_sb[1]])
                for fc in range(8):
                    w = min(FCW, 4095 - fc * FCW)
                    p.op("t", (lambda q, fc=fc, w=w: q.matmul(
                        ps_e[0][0:16, 0:w], re_sb[0][:, :],
                        oh_sb[0][:, fc * FCW:fc * FCW + w],
                        start=True, stop=True)),
                        reads=[re_sb[1], oh_sb[1]], writes=[ps_e[1]])
                    p.op("s", (lambda q, w=w: q.activation(
                        eb_ev[0][0:16, 0:w], ps_e[0][0:16, 0:w], AF.Exp)),
                        reads=[ps_e[1]], writes=[eb_ev[1]])
                    dst = b.dram["sp_expb"][:, fc * FCW:fc * FCW + w]
                    p.op("y", (lambda q, d=dst, w=w: q.dma_start(
                        out=d, in_=eb_ev[0][0:16, 0:w])),
                        reads=[eb_ev[1]], writes=[p.db(("sp_expb", fc))])

                def cb_q(j, fc, ps_h, ps_b):
                    h, bb = qf[j]
                    p.op("s", (lambda q, d=h, ps=ps_h, fc=fc: q.activation(
                        d[:, fc * FCW:(fc + 1) * FCW], ps[:, :], AF.Copy)),
                        reads=[ps_b], writes=[bb])

                def cb_k(j, fc, ps_h, ps_b):
                    h, bb = kf[j]
                    p.op("s", (lambda q, d=h, ps=ps_h, fc=fc: q.activation(
                        d[:, fc * FCW:(fc + 1) * FCW], ps[:, :], AF.Copy)),
                        reads=[ps_b], writes=[bb])

                b.linear("w2_q", b.xn_in(), cb_q, 1024, 1024)
                b.linear("w2_k", b.xn_in(), cb_k, 1024, 1024)
                wv_t = [p.sb([128, 1024], BF16) for _ in range(8)]
                for c in range(8):
                    h, bb = wv_t[c]
                    src = b.dram["w2_v"][c * 128:(c + 1) * 128, :]
                    b.wload(None, h[:, :], bb, src, 1024)
                _emit_vtok(b, vtok, wv_t, 64, 1024, 0)
            else:
                wq_re = [p.sb([128, 1024], BF16) for _ in range(8)]
                wk_re = [p.sb([128, 1024], BF16) for _ in range(8)]
                wv_re = [p.sb([128, 1024], BF16) for _ in range(8)]
                with p.phase():
                    wq_t = [p.sb([128, 3072], BF16) for _ in range(2)]
                    for c in range(8):
                        h, bb = wq_t[c % 2]
                        src = b.dram["w3_qkv"][c * 128:(c + 1) * 128, :]
                        b.wload(None, h[:, :], bb, src, 3072)
                        for off, re in ((0, wq_re), (64, wk_re),
                                        (128, wv_re)):
                            dh, db2 = re[c]
                            srcap = AP(tensor=h, offset=off,
                                       ap=[[3072, 128], [192, 16], [1, 64]])
                            p.op("v", (lambda q, d=dh, s=srcap:
                                       q.tensor_copy(d[:, :], s)),
                                 reads=[bb], writes=[db2])

                def cb_q3(j, fc, ps_h, ps_b):
                    h, bb = qf[j]
                    p.op("s", (lambda q, d=h, ps=ps_h, fc=fc: q.activation(
                        d[:, fc * FCW:(fc + 1) * FCW], ps[:, :], AF.Copy)),
                        reads=[ps_b], writes=[bb])

                def cb_k3(j, fc, ps_h, ps_b):
                    h, bb = kf[j]
                    p.op("s", (lambda q, d=h, ps=ps_h, fc=fc: q.activation(
                        d[:, fc * FCW:(fc + 1) * FCW], ps[:, :], AF.Copy)),
                        reads=[ps_b], writes=[bb])

                for re, cb in ((wq_re, cb_q3), (wk_re, cb_k3)):
                    for j in range(8):
                        for fc in range(NFC):
                            ph, pb = p.banks[fc % 2]

                            def emit(q, re=re, j=j, fc=fc, ph=ph):
                                ins = None
                                for c in range(8):
                                    ins = q.matmul(
                                        ph[:, :],
                                        re[c][0][:, j * 128:(j + 1) * 128],
                                        b.xn[c][0][:, fc * FCW:(fc + 1) * FCW],
                                        start=(c == 0), stop=(c == 7))
                                return ins
                            p.op("t", emit, reads=[t[1] for t in re] +
                                 [bb for (_, bb) in b.xn], writes=[pb])
                            cb(j, fc, ph, pb)
                _emit_vtok(b, vtok, wv_re, 64, 1024, 0)

            sqt = p.sb([128, FCW], F32)
            nrows = [p.sb([1, T], F32) for _ in range(2)]
            nreds = [p.sb([1, 1], F32) for _ in range(2)]
            ones_c, ones_cb = b.ones_col
            ones_r, ones_rb = b.ones_row
            bn = p.banks[5]
            bc2 = p.banks[6]
            for hh in range(NH):
                prow = (hh % 2) * 64
                for r, tiles in ((0, qf), (1, kf)):
                    th, tb2 = tiles[hh // 2]
                    nr, nrb = nrows[r]
                    for fc in range(NFC):
                        p.op("s", (lambda q, th=th, prow=prow, fc=fc:
                                   q.activation(sqt[0][0:64, :],
                                                th[prow:prow + 64,
                                                   fc * FCW:(fc + 1) * FCW],
                                                AF.Square)),
                             reads=[tb2], writes=[sqt[1]])
                        p.op("t", (lambda q: q.matmul(
                            bn[0][0:1, :], ones_c[0:64, :],
                            sqt[0][0:64, :], start=True, stop=True)),
                            reads=[sqt[1], ones_cb], writes=[bn[1]])
                        p.op("v", (lambda q, nr=nr, fc=fc: q.tensor_copy(
                            nr[0:1, fc * FCW:(fc + 1) * FCW],
                            bn[0][0:1, :])), reads=[bn[1]],
                            writes=[nrb])
                for r in range(2):
                    p.op("v", (lambda q, r=r: q.reduce_max(
                        nreds[r][0][0:1, :], nrows[r][0][0:1, :], axis=AX.X)),
                        reads=[nrows[r][1]], writes=[nreds[r][1]], tail=True)
                p.op("v", (lambda q: q.tensor_tensor(
                    nreds[0][0][0:1, :], nreds[0][0][0:1, :],
                    nreds[1][0][0:1, :], ALU.mult)),
                    reads=[nreds[0][1], nreds[1][1]], writes=[nreds[0][1]],
                    tail=True)
                p.op("s", (lambda q: q.activation(nreds[0][0][0:1, :],
                                                  nreds[0][0][0:1, :], AF.Sqrt)),
                     reads=[nreds[0][1]], writes=[nreds[0][1]])
                p.op("t", (lambda q: q.matmul(bc2[0][:, 0:1], ones_r[:, :],
                                              nreds[0][0][0:1, :], start=True,
                                              stop=True)),
                     reads=[nreds[0][1], ones_rb], writes=[bc2[1]])
                p.op("v", (lambda q, hh=hh: q.tensor_scalar_mul(
                    negc[0][:, hh:hh + 1], bc2[0][:, 0:1], -0.125)),
                    reads=[bc2[1]], writes=[negc[1]])

        with p.phase():
            pt = [p.sb([128, T], BF16) for _ in range(2)]
            ebt = [p.sb([128, 3968], BF16) for _ in range(2)]
            ctx_sc = p.sb([128, FCW], F32)
            ctx_bf = p.sb([128, FCW], BF16)
            for hh in range(NH):
                if layer == 2:
                    eh, ebb = ebt[hh % 2]
                    src = AP(tensor=b.dram["sp_expb"].tensor, offset=hh * 4095,
                             ap=[[1, 128], [1, 3968]])
                    p.op("y", (lambda q, d=eh, s=src: q.dma_start(
                        out=d[:, :], in_=s)),
                        reads=[p.db(("sp_expb", k)) for k in range(8)],
                        writes=[ebb])
                qt, qtb = qf[hh // 2]
                kt, ktb = kf[hh // 2]
                prow = (hh % 2) * 64
                for kb in range(16):
                    pth, ptb = pt[kb % 2]
                    vt, vtb = vtok[kb]
                    for qc in range(NFC):
                        sh, sb2 = p.banks[qc]
                        p.op("t", (lambda q, sh=sh, kt=kt, qt=qt, prow=prow,
                                   kb=kb, qc=qc: q.matmul(
                                       sh[:, :],
                                       kt[prow:prow + 64,
                                          kb * 128:(kb + 1) * 128],
                                       qt[prow:prow + 64,
                                          qc * FCW:(qc + 1) * FCW],
                                       start=True, stop=True)),
                             reads=[ktb, qtb], writes=[sb2])
                        p.op("s", (lambda q, sh=sh, pth=pth, qc=qc, hh=hh:
                                   q.activation(
                                       pth[:, qc * FCW:(qc + 1) * FCW],
                                       sh[:, :], AF.Exp, scale=0.125,
                                       bias=negc[0][:, hh:hh + 1])),
                             reads=[sb2, negc[1]], writes=[ptb])
                        if layer == 2:
                            rev = AP(tensor=ebt[hh % 2][0],
                                     offset=kb * 128 + 2047 - qc * FCW,
                                     ap=[[3968, 128], [-1, FCW]])
                            p.op("v", (lambda q, pth=pth, rev=rev, qc=qc:
                                       q.tensor_tensor(
                                           pth[:, qc * FCW:(qc + 1) * FCW],
                                           pth[:, qc * FCW:(qc + 1) * FCW],
                                           rev, ALU.mult)),
                                 reads=[ebt[hh % 2][1], ptb], writes=[ptb])
                    for qc in range(NFC):
                        ch, cb3 = p.banks[4 + qc]
                        p.op("t", (lambda q, ch=ch, vt=vt, pth=pth, hh=hh,
                                   kb=kb, qc=qc: q.matmul(
                                       ch[0:65, :], vt[:, hh * 65:(hh + 1) * 65],
                                       pth[:, qc * FCW:(qc + 1) * FCW],
                                       start=(kb == 0), stop=(kb == 15))),
                             reads=[vtb, ptb], writes=[cb3])
                for qc in range(NFC):
                    ch, cb3 = p.banks[4 + qc]
                    p.op("v", (lambda q, ch=ch: q.tensor_copy(
                        ctx_sc[0][0:65, :], ch[0:65, :])),
                        reads=[cb3], writes=[ctx_sc[1]])
                    dst = b.dram["sp_ctx"][hh * 64:(hh + 1) * 64,
                                           qc * FCW:(qc + 1) * FCW]
                    p.op("v", (lambda q: q.tensor_copy(
                        ctx_bf[0][0:64, :], ctx_sc[0][0:64, :])),
                        reads=[ctx_sc[1]], writes=[ctx_bf[1]])
                    p.op("y", (lambda q, d=dst: q.dma_start(
                        out=d, in_=ctx_bf[0][0:64, :])),
                        reads=[ctx_bf[1]], writes=[p.db(("sp_ctx", hh, qc))])
                    dst2 = b.dram["sp_den"][hh:hh + 1,
                                            qc * FCW:(qc + 1) * FCW]
                    p.op("y", (lambda q, d=dst2: q.dma_start(
                        out=d, in_=ctx_sc[0][64:65, :])),
                        reads=[ctx_sc[1]], writes=[p.db(("sp_den", hh, qc))])

    with p.phase():
        den = p.sb([16, T], F32)
        p.op("y", (lambda q: q.dma_start(out=den[0][:, :], in_=b.dram["sp_den"])),
             reads=[p.db(("sp_den", hh, qc)) for hh in range(16)
                    for qc in range(NFC)], writes=[den[1]])
        p.op("v", (lambda q: q.reciprocal(den[0][:, :], den[0][:, :])),
             reads=[den[1]], writes=[den[1]])
        p.op("y", (lambda q: q.dma_start(out=b.dram["sp_den"], in_=den[0][:, :])),
             reads=[den[1]], writes=[p.db("sp_rd")])

        ctx_t = [p.sb([128, T], BF16) for _ in range(8)]
        rdb = [p.sb([128, FCW], F32) for _ in range(2)]
        for cblk in range(8):
            h, bb = ctx_t[cblk]
            src = b.dram["sp_ctx"][cblk * 128:(cblk + 1) * 128, :]
            p.op("y", (lambda q, d=h, s=src: q.dma_start(out=d[:, :], in_=s)),
                 reads=[p.db(("sp_ctx", cblk * 2, qq)) for qq in range(NFC)] +
                       [p.db(("sp_ctx", cblk * 2 + 1, qq)) for qq in range(NFC)],
                 writes=[bb])
            for fc in range(NFC):
                rh, rb2 = rdb[fc % 2]
                s_up = b.dram["sp_den"][cblk * 2:cblk * 2 + 1,
                                        fc * FCW:(fc + 1) * FCW].broadcast_to(
                                            (64, FCW))
                s_dn = b.dram["sp_den"][cblk * 2 + 1:cblk * 2 + 2,
                                        fc * FCW:(fc + 1) * FCW].broadcast_to(
                                            (64, FCW))
                p.op("y", (lambda q, d=rh, s=s_up: q.dma_start(out=d[0:64, :],
                                                               in_=s)),
                     reads=[p.db("sp_rd")], writes=[rb2])
                p.op("y", (lambda q, d=rh, s=s_dn: q.dma_start(out=d[64:128, :],
                                                               in_=s)),
                     reads=[p.db("sp_rd")], writes=[rb2])
                p.op("v", (lambda q, h=h, rh=rh, fc=fc: q.tensor_tensor(
                    h[:, fc * FCW:(fc + 1) * FCW],
                    h[:, fc * FCW:(fc + 1) * FCW], rh[:, :], ALU.mult)),
                    reads=[rb2, bb], writes=[bb])

        if layer == 2:
            ctx_in = [((lambda fc, h=h: h[:, fc * FCW:(fc + 1) * FCW]), bb)
                      for (h, bb) in ctx_t]

            def cb_wo(j, fc, ps_h, ps_b):
                b.merge(j, fc, ps_h, ps_b, src_key=src_key, dst_key="xres")

            b.linear("w2_o", ctx_in, cb_wo, 1024, 1024)
        else:
            for j in range(8):
                h, bb = ctx_t[j]
                for fc in range(NFC):
                    b.merge_sb(j, fc, h[:, fc * FCW:(fc + 1) * FCW], bb,
                               src_key=src_key, dst_key="xres")


def _emit_vtok(b, vtok, w_tiles, head_stride, ncol, v_off):
    p = b.p
    for tb in range(16):
        vt, vtb = vtok[tb]
        p.op("v", (lambda q, vt=vt: q.memset(vt[:, :], 1.0)), writes=[vtb])
        for hg in range(2):
            ph, pb = p.banks[hg % 2]

            def emit(q, tb=tb, hg=hg, ph=ph):
                ins = None
                for c in range(8):
                    wt, _ = w_tiles[c]
                    src = AP(tensor=wt[0] if isinstance(wt, tuple) else wt,
                             offset=v_off + hg * 8 * head_stride,
                             ap=[[ncol, 128], [head_stride, 8], [1, 64]])
                    ins = q.matmul(ph[:, :],
                                   b.xn[c][0][:, tb * 128:(tb + 1) * 128],
                                   src, start=(c == 0), stop=(c == 7))
                return ins

            p.op("t", emit, reads=[wb for (_, wb) in w_tiles] +
                 [bb for (_, bb) in b.xn], writes=[pb])
            dst = AP(tensor=vt, offset=hg * 8 * 65,
                     ap=[[16 * 65, 128], [65, 8], [1, 64]])
            p.op("v", (lambda q, ph=ph, dst=dst: q.tensor_copy(
                dst, ph[:, :].rearrange("p (h d) -> p h d", h=8))),
                reads=[pb], writes=[vtb])



def emit_moe(b, src_key, dst_key):
    p = b.p
    idh, idb = b.ident
    ones_r, ones_rb = b.ones_row
    with p.phase():
        comb_bf = p.sb([10, T], BF16)
        with p.phase():
            wr = p.sb([128, 8 * 8], BF16)
            src = AP(tensor=b.dram["r3_w"].tensor, offset=0,
                     ap=[[8, 128], [128 * 8, 8], [1, 8]])
            p.op("g", (lambda q, d=wr[0][:, :].rearrange("p (c k) -> p c k", c=8),
                       s=src: q.dma_start(out=d, in_=s)), writes=[wr[1]])
            gates = p.sb([8, T], F32)
            rbc = b.col("L3_rb", rows=8)
            bg = p.banks[4]
            for fc in range(NFC):
                def emit(q, fc=fc):
                    ins = None
                    for c in range(8):
                        ins = q.matmul(bg[0][0:8, :], wr[0][:, c * 8:(c + 1) * 8],
                                       b.xn[c][0][:, fc * FCW:(fc + 1) * FCW],
                                       start=(c == 0), stop=(c == 7))
                    return ins
                p.op("t", emit, reads=[wr[1]] + [bb for (_, bb) in b.xn],
                     writes=[bg[1]])
                p.op("s", (lambda q, fc=fc: q.activation(
                    gates[0][0:8, fc * FCW:(fc + 1) * FCW], bg[0][0:8, :],
                    AF.Identity, bias=rbc[0])),
                    reads=[bg[1], rbc[1]], writes=[gates[1]])

            comb = p.sb([10, T], F32)
            comb_bf = p.sb([10, T], BF16)
            p.op("v", (lambda q: q.memset(comb[0][:, :], 0.5)), writes=[comb[1]])
            NT = 16
            gt = [p.sb([128, 8], F32) for _ in range(NT)]
            m1 = [p.sb([128, 1], F32) for _ in range(NT)]
            m2 = [p.sb([128, 1], F32) for _ in range(NT)]
            eq1 = [p.sb([128, 8], F32) for _ in range(NT)]
            eq2 = [p.sb([128, 8], F32) for _ in range(NT)]
            gm = [p.sb([128, 8], F32) for _ in range(NT)]
            e21 = [p.sb([128, 1], F32) for _ in range(NT)]
            p1c = [p.sb([128, 1], F32) for _ in range(NT)]
            p2c = [p.sb([128, 1], F32) for _ in range(NT)]
            btr = p.banks[5]
            btr2 = p.banks[6]
            for tb in range(NT):
                p.op("t", (lambda q, tb=tb: q.transpose(
                    btr[0][:, 0:8], gates[0][0:8, tb * 128:(tb + 1) * 128],
                    idh[0:8, 0:8])), reads=[gates[1], idb], writes=[btr[1]])
                p.op("v", (lambda q, tb=tb: q.tensor_copy(gt[tb][0][:, :],
                                                          btr[0][:, 0:8])),
                     reads=[btr[1]], writes=[gt[tb][1]])
            for tb in range(NT):
                p.op("v", (lambda q, tb=tb: q.reduce_max(
                    m1[tb][0][:, :], gt[tb][0][:, :], axis=AX.X)),
                    reads=[gt[tb][1]], writes=[m1[tb][1]])
            for tb in range(NT):
                p.op("v", (lambda q, tb=tb: q.tensor_tensor(
                    eq1[tb][0][:, :], gt[tb][0][:, :],
                    m1[tb][0][:, 0:1].broadcast_to((128, 8)), ALU.is_equal)),
                    reads=[gt[tb][1], m1[tb][1]], writes=[eq1[tb][1]])
            for tb in range(NT):
                p.op("v", (lambda q, tb=tb: q.scalar_tensor_tensor(
                    gm[tb][0][:, :], eq1[tb][0][:, :], -1e30, gt[tb][0][:, :],
                    ALU.mult, ALU.add)),
                    reads=[eq1[tb][1], gt[tb][1]], writes=[gm[tb][1]])
            for tb in range(NT):
                p.op("v", (lambda q, tb=tb: q.reduce_max(
                    m2[tb][0][:, :], gm[tb][0][:, :], axis=AX.X)),
                    reads=[gm[tb][1]], writes=[m2[tb][1]])
            for tb in range(NT):
                p.op("v", (lambda q, tb=tb: q.tensor_tensor(
                    eq2[tb][0][:, :], gm[tb][0][:, :],
                    m2[tb][0][:, 0:1].broadcast_to((128, 8)), ALU.is_equal)),
                    reads=[gm[tb][1], m2[tb][1]], writes=[eq2[tb][1]])
            for tb in range(NT):
                p.op("v", (lambda q, tb=tb: q.tensor_tensor(
                    e21[tb][0][:, :], m2[tb][0][:, :], m1[tb][0][:, :],
                    ALU.subtract)),
                    reads=[m1[tb][1], m2[tb][1]], writes=[e21[tb][1]])
            for tb in range(NT):
                p.op("s", (lambda q, tb=tb: q.activation(
                    e21[tb][0][:, :], e21[tb][0][:, :], AF.Exp)),
                    reads=[e21[tb][1]], writes=[e21[tb][1]])
            for tb in range(NT):
                p.op("v", (lambda q, tb=tb: q.tensor_scalar(
                    p1c[tb][0][:, :], e21[tb][0][:, :], 1.0, None, ALU.add)),
                    reads=[e21[tb][1]], writes=[p1c[tb][1]])
            for tb in range(NT):
                p.op("v", (lambda q, tb=tb: q.reciprocal(
                    p1c[tb][0][:, :], p1c[tb][0][:, :])),
                    reads=[p1c[tb][1]], writes=[p1c[tb][1]])
            for tb in range(NT):
                p.op("v", (lambda q, tb=tb: q.tensor_tensor(
                    p2c[tb][0][:, :], e21[tb][0][:, :], p1c[tb][0][:, :],
                    ALU.mult)),
                    reads=[e21[tb][1], p1c[tb][1]], writes=[p2c[tb][1]])
            for tb in range(NT):
                p.op("v", (lambda q, tb=tb: q.tensor_scalar(
                    eq1[tb][0][:, :], eq1[tb][0][:, :], p1c[tb][0][:, :], None,
                    ALU.mult)),
                    reads=[p1c[tb][1], eq1[tb][1]], writes=[eq1[tb][1]])
            for tb in range(NT):
                p.op("v", (lambda q, tb=tb: q.tensor_scalar(
                    eq2[tb][0][:, :], eq2[tb][0][:, :], p2c[tb][0][:, :], None,
                    ALU.mult)),
                    reads=[p2c[tb][1], eq2[tb][1]], writes=[eq2[tb][1]])
            for tb in range(NT):
                p.op("v", (lambda q, tb=tb: q.tensor_tensor(
                    eq1[tb][0][:, :], eq1[tb][0][:, :], eq2[tb][0][:, :],
                    ALU.add)),
                    reads=[eq2[tb][1], eq1[tb][1]], writes=[eq1[tb][1]])
            for tb in range(NT):
                p.op("t", (lambda q, tb=tb: q.transpose(
                    btr2[0][0:8, 0:128], eq1[tb][0][:, 0:8], idh[:, :])),
                    reads=[eq1[tb][1], idb], writes=[btr2[1]])
                p.op("v", (lambda q, tb=tb: q.tensor_copy(
                    comb[0][0:8, tb * 128:(tb + 1) * 128], btr2[0][0:8, 0:128])),
                    reads=[btr2[1]], writes=[comb[1]])
            p.op("v", (lambda q: q.tensor_copy(comb_bf[0][:, :], comb[0][:, :])),
                 reads=[comb[1]], writes=[comb_bf[1]], tail=True)
            p.op("y", (lambda q: q.dma_start(out=b.dram["sp_comb"],
                                             in_=comb_bf[0][:, :])),
                 reads=[comb_bf[1]], writes=[p.db("sp_comb")])

        b2sb = p.sb([10, 1024], BF16)
        comb_rl = [p.sb([10, FCW], BF16) for _ in range(2)]
        b.wload(None, b2sb[0][:, :], b2sb[1], b.dram["b2ext"], 1024)

        yacc = [p.sb([128, T], F32) for _ in range(8)]
        hbf = [p.sb([128, FCW], BF16) for _ in range(16)]
        gel = p.sb([128, FCW], BF16)
        cbb = [p.sb([128, FCW], BF16) for _ in range(2)]
        w1t = [p.sb([128, 8 * 128], BF16) for _ in range(16)]
        w2t = [p.sb([128, 16 * 128], BF16) for _ in range(8)]

        for e in range(11):
            if e < 8:
                w1n, w1o = "m3_w1", e * 1024 * 2048
                w2n, w2o = "m3_w2", e * 2048 * 1024
                b1n, b1k = "L3_b1", e * 16
            elif e < 10:
                w1n, w1o = "m3_sw1", (e - 8) * 1024 * 2048
                w2n, w2o = "m3_sw2", (e - 8) * 2048 * 1024
                b1n, b1k = "L3_sb1", (e - 8) * 16
            if e < 10:
                for j in range(16):
                    wh, wb2 = w1t[j]
                    for hf in range(2):
                        src = AP(tensor=b.dram[w1n].tensor,
                                 offset=w1o + hf * 4 * 128 * 2048 + j * 128,
                                 ap=[[2048, 128], [128 * 2048, 4], [1, 128]])
                        stage = None
                        sview = 4
                        b.wload(stage, wh[:, hf * 512:(hf + 1) * 512], wb2,
                                src, 4 * 128, view=sview)
                for pb2 in range(8):
                    wh, wb2 = w2t[pb2]
                    for cg in range(4):
                        src = AP(tensor=b.dram[w2n].tensor,
                                 offset=w2o + pb2 * 128 + cg * 4 * 128 * 1024,
                                 ap=[[1024, 128], [128 * 1024, 4], [1, 128]])
                        stage = None
                        sview = 4
                        b.wload(stage, wh[:, cg * 512:(cg + 1) * 512], wb2,
                                src, 4 * 128, view=sview)
            for tc in range(NFC):
                fsl = slice(tc * FCW, (tc + 1) * FCW)
                if e < 10:
                    ch, chb = cbb[tc % 2]
                    src = b.dram["sp_comb"][e:e + 1, fsl].broadcast_to(
                        (128, FCW))
                    p.op("y", (lambda q, d=ch, s=src: q.dma_start(
                        out=d[:, :], in_=s)), reads=[p.db("sp_comb")],
                        writes=[chb])
                    for j in range(16):
                        ph, pb = p.banks[j % 2]
                        wh, wb2 = w1t[j]

                        def emit(q, wh=wh, ph=ph, fsl=fsl):
                            ins = None
                            for c in range(8):
                                ins = q.matmul(ph[:, :],
                                               wh[:, c * 128:(c + 1) * 128],
                                               b.xn[c][0][:, fsl],
                                               start=(c == 0), stop=(c == 7))
                            return ins
                        p.op("t", emit, reads=[wb2] + [bb for (_, bb) in b.xn],
                             writes=[pb])
                        bcol = b.col(b1n, b1k + j)
                        p.op("s", (lambda q, ph=ph, bc=bcol[0]: q.activation(
                            gel[0][:, :], ph[:, :], AF.Gelu, bias=bc)),
                            reads=[pb, bcol[1]], writes=[gel[1]])
                        hh, hb = hbf[j]
                        p.op("v", (lambda q, hh=hh, ch=ch: q.tensor_tensor(
                            hh[:, :], gel[0][:, :], ch[:, :], ALU.mult)),
                            reads=[gel[1], chb], writes=[hb])
                for pblk in range(8):
                    ph2, pb3 = p.banks[2 + pblk % 2]
                    if e < 10:
                        wh, wb2 = w2t[pblk]

                        def emity(q, wh=wh, ph2=ph2):
                            ins = None
                            for c in range(16):
                                ins = q.matmul(ph2[:, :],
                                               wh[:, c * 128:(c + 1) * 128],
                                               hbf[c][0][:, :],
                                               start=(c == 0), stop=(c == 15))
                            return ins
                        p.op("t", emity, reads=[wb2] + [hb for (_, hb) in hbf],
                             writes=[pb3])
                    else:
                        cr_h, cr_b = comb_rl[tc % 2]
                        if pblk == 0:
                            src2 = b.dram["sp_comb"][:, fsl]
                            p.op("y", (lambda q, d=cr_h, s=src2: q.dma_start(
                                out=d[:, :], in_=s)),
                                reads=[p.db("sp_comb")], writes=[cr_b])

                        def emity(q, ph2=ph2, pblk=pblk, cr_h=cr_h):
                            return q.matmul(
                                ph2[:, :],
                                b2sb[0][:, pblk * 128:(pblk + 1) * 128],
                                cr_h[:, :], start=True, stop=True)
                        p.op("t", emity, reads=[b2sb[1], cr_b],
                             writes=[pb3])
                    yh, yb = yacc[pblk]
                    if e == 0:
                        p.op("v", (lambda q, yh=yh, ph2=ph2, fsl=fsl:
                                   q.tensor_copy(yh[:, fsl], ph2[:, :])),
                             reads=[pb3], writes=[yb])
                    else:
                        p.op("v", (lambda q, yh=yh, ph2=ph2, fsl=fsl:
                                   q.tensor_tensor(yh[:, fsl], yh[:, fsl],
                                                   ph2[:, :], ALU.add)),
                             reads=[pb3, yb], writes=[yb])

        for j in range(8):
            yh, yb = yacc[j]
            for fc in range(NFC):
                b.merge_sb(j, fc, yh[:, fc * FCW:(fc + 1) * FCW], yb,
                           src_key=src_key, dst_key=dst_key)



DBG_HOOK = None


def build_program(layers, warrs):
    nc = bass.Bass("TRN2", target_bir_lowering=False, debug=False)
    p = P(nc)
    dram = {}
    dram["xin"] = nc.dram_tensor("xin", [D, T], F32, kind="ExternalInput").ap()
    dram["xout"] = nc.dram_tensor("xout", [D, T], F32,
                                  kind="ExternalOutput").ap()
    dram["cols_all"] = nc.dram_tensor("cols_all", [128, NCOLS], F32,
                                      kind="ExternalInput").ap()
    dram["ident"] = nc.dram_tensor("ident", [128, 128], F32,
                                   kind="ExternalInput").ap()
    for L in layers:
        for name, _ in _wspec(L):
            arr = warrs[name]
            dt = I32 if arr.dtype == np.int32 else F32
            dram[name] = nc.dram_tensor(name, list(arr.shape), dt,
                                        kind="ExternalInput").ap()
    for k, (shape, dt) in {
        "xres": ([D, T], F32),
        "sp_u": ([2048, T], BF16),
        "sp_z": ([2048, T], BF16),
        "sp_dt": ([2048, T], F32),
        "sp_bc": ([32, T], BF16),
        "sp_hb": ([4096, T], BF16),
        "sp_hc": ([4096, T], BF16),
        "sp_ctx": ([D, T], BF16),
        "sp_den": ([16, T], F32),
        "sp_comb": ([10, T], BF16),
        "sp_expb": ([16, 4095], BF16),
    }.items():
        dram[k] = nc.dram_tensor(k, shape, dt, kind="Internal").ap()

    b = B(p, dram)
    b.cols = p.sb([128, NCOLS], F32, "cols")
    b.ident = p.sb([128, 128], F32, "identsb")
    b.ones_col = p.sb([128, 1], F32, "ones_col")
    b.ones_col_bf = p.sb([128, 1], BF16, "ones_col_bf")
    b.ones_row = p.sb([1, 128], F32, "ones_row")
    b.xn = [p.sb([128, T], BF16, f"xn{i}") for i in range(8)]
    b.mt = [p.sb([128, FCW], F32) for _ in range(2)]
    b.mt2 = [p.sb([128, FCW], F32) for _ in range(2)] if MERGE_ACC else None
    b.mti = 0

    p.op("y", lambda q: q.dma_start(out=b.cols[0][:, :], in_=dram["cols_all"]),
         writes=[b.cols[1]])
    p.op("y", lambda q: q.dma_start(out=b.ident[0][:, :], in_=dram["ident"]),
         writes=[b.ident[1]])
    p.op("v", lambda q: q.memset(b.ones_col[0][:, :], 1.0),
         writes=[b.ones_col[1]])
    p.op("v", lambda q: q.memset(b.ones_col_bf[0][:, :], 1.0),
         writes=[b.ones_col_bf[1]])
    p.op("v", lambda q: q.memset(b.ones_row[0][:, :], 1.0),
         writes=[b.ones_row[1]])
    for cv in (1e-5, -0.5, 0.125, 1.0 / 127.0, 2.0, 0.5, -1.0, 1.0):
        cth = p.sb([128, 1], F32)
        p.op("v", (lambda q, t=cth[0], cv=cv: q.memset(t[:, :], cv)),
             writes=[cth[1]])
        nc.const_aps.register(F32, cv, cth[0][:, :])
    for j in range(8 if MERGE_ACC else 0):
        for fc in range(NFC):
            srcp = dram["xin"][j * 128:(j + 1) * 128, fc * FCW:(fc + 1) * FCW]
            dstp = dram["xres"][j * 128:(j + 1) * 128, fc * FCW:(fc + 1) * FCW]
            p.op("y", (lambda q, s=srcp, d=dstp: q.dma_start(out=d, in_=s)),
                 reads=[p.db(("xin", j, fc))], writes=[p.db(("xres", j, fc))])
    p.fence()
    if 1 in layers:
        o1 = COLS["L1_bdt"][0]; o2 = COLS["L1_dtb"][0]; o3 = COLS["L1_dtsum"][0]
        ch, cb = b.cols
        p.op("v", lambda q: q.tensor_tensor(ch[:, o3:o3 + 16],
                                            ch[:, o1:o1 + 16],
                                            ch[:, o2:o2 + 16], ALU.add),
             reads=[cb], writes=[cb])

    first, last = layers[0], layers[-1]
    for L in layers:
        src_key = "xin" if L == first else "xres"
        dst_key = "xout" if L == last else "xres"
        if L == 0:
            p.paranoid = True
        (emit_layernorm if LN_HARD else emit_layernorm_v0)(b, src_key)
        if L == 0:
            emit_mamba0(b, src_key)
            p.paranoid = False
            emit_ffn_ternary(b, "L0", "xres", dst_key)
        elif L == 1:
            emit_mamba1(b, src_key)
            emit_ffn_quant(b, "xres", dst_key)
        elif L == 2:
            _emit_attention(b, src_key, 2)
            emit_ffn_ternary(b, "L2", "xres", dst_key)
        else:
            _emit_attention(b, src_key, 3)
            emit_moe(b, "xres", dst_key)

    for j in range(8 if MERGE_ACC else 0):
        for fc in range(NFC):
            srcp = dram["xres"][j * 128:(j + 1) * 128, fc * FCW:(fc + 1) * FCW]
            dstp = dram["xout"][j * 128:(j + 1) * 128, fc * FCW:(fc + 1) * FCW]
            p.op("y", (lambda q, s=srcp, d=dstp: q.dma_start(out=d, in_=s)),
                 reads=[p.db(("xres", j, fc))], writes=[p.db(("xout", j, fc))])
    if DBG_HOOK is not None:
        DBG_HOOK(p, dram, b)
    for k in P.ENG:
        p.raw_wait("y", k, p.cnt[k])
    with nc.Block() as block:
        p.finalize(block)
    return nc, p



def _host_prep(inputs):
    x = np.asarray(inputs["L0_attn_x"])
    xT = np.ascontiguousarray(x.transpose(0, 2, 1))
    cols = _build_cols(inputs)
    ident = np.eye(128, dtype=np.float32)
    warrs = {}
    for L in range(4):
        for name, fn in _wspec(L):
            a = np.asarray(fn(inputs))
            if a.dtype != np.int32:
                a = a.astype(np.float32)
            warrs[name] = np.ascontiguousarray(a)
    return xT, cols, ident, warrs


def kernel(**inputs):
    xT, cols, ident, warrs = _host_prep(inputs)
    groups = [[0, 1, 2, 3]] if FUSED else [[0], [1], [2], [3]]
    state = [xT[c] for c in range(8)]
    for layers in groups:
        nc, p = build_program(layers, warrs)
        in_maps = []
        for c in range(8):
            m = {"xin": state[c], "cols_all": cols, "ident": ident}
            for L in layers:
                for name, _ in _wspec(L):
                    m[name] = warrs[name]
            in_maps.append(m)
        res = run_bass_kernel_spmd(nc, in_maps, core_ids=list(range(8)))
        state = [np.asarray(r["xout"]) for r in res.results]
        p.root.close()
    out = np.stack([s.T for s in state], axis=0).astype(np.float32)
    return np.ascontiguousarray(out)


# version 15
# speedup vs baseline: 1.1603x; 1.1603x over previous
import math
from contextlib import ExitStack

import numpy as np
import ml_dtypes

import concourse.bass as bass
import concourse.mybir as mybir
from concourse.bass_utils import run_bass_kernel_spmd

F32 = mybir.dt.float32
BF16 = mybir.dt.bfloat16
I32 = mybir.dt.int32
AF = mybir.ActivationFunctionType
ALU = mybir.AluOpType
AX = mybir.AxisListType
AP = bass.AP

T = 2048
D = 1024
FCW = 512
NFC = T // FCW

FUSED = True
LN_HARD = False
GN_HARD = False
MERGE_ACC = False
YG_REUSE = True


def _t(a):
    return np.ascontiguousarray(np.asarray(a).T)


def _cols(vec, n128):
    return np.ascontiguousarray(np.asarray(vec).reshape(n128, 128).T)


def _acols(alog):
    a = np.asarray(alog).reshape(16, 128, 16).transpose(1, 0, 2).reshape(128, 256)
    return np.ascontiguousarray(a)


def _bucket_onehot():
    r = np.arange(4095) - 2047
    nb = 16
    rb = (r > 0).astype(np.int64) * nb
    ra = np.abs(r)
    me = 8
    rp_f = np.maximum(ra, 1).astype(np.float32)
    ril = me + (np.log(rp_f / me) / math.log(128 / me) * (nb - me)).astype(np.int32)
    ril = np.minimum(ril, nb - 1)
    b = rb + np.where(ra < me, ra, ril)
    oh = np.zeros((32, 4095), np.float32)
    oh[b, np.arange(4095)] = 1.0
    return oh


COLS = {}
_nxt = [0]
for _n, _w in [
    ("L0_b_in", 32), ("L0_b_xproj", 1), ("L0_b_dt", 16), ("L0_b_out", 8),
    ("L0_D", 16), ("L0_w1s", 32), ("L0_w2s", 32), ("L0_h1s", 32),
    ("L0_h2s", 32), ("L0_w3s", 8),
    ("L1_bdt", 16), ("L1_dtb", 16), ("L1_gamma", 16), ("L1_beta", 16),
    ("L2_w1s", 32), ("L2_w2s", 32), ("L2_h1s", 32), ("L2_h2s", 32),
    ("L2_w3s", 8),
    ("L3_rb", 1), ("L3_b1", 128), ("L3_sb1", 32),
    ("L1_dtsum", 16),
]:
    COLS[_n] = (_nxt[0], _w)
    _nxt[0] += _w
NCOLS = _nxt[0]


def _build_cols(inp):
    c = np.zeros((128, NCOLS), np.float32)

    def put(name, vec):
        o, w = COLS[name]
        v = np.asarray(vec).astype(np.float32).reshape(-1)
        if w == 1:
            c[: v.shape[0], o] = v
        else:
            c[:, o:o + w] = _cols(v, w)

    put("L0_b_in", inp["L0_attn_b_in"]); put("L0_b_xproj", inp["L0_attn_b_xproj"])
    put("L0_b_dt", inp["L0_attn_b_dt"]); put("L0_b_out", inp["L0_attn_b_out"])
    put("L0_D", inp["L0_attn_D_param"])
    put("L0_w1s", inp["L0_ffn_w1_s"]); put("L0_w2s", inp["L0_ffn_w2_s"])
    put("L0_h1s", inp["L0_ffn_h1_s"]); put("L0_h2s", inp["L0_ffn_h2_s"])
    put("L0_w3s", inp["L0_ffn_w3_s"])
    put("L1_bdt", inp["L1_attn_bdt"]); put("L1_dtb", inp["L1_attn_dt_bias"])
    put("L1_gamma", inp["L1_attn_gamma"]); put("L1_beta", inp["L1_attn_beta"])
    put("L2_w1s", inp["L2_ffn_w1_s"]); put("L2_w2s", inp["L2_ffn_w2_s"])
    put("L2_h1s", inp["L2_ffn_h1_s"]); put("L2_h2s", inp["L2_ffn_h2_s"])
    put("L2_w3s", inp["L2_ffn_w3_s"])
    put("L3_rb", inp["L3_ffn_router_b"]); put("L3_b1", inp["L3_ffn_b1"])
    put("L3_sb1", inp["L3_ffn_sb1"])
    return c


def _wspec(layer):
    if layer == 0:
        return [
            ("w0_in", lambda i: i["L0_attn_W_in"]),
            ("w0_xp", lambda i: i["L0_attn_W_xproj"]),
            ("w0_dt", lambda i: i["L0_attn_W_dt"]),
            ("w0_out", lambda i: i["L0_attn_W_out"]),
            ("a0", lambda i: _acols(i["L0_attn_A_log"])),
            ("f0_w1", lambda i: _t(i["L0_ffn_w1_w"])),
            ("f0_w2", lambda i: _t(i["L0_ffn_w2_w"])),
            ("f0_h1", lambda i: _t(i["L0_ffn_h1_w"])),
            ("f0_h2", lambda i: _t(i["L0_ffn_h2_w"])),
            ("f0_w3", lambda i: _t(i["L0_ffn_w3_w"])),
        ]
    if layer == 1:
        return [
            ("w1_in", lambda i: _t(i["L1_attn_Win"])),
            ("w1_bc", lambda i: np.concatenate(
                [_t(i["L1_attn_WB"]), _t(i["L1_attn_WC"])], axis=1)),
            ("w1_dt", lambda i: _t(i["L1_attn_Wdt"])),
            ("w1_out", lambda i: _t(i["L1_attn_Wout"])),
            ("a1", lambda i: _acols(i["L1_attn_A_log"])),
            ("f1_wg", lambda i: _t(i["L1_ffn_w_gate"])),
            ("f1_wu", lambda i: _t(i["L1_ffn_w_up"])),
            ("f1_wd", lambda i: _t(i["L1_ffn_w_down"])),
        ]
    if layer == 2:
        return [
            ("w2_q", lambda i: _t(i["L2_attn_Wq"])),
            ("w2_k", lambda i: _t(i["L2_attn_Wk"])),
            ("w2_v", lambda i: _t(i["L2_attn_Wv"])),
            ("w2_o", lambda i: _t(i["L2_attn_Wo"])),
            ("rel_emb", lambda i: i["L2_attn_rel_emb"]),
            ("onehot", lambda i: _bucket_onehot()),
            ("f2_w1", lambda i: _t(i["L2_ffn_w1_w"])),
            ("f2_w2", lambda i: _t(i["L2_ffn_w2_w"])),
            ("f2_h1", lambda i: _t(i["L2_ffn_h1_w"])),
            ("f2_h2", lambda i: _t(i["L2_ffn_h2_w"])),
            ("f2_w3", lambda i: _t(i["L2_ffn_w3_w"])),
        ]
    return [
        ("w3_qkv", lambda i: _t(i["L3_attn_W_qkv"])),
        ("r3_w", lambda i: i["L3_ffn_router_w"]),
        ("m3_w1", lambda i: i["L3_ffn_w1"]),
        ("m3_w2", lambda i: i["L3_ffn_w2"]),
        ("m3_sw1", lambda i: i["L3_ffn_sw1"]),
        ("m3_sw2", lambda i: i["L3_ffn_sw2"]),
        ("b2ext", lambda i: np.concatenate(
            [i["L3_ffn_b2"], i["L3_ffn_sb2"]], axis=0)),
    ]



class Buf:
    __slots__ = ("w", "r")

    def __init__(self):
        self.w = None
        self.r = []


_NULL = Buf()


class P:
    ENG = ("t", "v", "s", "g", "y")

    def __init__(self, nc):
        self.nc = nc
        self.root = ExitStack()
        self.stacks = [self.root]
        self.streams = {k: [] for k in self.ENG}
        self.cnt = dict.fromkeys(self.ENG, 0)
        self.seen = {k: dict.fromkeys(self.ENG, 0) for k in self.ENG}
        self.sems = {k: self.root.enter_context(nc.semaphore("sem_" + k))
                     for k in self.ENG}
        self.dbufs = {}
        self._uid = 0
        self.banks = []
        for i in range(8):
            h = self.root.enter_context(
                nc.psum_tensor(f"bank{i}", [128, FCW], F32))
            self.banks.append((h, Buf()))

    def sb(self, shape, dt, name=None):
        self._uid += 1
        h = self.stacks[-1].enter_context(
            self.nc.sbuf_tensor(name or f"sb{self._uid}", list(shape), dt))
        return (h, Buf())

    def db(self, key):
        b = self.dbufs.get(key)
        if b is None:
            b = self.dbufs[key] = Buf()
        return b

    paranoid = False
    PARANOID_FULL = {k: ("v", "s", "y", "g")
                     for k in ("t", "v", "s", "g", "y")}
    PARANOID_NOSELF = {"t": ("v", "s", "y", "g"), "v": ("s", "y", "g"),
                       "s": ("v", "y", "g"), "y": ("v", "s", "g"),
                       "g": ("v", "s", "y")}
    PARANOID_FAST = {"v": ("y", "g"), "s": ("v", "y", "g"),
                     "t": (), "y": (), "g": ()}

    def op(self, ek, emit, reads=(), writes=(), tail=False):
        need = {}
        if self.paranoid:
            pset = self.paranoid if isinstance(self.paranoid, dict) else                 P.PARANOID_FULL
            for sk in pset.get(ek, ()):
                if self.cnt[sk] > need.get(sk, 0):
                    need[sk] = self.cnt[sk]
        for b in reads:
            if b.w is not None:
                sk, v = b.w
                if (sk != ek or tail) and v > need.get(sk, 0):
                    need[sk] = v
        for b in writes:
            deps = ([b.w] if b.w is not None else []) + b.r
            for sk, v in deps:
                if sk != ek and v > need.get(sk, 0):
                    need[sk] = v
        st = self.streams[ek]
        for sk in self.ENG:
            v = need.get(sk, 0)
            if v > self.seen[ek].get(sk, 0):
                self.seen[ek][sk] = v
                st.append(("w", sk, v))
        n = 16 if ek in ("g", "y") else 1
        self.cnt[ek] += n
        tok = (ek, self.cnt[ek])
        st.append(("e", emit, ek, n))
        for b in reads:
            b.r.append(tok)
        for b in writes:
            b.w = tok
            b.r = []
        return tok

    def raw_wait(self, ek, sk, v):
        if v > self.seen[ek][sk]:
            self.seen[ek][sk] = v
            self.streams[ek].append(("w", sk, v))

    def fence(self):
        for ek in self.ENG:
            for sk in self.ENG:
                if sk != ek:
                    self.raw_wait(ek, sk, self.cnt[sk])

    class _Phase:
        def __init__(self, p):
            self.p = p

        def __enter__(self):
            self.p.stacks.append(ExitStack())
            return self

        def __exit__(self, *a):
            st = self.p.stacks.pop()
            st.close()
            self.p.fence()
            return False

    def phase(self):
        return P._Phase(self)

    def finalize(self, block):
        sems = self.sems
        names = {"t": "tensor", "v": "vector", "s": "scalar",
                 "g": "gpsimd", "y": "sync"}
        for k in self.ENG:
            items = self.streams[k]

            def make(items=items):
                def f(q):
                    for it in items:
                        if it[0] == "w":
                            q.wait_ge(sems[it[1]], it[2])
                        else:
                            it[1](q).then_inc(sems[it[2]], it[3])
                return f

            getattr(block, names[k])(make())



class B:
    def __init__(self, p, dram):
        self.p = p
        self.dram = dram

    def col(self, name, blk=0, rows=128):
        o, w = COLS[name]
        h, b = self.cols
        return (h[0:rows, o + blk:o + blk + 1], b)

    def launder(self, tiles, out_tiles):
        p = self.p
        for (sh, sb2), (dh, db2) in zip(tiles, out_tiles):
            p.op("y", (lambda q, s=sh, d=dh: q.dma_start(out=d[:, :],
                                                         in_=s[:, :])),
                 reads=[sb2], writes=[db2])
        return out_tiles

    def wload(self, stage, dst_ap, dst_buf, src_ap, cols, view=None):
        p = self.p
        dview = dst_ap if view is None else dst_ap.rearrange(
            "p (c k) -> p c k", c=int(view))
        p.op("g", (lambda q, d=dview, s=src_ap: q.dma_start(out=d, in_=s)),
             reads=[], writes=[dst_buf])

    def linear(self, wname, in_tiles, out_cb, in_dim, out_dim, banks=(0, 1),
               woff=0, wrowlen=None):
        p = self.p
        nci = in_dim // 128
        ncj = out_dim // 128
        wrl = wrowlen if wrowlen is not None else out_dim
        w_d = self.dram[wname]
        wb = [p.sb([128, nci * 128], BF16) for _ in range(2)]
        for j in range(ncj):
            wt, wbuf = wb[j % 2]
            for cgi, cg0 in enumerate(range(0, nci, 4)):
                cgn = min(4, nci - cg0)
                src = AP(tensor=w_d.tensor, offset=woff + cg0 * 128 * wrl + j * 128,
                         ap=[[wrl, 128], [128 * wrl, cgn], [1, 128]])
                stage = None
                sview = cgn
                dst = wt[:, cg0 * 128:(cg0 + cgn) * 128]
                self.wload(stage, dst, wbuf, src, cgn * 128, view=sview)
            for fc in range(NFC):
                bh, bb = self.p.banks[banks[(j * NFC + fc) % len(banks)]]

                def emit(q, wt=wt, bh=bh, fc=fc, nci=nci, its=in_tiles):
                    ins = None
                    for c in range(nci):
                        ins = q.matmul(bh[:, :], wt[:, c * 128:(c + 1) * 128],
                                       its[c][0](fc), start=(c == 0),
                                       stop=(c == nci - 1))
                    return ins

                p.op("t", emit, reads=[wbuf] + [t[1] for t in in_tiles],
                     writes=[bb])
                out_cb(j, fc, bh, bb)

    def linear_stream(self, wname, rhs_dram, out_cb, in_dim, out_dim,
                      banks=(0, 1), jg=8, rhs_bf=True):
        p = self.p
        nci = in_dim // 128
        ncj = out_dim // 128
        w_d = self.dram[wname]
        with p.phase():
            panels = [p.sb([128, nci * 128], BF16) for _ in range(jg)]
            rts = [p.sb([128, FCW], BF16 if rhs_bf else F32)
                   for _ in range(nci)]
            nst = 0
            for jg0 in range(0, ncj, jg):
                jn = min(jg, ncj - jg0)
                for ji in range(jn):
                    j = jg0 + ji
                    wt, wbuf = panels[ji]
                    for cg0 in range(0, nci, 4):
                        cgn = min(4, nci - cg0)
                        src = AP(tensor=w_d.tensor,
                                 offset=cg0 * 128 * out_dim + j * 128,
                                 ap=[[out_dim, 128], [128 * out_dim, cgn],
                                     [1, 128]])
                        stage = None
                        nst += 1
                        sview = cgn
                        dst = wt[:, cg0 * 128:(cg0 + cgn) * 128]
                        self.wload(stage, dst, wbuf, src, cgn * 128,
                                   view=sview)
                for fc in range(NFC):
                    for c in range(nci):
                        h, bb = rts[c]
                        src = self.dram[rhs_dram][c * 128:(c + 1) * 128,
                                                  fc * FCW:(fc + 1) * FCW]
                        p.op("y", (lambda q, d=h, s=src:
                                   q.dma_start(out=d[:, :], in_=s)),
                             reads=[p.db((rhs_dram, c, fc))], writes=[bb])
                    for ji in range(jn):
                        j = jg0 + ji
                        wt, wbuf = panels[ji]
                        bh, bb = p.banks[banks[(j * NFC + fc) % len(banks)]]

                        def emit(q, wt=wt, bh=bh, nci=nci):
                            ins = None
                            for c in range(nci):
                                ins = q.matmul(bh[:, :],
                                               wt[:, c * 128:(c + 1) * 128],
                                               rts[c][0][:, :], start=(c == 0),
                                               stop=(c == nci - 1))
                            return ins

                        p.op("t", emit, reads=[wbuf] + [t[1] for t in rts],
                             writes=[bb])
                        out_cb(j, fc, bh, bb)

    def merge(self, j, fc, ps_h, ps_b, scalar=1.0, scalar_buf=None,
              op0=ALU.mult, src_key="xres", dst_key="xres"):
        p = self.p
        if not MERGE_ACC:
            return self.merge_v0(j, fc, ps_h, ps_b, scalar, scalar_buf, op0,
                                 src_key, dst_key)
        h, b = self.mt[self.mti]
        h2, b2 = self.mt2[self.mti]
        self.mti = (self.mti + 1) % len(self.mt)
        dbuf = p.db(("xres", j, fc))
        dst = self.dram["xres"][j * 128:(j + 1) * 128, fc * FCW:(fc + 1) * FCW]
        sb_extra = [scalar_buf] if scalar_buf is not None else []
        p.op("v", (lambda q, d=h, ps=ps_h, sc=scalar, op0=op0:
                   q.tensor_scalar(d[:, :], ps[:, :], sc, None, op0)),
             reads=[ps_b] + sb_extra, writes=[b])
        p.op("y", (lambda q, s=h, d=h2: q.dma_start(out=d[:, :], in_=s[:, :])),
             reads=[b], writes=[b2])
        p.op("g", (lambda q, s=h2, d=dst: q.dma_start(
            out=d, in_=s[:, :], accum_op=ALU.add)),
             reads=[b2, dbuf], writes=[dbuf], tail=True)

    def merge_sb(self, j, fc, src_ap, src_buf, src_key="xres", dst_key="xres"):
        p = self.p
        if not MERGE_ACC:
            return self.merge_sb_v0(j, fc, src_ap, src_buf, src_key, dst_key)
        h2, b2 = self.mt2[self.mti]
        self.mti = (self.mti + 1) % len(self.mt)
        dbuf = p.db(("xres", j, fc))
        dst = self.dram["xres"][j * 128:(j + 1) * 128, fc * FCW:(fc + 1) * FCW]
        p.op("y", (lambda q, s=src_ap, d=h2: q.dma_start(out=d[:, :], in_=s)),
             reads=[src_buf], writes=[b2])
        p.op("g", (lambda q, s=h2, d=dst: q.dma_start(
            out=d, in_=s[:, :], accum_op=ALU.add)),
             reads=[b2, dbuf], writes=[dbuf], tail=True)

    def merge_v0(self, j, fc, ps_h, ps_b, scalar=1.0, scalar_buf=None,
                 op0=ALU.mult, src_key="xres", dst_key="xres"):
        p = self.p
        h, b = self.mt[(j + fc) % 2]
        sbuf = p.db((src_key, j, fc))
        dbuf = p.db((dst_key, j, fc))
        src = self.dram[src_key][j * 128:(j + 1) * 128, fc * FCW:(fc + 1) * FCW]
        dst = self.dram[dst_key][j * 128:(j + 1) * 128, fc * FCW:(fc + 1) * FCW]
        p.op("y", (lambda q, d=h, s=src: q.dma_start(out=d[:, :], in_=s)),
             reads=[sbuf], writes=[b])
        sb_extra = [scalar_buf] if scalar_buf is not None else []
        p.op("v", (lambda q, d=h, ps=ps_h, sc=scalar, op0=op0:
                   q.scalar_tensor_tensor(d[:, :], ps[:, :], sc, d[:, :],
                                          op0, ALU.add)),
             reads=[ps_b, b] + sb_extra, writes=[b])
        p.op("y", (lambda q, s=h, d=dst: q.dma_start(out=d, in_=s[:, :])),
             reads=[b], writes=[dbuf])

    def merge_sb_v0(self, j, fc, src_ap, src_buf, src_key="xres",
                    dst_key="xres"):
        p = self.p
        h, b = self.mt[(j + fc) % 2]
        sbuf = p.db((src_key, j, fc))
        dbuf = p.db((dst_key, j, fc))
        src = self.dram[src_key][j * 128:(j + 1) * 128, fc * FCW:(fc + 1) * FCW]
        dst = self.dram[dst_key][j * 128:(j + 1) * 128, fc * FCW:(fc + 1) * FCW]
        p.op("y", (lambda q, d=h, s=src: q.dma_start(out=d[:, :], in_=s)),
             reads=[sbuf], writes=[b])
        p.op("v", (lambda q, d=h, sa=src_ap: q.tensor_tensor(
            d[:, :], d[:, :], sa, ALU.add)), reads=[src_buf, b], writes=[b])
        p.op("y", (lambda q, s=h, d=dst: q.dma_start(out=d, in_=s[:, :])),
             reads=[b], writes=[dbuf])

    def xn_in(self):
        return [((lambda fc, h=h: h[:, fc * FCW:(fc + 1) * FCW]), bb)
                for (h, bb) in self.xn]



def emit_layernorm(b, src_key):
    p = b.p
    x_d = b.dram[src_key]
    ones_c, ones_cb = b.ones_col
    ones_r, ones_rb = b.ones_row
    with p.phase():
        xt = [p.sb([128, FCW], F32) for _ in range(8)]
        sq = [p.sb([128, FCW], F32) for _ in range(2)]
        sq2 = [p.sb([128, FCW], F32) for _ in range(2)]
        rws = [p.sb([1, FCW], F32) for _ in range(7)]
        (rmu, rmub), (rex, rexb), (rm2, rm2b), (rvar, rvarb), (rstd, rstdb), \
            (rbn, rbnb), (pad, padb) = rws
        rvL, rvLb = rm2, rm2b
        lnv, lnvb = rex, rexb
        rsL, rsLb = rvar, rvarb
        bsx = p.banks[4]
        bsq = p.banks[5]
        ba = p.banks[6]
        bbc = p.banks[7]
        for fc in range(NFC):
            for i in range(8):
                h, bb = xt[i]
                src = x_d[i * 128:(i + 1) * 128, fc * FCW:(fc + 1) * FCW]
                p.op("y", (lambda q, d=h, s=src: q.dma_start(out=d[:, :], in_=s)),
                     reads=[p.db((src_key, i, fc))], writes=[bb])
            for i in range(8):
                h, bb = xt[i]
                sh, sb2 = sq[i % 2]
                s2h, s2b = sq2[i % 2]
                p.op("s", (lambda q, d=sh, s=h: q.activation(
                    d[:, :], s[:, :], AF.Square)), reads=[bb], writes=[sb2])
                p.op("y", (lambda q, s=sh, d=s2h: q.dma_start(
                    out=d[:, :], in_=s[:, :])), reads=[sb2], writes=[s2b])
                p.op("t", (lambda q, s=h, i=i: q.matmul(
                    bsx[0][0:1, :], ones_c[:, :], s[:, :], start=(i == 0),
                    stop=(i == 7))), reads=[bb, ones_cb], writes=[bsx[1]])
                p.op("t", (lambda q, s=s2h, i=i: q.matmul(
                    bsq[0][0:1, :], ones_c[:, :], s[:, :], start=(i == 0),
                    stop=(i == 7))), reads=[s2b, ones_cb], writes=[bsq[1]])
            p.op("v", (lambda q: q.tensor_scalar(rmu[0:1, :], bsx[0][0:1, :],
                                                 1.0 / D, None, ALU.mult)),
                 reads=[bsx[1]], writes=[rmub])
            p.op("v", (lambda q: q.tensor_scalar(rex[0:1, :], bsq[0][0:1, :],
                                                 1.0 / D, None, ALU.mult)),
                 reads=[bsq[1]], writes=[rexb])
            p.op("v", (lambda q: q.tensor_tensor(rm2[0:1, :], rmu[0:1, :],
                                                 rmu[0:1, :], ALU.mult)),
                 reads=[rmub], writes=[rm2b])
            p.op("v", (lambda q: q.memset(pad[0:1, :], 0.0)), writes=[padb])
            p.op("v", (lambda q: q.tensor_tensor(rvar[0:1, :], rex[0:1, :],
                                                 rm2[0:1, :], ALU.subtract)),
                 reads=[rexb, rm2b], writes=[rvarb])
            p.op("y", (lambda q: q.dma_start(out=rvL[0:1, :], in_=rvar[0:1, :])),
                 reads=[rvarb], writes=[rvLb])
            p.op("s", (lambda q: q.activation(lnv[0:1, :], rvL[0:1, :], AF.Ln,
                                              bias=1e-5)), reads=[rvLb], writes=[lnvb])
            p.op("s", (lambda q: q.activation(pad[0:1, :], pad[0:1, :],
                                              AF.Identity)), reads=[padb],
                 writes=[padb])
            p.op("s", (lambda q: q.activation(rstd[0:1, :], lnv[0:1, :], AF.Exp,
                                              scale=-0.5)), reads=[lnvb], writes=[rstdb])
            p.op("y", (lambda q: q.dma_start(out=rsL[0:1, :], in_=rstd[0:1, :])),
                 reads=[rstdb], writes=[rsLb])
            p.op("v", (lambda q: q.scalar_tensor_tensor(
                rbn[0:1, :], rmu[0:1, :], -1.0, rsL[0:1, :], ALU.mult, ALU.mult)),
                reads=[rmub, rsLb], writes=[rbnb])
            p.op("y", (lambda q: q.dma_start(out=lnv[0:1, :], in_=rbn[0:1, :])),
                 reads=[rbnb], writes=[lnvb])
            p.op("t", (lambda q: q.matmul(ba[0][:, :], ones_r[:, :], rsL[0:1, :],
                                          start=True, stop=True)),
                 reads=[rsLb, ones_rb], writes=[ba[1]])
            p.op("t", (lambda q: q.matmul(bbc[0][:, :], ones_r[:, :], lnv[0:1, :],
                                          start=True, stop=True)),
                 reads=[lnvb, ones_rb], writes=[bbc[1]])
            for i in range(8):
                h, bb = xt[i]
                sh, sb2 = sq[i % 2]
                xh, xb = b.xn[i]
                p.op("v", (lambda q, s=h, d=sh: q.tensor_tensor(
                    d[:, :], s[:, :], ba[0][:, :], ALU.mult)),
                    reads=[bb, ba[1]], writes=[sb2])
                p.op("v", (lambda q, s=sh, d=xh, fc=fc: q.tensor_tensor(
                    d[:, fc * FCW:(fc + 1) * FCW], s[:, :], bbc[0][:, :],
                    ALU.add)), reads=[sb2, bbc[1]], writes=[xb])



def _softplus_cb(b, tiles, bias_name):
    p = b.p

    def cb(j, fc, ps_h, ps_b):
        (v, vb), (a, ab), (l, lb), (r, rb2) = tiles
        bias = b.col(bias_name, j)
        p.op("s", (lambda q, ps=ps_h, bc=bias[0]: q.activation(
            v[:, :], ps[:, :], AF.Identity, bias=bc)),
            reads=[ps_b, bias[1]], writes=[vb])
        p.op("s", (lambda q: q.activation(a[:, :], v[:, :], AF.Abs)),
             reads=[vb], writes=[ab])
        p.op("s", (lambda q: q.activation(a[:, :], a[:, :], AF.Exp, scale=-1.0)),
             reads=[ab], writes=[ab])
        p.op("s", (lambda q: q.activation(l[:, :], a[:, :], AF.Ln, bias=1.0)),
             reads=[ab], writes=[lb])
        p.op("v", (lambda q: q.tensor_scalar_max(r[:, :], v[:, :], 0.0)),
             reads=[vb], writes=[rb2])
        p.op("v", (lambda q: q.tensor_tensor(r[:, :], r[:, :], l[:, :],
                                             ALU.add)), reads=[lb, rb2],
             writes=[rb2])
        dst = b.dram["sp_dt"][j * 128:(j + 1) * 128, fc * FCW:(fc + 1) * FCW]
        p.op("y", (lambda q, s=r, d=dst: q.dma_start(out=d, in_=s[:, :])),
             reads=[rb2], writes=[p.db(("sp_dt", j, fc))])

    return cb



def emit_mamba0(b, src_key):
    p = b.p
    with p.phase():
        ev = [p.sb([128, FCW], BF16) for _ in range(2)]

        def cb_win(j, fc, ps_h, ps_b):
            h, bb = ev[(j * NFC + fc) % 2]
            bias = b.col("L0_b_in", j)
            p.op("s", (lambda q, d=h, ps=ps_h, bc=bias[0]: q.activation(
                d[:, :], ps[:, :], AF.Silu, bias=bc)),
                reads=[ps_b, bias[1]], writes=[bb])
            tgt = "sp_u" if j < 16 else "sp_z"
            jj = j if j < 16 else j - 16
            dst = b.dram[tgt][jj * 128:(jj + 1) * 128, fc * FCW:(fc + 1) * FCW]
            p.op("y", (lambda q, s=h, d=dst: q.dma_start(out=d, in_=s[:, :])),
                 reads=[bb], writes=[p.db((tgt, jj, fc))])

        b.linear("w0_in", b.xn_in(), cb_win, 1024, 4096)

    with p.phase():
        u_t = [p.sb([128, T], BF16) for _ in range(16)]
        for c in range(16):
            h, bb = u_t[c]
            src = b.dram["sp_u"][c * 128:(c + 1) * 128, :]
            p.op("y", (lambda q, d=h, s=src: q.dma_start(out=d[:, :], in_=s)),
                 reads=[p.db(("sp_u", c, fc2)) for fc2 in range(NFC)],
                 writes=[bb])
        u_in = [((lambda fc, h=h: h[:, fc * FCW:(fc + 1) * FCW]), bb)
                for (h, bb) in u_t]

        wxp = p.sb([128, 16 * 32], BF16)
        w_d = b.dram["w0_xp"]
        for gi, cg0 in enumerate((0, 8)):
            src = AP(tensor=w_d.tensor, offset=cg0 * 128 * 32,
                     ap=[[32, 128], [128 * 32, 8], [1, 32]])
            stage = None
            sview = 8
            b.wload(stage, wxp[0][:, cg0 * 32:(cg0 + 8) * 32], wxp[1], src,
                    8 * 32, view=sview)
        xd_ev = p.sb([32, FCW], BF16)
        bxp = p.banks[4]
        for fc in range(NFC):
            def emit(q, fc=fc):
                ins = None
                for c in range(16):
                    ins = q.matmul(bxp[0][0:32, :], wxp[0][:, c * 32:(c + 1) * 32],
                                   u_t[c][0][:, fc * FCW:(fc + 1) * FCW],
                                   start=(c == 0), stop=(c == 15))
                return ins
            p.op("t", emit, reads=[wxp[1]] + [bb for (_, bb) in u_t],
                 writes=[bxp[1]])
            h, bb = xd_ev
            bias = b.col("L0_b_xproj", rows=32)
            p.op("s", (lambda q, d=h, bc=bias[0]: q.activation(
                d[:, :], bxp[0][0:32, :], AF.Identity, bias=bc)),
                reads=[bxp[1], bias[1]], writes=[bb])
            dst = b.dram["sp_bc"][:, fc * FCW:(fc + 1) * FCW]
            p.op("y", (lambda q, s=h, d=dst: q.dma_start(out=d, in_=s[:, :])),
                 reads=[bb], writes=[p.db(("sp_bc", fc))])

        sp_tiles = [p.sb([128, FCW], F32) for _ in range(4)]
        b.linear("w0_dt", u_in, _softplus_cb(b, sp_tiles, "L0_b_dt"),
                 2048, 2048)

    emit_scan(b, mode=0, aname="a0", src_key=src_key)


def emit_mamba1(b, src_key):
    p = b.p
    with p.phase():
        ev = [p.sb([128, FCW], BF16) for _ in range(2)]

        def cb_win(j, fc, ps_h, ps_b):
            h, bb = ev[(j * NFC + fc) % 2]
            if j < 16:
                p.op("s", (lambda q, d=h, ps=ps_h: q.activation(
                    d[:, :], ps[:, :], AF.Copy)), reads=[ps_b], writes=[bb])
                tgt, jj = "sp_u", j
            else:
                p.op("s", (lambda q, d=h, ps=ps_h: q.activation(
                    d[:, :], ps[:, :], AF.Silu)), reads=[ps_b], writes=[bb])
                tgt, jj = "sp_z", j - 16
            dst = b.dram[tgt][jj * 128:(jj + 1) * 128, fc * FCW:(fc + 1) * FCW]
            p.op("y", (lambda q, s=h, d=dst: q.dma_start(out=d, in_=s[:, :])),
                 reads=[bb], writes=[p.db((tgt, jj, fc))])

        b.linear("w1_in", b.xn_in(), cb_win, 1024, 4096)

    with p.phase():
        wbc = p.sb([128, 8 * 32], BF16)
        w_d = b.dram["w1_bc"]
        src = AP(tensor=w_d.tensor, offset=0, ap=[[32, 128], [128 * 32, 8], [1, 32]])
        b.wload(None, wbc[0][:, :], wbc[1], src, 8 * 32,
                view=8)
        xd_ev = p.sb([32, FCW], BF16)
        bxp = p.banks[4]
        for fc in range(NFC):
            def emit(q, fc=fc):
                ins = None
                for c in range(8):
                    ins = q.matmul(bxp[0][0:32, :], wbc[0][:, c * 32:(c + 1) * 32],
                                   b.xn[c][0][:, fc * FCW:(fc + 1) * FCW],
                                   start=(c == 0), stop=(c == 7))
                return ins
            p.op("t", emit, reads=[wbc[1]] + [bb for (_, bb) in b.xn],
                 writes=[bxp[1]])
            h, bb = xd_ev
            p.op("s", (lambda q, d=h: q.activation(d[:, :], bxp[0][0:32, :],
                                                   AF.Copy)),
                 reads=[bxp[1]], writes=[bb])
            dst = b.dram["sp_bc"][:, fc * FCW:(fc + 1) * FCW]
            p.op("y", (lambda q, s=h, d=dst: q.dma_start(out=d, in_=s[:, :])),
                 reads=[bb], writes=[p.db(("sp_bc", fc))])

        sp_tiles = [p.sb([128, FCW], F32) for _ in range(4)]
        b.linear("w1_dt", b.xn_in(), _softplus_cb(b, sp_tiles, "L1_dtsum"),
                 1024, 2048)

    emit_scan(b, mode=1, aname="a1", src_key=src_key)


def emit_scan(b, mode, aname, src_key):
    p = b.p
    p.paranoid = P.PARANOID_NOSELF
    with p.phase():
        a_sb = p.sb([128, 256], F32)
        p.op("y", (lambda q: q.dma_start(out=a_sb[0][:, :], in_=b.dram[aname])),
             writes=[a_sb[1]])
        p.op("s", (lambda q: q.activation(a_sb[0][:, :], a_sb[0][:, :], AF.Exp)),
             reads=[a_sb[1]], writes=[a_sb[1]])
        p.op("s", (lambda q, m=(-1.0 if mode == 0 else 0.5):
                   q.activation(a_sb[0][:, :], a_sb[0][:, :], AF.Identity,
                                scale=m)),
             reads=[a_sb[1]], writes=[a_sb[1]])

        wname = "w0_out" if mode == 0 else "w1_out"
        w_d = b.dram[wname]
        wout = [p.sb([128, 16 * 128], BF16) for _ in range(8)]
        for j in range(8):
            for cg0 in (0, 4, 8, 12):
                wsrc = AP(tensor=w_d.tensor,
                          offset=cg0 * 128 * 1024 + j * 128,
                          ap=[[1024, 128], [128 * 1024, 4], [1, 128]])
                b.wload(None, wout[j][0][:, cg0 * 128:(cg0 + 4) * 128],
                        wout[j][1], wsrc, 4 * 128, view=4)

        hstate = [p.sb([128, 16], F32) for _ in range(16)]
        for i in range(16):
            p.op("v", (lambda q, h=hstate[i][0]: q.memset(h[:, :], 0.0)),
                 writes=[hstate[i][1]])

        u_c = [p.sb([128, FCW], BF16) for _ in range(2)]
        dt_c = [p.sb([128, FCW], F32) for _ in range(2)]
        z_c = [p.sb([128, FCW], BF16) for _ in range(2)]
        bbc = [p.sb([128, FCW], BF16) for _ in range(16)]
        cbc = [p.sb([128, FCW], BF16) for _ in range(16)]
        dA = [[p.sb([128, FCW], F32) for _ in range(2)] for _ in range(2)]
        bt = [[p.sb([128, FCW], F32) for _ in range(2)] for _ in range(2)]
        ht = [[p.sb([128, FCW], F32) for _ in range(2)] for _ in range(2)]
        if mode == 0:
            dA2 = [[p.sb([128, FCW], F32) for _ in range(2)]
                   for _ in range(2)]
            rr = None
        else:
            dA2 = None
            rr = [[p.sb([128, FCW], F32) for _ in range(2)] for _ in range(2)]
        tmp = [p.sb([128, FCW], F32) for _ in range(2)]
        du = [p.sb([128, FCW], F32) for _ in range(2)]
        yac = [p.sb([128, FCW], F32) for _ in range(2)]
        yg = [p.sb([128, FCW], BF16) for _ in range(16)]
        yg2 = [p.sb([128, FCW], BF16) for _ in range(16)] if mode == 0 \
            else None

        gn = None
        if mode == 1:
            gn = dict(
                sq=[p.sb([128, FCW], BF16) for _ in range(2 if GN_HARD else 1)],
                sq2=([p.sb([128, FCW], BF16) for _ in range(2)]
                     if GN_HARD else None),
                rws=[p.sb([1, FCW], F32) for _ in range(7 if GN_HARD else 6)],
                nt=[p.sb([128, FCW], BF16) for _ in range(16)],
                t1=p.sb([128, FCW], F32),
            )

        for tc in range(NFC):
            fsl = slice(tc * FCW, (tc + 1) * FCW)
            for s_i in range(16):
                for (tiles, row) in ((bbc, s_i), (cbc, 16 + s_i)):
                    h, bb = tiles[s_i]
                    src = b.dram["sp_bc"][row:row + 1, fsl].broadcast_to(
                        (128, FCW))
                    p.op("y", (lambda q, d=h, s=src: q.dma_start(
                        out=d[:, :], in_=s)),
                        reads=[p.db(("sp_bc", tc))], writes=[bb])
            for ip in range(8):
                ii = (2 * ip, 2 * ip + 1)
                for k in range(2):
                    i = ii[k]
                    for (tt, dname) in ((u_c[k], "sp_u"), (dt_c[k], "sp_dt"),
                                        (z_c[k], "sp_z")):
                        h2, bb2 = tt
                        src = b.dram[dname][i * 128:(i + 1) * 128, fsl]
                        p.op("y", (lambda q, d=h2, s=src: q.dma_start(
                            out=d[:, :], in_=s)),
                            reads=[p.db((dname, i, tc))], writes=[bb2])
                for k in range(2):
                    p.op("v", (lambda q, k=k: q.tensor_tensor(
                        du[k][0][:, :], dt_c[k][0][:, :], u_c[k][0][:, :],
                        ALU.mult)),
                        reads=[u_c[k][1], dt_c[k][1]], writes=[du[k][1]])
                for k in range(2):
                    p.op("v", (lambda q, k=k: q.memset(yac[k][0][:, :], 0.0)),
                         writes=[yac[k][1]])
                if mode == 0:
                    for k in range(2):
                        a2 = a_sb[0][:, ii[k] * 16:ii[k] * 16 + 1]
                        p.op("s", (lambda q, k=k, a2=a2: q.activation(
                            dA[k][0][0][:, :], dt_c[k][0][:, :], AF.Exp,
                            scale=a2)),
                            reads=[dt_c[k][1], a_sb[1]], writes=[dA[k][0][1]])
                    for k in range(2):
                        p.op("y", (lambda q, k=k: q.dma_start(
                            out=dA2[k][0][0][:, :], in_=dA[k][0][0][:, :])),
                            reads=[dA[k][0][1]], writes=[dA2[k][0][1]])
                for s_i in range(16):
                    sp = s_i % 2
                    ac = [a_sb[0][:, ii[k] * 16 + s_i:ii[k] * 16 + s_i + 1]
                          for k in range(2)]
                    if mode == 0:
                        if s_i + 1 < 16:
                            spn = (s_i + 1) % 2
                            for k in range(2):
                                a2 = a_sb[0][:, ii[k] * 16 + s_i + 1:
                                             ii[k] * 16 + s_i + 2]
                                p.op("s", (lambda q, k=k, spn=spn, a2=a2:
                                           q.activation(dA[k][spn][0][:, :],
                                                        dt_c[k][0][:, :],
                                                        AF.Exp, scale=a2)),
                                     reads=[dt_c[k][1], a_sb[1]],
                                     writes=[dA[k][spn][1]])
                            for k in range(2):
                                p.op("y", (lambda q, k=k, spn=spn: q.dma_start(
                                    out=dA2[k][spn][0][:, :],
                                    in_=dA[k][spn][0][:, :])),
                                    reads=[dA[k][spn][1]],
                                    writes=[dA2[k][spn][1]])
                        for k in range(2):
                            p.op("v", (lambda q, k=k, sp=sp, s1=bbc[s_i][0]:
                                       q.tensor_tensor(bt[k][sp][0][:, :],
                                                       du[k][0][:, :],
                                                       s1[:, :], ALU.mult)),
                                 reads=[du[k][1], bbc[s_i][1]],
                                 writes=[bt[k][sp][1]])
                    else:
                        for k in range(2):
                            p.op("s", (lambda q, k=k, sp=sp, a2=ac[k]:
                                       q.activation(rr[k][sp][0][:, :],
                                                    dt_c[k][0][:, :], AF.Ln,
                                                    scale=a2, bias=1.0)),
                                 reads=[dt_c[k][1], a_sb[1]],
                                 writes=[rr[k][sp][1]])
                        for k in range(2):
                            p.op("s", (lambda q, k=k, sp=sp: q.activation(
                                rr[k][sp][0][:, :], rr[k][sp][0][:, :],
                                AF.Exp, scale=-1.0)),
                                reads=[rr[k][sp][1]], writes=[rr[k][sp][1]])
                        for k in range(2):
                            p.op("v", (lambda q, k=k, sp=sp: q.tensor_scalar(
                                dA[k][sp][0][:, :], rr[k][sp][0][:, :], 2.0,
                                -1.0, ALU.mult, ALU.add)),
                                reads=[rr[k][sp][1]], writes=[dA[k][sp][1]])
                        for k in range(2):
                            p.op("v", (lambda q, k=k, sp=sp, s1=bbc[s_i][0]:
                                       q.tensor_tensor(bt[k][sp][0][:, :],
                                                       du[k][0][:, :],
                                                       s1[:, :], ALU.mult)),
                                 reads=[du[k][1], bbc[s_i][1]],
                                 writes=[bt[k][sp][1]])
                        for k in range(2):
                            p.op("v", (lambda q, k=k, sp=sp: q.tensor_tensor(
                                bt[k][sp][0][:, :], bt[k][sp][0][:, :],
                                rr[k][sp][0][:, :], ALU.mult)),
                                reads=[rr[k][sp][1], bt[k][sp][1]],
                                writes=[bt[k][sp][1]])
                    for k in range(2):
                        hs_h, hs_b = hstate[ii[k]]
                        da_use = dA2 if mode == 0 else dA
                        p.op("v", (lambda q, k=k, sp=sp, hs=hs_h, s_i=s_i,
                                   du2=da_use: q.tensor_tensor_scan(
                                       ht[k][sp][0][:, :], du2[k][sp][0][:, :],
                                       bt[k][sp][0][:, :],
                                       hs[:, s_i:s_i + 1], ALU.mult, ALU.add)),
                             reads=[da_use[k][sp][1], bt[k][sp][1], hs_b],
                             writes=[ht[k][sp][1]])
                    for k in range(2):
                        hs_h, hs_b = hstate[ii[k]]
                        p.op("s", (lambda q, k=k, sp=sp, hs=hs_h, s_i=s_i:
                                   q.copy(hs[:, s_i:s_i + 1],
                                          ht[k][sp][0][:, FCW - 1:FCW])),
                             reads=[ht[k][sp][1]], writes=[hs_b])
                    for k in range(2):
                        p.op("v", (lambda q, k=k, sp=sp, c1=cbc[s_i][0]:
                                   q.tensor_tensor(tmp[k][0][:, :],
                                                   ht[k][sp][0][:, :],
                                                   c1[:, :], ALU.mult)),
                             reads=[ht[k][sp][1], cbc[s_i][1]],
                             writes=[tmp[k][1]])
                    for k in range(2):
                        p.op("v", (lambda q, k=k: q.tensor_tensor(
                            yac[k][0][:, :], yac[k][0][:, :], tmp[k][0][:, :],
                            ALU.add)),
                            reads=[tmp[k][1], yac[k][1]], writes=[yac[k][1]])
                if mode == 0:
                    for k in range(2):
                        dcol = b.col("L0_D", ii[k])
                        p.op("v", (lambda q, k=k, dc=dcol[0]:
                                   q.scalar_tensor_tensor(
                                       yac[k][0][:, :], u_c[k][0][:, :], dc,
                                       yac[k][0][:, :], ALU.mult, ALU.add)),
                             reads=[u_c[k][1], dcol[1], yac[k][1]],
                             writes=[yac[k][1]])
                for k in range(2):
                    gh, gb = yg[ii[k]]
                    p.op("v", (lambda q, k=k, gh=gh: q.tensor_tensor(
                        gh[:, :], yac[k][0][:, :], z_c[k][0][:, :], ALU.mult)),
                        reads=[yac[k][1], z_c[k][1]], writes=[gb])
            if mode == 1:
                (_gn_chunk if GN_HARD else _gn_chunk_v0)(b, gn, yg)
            if mode == 0:
                src_tiles = b.launder(yg, yg2)
            else:
                src_tiles = b.launder(gn["nt"], yg)
            for j in range(8):
                wt, wbuf = wout[j]
                bh, bb = p.banks[6 + j % 2]

                def emit(q, j=j, bh=bh, wt=wt, st=src_tiles):
                    ins = None
                    for c in range(16):
                        ins = q.matmul(bh[:, :],
                                       wt[:, c * 128:(c + 1) * 128],
                                       st[c][0][:, :], start=(c == 0),
                                       stop=(c == 15))
                    return ins
                p.op("t", emit, reads=[wbuf] + [t[1] for t in src_tiles],
                     writes=[bb])
                if mode == 0:
                    bias = b.col("L0_b_out", j)
                    b.merge(j, tc, bh, bb, scalar=bias[0], scalar_buf=bias[1],
                            op0=ALU.add, src_key=src_key)
                else:
                    b.merge(j, tc, bh, bb, src_key=src_key)
        p.paranoid = False


def _gn_chunk(b, gn, yg):
    p = b.p
    old_par = p.paranoid
    p.paranoid = True
    ones_cb_h, ones_cb_b = b.ones_col_bf
    ones_r, ones_rb = b.ones_row
    (rmu, rmub), (rex, rexb), (rm2, rm2b), (rvar, rvarb), (rstd, rstdb), \
        (rbnr, rbnrb), (pad, padb) = gn["rws"]
    rvL, rvLb = rm2, rm2b
    lnv, lnvb = rex, rexb
    rsL, rsLb = rvar, rvarb
    bsx = p.banks[4]; bsq = p.banks[5]; ba = p.banks[6]; bbn = p.banks[7]
    for i in range(16):
        gh, gb = yg[i]
        sh, sb2 = gn["sq"][i % 2]
        s2h, s2b = gn["sq2"][i % 2]
        p.op("s", (lambda q, d=sh, s=gh: q.activation(d[:, :], s[:, :],
                                                      AF.Square)),
             reads=[gb], writes=[sb2])
        p.op("y", (lambda q, s=sh, d=s2h: q.dma_start(out=d[:, :], in_=s[:, :])),
             reads=[sb2], writes=[s2b])
        p.op("t", (lambda q, s=gh, i=i: q.matmul(
            bsx[0][0:1, :], ones_cb_h[:, :], s[:, :], start=(i == 0),
            stop=(i == 15))), reads=[gb, ones_cb_b], writes=[bsx[1]])
        p.op("t", (lambda q, s=s2h, i=i: q.matmul(
            bsq[0][0:1, :], ones_cb_h[:, :], s[:, :], start=(i == 0),
            stop=(i == 15))), reads=[s2b, ones_cb_b], writes=[bsq[1]])
    p.op("v", (lambda q: q.tensor_scalar(rmu[0:1, :], bsx[0][0:1, :], 1.0 / 2048,
                                         None, ALU.mult)),
         reads=[bsx[1]], writes=[rmub])
    p.op("v", (lambda q: q.tensor_scalar(rex[0:1, :], bsq[0][0:1, :], 1.0 / 2048,
                                         None, ALU.mult)),
         reads=[bsq[1]], writes=[rexb])
    p.op("v", (lambda q: q.tensor_tensor(rm2[0:1, :], rmu[0:1, :], rmu[0:1, :],
                                         ALU.mult)), reads=[rmub], writes=[rm2b])
    p.op("v", (lambda q: q.memset(pad[0:1, :], 0.0)), writes=[padb])
    p.op("v", (lambda q: q.tensor_tensor(rvar[0:1, :], rex[0:1, :], rm2[0:1, :],
                                         ALU.subtract)), reads=[rexb, rm2b], writes=[rvarb])
    p.op("y", (lambda q: q.dma_start(out=rvL[0:1, :], in_=rvar[0:1, :])),
         reads=[rvarb], writes=[rvLb])
    p.op("s", (lambda q: q.activation(lnv[0:1, :], rvL[0:1, :], AF.Ln, bias=1e-5)),
         reads=[rvLb], writes=[lnvb])
    p.op("s", (lambda q: q.activation(pad[0:1, :], pad[0:1, :], AF.Identity)),
         reads=[padb], writes=[padb])
    p.op("s", (lambda q: q.activation(rstd[0:1, :], lnv[0:1, :], AF.Exp, scale=-0.5)),
         reads=[lnvb], writes=[rstdb])
    p.op("y", (lambda q: q.dma_start(out=rsL[0:1, :], in_=rstd[0:1, :])),
         reads=[rstdb], writes=[rsLb])
    p.op("v", (lambda q: q.scalar_tensor_tensor(rbnr[0:1, :], rmu[0:1, :], -1.0,
                                                rsL[0:1, :], ALU.mult, ALU.mult)),
         reads=[rmub, rsLb], writes=[rbnrb])
    p.op("y", (lambda q: q.dma_start(out=lnv[0:1, :], in_=rbnr[0:1, :])),
         reads=[rbnrb], writes=[lnvb])
    p.op("t", (lambda q: q.matmul(ba[0][:, :], ones_r[:, :], rsL[0:1, :],
                                  start=True, stop=True)),
         reads=[rsLb, ones_rb], writes=[ba[1]])
    p.op("t", (lambda q: q.matmul(bbn[0][:, :], ones_r[:, :], lnv[0:1, :],
                                  start=True, stop=True)),
         reads=[lnvb, ones_rb], writes=[bbn[1]])
    th, tb = gn["t1"]
    for i in range(16):
        gh, gb = yg[i]
        nh, nb = gn["nt"][i]
        p.op("v", (lambda q, s=gh: q.tensor_tensor(
            th[:, :], s[:, :], ba[0][:, :], ALU.mult)),
            reads=[gb, ba[1]], writes=[tb])
        p.op("v", (lambda q: q.tensor_tensor(th[:, :], th[:, :], bbn[0][:, :],
                                             ALU.add)),
             reads=[bbn[1], tb], writes=[tb])
        gc = b.col("L1_gamma", i)
        bc = b.col("L1_beta", i)
        p.op("s", (lambda q, d=nh, g2=gc[0], b2=bc[0]: q.activation(
            d[:, :], th[:, :], AF.Identity, scale=g2, bias=b2)),
            reads=[tb, gc[1], bc[1]], writes=[nb])
    p.paranoid = old_par




def emit_layernorm_v0(b, src_key):
    p = b.p
    x_d = b.dram[src_key]
    ones_c, ones_cb = b.ones_col
    ones_r, ones_rb = b.ones_row
    with p.phase():
        xt = [p.sb([128, FCW], F32) for _ in range(8)]
        sq = [p.sb([128, FCW], F32) for _ in range(2)]
        rws = [p.sb([1, FCW], F32) for _ in range(6)]
        (rmu, rmub), (rex, rexb), (rm2, rm2b), (rvar, rvarb), (rstd, rstdb), (rbn, rbnb) = rws
        bsx = p.banks[4]
        bsq = p.banks[5]
        ba = p.banks[6]
        bbc = p.banks[7]
        for fc in range(NFC):
            for i in range(8):
                h, bb = xt[i]
                src = x_d[i * 128:(i + 1) * 128, fc * FCW:(fc + 1) * FCW]
                p.op("y", (lambda q, d=h, s=src: q.dma_start(out=d[:, :], in_=s)),
                     reads=[p.db((src_key, i, fc))], writes=[bb])
            for i in range(8):
                h, bb = xt[i]
                sh, sb2 = sq[i % 2]
                p.op("s", (lambda q, d=sh, s=h: q.activation(
                    d[:, :], s[:, :], AF.Square)), reads=[bb], writes=[sb2])
                p.op("t", (lambda q, s=h, i=i: q.matmul(
                    bsx[0][0:1, :], ones_c[:, :], s[:, :], start=(i == 0),
                    stop=(i == 7))), reads=[bb, ones_cb], writes=[bsx[1]])
                p.op("t", (lambda q, s=sh, i=i: q.matmul(
                    bsq[0][0:1, :], ones_c[:, :], s[:, :], start=(i == 0),
                    stop=(i == 7))), reads=[sb2, ones_cb], writes=[bsq[1]])
            p.op("v", (lambda q: q.tensor_scalar(rmu[0:1, :], bsx[0][0:1, :],
                                                 1.0 / D, None, ALU.mult)),
                 reads=[bsx[1]], writes=[rmub])
            p.op("v", (lambda q: q.tensor_scalar(rex[0:1, :], bsq[0][0:1, :],
                                                 1.0 / D, None, ALU.mult)),
                 reads=[bsq[1]], writes=[rexb])
            p.op("s", (lambda q: q.activation(rm2[0:1, :], rmu[0:1, :], AF.Square)),
                 reads=[rmub], writes=[rm2b])
            p.op("v", (lambda q: q.tensor_tensor(rvar[0:1, :], rex[0:1, :],
                                                 rm2[0:1, :], ALU.subtract)),
                 reads=[rexb, rm2b], writes=[rvarb])
            p.op("s", (lambda q: q.activation(rvar[0:1, :], rvar[0:1, :], AF.Ln,
                                              bias=1e-5)), reads=[rvarb], writes=[rvarb])
            p.op("s", (lambda q: q.activation(rstd[0:1, :], rvar[0:1, :], AF.Exp,
                                              scale=-0.5)), reads=[rvarb], writes=[rstdb])
            p.op("v", (lambda q: q.scalar_tensor_tensor(
                rbn[0:1, :], rmu[0:1, :], -1.0, rstd[0:1, :], ALU.mult, ALU.mult)),
                reads=[rmub, rstdb], writes=[rbnb])
            p.op("t", (lambda q: q.matmul(ba[0][:, :], ones_r[:, :], rstd[0:1, :],
                                          start=True, stop=True)),
                 reads=[rstdb, ones_rb], writes=[ba[1]])
            p.op("t", (lambda q: q.matmul(bbc[0][:, :], ones_r[:, :], rbn[0:1, :],
                                          start=True, stop=True)),
                 reads=[rbnb, ones_rb], writes=[bbc[1]])
            for i in range(8):
                h, bb = xt[i]
                sh, sb2 = sq[i % 2]
                xh, xb = b.xn[i]
                p.op("v", (lambda q, s=h, d=sh: q.tensor_tensor(
                    d[:, :], s[:, :], ba[0][:, :], ALU.mult)),
                    reads=[bb, ba[1]], writes=[sb2])
                p.op("v", (lambda q, s=sh, d=xh, fc=fc: q.tensor_tensor(
                    d[:, fc * FCW:(fc + 1) * FCW], s[:, :], bbc[0][:, :],
                    ALU.add)), reads=[sb2, bbc[1]], writes=[xb])




def _gn_chunk_v0(b, gn, yg):
    p = b.p
    old_par = p.paranoid
    p.paranoid = True
    ones_cb_h, ones_cb_b = b.ones_col_bf
    ones_r, ones_rb = b.ones_row
    (rmu, rmub), (rex, rexb), (rm2, rm2b), (rvar, rvarb), (rstd, rstdb), (rbnr, rbnrb) = gn["rws"]
    bsx = p.banks[4]; bsq = p.banks[5]; ba = p.banks[6]; bbn = p.banks[7]
    sh, sb2 = gn["sq"][0]
    for i in range(16):
        gh, gb = yg[i]
        p.op("s", (lambda q, s=gh: q.activation(sh[:, :], s[:, :], AF.Square)),
             reads=[gb], writes=[sb2])
        p.op("t", (lambda q, s=gh, i=i: q.matmul(
            bsx[0][0:1, :], ones_cb_h[:, :], s[:, :], start=(i == 0),
            stop=(i == 15))), reads=[gb, ones_cb_b], writes=[bsx[1]])
        p.op("t", (lambda q, i=i: q.matmul(
            bsq[0][0:1, :], ones_cb_h[:, :], sh[:, :], start=(i == 0),
            stop=(i == 15))), reads=[sb2, ones_cb_b], writes=[bsq[1]])
    p.op("v", (lambda q: q.tensor_scalar(rmu[0:1, :], bsx[0][0:1, :], 1.0 / 2048,
                                         None, ALU.mult)),
         reads=[bsx[1]], writes=[rmub])
    p.op("v", (lambda q: q.tensor_scalar(rex[0:1, :], bsq[0][0:1, :], 1.0 / 2048,
                                         None, ALU.mult)),
         reads=[bsq[1]], writes=[rexb])
    p.op("s", (lambda q: q.activation(rm2[0:1, :], rmu[0:1, :], AF.Square)),
         reads=[rmub], writes=[rm2b])
    p.op("v", (lambda q: q.tensor_tensor(rvar[0:1, :], rex[0:1, :], rm2[0:1, :],
                                         ALU.subtract)), reads=[rexb, rm2b], writes=[rvarb])
    p.op("s", (lambda q: q.activation(rvar[0:1, :], rvar[0:1, :], AF.Ln, bias=1e-5)),
         reads=[rvarb], writes=[rvarb])
    p.op("s", (lambda q: q.activation(rstd[0:1, :], rvar[0:1, :], AF.Exp, scale=-0.5)),
         reads=[rvarb], writes=[rstdb])
    p.op("v", (lambda q: q.scalar_tensor_tensor(rbnr[0:1, :], rmu[0:1, :], -1.0,
                                                rstd[0:1, :], ALU.mult, ALU.mult)),
         reads=[rmub, rstdb], writes=[rbnrb])
    p.op("t", (lambda q: q.matmul(ba[0][:, :], ones_r[:, :], rstd[0:1, :],
                                  start=True, stop=True)),
         reads=[rstdb, ones_rb], writes=[ba[1]])
    p.op("t", (lambda q: q.matmul(bbn[0][:, :], ones_r[:, :], rbnr[0:1, :],
                                  start=True, stop=True)),
         reads=[rbnrb, ones_rb], writes=[bbn[1]])
    th, tb = gn["t1"]
    for i in range(16):
        gh, gb = yg[i]
        nh, nb = gn["nt"][i]
        p.op("v", (lambda q, s=gh: q.tensor_tensor(
            th[:, :], s[:, :], ba[0][:, :], ALU.mult)),
            reads=[gb, ba[1]], writes=[tb])
        p.op("v", (lambda q: q.tensor_tensor(th[:, :], th[:, :], bbn[0][:, :],
                                             ALU.add)),
             reads=[bbn[1], tb], writes=[tb])
        gc = b.col("L1_gamma", i)
        bc = b.col("L1_beta", i)
        p.op("s", (lambda q, d=nh, g2=gc[0], b2=bc[0]: q.activation(
            d[:, :], th[:, :], AF.Identity, scale=g2, bias=b2)),
            reads=[tb, gc[1], bc[1]], writes=[nb])
    p.paranoid = old_par




def emit_ffn_ternary(b, pref, src_key, dst_key):
    p = b.p
    lp = "f0" if pref == "L0" else "f2"
    with p.phase():
        ha = [p.sb([128, T], BF16) for _ in range(32)]
        t1e = [p.sb([128, FCW], BF16) for _ in range(2)]
        w1d, w2d = b.dram[lp + "_w1"], b.dram[lp + "_w2"]
        wt1 = [p.sb([128, 8 * 128], BF16) for _ in range(2)]
        wt2 = [p.sb([128, 8 * 128], BF16) for _ in range(2)]
        for j in range(32):
            b1h, b1b = wt1[j % 2]
            b2h, b2b = wt2[j % 2]
            for wi2, (wd, wh, wb2) in enumerate(((w1d, b1h, b1b),
                                                 (w2d, b2h, b2b))):
                for hf in range(2):
                    src = AP(tensor=wd.tensor,
                             offset=hf * 4 * 128 * 4096 + j * 128,
                             ap=[[4096, 128], [128 * 4096, 4], [1, 128]])
                    stage = None
                    sview = 4
                    b.wload(stage, wh[:, hf * 512:(hf + 1) * 512], wb2, src,
                            4 * 128, view=sview)
            for fc in range(NFC):
                p1h, p1b = p.banks[fc % 2]
                p2h, p2b = p.banks[2 + fc % 2]
                fsl = slice(fc * FCW, (fc + 1) * FCW)

                def emit1(q, wh=b1h, ph=p1h, fsl=fsl):
                    ins = None
                    for c in range(8):
                        ins = q.matmul(ph[:, :], wh[:, c * 128:(c + 1) * 128],
                                       b.xn[c][0][:, fsl], start=(c == 0),
                                       stop=(c == 7))
                    return ins

                def emit2(q, wh=b2h, ph=p2h, fsl=fsl):
                    ins = None
                    for c in range(8):
                        ins = q.matmul(ph[:, :], wh[:, c * 128:(c + 1) * 128],
                                       b.xn[c][0][:, fsl], start=(c == 0),
                                       stop=(c == 7))
                    return ins

                p.op("t", emit1, reads=[b1b] + [bb for (_, bb) in b.xn],
                     writes=[p1b])
                p.op("t", emit2, reads=[b2b] + [bb for (_, bb) in b.xn],
                     writes=[p2b])
                th, tb = t1e[fc % 2]
                s1 = b.col(pref + "_w1s", j)
                s2 = b.col(pref + "_w2s", j)
                p.op("s", (lambda q, d=th, ph=p1h, sc=s1[0]: q.activation(
                    d[:, :], ph[:, :], AF.Silu, scale=sc)),
                    reads=[p1b, s1[1]], writes=[tb])
                hh, hb = ha[j]
                p.op("v", (lambda q, d=hh, ph=p2h, s=th, sc=s2[0], fsl=fsl:
                           q.scalar_tensor_tensor(d[:, fsl], ph[:, :], sc,
                                                  s[:, :], ALU.mult, ALU.mult)),
                     reads=[p2b, s2[1], tb], writes=[hb])

        ha_in = [((lambda fc, h=h: h[:, fc * FCW:(fc + 1) * FCW]), bb)
                 for (h, bb) in ha]
        hb_ev = [p.sb([128, FCW], BF16) for _ in range(2)]

        def cb_h1(j, fc, ps_h, ps_b):
            h, bb = hb_ev[(j * NFC + fc) % 2]
            sc = b.col(pref + "_h1s", j)
            p.op("s", (lambda q, d=h, ps=ps_h, sc2=sc[0]: q.activation(
                d[:, :], ps[:, :], AF.Gelu, scale=sc2)),
                reads=[ps_b, sc[1]], writes=[bb])
            dst = b.dram["sp_hb"][j * 128:(j + 1) * 128,
                                  fc * FCW:(fc + 1) * FCW]
            p.op("y", (lambda q, s=h, d=dst: q.dma_start(out=d, in_=s[:, :])),
                 reads=[bb], writes=[p.db(("sp_hb", j, fc))])

        b.linear(lp + "_h1", ha_in, cb_h1, 4096, 4096, banks=(4, 5))

    hc_ev = [p.sb([128, FCW], BF16) for _ in range(2)]

    def cb_h2(j, fc, ps_h, ps_b):
        h, bb = hc_ev[(j * NFC + fc) % 2]
        sc = b.col(pref + "_h2s", j)
        b.p.op("s", (lambda q, d=h, ps=ps_h, sc2=sc[0]: q.activation(
            d[:, :], ps[:, :], AF.Gelu, scale=sc2)),
            reads=[ps_b, sc[1]], writes=[bb])
        dst = b.dram["sp_hc"][j * 128:(j + 1) * 128, fc * FCW:(fc + 1) * FCW]
        b.p.op("y", (lambda q, s=h, d=dst: q.dma_start(out=d, in_=s[:, :])),
               reads=[bb], writes=[b.p.db(("sp_hc", j, fc))])

    b.linear_stream(lp + "_h2", "sp_hb", cb_h2, 4096, 4096)

    def cb_w3(j, fc, ps_h, ps_b):
        sc = b.col(pref + "_w3s", j)
        b.merge(j, fc, ps_h, ps_b, scalar=sc[0], scalar_buf=sc[1],
                op0=ALU.mult, src_key=src_key, dst_key=dst_key)

    b.linear_stream(lp + "_w3", "sp_hc", cb_w3, 4096, 1024)


def emit_ffn_quant(b, src_key, dst_key):
    p = b.p
    with p.phase():
        hq = [p.sb([128, T], BF16) for _ in range(24)]
        t1e = [p.sb([128, FCW], BF16) for _ in range(2)]
        wgd, wud = b.dram["f1_wg"], b.dram["f1_wu"]
        wt1 = [p.sb([128, 8 * 128], BF16) for _ in range(2)]
        wt2 = [p.sb([128, 8 * 128], BF16) for _ in range(2)]
        for j in range(24):
            b1h, b1b = wt1[j % 2]
            b2h, b2b = wt2[j % 2]
            for wi2, (wd, wh, wb2) in enumerate(((wgd, b1h, b1b),
                                                 (wud, b2h, b2b))):
                for hf in range(2):
                    src = AP(tensor=wd.tensor,
                             offset=hf * 4 * 128 * 3072 + j * 128,
                             ap=[[3072, 128], [128 * 3072, 4], [1, 128]])
                    stage = None
                    sview = 4
                    b.wload(stage, wh[:, hf * 512:(hf + 1) * 512], wb2, src,
                            4 * 128, view=sview)
            for fc in range(NFC):
                p1h, p1b = p.banks[fc % 2]
                p2h, p2b = p.banks[2 + fc % 2]
                fsl = slice(fc * FCW, (fc + 1) * FCW)

                def emit1(q, wh=b1h, ph=p1h, fsl=fsl):
                    ins = None
                    for c in range(8):
                        ins = q.matmul(ph[:, :], wh[:, c * 128:(c + 1) * 128],
                                       b.xn[c][0][:, fsl], start=(c == 0),
                                       stop=(c == 7))
                    return ins

                def emit2(q, wh=b2h, ph=p2h, fsl=fsl):
                    ins = None
                    for c in range(8):
                        ins = q.matmul(ph[:, :], wh[:, c * 128:(c + 1) * 128],
                                       b.xn[c][0][:, fsl], start=(c == 0),
                                       stop=(c == 7))
                    return ins

                p.op("t", emit1, reads=[b1b] + [bb for (_, bb) in b.xn],
                     writes=[p1b])
                p.op("t", emit2, reads=[b2b] + [bb for (_, bb) in b.xn],
                     writes=[p2b])
                th, tb = t1e[fc % 2]
                p.op("s", (lambda q, d=th, ph=p1h: q.activation(
                    d[:, :], ph[:, :], AF.Silu, scale=1.0 / 127.0)),
                    reads=[p1b], writes=[tb])
                hh, hb = hq[j]
                p.op("v", (lambda q, d=hh, ph=p2h, s=th, fsl=fsl:
                           q.scalar_tensor_tensor(d[:, fsl], ph[:, :],
                                                  1.0 / 127.0, s[:, :],
                                                  ALU.mult, ALU.mult)),
                     reads=[p2b, tb], writes=[hb])

        hq_in = [((lambda fc, h=h: h[:, fc * FCW:(fc + 1) * FCW]), bb)
                 for (h, bb) in hq]

        def cb_wd(j, fc, ps_h, ps_b):
            b.merge(j, fc, ps_h, ps_b, scalar=1.0 / 127.0, op0=ALU.mult,
                    src_key=src_key, dst_key=dst_key)

        b.linear("f1_wd", hq_in, cb_wd, 3072, 1024, banks=(4, 5))



def _emit_attention(b, src_key, layer):
    p = b.p
    NH = 16
    with p.phase():
        qf = [p.sb([128, T], BF16) for _ in range(8)]
        kf = [p.sb([128, T], BF16) for _ in range(8)]
        vtok = [p.sb([128, NH * 65], BF16) for _ in range(16)]
        negc = p.sb([128, NH], F32)

        with p.phase():
            if layer == 2:
                ps_e = p.banks[4]
                re_sb = p.sb([32, 16], F32)
                oh_sb = p.sb([32, 4095], F32)
                eb_ev = p.sb([16, FCW], BF16)
                p.op("y", (lambda q: q.dma_start(out=re_sb[0][:, :],
                                                 in_=b.dram["rel_emb"])),
                     writes=[re_sb[1]])
                p.op("y", (lambda q: q.dma_start(out=oh_sb[0][:, :],
                                                 in_=b.dram["onehot"])),
                     writes=[oh_sb[1]])
                for fc in range(8):
                    w = min(FCW, 4095 - fc * FCW)
                    p.op("t", (lambda q, fc=fc, w=w: q.matmul(
                        ps_e[0][0:16, 0:w], re_sb[0][:, :],
                        oh_sb[0][:, fc * FCW:fc * FCW + w],
                        start=True, stop=True)),
                        reads=[re_sb[1], oh_sb[1]], writes=[ps_e[1]])
                    p.op("s", (lambda q, w=w: q.activation(
                        eb_ev[0][0:16, 0:w], ps_e[0][0:16, 0:w], AF.Exp)),
                        reads=[ps_e[1]], writes=[eb_ev[1]])
                    dst = b.dram["sp_expb"][:, fc * FCW:fc * FCW + w]
                    p.op("y", (lambda q, d=dst, w=w: q.dma_start(
                        out=d, in_=eb_ev[0][0:16, 0:w])),
                        reads=[eb_ev[1]], writes=[p.db(("sp_expb", fc))])

                def cb_q(j, fc, ps_h, ps_b):
                    h, bb = qf[j]
                    p.op("s", (lambda q, d=h, ps=ps_h, fc=fc: q.activation(
                        d[:, fc * FCW:(fc + 1) * FCW], ps[:, :], AF.Copy)),
                        reads=[ps_b], writes=[bb])

                def cb_k(j, fc, ps_h, ps_b):
                    h, bb = kf[j]
                    p.op("s", (lambda q, d=h, ps=ps_h, fc=fc: q.activation(
                        d[:, fc * FCW:(fc + 1) * FCW], ps[:, :], AF.Copy)),
                        reads=[ps_b], writes=[bb])

                b.linear("w2_q", b.xn_in(), cb_q, 1024, 1024)
                b.linear("w2_k", b.xn_in(), cb_k, 1024, 1024)
                wv_t = [p.sb([128, 1024], BF16) for _ in range(8)]
                for c in range(8):
                    h, bb = wv_t[c]
                    src = b.dram["w2_v"][c * 128:(c + 1) * 128, :]
                    b.wload(None, h[:, :], bb, src, 1024)
                _emit_vtok(b, vtok, wv_t, 64, 1024, 0)
            else:
                wq_re = [p.sb([128, 1024], BF16) for _ in range(8)]
                wk_re = [p.sb([128, 1024], BF16) for _ in range(8)]
                wv_re = [p.sb([128, 1024], BF16) for _ in range(8)]
                with p.phase():
                    wq_t = [p.sb([128, 3072], BF16) for _ in range(2)]
                    for c in range(8):
                        h, bb = wq_t[c % 2]
                        src = b.dram["w3_qkv"][c * 128:(c + 1) * 128, :]
                        b.wload(None, h[:, :], bb, src, 3072)
                        for off, re in ((0, wq_re), (64, wk_re),
                                        (128, wv_re)):
                            dh, db2 = re[c]
                            srcap = AP(tensor=h, offset=off,
                                       ap=[[3072, 128], [192, 16], [1, 64]])
                            p.op("v", (lambda q, d=dh, s=srcap:
                                       q.tensor_copy(d[:, :], s)),
                                 reads=[bb], writes=[db2])

                def cb_q3(j, fc, ps_h, ps_b):
                    h, bb = qf[j]
                    p.op("s", (lambda q, d=h, ps=ps_h, fc=fc: q.activation(
                        d[:, fc * FCW:(fc + 1) * FCW], ps[:, :], AF.Copy)),
                        reads=[ps_b], writes=[bb])

                def cb_k3(j, fc, ps_h, ps_b):
                    h, bb = kf[j]
                    p.op("s", (lambda q, d=h, ps=ps_h, fc=fc: q.activation(
                        d[:, fc * FCW:(fc + 1) * FCW], ps[:, :], AF.Copy)),
                        reads=[ps_b], writes=[bb])

                for re, cb in ((wq_re, cb_q3), (wk_re, cb_k3)):
                    for j in range(8):
                        for fc in range(NFC):
                            ph, pb = p.banks[fc % 2]

                            def emit(q, re=re, j=j, fc=fc, ph=ph):
                                ins = None
                                for c in range(8):
                                    ins = q.matmul(
                                        ph[:, :],
                                        re[c][0][:, j * 128:(j + 1) * 128],
                                        b.xn[c][0][:, fc * FCW:(fc + 1) * FCW],
                                        start=(c == 0), stop=(c == 7))
                                return ins
                            p.op("t", emit, reads=[t[1] for t in re] +
                                 [bb for (_, bb) in b.xn], writes=[pb])
                            cb(j, fc, ph, pb)
                _emit_vtok(b, vtok, wv_re, 64, 1024, 0)

            sqt = p.sb([128, FCW], F32)
            nrows = [p.sb([1, T], F32) for _ in range(2)]
            nreds = [p.sb([1, 1], F32) for _ in range(2)]
            ones_c, ones_cb = b.ones_col
            ones_r, ones_rb = b.ones_row
            bn = p.banks[5]
            bc2 = p.banks[6]
            for hh in range(NH):
                prow = (hh % 2) * 64
                for r, tiles in ((0, qf), (1, kf)):
                    th, tb2 = tiles[hh // 2]
                    nr, nrb = nrows[r]
                    for fc in range(NFC):
                        p.op("s", (lambda q, th=th, prow=prow, fc=fc:
                                   q.activation(sqt[0][0:64, :],
                                                th[prow:prow + 64,
                                                   fc * FCW:(fc + 1) * FCW],
                                                AF.Square)),
                             reads=[tb2], writes=[sqt[1]])
                        p.op("t", (lambda q: q.matmul(
                            bn[0][0:1, :], ones_c[0:64, :],
                            sqt[0][0:64, :], start=True, stop=True)),
                            reads=[sqt[1], ones_cb], writes=[bn[1]])
                        p.op("v", (lambda q, nr=nr, fc=fc: q.tensor_copy(
                            nr[0:1, fc * FCW:(fc + 1) * FCW],
                            bn[0][0:1, :])), reads=[bn[1]],
                            writes=[nrb])
                for r in range(2):
                    p.op("v", (lambda q, r=r: q.reduce_max(
                        nreds[r][0][0:1, :], nrows[r][0][0:1, :], axis=AX.X)),
                        reads=[nrows[r][1]], writes=[nreds[r][1]], tail=True)
                p.op("v", (lambda q: q.tensor_tensor(
                    nreds[0][0][0:1, :], nreds[0][0][0:1, :],
                    nreds[1][0][0:1, :], ALU.mult)),
                    reads=[nreds[0][1], nreds[1][1]], writes=[nreds[0][1]],
                    tail=True)
                p.op("s", (lambda q: q.activation(nreds[0][0][0:1, :],
                                                  nreds[0][0][0:1, :], AF.Sqrt)),
                     reads=[nreds[0][1]], writes=[nreds[0][1]])
                p.op("t", (lambda q: q.matmul(bc2[0][:, 0:1], ones_r[:, :],
                                              nreds[0][0][0:1, :], start=True,
                                              stop=True)),
                     reads=[nreds[0][1], ones_rb], writes=[bc2[1]])
                p.op("v", (lambda q, hh=hh: q.tensor_scalar_mul(
                    negc[0][:, hh:hh + 1], bc2[0][:, 0:1], -0.125)),
                    reads=[bc2[1]], writes=[negc[1]])

        with p.phase():
            pt = [p.sb([128, T], BF16) for _ in range(2)]
            ebt = [p.sb([128, 3968], BF16) for _ in range(2)]
            ctx_sc = p.sb([128, FCW], F32)
            ctx_bf = p.sb([128, FCW], BF16)
            for hh in range(NH):
                if layer == 2:
                    eh, ebb = ebt[hh % 2]
                    src = AP(tensor=b.dram["sp_expb"].tensor, offset=hh * 4095,
                             ap=[[1, 128], [1, 3968]])
                    p.op("y", (lambda q, d=eh, s=src: q.dma_start(
                        out=d[:, :], in_=s)),
                        reads=[p.db(("sp_expb", k)) for k in range(8)],
                        writes=[ebb])
                qt, qtb = qf[hh // 2]
                kt, ktb = kf[hh // 2]
                prow = (hh % 2) * 64
                for kb in range(16):
                    pth, ptb = pt[kb % 2]
                    vt, vtb = vtok[kb]
                    for qc in range(NFC):
                        sh, sb2 = p.banks[qc]
                        p.op("t", (lambda q, sh=sh, kt=kt, qt=qt, prow=prow,
                                   kb=kb, qc=qc: q.matmul(
                                       sh[:, :],
                                       kt[prow:prow + 64,
                                          kb * 128:(kb + 1) * 128],
                                       qt[prow:prow + 64,
                                          qc * FCW:(qc + 1) * FCW],
                                       start=True, stop=True)),
                             reads=[ktb, qtb], writes=[sb2])
                        p.op("s", (lambda q, sh=sh, pth=pth, qc=qc, hh=hh:
                                   q.activation(
                                       pth[:, qc * FCW:(qc + 1) * FCW],
                                       sh[:, :], AF.Exp, scale=0.125,
                                       bias=negc[0][:, hh:hh + 1])),
                             reads=[sb2, negc[1]], writes=[ptb])
                        if layer == 2:
                            rev = AP(tensor=ebt[hh % 2][0],
                                     offset=kb * 128 + 2047 - qc * FCW,
                                     ap=[[3968, 128], [-1, FCW]])
                            p.op("v", (lambda q, pth=pth, rev=rev, qc=qc:
                                       q.tensor_tensor(
                                           pth[:, qc * FCW:(qc + 1) * FCW],
                                           pth[:, qc * FCW:(qc + 1) * FCW],
                                           rev, ALU.mult)),
                                 reads=[ebt[hh % 2][1], ptb], writes=[ptb])
                    for qc in range(NFC):
                        ch, cb3 = p.banks[4 + qc]
                        p.op("t", (lambda q, ch=ch, vt=vt, pth=pth, hh=hh,
                                   kb=kb, qc=qc: q.matmul(
                                       ch[0:65, :], vt[:, hh * 65:(hh + 1) * 65],
                                       pth[:, qc * FCW:(qc + 1) * FCW],
                                       start=(kb == 0), stop=(kb == 15))),
                             reads=[vtb, ptb], writes=[cb3])
                for qc in range(NFC):
                    ch, cb3 = p.banks[4 + qc]
                    p.op("v", (lambda q, ch=ch: q.tensor_copy(
                        ctx_sc[0][0:65, :], ch[0:65, :])),
                        reads=[cb3], writes=[ctx_sc[1]])
                    dst = b.dram["sp_ctx"][hh * 64:(hh + 1) * 64,
                                           qc * FCW:(qc + 1) * FCW]
                    p.op("v", (lambda q: q.tensor_copy(
                        ctx_bf[0][0:64, :], ctx_sc[0][0:64, :])),
                        reads=[ctx_sc[1]], writes=[ctx_bf[1]])
                    p.op("y", (lambda q, d=dst: q.dma_start(
                        out=d, in_=ctx_bf[0][0:64, :])),
                        reads=[ctx_bf[1]], writes=[p.db(("sp_ctx", hh, qc))])
                    dst2 = b.dram["sp_den"][hh:hh + 1,
                                            qc * FCW:(qc + 1) * FCW]
                    p.op("y", (lambda q, d=dst2: q.dma_start(
                        out=d, in_=ctx_sc[0][64:65, :])),
                        reads=[ctx_sc[1]], writes=[p.db(("sp_den", hh, qc))])

    with p.phase():
        den = p.sb([16, T], F32)
        p.op("y", (lambda q: q.dma_start(out=den[0][:, :], in_=b.dram["sp_den"])),
             reads=[p.db(("sp_den", hh, qc)) for hh in range(16)
                    for qc in range(NFC)], writes=[den[1]])
        p.op("v", (lambda q: q.reciprocal(den[0][:, :], den[0][:, :])),
             reads=[den[1]], writes=[den[1]])
        p.op("y", (lambda q: q.dma_start(out=b.dram["sp_den"], in_=den[0][:, :])),
             reads=[den[1]], writes=[p.db("sp_rd")])

        ctx_t = [p.sb([128, T], BF16) for _ in range(8)]
        rdb = [p.sb([128, FCW], F32) for _ in range(2)]
        for cblk in range(8):
            h, bb = ctx_t[cblk]
            src = b.dram["sp_ctx"][cblk * 128:(cblk + 1) * 128, :]
            p.op("y", (lambda q, d=h, s=src: q.dma_start(out=d[:, :], in_=s)),
                 reads=[p.db(("sp_ctx", cblk * 2, qq)) for qq in range(NFC)] +
                       [p.db(("sp_ctx", cblk * 2 + 1, qq)) for qq in range(NFC)],
                 writes=[bb])
            for fc in range(NFC):
                rh, rb2 = rdb[fc % 2]
                s_up = b.dram["sp_den"][cblk * 2:cblk * 2 + 1,
                                        fc * FCW:(fc + 1) * FCW].broadcast_to(
                                            (64, FCW))
                s_dn = b.dram["sp_den"][cblk * 2 + 1:cblk * 2 + 2,
                                        fc * FCW:(fc + 1) * FCW].broadcast_to(
                                            (64, FCW))
                p.op("y", (lambda q, d=rh, s=s_up: q.dma_start(out=d[0:64, :],
                                                               in_=s)),
                     reads=[p.db("sp_rd")], writes=[rb2])
                p.op("y", (lambda q, d=rh, s=s_dn: q.dma_start(out=d[64:128, :],
                                                               in_=s)),
                     reads=[p.db("sp_rd")], writes=[rb2])
                p.op("v", (lambda q, h=h, rh=rh, fc=fc: q.tensor_tensor(
                    h[:, fc * FCW:(fc + 1) * FCW],
                    h[:, fc * FCW:(fc + 1) * FCW], rh[:, :], ALU.mult)),
                    reads=[rb2, bb], writes=[bb])

        if layer == 2:
            ctx_in = [((lambda fc, h=h: h[:, fc * FCW:(fc + 1) * FCW]), bb)
                      for (h, bb) in ctx_t]

            def cb_wo(j, fc, ps_h, ps_b):
                b.merge(j, fc, ps_h, ps_b, src_key=src_key, dst_key="xres")

            b.linear("w2_o", ctx_in, cb_wo, 1024, 1024)
        else:
            for j in range(8):
                h, bb = ctx_t[j]
                for fc in range(NFC):
                    b.merge_sb(j, fc, h[:, fc * FCW:(fc + 1) * FCW], bb,
                               src_key=src_key, dst_key="xres")


def _emit_vtok(b, vtok, w_tiles, head_stride, ncol, v_off):
    p = b.p
    for tb in range(16):
        vt, vtb = vtok[tb]
        p.op("v", (lambda q, vt=vt: q.memset(vt[:, :], 1.0)), writes=[vtb])
        for hg in range(2):
            ph, pb = p.banks[hg % 2]

            def emit(q, tb=tb, hg=hg, ph=ph):
                ins = None
                for c in range(8):
                    wt, _ = w_tiles[c]
                    src = AP(tensor=wt[0] if isinstance(wt, tuple) else wt,
                             offset=v_off + hg * 8 * head_stride,
                             ap=[[ncol, 128], [head_stride, 8], [1, 64]])
                    ins = q.matmul(ph[:, :],
                                   b.xn[c][0][:, tb * 128:(tb + 1) * 128],
                                   src, start=(c == 0), stop=(c == 7))
                return ins

            p.op("t", emit, reads=[wb for (_, wb) in w_tiles] +
                 [bb for (_, bb) in b.xn], writes=[pb])
            dst = AP(tensor=vt, offset=hg * 8 * 65,
                     ap=[[16 * 65, 128], [65, 8], [1, 64]])
            p.op("v", (lambda q, ph=ph, dst=dst: q.tensor_copy(
                dst, ph[:, :].rearrange("p (h d) -> p h d", h=8))),
                reads=[pb], writes=[vtb])



def emit_moe(b, src_key, dst_key):
    p = b.p
    idh, idb = b.ident
    ones_r, ones_rb = b.ones_row
    with p.phase():
        comb_bf = p.sb([10, T], BF16)
        with p.phase():
            wr = p.sb([128, 8 * 8], BF16)
            src = AP(tensor=b.dram["r3_w"].tensor, offset=0,
                     ap=[[8, 128], [128 * 8, 8], [1, 8]])
            p.op("g", (lambda q, d=wr[0][:, :].rearrange("p (c k) -> p c k", c=8),
                       s=src: q.dma_start(out=d, in_=s)), writes=[wr[1]])
            gates = p.sb([8, T], F32)
            rbc = b.col("L3_rb", rows=8)
            bg = p.banks[4]
            for fc in range(NFC):
                def emit(q, fc=fc):
                    ins = None
                    for c in range(8):
                        ins = q.matmul(bg[0][0:8, :], wr[0][:, c * 8:(c + 1) * 8],
                                       b.xn[c][0][:, fc * FCW:(fc + 1) * FCW],
                                       start=(c == 0), stop=(c == 7))
                    return ins
                p.op("t", emit, reads=[wr[1]] + [bb for (_, bb) in b.xn],
                     writes=[bg[1]])
                p.op("s", (lambda q, fc=fc: q.activation(
                    gates[0][0:8, fc * FCW:(fc + 1) * FCW], bg[0][0:8, :],
                    AF.Identity, bias=rbc[0])),
                    reads=[bg[1], rbc[1]], writes=[gates[1]])

            comb = p.sb([10, T], F32)
            comb_bf = p.sb([10, T], BF16)
            p.op("v", (lambda q: q.memset(comb[0][:, :], 0.5)), writes=[comb[1]])
            NT = 16
            gt = [p.sb([128, 8], F32) for _ in range(NT)]
            m1 = [p.sb([128, 1], F32) for _ in range(NT)]
            m2 = [p.sb([128, 1], F32) for _ in range(NT)]
            eq1 = [p.sb([128, 8], F32) for _ in range(NT)]
            eq2 = [p.sb([128, 8], F32) for _ in range(NT)]
            gm = [p.sb([128, 8], F32) for _ in range(NT)]
            e21 = [p.sb([128, 1], F32) for _ in range(NT)]
            p1c = [p.sb([128, 1], F32) for _ in range(NT)]
            p2c = [p.sb([128, 1], F32) for _ in range(NT)]
            btr = p.banks[5]
            btr2 = p.banks[6]
            for tb in range(NT):
                p.op("t", (lambda q, tb=tb: q.transpose(
                    btr[0][:, 0:8], gates[0][0:8, tb * 128:(tb + 1) * 128],
                    idh[0:8, 0:8])), reads=[gates[1], idb], writes=[btr[1]])
                p.op("v", (lambda q, tb=tb: q.tensor_copy(gt[tb][0][:, :],
                                                          btr[0][:, 0:8])),
                     reads=[btr[1]], writes=[gt[tb][1]])
            for tb in range(NT):
                p.op("v", (lambda q, tb=tb: q.reduce_max(
                    m1[tb][0][:, :], gt[tb][0][:, :], axis=AX.X)),
                    reads=[gt[tb][1]], writes=[m1[tb][1]])
            for tb in range(NT):
                p.op("v", (lambda q, tb=tb: q.tensor_tensor(
                    eq1[tb][0][:, :], gt[tb][0][:, :],
                    m1[tb][0][:, 0:1].broadcast_to((128, 8)), ALU.is_equal)),
                    reads=[gt[tb][1], m1[tb][1]], writes=[eq1[tb][1]])
            for tb in range(NT):
                p.op("v", (lambda q, tb=tb: q.scalar_tensor_tensor(
                    gm[tb][0][:, :], eq1[tb][0][:, :], -1e30, gt[tb][0][:, :],
                    ALU.mult, ALU.add)),
                    reads=[eq1[tb][1], gt[tb][1]], writes=[gm[tb][1]])
            for tb in range(NT):
                p.op("v", (lambda q, tb=tb: q.reduce_max(
                    m2[tb][0][:, :], gm[tb][0][:, :], axis=AX.X)),
                    reads=[gm[tb][1]], writes=[m2[tb][1]])
            for tb in range(NT):
                p.op("v", (lambda q, tb=tb: q.tensor_tensor(
                    eq2[tb][0][:, :], gm[tb][0][:, :],
                    m2[tb][0][:, 0:1].broadcast_to((128, 8)), ALU.is_equal)),
                    reads=[gm[tb][1], m2[tb][1]], writes=[eq2[tb][1]])
            for tb in range(NT):
                p.op("v", (lambda q, tb=tb: q.tensor_tensor(
                    e21[tb][0][:, :], m2[tb][0][:, :], m1[tb][0][:, :],
                    ALU.subtract)),
                    reads=[m1[tb][1], m2[tb][1]], writes=[e21[tb][1]])
            for tb in range(NT):
                p.op("s", (lambda q, tb=tb: q.activation(
                    e21[tb][0][:, :], e21[tb][0][:, :], AF.Exp)),
                    reads=[e21[tb][1]], writes=[e21[tb][1]])
            for tb in range(NT):
                p.op("v", (lambda q, tb=tb: q.tensor_scalar(
                    p1c[tb][0][:, :], e21[tb][0][:, :], 1.0, None, ALU.add)),
                    reads=[e21[tb][1]], writes=[p1c[tb][1]])
            for tb in range(NT):
                p.op("v", (lambda q, tb=tb: q.reciprocal(
                    p1c[tb][0][:, :], p1c[tb][0][:, :])),
                    reads=[p1c[tb][1]], writes=[p1c[tb][1]])
            for tb in range(NT):
                p.op("v", (lambda q, tb=tb: q.tensor_tensor(
                    p2c[tb][0][:, :], e21[tb][0][:, :], p1c[tb][0][:, :],
                    ALU.mult)),
                    reads=[e21[tb][1], p1c[tb][1]], writes=[p2c[tb][1]])
            for tb in range(NT):
                p.op("v", (lambda q, tb=tb: q.tensor_scalar(
                    eq1[tb][0][:, :], eq1[tb][0][:, :], p1c[tb][0][:, :], None,
                    ALU.mult)),
                    reads=[p1c[tb][1], eq1[tb][1]], writes=[eq1[tb][1]])
            for tb in range(NT):
                p.op("v", (lambda q, tb=tb: q.tensor_scalar(
                    eq2[tb][0][:, :], eq2[tb][0][:, :], p2c[tb][0][:, :], None,
                    ALU.mult)),
                    reads=[p2c[tb][1], eq2[tb][1]], writes=[eq2[tb][1]])
            for tb in range(NT):
                p.op("v", (lambda q, tb=tb: q.tensor_tensor(
                    eq1[tb][0][:, :], eq1[tb][0][:, :], eq2[tb][0][:, :],
                    ALU.add)),
                    reads=[eq2[tb][1], eq1[tb][1]], writes=[eq1[tb][1]])
            for tb in range(NT):
                p.op("t", (lambda q, tb=tb: q.transpose(
                    btr2[0][0:8, 0:128], eq1[tb][0][:, 0:8], idh[:, :])),
                    reads=[eq1[tb][1], idb], writes=[btr2[1]])
                p.op("v", (lambda q, tb=tb: q.tensor_copy(
                    comb[0][0:8, tb * 128:(tb + 1) * 128], btr2[0][0:8, 0:128])),
                    reads=[btr2[1]], writes=[comb[1]])
            p.op("v", (lambda q: q.tensor_copy(comb_bf[0][:, :], comb[0][:, :])),
                 reads=[comb[1]], writes=[comb_bf[1]], tail=True)
            p.op("y", (lambda q: q.dma_start(out=b.dram["sp_comb"],
                                             in_=comb_bf[0][:, :])),
                 reads=[comb_bf[1]], writes=[p.db("sp_comb")])

        b2sb = p.sb([10, 1024], BF16)
        comb_rl = [p.sb([10, FCW], BF16) for _ in range(2)]
        b.wload(None, b2sb[0][:, :], b2sb[1], b.dram["b2ext"], 1024)

        yacc = [p.sb([128, T], F32) for _ in range(8)]
        hbf = [p.sb([128, FCW], BF16) for _ in range(16)]
        gel = p.sb([128, FCW], BF16)
        cbb = [p.sb([128, FCW], BF16) for _ in range(2)]
        w1t = [p.sb([128, 8 * 128], BF16) for _ in range(16)]
        w2t = [p.sb([128, 16 * 128], BF16) for _ in range(8)]

        for e in range(11):
            if e < 8:
                w1n, w1o = "m3_w1", e * 1024 * 2048
                w2n, w2o = "m3_w2", e * 2048 * 1024
                b1n, b1k = "L3_b1", e * 16
            elif e < 10:
                w1n, w1o = "m3_sw1", (e - 8) * 1024 * 2048
                w2n, w2o = "m3_sw2", (e - 8) * 2048 * 1024
                b1n, b1k = "L3_sb1", (e - 8) * 16
            if e < 10:
                for j in range(16):
                    wh, wb2 = w1t[j]
                    for hf in range(2):
                        src = AP(tensor=b.dram[w1n].tensor,
                                 offset=w1o + hf * 4 * 128 * 2048 + j * 128,
                                 ap=[[2048, 128], [128 * 2048, 4], [1, 128]])
                        stage = None
                        sview = 4
                        b.wload(stage, wh[:, hf * 512:(hf + 1) * 512], wb2,
                                src, 4 * 128, view=sview)
                for pb2 in range(8):
                    wh, wb2 = w2t[pb2]
                    for cg in range(4):
                        src = AP(tensor=b.dram[w2n].tensor,
                                 offset=w2o + pb2 * 128 + cg * 4 * 128 * 1024,
                                 ap=[[1024, 128], [128 * 1024, 4], [1, 128]])
                        stage = None
                        sview = 4
                        b.wload(stage, wh[:, cg * 512:(cg + 1) * 512], wb2,
                                src, 4 * 128, view=sview)
            for tc in range(NFC):
                fsl = slice(tc * FCW, (tc + 1) * FCW)
                if e < 10:
                    ch, chb = cbb[tc % 2]
                    src = b.dram["sp_comb"][e:e + 1, fsl].broadcast_to(
                        (128, FCW))
                    p.op("y", (lambda q, d=ch, s=src: q.dma_start(
                        out=d[:, :], in_=s)), reads=[p.db("sp_comb")],
                        writes=[chb])
                    for j in range(16):
                        ph, pb = p.banks[j % 2]
                        wh, wb2 = w1t[j]

                        def emit(q, wh=wh, ph=ph, fsl=fsl):
                            ins = None
                            for c in range(8):
                                ins = q.matmul(ph[:, :],
                                               wh[:, c * 128:(c + 1) * 128],
                                               b.xn[c][0][:, fsl],
                                               start=(c == 0), stop=(c == 7))
                            return ins
                        p.op("t", emit, reads=[wb2] + [bb for (_, bb) in b.xn],
                             writes=[pb])
                        bcol = b.col(b1n, b1k + j)
                        p.op("s", (lambda q, ph=ph, bc=bcol[0]: q.activation(
                            gel[0][:, :], ph[:, :], AF.Gelu, bias=bc)),
                            reads=[pb, bcol[1]], writes=[gel[1]])
                        hh, hb = hbf[j]
                        p.op("v", (lambda q, hh=hh, ch=ch: q.tensor_tensor(
                            hh[:, :], gel[0][:, :], ch[:, :], ALU.mult)),
                            reads=[gel[1], chb], writes=[hb])
                for pblk in range(8):
                    ph2, pb3 = p.banks[2 + pblk % 2]
                    if e < 10:
                        wh, wb2 = w2t[pblk]

                        def emity(q, wh=wh, ph2=ph2):
                            ins = None
                            for c in range(16):
                                ins = q.matmul(ph2[:, :],
                                               wh[:, c * 128:(c + 1) * 128],
                                               hbf[c][0][:, :],
                                               start=(c == 0), stop=(c == 15))
                            return ins
                        p.op("t", emity, reads=[wb2] + [hb for (_, hb) in hbf],
                             writes=[pb3])
                    else:
                        cr_h, cr_b = comb_rl[tc % 2]
                        if pblk == 0:
                            src2 = b.dram["sp_comb"][:, fsl]
                            p.op("y", (lambda q, d=cr_h, s=src2: q.dma_start(
                                out=d[:, :], in_=s)),
                                reads=[p.db("sp_comb")], writes=[cr_b])

                        def emity(q, ph2=ph2, pblk=pblk, cr_h=cr_h):
                            return q.matmul(
                                ph2[:, :],
                                b2sb[0][:, pblk * 128:(pblk + 1) * 128],
                                cr_h[:, :], start=True, stop=True)
                        p.op("t", emity, reads=[b2sb[1], cr_b],
                             writes=[pb3])
                    yh, yb = yacc[pblk]
                    if e == 0:
                        p.op("v", (lambda q, yh=yh, ph2=ph2, fsl=fsl:
                                   q.tensor_copy(yh[:, fsl], ph2[:, :])),
                             reads=[pb3], writes=[yb])
                    else:
                        p.op("v", (lambda q, yh=yh, ph2=ph2, fsl=fsl:
                                   q.tensor_tensor(yh[:, fsl], yh[:, fsl],
                                                   ph2[:, :], ALU.add)),
                             reads=[pb3, yb], writes=[yb])

        for j in range(8):
            yh, yb = yacc[j]
            for fc in range(NFC):
                b.merge_sb(j, fc, yh[:, fc * FCW:(fc + 1) * FCW], yb,
                           src_key=src_key, dst_key=dst_key)



DBG_HOOK = None


def build_program(layers, warrs):
    nc = bass.Bass("TRN2", target_bir_lowering=False, debug=False)
    p = P(nc)
    dram = {}
    dram["xin"] = nc.dram_tensor("xin", [D, T], F32, kind="ExternalInput").ap()
    dram["xout"] = nc.dram_tensor("xout", [D, T], F32,
                                  kind="ExternalOutput").ap()
    dram["cols_all"] = nc.dram_tensor("cols_all", [128, NCOLS], F32,
                                      kind="ExternalInput").ap()
    dram["ident"] = nc.dram_tensor("ident", [128, 128], F32,
                                   kind="ExternalInput").ap()
    for L in layers:
        for name, _ in _wspec(L):
            arr = warrs[name]
            dt = I32 if arr.dtype == np.int32 else F32
            dram[name] = nc.dram_tensor(name, list(arr.shape), dt,
                                        kind="ExternalInput").ap()
    for k, (shape, dt) in {
        "xres": ([D, T], F32),
        "sp_u": ([2048, T], BF16),
        "sp_z": ([2048, T], BF16),
        "sp_dt": ([2048, T], F32),
        "sp_bc": ([32, T], BF16),
        "sp_hb": ([4096, T], BF16),
        "sp_hc": ([4096, T], BF16),
        "sp_ctx": ([D, T], BF16),
        "sp_den": ([16, T], F32),
        "sp_comb": ([10, T], BF16),
        "sp_expb": ([16, 4095], BF16),
    }.items():
        dram[k] = nc.dram_tensor(k, shape, dt, kind="Internal").ap()

    b = B(p, dram)
    b.cols = p.sb([128, NCOLS], F32, "cols")
    b.ident = p.sb([128, 128], F32, "identsb")
    b.ones_col = p.sb([128, 1], F32, "ones_col")
    b.ones_col_bf = p.sb([128, 1], BF16, "ones_col_bf")
    b.ones_row = p.sb([1, 128], F32, "ones_row")
    b.xn = [p.sb([128, T], BF16, f"xn{i}") for i in range(8)]
    b.mt = [p.sb([128, FCW], F32) for _ in range(2)]
    b.mt2 = [p.sb([128, FCW], F32) for _ in range(2)] if MERGE_ACC else None
    b.mti = 0

    p.op("y", lambda q: q.dma_start(out=b.cols[0][:, :], in_=dram["cols_all"]),
         writes=[b.cols[1]])
    p.op("y", lambda q: q.dma_start(out=b.ident[0][:, :], in_=dram["ident"]),
         writes=[b.ident[1]])
    p.op("v", lambda q: q.memset(b.ones_col[0][:, :], 1.0),
         writes=[b.ones_col[1]])
    p.op("v", lambda q: q.memset(b.ones_col_bf[0][:, :], 1.0),
         writes=[b.ones_col_bf[1]])
    p.op("v", lambda q: q.memset(b.ones_row[0][:, :], 1.0),
         writes=[b.ones_row[1]])
    for cv in (1e-5, -0.5, 0.125, 1.0 / 127.0, 2.0, 0.5, -1.0, 1.0):
        cth = p.sb([128, 1], F32)
        p.op("v", (lambda q, t=cth[0], cv=cv: q.memset(t[:, :], cv)),
             writes=[cth[1]])
        nc.const_aps.register(F32, cv, cth[0][:, :])
    for j in range(8 if MERGE_ACC else 0):
        for fc in range(NFC):
            srcp = dram["xin"][j * 128:(j + 1) * 128, fc * FCW:(fc + 1) * FCW]
            dstp = dram["xres"][j * 128:(j + 1) * 128, fc * FCW:(fc + 1) * FCW]
            p.op("y", (lambda q, s=srcp, d=dstp: q.dma_start(out=d, in_=s)),
                 reads=[p.db(("xin", j, fc))], writes=[p.db(("xres", j, fc))])
    p.fence()
    if 1 in layers:
        o1 = COLS["L1_bdt"][0]; o2 = COLS["L1_dtb"][0]; o3 = COLS["L1_dtsum"][0]
        ch, cb = b.cols
        p.op("v", lambda q: q.tensor_tensor(ch[:, o3:o3 + 16],
                                            ch[:, o1:o1 + 16],
                                            ch[:, o2:o2 + 16], ALU.add),
             reads=[cb], writes=[cb])

    first, last = layers[0], layers[-1]
    for L in layers:
        src_key = "xin" if L == first else "xres"
        dst_key = "xout" if L == last else "xres"
        if L == 0:
            p.paranoid = True
        (emit_layernorm if LN_HARD else emit_layernorm_v0)(b, src_key)
        if L == 0:
            emit_mamba0(b, src_key)
            p.paranoid = False
            emit_ffn_ternary(b, "L0", "xres", dst_key)
        elif L == 1:
            emit_mamba1(b, src_key)
            emit_ffn_quant(b, "xres", dst_key)
        elif L == 2:
            _emit_attention(b, src_key, 2)
            emit_ffn_ternary(b, "L2", "xres", dst_key)
        else:
            _emit_attention(b, src_key, 3)
            emit_moe(b, "xres", dst_key)

    for j in range(8 if MERGE_ACC else 0):
        for fc in range(NFC):
            srcp = dram["xres"][j * 128:(j + 1) * 128, fc * FCW:(fc + 1) * FCW]
            dstp = dram["xout"][j * 128:(j + 1) * 128, fc * FCW:(fc + 1) * FCW]
            p.op("y", (lambda q, s=srcp, d=dstp: q.dma_start(out=d, in_=s)),
                 reads=[p.db(("xres", j, fc))], writes=[p.db(("xout", j, fc))])
    if DBG_HOOK is not None:
        DBG_HOOK(p, dram, b)
    for k in P.ENG:
        p.raw_wait("y", k, p.cnt[k])
    with nc.Block() as block:
        p.finalize(block)
    return nc, p



def _host_prep(inputs):
    x = np.asarray(inputs["L0_attn_x"])
    xT = np.ascontiguousarray(x.transpose(0, 2, 1))
    cols = _build_cols(inputs)
    ident = np.eye(128, dtype=np.float32)
    warrs = {}
    for L in range(4):
        for name, fn in _wspec(L):
            a = np.asarray(fn(inputs))
            if a.dtype != np.int32:
                a = a.astype(np.float32)
            warrs[name] = np.ascontiguousarray(a)
    return xT, cols, ident, warrs


def kernel(**inputs):
    xT, cols, ident, warrs = _host_prep(inputs)
    groups = [[0, 1, 2, 3]] if FUSED else [[0], [1], [2], [3]]
    state = [xT[c] for c in range(8)]
    for layers in groups:
        nc, p = build_program(layers, warrs)
        in_maps = []
        for c in range(8):
            m = {"xin": state[c], "cols_all": cols, "ident": ident}
            for L in layers:
                for name, _ in _wspec(L):
                    m[name] = warrs[name]
            in_maps.append(m)
        res = run_bass_kernel_spmd(nc, in_maps, core_ids=list(range(8)))
        state = [np.asarray(r["xout"]) for r in res.results]
        p.root.close()
    out = np.stack([s.T for s in state], axis=0).astype(np.float32)
    return np.ascontiguousarray(out)


# version 16
# speedup vs baseline: 1.1756x; 1.0132x over previous
import math
from contextlib import ExitStack

import numpy as np
import ml_dtypes

import concourse.bass as bass
import concourse.mybir as mybir
from concourse.bass_utils import run_bass_kernel_spmd

F32 = mybir.dt.float32
BF16 = mybir.dt.bfloat16
I32 = mybir.dt.int32
AF = mybir.ActivationFunctionType
ALU = mybir.AluOpType
AX = mybir.AxisListType
AP = bass.AP

T = 2048
D = 1024
FCW = 512
NFC = T // FCW

FUSED = True
LN_HARD = False
GN_HARD = False
MERGE_ACC = False
YG_REUSE = True


def _t(a):
    return np.ascontiguousarray(np.asarray(a).T)


def _cols(vec, n128):
    return np.ascontiguousarray(np.asarray(vec).reshape(n128, 128).T)


def _acols(alog):
    a = np.asarray(alog).reshape(16, 128, 16).transpose(1, 0, 2).reshape(128, 256)
    return np.ascontiguousarray(a)


def _bucket_onehot():
    r = np.arange(4095) - 2047
    nb = 16
    rb = (r > 0).astype(np.int64) * nb
    ra = np.abs(r)
    me = 8
    rp_f = np.maximum(ra, 1).astype(np.float32)
    ril = me + (np.log(rp_f / me) / math.log(128 / me) * (nb - me)).astype(np.int32)
    ril = np.minimum(ril, nb - 1)
    b = rb + np.where(ra < me, ra, ril)
    oh = np.zeros((32, 4095), np.float32)
    oh[b, np.arange(4095)] = 1.0
    return oh


COLS = {}
_nxt = [0]
for _n, _w in [
    ("L0_b_in", 32), ("L0_b_xproj", 1), ("L0_b_dt", 16), ("L0_b_out", 8),
    ("L0_D", 16), ("L0_w1s", 32), ("L0_w2s", 32), ("L0_h1s", 32),
    ("L0_h2s", 32), ("L0_w3s", 8),
    ("L1_bdt", 16), ("L1_dtb", 16), ("L1_gamma", 16), ("L1_beta", 16),
    ("L2_w1s", 32), ("L2_w2s", 32), ("L2_h1s", 32), ("L2_h2s", 32),
    ("L2_w3s", 8),
    ("L3_rb", 1), ("L3_b1", 128), ("L3_sb1", 32),
    ("L1_dtsum", 16),
]:
    COLS[_n] = (_nxt[0], _w)
    _nxt[0] += _w
NCOLS = _nxt[0]


def _build_cols(inp):
    c = np.zeros((128, NCOLS), np.float32)

    def put(name, vec):
        o, w = COLS[name]
        v = np.asarray(vec).astype(np.float32).reshape(-1)
        if w == 1:
            c[: v.shape[0], o] = v
        else:
            c[:, o:o + w] = _cols(v, w)

    put("L0_b_in", inp["L0_attn_b_in"]); put("L0_b_xproj", inp["L0_attn_b_xproj"])
    put("L0_b_dt", inp["L0_attn_b_dt"]); put("L0_b_out", inp["L0_attn_b_out"])
    put("L0_D", inp["L0_attn_D_param"])
    put("L0_w1s", inp["L0_ffn_w1_s"]); put("L0_w2s", inp["L0_ffn_w2_s"])
    put("L0_h1s", inp["L0_ffn_h1_s"]); put("L0_h2s", inp["L0_ffn_h2_s"])
    put("L0_w3s", inp["L0_ffn_w3_s"])
    put("L1_bdt", inp["L1_attn_bdt"]); put("L1_dtb", inp["L1_attn_dt_bias"])
    put("L1_gamma", inp["L1_attn_gamma"]); put("L1_beta", inp["L1_attn_beta"])
    put("L2_w1s", inp["L2_ffn_w1_s"]); put("L2_w2s", inp["L2_ffn_w2_s"])
    put("L2_h1s", inp["L2_ffn_h1_s"]); put("L2_h2s", inp["L2_ffn_h2_s"])
    put("L2_w3s", inp["L2_ffn_w3_s"])
    put("L3_rb", inp["L3_ffn_router_b"]); put("L3_b1", inp["L3_ffn_b1"])
    put("L3_sb1", inp["L3_ffn_sb1"])
    return c


def _wspec(layer):
    if layer == 0:
        return [
            ("w0_in", lambda i: i["L0_attn_W_in"]),
            ("w0_xp", lambda i: i["L0_attn_W_xproj"]),
            ("w0_dt", lambda i: i["L0_attn_W_dt"]),
            ("w0_out", lambda i: i["L0_attn_W_out"]),
            ("a0", lambda i: _acols(i["L0_attn_A_log"])),
            ("f0_w1", lambda i: _t(i["L0_ffn_w1_w"])),
            ("f0_w2", lambda i: _t(i["L0_ffn_w2_w"])),
            ("f0_h1", lambda i: _t(i["L0_ffn_h1_w"])),
            ("f0_h2", lambda i: _t(i["L0_ffn_h2_w"])),
            ("f0_w3", lambda i: _t(i["L0_ffn_w3_w"])),
        ]
    if layer == 1:
        return [
            ("w1_in", lambda i: _t(i["L1_attn_Win"])),
            ("w1_bc", lambda i: np.concatenate(
                [_t(i["L1_attn_WB"]), _t(i["L1_attn_WC"])], axis=1)),
            ("w1_dt", lambda i: _t(i["L1_attn_Wdt"])),
            ("w1_out", lambda i: _t(i["L1_attn_Wout"])),
            ("a1", lambda i: _acols(i["L1_attn_A_log"])),
            ("f1_wg", lambda i: _t(i["L1_ffn_w_gate"])),
            ("f1_wu", lambda i: _t(i["L1_ffn_w_up"])),
            ("f1_wd", lambda i: _t(i["L1_ffn_w_down"])),
        ]
    if layer == 2:
        return [
            ("w2_q", lambda i: _t(i["L2_attn_Wq"])),
            ("w2_k", lambda i: _t(i["L2_attn_Wk"])),
            ("w2_v", lambda i: _t(i["L2_attn_Wv"])),
            ("w2_o", lambda i: _t(i["L2_attn_Wo"])),
            ("rel_emb", lambda i: i["L2_attn_rel_emb"]),
            ("onehot", lambda i: _bucket_onehot()),
            ("f2_w1", lambda i: _t(i["L2_ffn_w1_w"])),
            ("f2_w2", lambda i: _t(i["L2_ffn_w2_w"])),
            ("f2_h1", lambda i: _t(i["L2_ffn_h1_w"])),
            ("f2_h2", lambda i: _t(i["L2_ffn_h2_w"])),
            ("f2_w3", lambda i: _t(i["L2_ffn_w3_w"])),
        ]
    return [
        ("w3_qkv", lambda i: _t(i["L3_attn_W_qkv"])),
        ("r3_w", lambda i: i["L3_ffn_router_w"]),
        ("m3_w1", lambda i: i["L3_ffn_w1"]),
        ("m3_w2", lambda i: i["L3_ffn_w2"]),
        ("m3_sw1", lambda i: i["L3_ffn_sw1"]),
        ("m3_sw2", lambda i: i["L3_ffn_sw2"]),
        ("b2ext", lambda i: np.concatenate(
            [i["L3_ffn_b2"], i["L3_ffn_sb2"]], axis=0)),
    ]



class Buf:
    __slots__ = ("w", "r")

    def __init__(self):
        self.w = None
        self.r = []


_NULL = Buf()


class P:
    ENG = ("t", "v", "s", "g", "y")

    def __init__(self, nc):
        self.nc = nc
        self.root = ExitStack()
        self.stacks = [self.root]
        self.streams = {k: [] for k in self.ENG}
        self.cnt = dict.fromkeys(self.ENG, 0)
        self.seen = {k: dict.fromkeys(self.ENG, 0) for k in self.ENG}
        self.sems = {k: self.root.enter_context(nc.semaphore("sem_" + k))
                     for k in self.ENG}
        self.dbufs = {}
        self._uid = 0
        self.banks = []
        for i in range(8):
            h = self.root.enter_context(
                nc.psum_tensor(f"bank{i}", [128, FCW], F32))
            self.banks.append((h, Buf()))

    def sb(self, shape, dt, name=None):
        self._uid += 1
        h = self.stacks[-1].enter_context(
            self.nc.sbuf_tensor(name or f"sb{self._uid}", list(shape), dt))
        return (h, Buf())

    def db(self, key):
        b = self.dbufs.get(key)
        if b is None:
            b = self.dbufs[key] = Buf()
        return b

    paranoid = False
    PARANOID_FULL = {k: ("v", "s", "y", "g")
                     for k in ("t", "v", "s", "g", "y")}
    PARANOID_NOSELF = {"t": ("v", "s", "y", "g"), "v": ("s", "y", "g"),
                       "s": ("v", "y", "g"), "y": ("v", "s", "g"),
                       "g": ("v", "s", "y")}
    PARANOID_FAST = {"v": ("y", "g"), "s": ("v", "y", "g"),
                     "t": (), "y": (), "g": ()}

    def op(self, ek, emit, reads=(), writes=(), tail=False):
        need = {}
        if self.paranoid:
            pset = self.paranoid if isinstance(self.paranoid, dict) else                 P.PARANOID_FULL
            for sk in pset.get(ek, ()):
                if self.cnt[sk] > need.get(sk, 0):
                    need[sk] = self.cnt[sk]
        for b in reads:
            if b.w is not None:
                sk, v = b.w
                if (sk != ek or tail) and v > need.get(sk, 0):
                    need[sk] = v
        for b in writes:
            deps = ([b.w] if b.w is not None else []) + b.r
            for sk, v in deps:
                if sk != ek and v > need.get(sk, 0):
                    need[sk] = v
        st = self.streams[ek]
        for sk in self.ENG:
            v = need.get(sk, 0)
            if v > self.seen[ek].get(sk, 0):
                self.seen[ek][sk] = v
                st.append(("w", sk, v))
        n = 16 if ek in ("g", "y") else 1
        self.cnt[ek] += n
        tok = (ek, self.cnt[ek])
        st.append(("e", emit, ek, n))
        for b in reads:
            b.r.append(tok)
        for b in writes:
            b.w = tok
            b.r = []
        return tok

    def raw_wait(self, ek, sk, v):
        if v > self.seen[ek][sk]:
            self.seen[ek][sk] = v
            self.streams[ek].append(("w", sk, v))

    def fence(self):
        for ek in self.ENG:
            for sk in self.ENG:
                if sk != ek:
                    self.raw_wait(ek, sk, self.cnt[sk])

    class _Phase:
        def __init__(self, p):
            self.p = p

        def __enter__(self):
            self.p.stacks.append(ExitStack())
            return self

        def __exit__(self, *a):
            st = self.p.stacks.pop()
            st.close()
            self.p.fence()
            return False

    def phase(self):
        return P._Phase(self)

    def finalize(self, block):
        sems = self.sems
        names = {"t": "tensor", "v": "vector", "s": "scalar",
                 "g": "gpsimd", "y": "sync"}
        for k in self.ENG:
            items = self.streams[k]

            def make(items=items):
                def f(q):
                    for it in items:
                        if it[0] == "w":
                            q.wait_ge(sems[it[1]], it[2])
                        else:
                            it[1](q).then_inc(sems[it[2]], it[3])
                return f

            getattr(block, names[k])(make())



class B:
    def __init__(self, p, dram):
        self.p = p
        self.dram = dram

    def col(self, name, blk=0, rows=128):
        o, w = COLS[name]
        h, b = self.cols
        return (h[0:rows, o + blk:o + blk + 1], b)

    def launder(self, tiles, out_tiles):
        p = self.p
        for (sh, sb2), (dh, db2) in zip(tiles, out_tiles):
            p.op("y", (lambda q, s=sh, d=dh: q.dma_start(out=d[:, :],
                                                         in_=s[:, :])),
                 reads=[sb2], writes=[db2])
        return out_tiles

    def wload(self, stage, dst_ap, dst_buf, src_ap, cols, view=None):
        p = self.p
        dview = dst_ap if view is None else dst_ap.rearrange(
            "p (c k) -> p c k", c=int(view))
        p.op("g", (lambda q, d=dview, s=src_ap: q.dma_start(out=d, in_=s)),
             reads=[], writes=[dst_buf])

    def linear(self, wname, in_tiles, out_cb, in_dim, out_dim, banks=(0, 1),
               woff=0, wrowlen=None):
        p = self.p
        nci = in_dim // 128
        ncj = out_dim // 128
        wrl = wrowlen if wrowlen is not None else out_dim
        w_d = self.dram[wname]
        wb = [p.sb([128, nci * 128], BF16) for _ in range(2)]
        for j in range(ncj):
            wt, wbuf = wb[j % 2]
            for cgi, cg0 in enumerate(range(0, nci, 4)):
                cgn = min(4, nci - cg0)
                src = AP(tensor=w_d.tensor, offset=woff + cg0 * 128 * wrl + j * 128,
                         ap=[[wrl, 128], [128 * wrl, cgn], [1, 128]])
                stage = None
                sview = cgn
                dst = wt[:, cg0 * 128:(cg0 + cgn) * 128]
                self.wload(stage, dst, wbuf, src, cgn * 128, view=sview)
            for fc in range(NFC):
                bh, bb = self.p.banks[banks[(j * NFC + fc) % len(banks)]]

                def emit(q, wt=wt, bh=bh, fc=fc, nci=nci, its=in_tiles):
                    ins = None
                    for c in range(nci):
                        ins = q.matmul(bh[:, :], wt[:, c * 128:(c + 1) * 128],
                                       its[c][0](fc), start=(c == 0),
                                       stop=(c == nci - 1))
                    return ins

                p.op("t", emit, reads=[wbuf] + [t[1] for t in in_tiles],
                     writes=[bb])
                out_cb(j, fc, bh, bb)

    def linear_stream(self, wname, rhs_dram, out_cb, in_dim, out_dim,
                      banks=(0, 1), jg=8, rhs_bf=True):
        p = self.p
        nci = in_dim // 128
        ncj = out_dim // 128
        w_d = self.dram[wname]
        with p.phase():
            panels = [p.sb([128, nci * 128], BF16) for _ in range(jg)]
            rts = [p.sb([128, FCW], BF16 if rhs_bf else F32)
                   for _ in range(nci)]
            nst = 0
            for jg0 in range(0, ncj, jg):
                jn = min(jg, ncj - jg0)
                for ji in range(jn):
                    j = jg0 + ji
                    wt, wbuf = panels[ji]
                    for cg0 in range(0, nci, 4):
                        cgn = min(4, nci - cg0)
                        src = AP(tensor=w_d.tensor,
                                 offset=cg0 * 128 * out_dim + j * 128,
                                 ap=[[out_dim, 128], [128 * out_dim, cgn],
                                     [1, 128]])
                        stage = None
                        nst += 1
                        sview = cgn
                        dst = wt[:, cg0 * 128:(cg0 + cgn) * 128]
                        self.wload(stage, dst, wbuf, src, cgn * 128,
                                   view=sview)
                for fc in range(NFC):
                    for c in range(nci):
                        h, bb = rts[c]
                        src = self.dram[rhs_dram][c * 128:(c + 1) * 128,
                                                  fc * FCW:(fc + 1) * FCW]
                        p.op("y", (lambda q, d=h, s=src:
                                   q.dma_start(out=d[:, :], in_=s)),
                             reads=[p.db((rhs_dram, c, fc))], writes=[bb])
                    for ji in range(jn):
                        j = jg0 + ji
                        wt, wbuf = panels[ji]
                        bh, bb = p.banks[banks[(j * NFC + fc) % len(banks)]]

                        def emit(q, wt=wt, bh=bh, nci=nci):
                            ins = None
                            for c in range(nci):
                                ins = q.matmul(bh[:, :],
                                               wt[:, c * 128:(c + 1) * 128],
                                               rts[c][0][:, :], start=(c == 0),
                                               stop=(c == nci - 1))
                            return ins

                        p.op("t", emit, reads=[wbuf] + [t[1] for t in rts],
                             writes=[bb])
                        out_cb(j, fc, bh, bb)

    def merge(self, j, fc, ps_h, ps_b, scalar=1.0, scalar_buf=None,
              op0=ALU.mult, src_key="xres", dst_key="xres"):
        p = self.p
        if not MERGE_ACC:
            return self.merge_v0(j, fc, ps_h, ps_b, scalar, scalar_buf, op0,
                                 src_key, dst_key)
        h, b = self.mt[self.mti]
        h2, b2 = self.mt2[self.mti]
        self.mti = (self.mti + 1) % len(self.mt)
        dbuf = p.db(("xres", j, fc))
        dst = self.dram["xres"][j * 128:(j + 1) * 128, fc * FCW:(fc + 1) * FCW]
        sb_extra = [scalar_buf] if scalar_buf is not None else []
        p.op("v", (lambda q, d=h, ps=ps_h, sc=scalar, op0=op0:
                   q.tensor_scalar(d[:, :], ps[:, :], sc, None, op0)),
             reads=[ps_b] + sb_extra, writes=[b])
        p.op("y", (lambda q, s=h, d=h2: q.dma_start(out=d[:, :], in_=s[:, :])),
             reads=[b], writes=[b2])
        p.op("g", (lambda q, s=h2, d=dst: q.dma_start(
            out=d, in_=s[:, :], accum_op=ALU.add)),
             reads=[b2, dbuf], writes=[dbuf], tail=True)

    def merge_sb(self, j, fc, src_ap, src_buf, src_key="xres", dst_key="xres"):
        p = self.p
        if not MERGE_ACC:
            return self.merge_sb_v0(j, fc, src_ap, src_buf, src_key, dst_key)
        h2, b2 = self.mt2[self.mti]
        self.mti = (self.mti + 1) % len(self.mt)
        dbuf = p.db(("xres", j, fc))
        dst = self.dram["xres"][j * 128:(j + 1) * 128, fc * FCW:(fc + 1) * FCW]
        p.op("y", (lambda q, s=src_ap, d=h2: q.dma_start(out=d[:, :], in_=s)),
             reads=[src_buf], writes=[b2])
        p.op("g", (lambda q, s=h2, d=dst: q.dma_start(
            out=d, in_=s[:, :], accum_op=ALU.add)),
             reads=[b2, dbuf], writes=[dbuf], tail=True)

    def merge_v0(self, j, fc, ps_h, ps_b, scalar=1.0, scalar_buf=None,
                 op0=ALU.mult, src_key="xres", dst_key="xres"):
        p = self.p
        h, b = self.mt[(j + fc) % 2]
        sbuf = p.db((src_key, j, fc))
        dbuf = p.db((dst_key, j, fc))
        src = self.dram[src_key][j * 128:(j + 1) * 128, fc * FCW:(fc + 1) * FCW]
        dst = self.dram[dst_key][j * 128:(j + 1) * 128, fc * FCW:(fc + 1) * FCW]
        p.op("y", (lambda q, d=h, s=src: q.dma_start(out=d[:, :], in_=s)),
             reads=[sbuf], writes=[b])
        sb_extra = [scalar_buf] if scalar_buf is not None else []
        p.op("v", (lambda q, d=h, ps=ps_h, sc=scalar, op0=op0:
                   q.scalar_tensor_tensor(d[:, :], ps[:, :], sc, d[:, :],
                                          op0, ALU.add)),
             reads=[ps_b, b] + sb_extra, writes=[b])
        p.op("y", (lambda q, s=h, d=dst: q.dma_start(out=d, in_=s[:, :])),
             reads=[b], writes=[dbuf])

    def merge_sb_v0(self, j, fc, src_ap, src_buf, src_key="xres",
                    dst_key="xres"):
        p = self.p
        h, b = self.mt[(j + fc) % 2]
        sbuf = p.db((src_key, j, fc))
        dbuf = p.db((dst_key, j, fc))
        src = self.dram[src_key][j * 128:(j + 1) * 128, fc * FCW:(fc + 1) * FCW]
        dst = self.dram[dst_key][j * 128:(j + 1) * 128, fc * FCW:(fc + 1) * FCW]
        p.op("y", (lambda q, d=h, s=src: q.dma_start(out=d[:, :], in_=s)),
             reads=[sbuf], writes=[b])
        p.op("v", (lambda q, d=h, sa=src_ap: q.tensor_tensor(
            d[:, :], d[:, :], sa, ALU.add)), reads=[src_buf, b], writes=[b])
        p.op("y", (lambda q, s=h, d=dst: q.dma_start(out=d, in_=s[:, :])),
             reads=[b], writes=[dbuf])

    def xn_in(self):
        return [((lambda fc, h=h: h[:, fc * FCW:(fc + 1) * FCW]), bb)
                for (h, bb) in self.xn]



def emit_layernorm(b, src_key):
    p = b.p
    x_d = b.dram[src_key]
    ones_c, ones_cb = b.ones_col
    ones_r, ones_rb = b.ones_row
    with p.phase():
        xt = [p.sb([128, FCW], F32) for _ in range(8)]
        sq = [p.sb([128, FCW], F32) for _ in range(2)]
        sq2 = [p.sb([128, FCW], F32) for _ in range(2)]
        rws = [p.sb([1, FCW], F32) for _ in range(7)]
        (rmu, rmub), (rex, rexb), (rm2, rm2b), (rvar, rvarb), (rstd, rstdb), \
            (rbn, rbnb), (pad, padb) = rws
        rvL, rvLb = rm2, rm2b
        lnv, lnvb = rex, rexb
        rsL, rsLb = rvar, rvarb
        bsx = p.banks[4]
        bsq = p.banks[5]
        ba = p.banks[6]
        bbc = p.banks[7]
        for fc in range(NFC):
            for i in range(8):
                h, bb = xt[i]
                src = x_d[i * 128:(i + 1) * 128, fc * FCW:(fc + 1) * FCW]
                p.op("y", (lambda q, d=h, s=src: q.dma_start(out=d[:, :], in_=s)),
                     reads=[p.db((src_key, i, fc))], writes=[bb])
            for i in range(8):
                h, bb = xt[i]
                sh, sb2 = sq[i % 2]
                s2h, s2b = sq2[i % 2]
                p.op("s", (lambda q, d=sh, s=h: q.activation(
                    d[:, :], s[:, :], AF.Square)), reads=[bb], writes=[sb2])
                p.op("y", (lambda q, s=sh, d=s2h: q.dma_start(
                    out=d[:, :], in_=s[:, :])), reads=[sb2], writes=[s2b])
                p.op("t", (lambda q, s=h, i=i: q.matmul(
                    bsx[0][0:1, :], ones_c[:, :], s[:, :], start=(i == 0),
                    stop=(i == 7))), reads=[bb, ones_cb], writes=[bsx[1]])
                p.op("t", (lambda q, s=s2h, i=i: q.matmul(
                    bsq[0][0:1, :], ones_c[:, :], s[:, :], start=(i == 0),
                    stop=(i == 7))), reads=[s2b, ones_cb], writes=[bsq[1]])
            p.op("v", (lambda q: q.tensor_scalar(rmu[0:1, :], bsx[0][0:1, :],
                                                 1.0 / D, None, ALU.mult)),
                 reads=[bsx[1]], writes=[rmub])
            p.op("v", (lambda q: q.tensor_scalar(rex[0:1, :], bsq[0][0:1, :],
                                                 1.0 / D, None, ALU.mult)),
                 reads=[bsq[1]], writes=[rexb])
            p.op("v", (lambda q: q.tensor_tensor(rm2[0:1, :], rmu[0:1, :],
                                                 rmu[0:1, :], ALU.mult)),
                 reads=[rmub], writes=[rm2b])
            p.op("v", (lambda q: q.memset(pad[0:1, :], 0.0)), writes=[padb])
            p.op("v", (lambda q: q.tensor_tensor(rvar[0:1, :], rex[0:1, :],
                                                 rm2[0:1, :], ALU.subtract)),
                 reads=[rexb, rm2b], writes=[rvarb])
            p.op("y", (lambda q: q.dma_start(out=rvL[0:1, :], in_=rvar[0:1, :])),
                 reads=[rvarb], writes=[rvLb])
            p.op("s", (lambda q: q.activation(lnv[0:1, :], rvL[0:1, :], AF.Ln,
                                              bias=1e-5)), reads=[rvLb], writes=[lnvb])
            p.op("s", (lambda q: q.activation(pad[0:1, :], pad[0:1, :],
                                              AF.Identity)), reads=[padb],
                 writes=[padb])
            p.op("s", (lambda q: q.activation(rstd[0:1, :], lnv[0:1, :], AF.Exp,
                                              scale=-0.5)), reads=[lnvb], writes=[rstdb])
            p.op("y", (lambda q: q.dma_start(out=rsL[0:1, :], in_=rstd[0:1, :])),
                 reads=[rstdb], writes=[rsLb])
            p.op("v", (lambda q: q.scalar_tensor_tensor(
                rbn[0:1, :], rmu[0:1, :], -1.0, rsL[0:1, :], ALU.mult, ALU.mult)),
                reads=[rmub, rsLb], writes=[rbnb])
            p.op("y", (lambda q: q.dma_start(out=lnv[0:1, :], in_=rbn[0:1, :])),
                 reads=[rbnb], writes=[lnvb])
            p.op("t", (lambda q: q.matmul(ba[0][:, :], ones_r[:, :], rsL[0:1, :],
                                          start=True, stop=True)),
                 reads=[rsLb, ones_rb], writes=[ba[1]])
            p.op("t", (lambda q: q.matmul(bbc[0][:, :], ones_r[:, :], lnv[0:1, :],
                                          start=True, stop=True)),
                 reads=[lnvb, ones_rb], writes=[bbc[1]])
            for i in range(8):
                h, bb = xt[i]
                sh, sb2 = sq[i % 2]
                xh, xb = b.xn[i]
                p.op("v", (lambda q, s=h, d=sh: q.tensor_tensor(
                    d[:, :], s[:, :], ba[0][:, :], ALU.mult)),
                    reads=[bb, ba[1]], writes=[sb2])
                p.op("v", (lambda q, s=sh, d=xh, fc=fc: q.tensor_tensor(
                    d[:, fc * FCW:(fc + 1) * FCW], s[:, :], bbc[0][:, :],
                    ALU.add)), reads=[sb2, bbc[1]], writes=[xb])



def _softplus_cb(b, tiles, bias_name):
    p = b.p

    def cb(j, fc, ps_h, ps_b):
        (v, vb), (a, ab), (l, lb), (r, rb2) = tiles
        bias = b.col(bias_name, j)
        p.op("s", (lambda q, ps=ps_h, bc=bias[0]: q.activation(
            v[:, :], ps[:, :], AF.Identity, bias=bc)),
            reads=[ps_b, bias[1]], writes=[vb])
        p.op("s", (lambda q: q.activation(a[:, :], v[:, :], AF.Abs)),
             reads=[vb], writes=[ab])
        p.op("s", (lambda q: q.activation(a[:, :], a[:, :], AF.Exp, scale=-1.0)),
             reads=[ab], writes=[ab])
        p.op("s", (lambda q: q.activation(l[:, :], a[:, :], AF.Ln, bias=1.0)),
             reads=[ab], writes=[lb])
        p.op("v", (lambda q: q.tensor_scalar_max(r[:, :], v[:, :], 0.0)),
             reads=[vb], writes=[rb2])
        p.op("v", (lambda q: q.tensor_tensor(r[:, :], r[:, :], l[:, :],
                                             ALU.add)), reads=[lb, rb2],
             writes=[rb2])
        dst = b.dram["sp_dt"][j * 128:(j + 1) * 128, fc * FCW:(fc + 1) * FCW]
        p.op("y", (lambda q, s=r, d=dst: q.dma_start(out=d, in_=s[:, :])),
             reads=[rb2], writes=[p.db(("sp_dt", j, fc))])

    return cb



def emit_mamba0(b, src_key):
    p = b.p
    with p.phase():
        ev = [p.sb([128, FCW], BF16) for _ in range(2)]

        def cb_win(j, fc, ps_h, ps_b):
            h, bb = ev[(j * NFC + fc) % 2]
            bias = b.col("L0_b_in", j)
            p.op("s", (lambda q, d=h, ps=ps_h, bc=bias[0]: q.activation(
                d[:, :], ps[:, :], AF.Silu, bias=bc)),
                reads=[ps_b, bias[1]], writes=[bb])
            tgt = "sp_u" if j < 16 else "sp_z"
            jj = j if j < 16 else j - 16
            dst = b.dram[tgt][jj * 128:(jj + 1) * 128, fc * FCW:(fc + 1) * FCW]
            p.op("y", (lambda q, s=h, d=dst: q.dma_start(out=d, in_=s[:, :])),
                 reads=[bb], writes=[p.db((tgt, jj, fc))])

        b.linear("w0_in", b.xn_in(), cb_win, 1024, 4096)

    with p.phase():
        u_t = [p.sb([128, T], BF16) for _ in range(16)]
        for c in range(16):
            h, bb = u_t[c]
            src = b.dram["sp_u"][c * 128:(c + 1) * 128, :]
            p.op("y", (lambda q, d=h, s=src: q.dma_start(out=d[:, :], in_=s)),
                 reads=[p.db(("sp_u", c, fc2)) for fc2 in range(NFC)],
                 writes=[bb])
        u_in = [((lambda fc, h=h: h[:, fc * FCW:(fc + 1) * FCW]), bb)
                for (h, bb) in u_t]

        wxp = p.sb([128, 16 * 32], BF16)
        w_d = b.dram["w0_xp"]
        for gi, cg0 in enumerate((0, 8)):
            src = AP(tensor=w_d.tensor, offset=cg0 * 128 * 32,
                     ap=[[32, 128], [128 * 32, 8], [1, 32]])
            stage = None
            sview = 8
            b.wload(stage, wxp[0][:, cg0 * 32:(cg0 + 8) * 32], wxp[1], src,
                    8 * 32, view=sview)
        xd_ev = p.sb([32, FCW], BF16)
        bxp = p.banks[4]
        for fc in range(NFC):
            def emit(q, fc=fc):
                ins = None
                for c in range(16):
                    ins = q.matmul(bxp[0][0:32, :], wxp[0][:, c * 32:(c + 1) * 32],
                                   u_t[c][0][:, fc * FCW:(fc + 1) * FCW],
                                   start=(c == 0), stop=(c == 15))
                return ins
            p.op("t", emit, reads=[wxp[1]] + [bb for (_, bb) in u_t],
                 writes=[bxp[1]])
            h, bb = xd_ev
            bias = b.col("L0_b_xproj", rows=32)
            p.op("s", (lambda q, d=h, bc=bias[0]: q.activation(
                d[:, :], bxp[0][0:32, :], AF.Identity, bias=bc)),
                reads=[bxp[1], bias[1]], writes=[bb])
            dst = b.dram["sp_bc"][:, fc * FCW:(fc + 1) * FCW]
            p.op("y", (lambda q, s=h, d=dst: q.dma_start(out=d, in_=s[:, :])),
                 reads=[bb], writes=[p.db(("sp_bc", fc))])

        sp_tiles = [p.sb([128, FCW], F32) for _ in range(4)]
        b.linear("w0_dt", u_in, _softplus_cb(b, sp_tiles, "L0_b_dt"),
                 2048, 2048)

    emit_scan(b, mode=0, aname="a0", src_key=src_key)


def emit_mamba1(b, src_key):
    p = b.p
    with p.phase():
        ev = [p.sb([128, FCW], BF16) for _ in range(2)]

        def cb_win(j, fc, ps_h, ps_b):
            h, bb = ev[(j * NFC + fc) % 2]
            if j < 16:
                p.op("s", (lambda q, d=h, ps=ps_h: q.activation(
                    d[:, :], ps[:, :], AF.Copy)), reads=[ps_b], writes=[bb])
                tgt, jj = "sp_u", j
            else:
                p.op("s", (lambda q, d=h, ps=ps_h: q.activation(
                    d[:, :], ps[:, :], AF.Silu)), reads=[ps_b], writes=[bb])
                tgt, jj = "sp_z", j - 16
            dst = b.dram[tgt][jj * 128:(jj + 1) * 128, fc * FCW:(fc + 1) * FCW]
            p.op("y", (lambda q, s=h, d=dst: q.dma_start(out=d, in_=s[:, :])),
                 reads=[bb], writes=[p.db((tgt, jj, fc))])

        b.linear("w1_in", b.xn_in(), cb_win, 1024, 4096)

    with p.phase():
        wbc = p.sb([128, 8 * 32], BF16)
        w_d = b.dram["w1_bc"]
        src = AP(tensor=w_d.tensor, offset=0, ap=[[32, 128], [128 * 32, 8], [1, 32]])
        b.wload(None, wbc[0][:, :], wbc[1], src, 8 * 32,
                view=8)
        xd_ev = p.sb([32, FCW], BF16)
        bxp = p.banks[4]
        for fc in range(NFC):
            def emit(q, fc=fc):
                ins = None
                for c in range(8):
                    ins = q.matmul(bxp[0][0:32, :], wbc[0][:, c * 32:(c + 1) * 32],
                                   b.xn[c][0][:, fc * FCW:(fc + 1) * FCW],
                                   start=(c == 0), stop=(c == 7))
                return ins
            p.op("t", emit, reads=[wbc[1]] + [bb for (_, bb) in b.xn],
                 writes=[bxp[1]])
            h, bb = xd_ev
            p.op("s", (lambda q, d=h: q.activation(d[:, :], bxp[0][0:32, :],
                                                   AF.Copy)),
                 reads=[bxp[1]], writes=[bb])
            dst = b.dram["sp_bc"][:, fc * FCW:(fc + 1) * FCW]
            p.op("y", (lambda q, s=h, d=dst: q.dma_start(out=d, in_=s[:, :])),
                 reads=[bb], writes=[p.db(("sp_bc", fc))])

        sp_tiles = [p.sb([128, FCW], F32) for _ in range(4)]
        b.linear("w1_dt", b.xn_in(), _softplus_cb(b, sp_tiles, "L1_dtsum"),
                 1024, 2048)

    emit_scan(b, mode=1, aname="a1", src_key=src_key)


def emit_scan(b, mode, aname, src_key):
    p = b.p
    p.paranoid = P.PARANOID_NOSELF
    with p.phase():
        a_sb = p.sb([128, 256], F32)
        p.op("y", (lambda q: q.dma_start(out=a_sb[0][:, :], in_=b.dram[aname])),
             writes=[a_sb[1]])
        p.op("s", (lambda q: q.activation(a_sb[0][:, :], a_sb[0][:, :], AF.Exp)),
             reads=[a_sb[1]], writes=[a_sb[1]])
        p.op("s", (lambda q, m=(-1.0 if mode == 0 else 0.5):
                   q.activation(a_sb[0][:, :], a_sb[0][:, :], AF.Identity,
                                scale=m)),
             reads=[a_sb[1]], writes=[a_sb[1]])

        wname = "w0_out" if mode == 0 else "w1_out"
        w_d = b.dram[wname]
        wout = [p.sb([128, 16 * 128], BF16) for _ in range(8)]
        for j in range(8):
            for cg0 in (0, 4, 8, 12):
                wsrc = AP(tensor=w_d.tensor,
                          offset=cg0 * 128 * 1024 + j * 128,
                          ap=[[1024, 128], [128 * 1024, 4], [1, 128]])
                b.wload(None, wout[j][0][:, cg0 * 128:(cg0 + 4) * 128],
                        wout[j][1], wsrc, 4 * 128, view=4)

        hstate = [p.sb([128, 16], F32) for _ in range(16)]
        for i in range(16):
            p.op("v", (lambda q, h=hstate[i][0]: q.memset(h[:, :], 0.0)),
                 writes=[hstate[i][1]])

        u_c = [p.sb([128, FCW], BF16) for _ in range(2)]
        dt_c = [p.sb([128, FCW], F32) for _ in range(2)]
        z_c = [p.sb([128, FCW], BF16) for _ in range(2)]
        bbc = [p.sb([128, FCW], BF16) for _ in range(16)]
        cbc = [p.sb([128, FCW], BF16) for _ in range(16)]
        dA = [[p.sb([128, FCW], F32) for _ in range(2)] for _ in range(2)]
        bt = [[p.sb([128, FCW], F32) for _ in range(2)] for _ in range(2)]
        ht = [[p.sb([128, FCW], F32) for _ in range(2)] for _ in range(2)]
        if mode == 0:
            dA2 = [[p.sb([128, FCW], F32) for _ in range(2)]
                   for _ in range(2)]
            rr = None
        else:
            dA2 = None
            rr = [[p.sb([128, FCW], F32) for _ in range(2)] for _ in range(2)]
        tmp = [p.sb([128, FCW], F32) for _ in range(2)]
        du = [p.sb([128, FCW], F32) for _ in range(2)]
        yac = [p.sb([128, FCW], F32) for _ in range(2)]
        yg = [p.sb([128, FCW], BF16) for _ in range(16)]
        yg2 = [p.sb([128, FCW], BF16) for _ in range(16)] if mode == 0 \
            else None

        gn = None
        if mode == 1:
            gn = dict(
                sq=[p.sb([128, FCW], BF16) for _ in range(2 if GN_HARD else 1)],
                sq2=([p.sb([128, FCW], BF16) for _ in range(2)]
                     if GN_HARD else None),
                rws=[p.sb([1, FCW], F32) for _ in range(7 if GN_HARD else 6)],
                nt=[p.sb([128, FCW], BF16) for _ in range(16)],
                t1=p.sb([128, FCW], F32),
            )

        for tc in range(NFC):
            fsl = slice(tc * FCW, (tc + 1) * FCW)
            for s_i in range(16):
                for (tiles, row) in ((bbc, s_i), (cbc, 16 + s_i)):
                    h, bb = tiles[s_i]
                    src = b.dram["sp_bc"][row:row + 1, fsl].broadcast_to(
                        (128, FCW))
                    p.op("y", (lambda q, d=h, s=src: q.dma_start(
                        out=d[:, :], in_=s)),
                        reads=[p.db(("sp_bc", tc))], writes=[bb])
            for ip in range(8):
                ii = (2 * ip, 2 * ip + 1)
                for k in range(2):
                    i = ii[k]
                    for (tt, dname) in ((u_c[k], "sp_u"), (dt_c[k], "sp_dt"),
                                        (z_c[k], "sp_z")):
                        h2, bb2 = tt
                        src = b.dram[dname][i * 128:(i + 1) * 128, fsl]
                        p.op("y", (lambda q, d=h2, s=src: q.dma_start(
                            out=d[:, :], in_=s)),
                            reads=[p.db((dname, i, tc))], writes=[bb2])
                for k in range(2):
                    p.op("v", (lambda q, k=k: q.tensor_tensor(
                        du[k][0][:, :], dt_c[k][0][:, :], u_c[k][0][:, :],
                        ALU.mult)),
                        reads=[u_c[k][1], dt_c[k][1]], writes=[du[k][1]])
                for k in range(2):
                    p.op("v", (lambda q, k=k: q.memset(yac[k][0][:, :], 0.0)),
                         writes=[yac[k][1]])
                if mode == 0:
                    for k in range(2):
                        a2 = a_sb[0][:, ii[k] * 16:ii[k] * 16 + 1]
                        p.op("s", (lambda q, k=k, a2=a2: q.activation(
                            dA[k][0][0][:, :], dt_c[k][0][:, :], AF.Exp,
                            scale=a2)),
                            reads=[dt_c[k][1], a_sb[1]], writes=[dA[k][0][1]])
                    for k in range(2):
                        p.op("y", (lambda q, k=k: q.dma_start(
                            out=dA2[k][0][0][:, :], in_=dA[k][0][0][:, :])),
                            reads=[dA[k][0][1]], writes=[dA2[k][0][1]])
                for s_i in range(16):
                    sp = s_i % 2
                    ac = [a_sb[0][:, ii[k] * 16 + s_i:ii[k] * 16 + s_i + 1]
                          for k in range(2)]
                    if mode == 0:
                        if s_i + 1 < 16:
                            spn = (s_i + 1) % 2
                            for k in range(2):
                                a2 = a_sb[0][:, ii[k] * 16 + s_i + 1:
                                             ii[k] * 16 + s_i + 2]
                                p.op("s", (lambda q, k=k, spn=spn, a2=a2:
                                           q.activation(dA[k][spn][0][:, :],
                                                        dt_c[k][0][:, :],
                                                        AF.Exp, scale=a2)),
                                     reads=[dt_c[k][1], a_sb[1]],
                                     writes=[dA[k][spn][1]])
                            for k in range(2):
                                p.op("y", (lambda q, k=k, spn=spn: q.dma_start(
                                    out=dA2[k][spn][0][:, :],
                                    in_=dA[k][spn][0][:, :])),
                                    reads=[dA[k][spn][1]],
                                    writes=[dA2[k][spn][1]])
                        for k in range(2):
                            p.op("v", (lambda q, k=k, sp=sp, s1=bbc[s_i][0]:
                                       q.tensor_tensor(bt[k][sp][0][:, :],
                                                       du[k][0][:, :],
                                                       s1[:, :], ALU.mult)),
                                 reads=[du[k][1], bbc[s_i][1]],
                                 writes=[bt[k][sp][1]])
                    else:
                        for k in range(2):
                            p.op("s", (lambda q, k=k, sp=sp, a2=ac[k]:
                                       q.activation(rr[k][sp][0][:, :],
                                                    dt_c[k][0][:, :], AF.Ln,
                                                    scale=a2, bias=1.0)),
                                 reads=[dt_c[k][1], a_sb[1]],
                                 writes=[rr[k][sp][1]])
                        for k in range(2):
                            p.op("s", (lambda q, k=k, sp=sp: q.activation(
                                rr[k][sp][0][:, :], rr[k][sp][0][:, :],
                                AF.Exp, scale=-1.0)),
                                reads=[rr[k][sp][1]], writes=[rr[k][sp][1]])
                        for k in range(2):
                            p.op("v", (lambda q, k=k, sp=sp: q.tensor_scalar(
                                dA[k][sp][0][:, :], rr[k][sp][0][:, :], 2.0,
                                -1.0, ALU.mult, ALU.add)),
                                reads=[rr[k][sp][1]], writes=[dA[k][sp][1]])
                        for k in range(2):
                            p.op("v", (lambda q, k=k, sp=sp, s1=bbc[s_i][0]:
                                       q.tensor_tensor(bt[k][sp][0][:, :],
                                                       du[k][0][:, :],
                                                       s1[:, :], ALU.mult)),
                                 reads=[du[k][1], bbc[s_i][1]],
                                 writes=[bt[k][sp][1]])
                        for k in range(2):
                            p.op("v", (lambda q, k=k, sp=sp: q.tensor_tensor(
                                bt[k][sp][0][:, :], bt[k][sp][0][:, :],
                                rr[k][sp][0][:, :], ALU.mult)),
                                reads=[rr[k][sp][1], bt[k][sp][1]],
                                writes=[bt[k][sp][1]])
                    for k in range(2):
                        hs_h, hs_b = hstate[ii[k]]
                        da_use = dA2 if mode == 0 else dA
                        p.op("v", (lambda q, k=k, sp=sp, hs=hs_h, s_i=s_i,
                                   du2=da_use: q.tensor_tensor_scan(
                                       ht[k][sp][0][:, :], du2[k][sp][0][:, :],
                                       bt[k][sp][0][:, :],
                                       hs[:, s_i:s_i + 1], ALU.mult, ALU.add)),
                             reads=[da_use[k][sp][1], bt[k][sp][1], hs_b],
                             writes=[ht[k][sp][1]])
                    for k in range(2):
                        hs_h, hs_b = hstate[ii[k]]
                        p.op("s", (lambda q, k=k, sp=sp, hs=hs_h, s_i=s_i:
                                   q.copy(hs[:, s_i:s_i + 1],
                                          ht[k][sp][0][:, FCW - 1:FCW])),
                             reads=[ht[k][sp][1]], writes=[hs_b])
                    for k in range(2):
                        p.op("v", (lambda q, k=k, sp=sp, c1=cbc[s_i][0]:
                                   q.tensor_tensor(tmp[k][0][:, :],
                                                   ht[k][sp][0][:, :],
                                                   c1[:, :], ALU.mult)),
                             reads=[ht[k][sp][1], cbc[s_i][1]],
                             writes=[tmp[k][1]])
                    for k in range(2):
                        p.op("v", (lambda q, k=k: q.tensor_tensor(
                            yac[k][0][:, :], yac[k][0][:, :], tmp[k][0][:, :],
                            ALU.add)),
                            reads=[tmp[k][1], yac[k][1]], writes=[yac[k][1]])
                if mode == 0:
                    for k in range(2):
                        dcol = b.col("L0_D", ii[k])
                        p.op("v", (lambda q, k=k, dc=dcol[0]:
                                   q.scalar_tensor_tensor(
                                       yac[k][0][:, :], u_c[k][0][:, :], dc,
                                       yac[k][0][:, :], ALU.mult, ALU.add)),
                             reads=[u_c[k][1], dcol[1], yac[k][1]],
                             writes=[yac[k][1]])
                for k in range(2):
                    gh, gb = yg[ii[k]]
                    p.op("v", (lambda q, k=k, gh=gh: q.tensor_tensor(
                        gh[:, :], yac[k][0][:, :], z_c[k][0][:, :], ALU.mult)),
                        reads=[yac[k][1], z_c[k][1]], writes=[gb])
            if mode == 1:
                (_gn_chunk if GN_HARD else _gn_chunk_v0)(b, gn, yg)
            if mode == 0:
                src_tiles = b.launder(yg, yg2)
            else:
                src_tiles = b.launder(gn["nt"], yg)
            for j in range(8):
                wt, wbuf = wout[j]
                bh, bb = p.banks[6 + j % 2]

                def emit(q, j=j, bh=bh, wt=wt, st=src_tiles):
                    ins = None
                    for c in range(16):
                        ins = q.matmul(bh[:, :],
                                       wt[:, c * 128:(c + 1) * 128],
                                       st[c][0][:, :], start=(c == 0),
                                       stop=(c == 15))
                    return ins
                p.op("t", emit, reads=[wbuf] + [t[1] for t in src_tiles],
                     writes=[bb])
                if mode == 0:
                    bias = b.col("L0_b_out", j)
                    b.merge(j, tc, bh, bb, scalar=bias[0], scalar_buf=bias[1],
                            op0=ALU.add, src_key=src_key)
                else:
                    b.merge(j, tc, bh, bb, src_key=src_key)
        p.paranoid = False


def _gn_chunk(b, gn, yg):
    p = b.p
    old_par = p.paranoid
    p.paranoid = True
    ones_cb_h, ones_cb_b = b.ones_col_bf
    ones_r, ones_rb = b.ones_row
    (rmu, rmub), (rex, rexb), (rm2, rm2b), (rvar, rvarb), (rstd, rstdb), \
        (rbnr, rbnrb), (pad, padb) = gn["rws"]
    rvL, rvLb = rm2, rm2b
    lnv, lnvb = rex, rexb
    rsL, rsLb = rvar, rvarb
    bsx = p.banks[4]; bsq = p.banks[5]; ba = p.banks[6]; bbn = p.banks[7]
    for i in range(16):
        gh, gb = yg[i]
        sh, sb2 = gn["sq"][i % 2]
        s2h, s2b = gn["sq2"][i % 2]
        p.op("s", (lambda q, d=sh, s=gh: q.activation(d[:, :], s[:, :],
                                                      AF.Square)),
             reads=[gb], writes=[sb2])
        p.op("y", (lambda q, s=sh, d=s2h: q.dma_start(out=d[:, :], in_=s[:, :])),
             reads=[sb2], writes=[s2b])
        p.op("t", (lambda q, s=gh, i=i: q.matmul(
            bsx[0][0:1, :], ones_cb_h[:, :], s[:, :], start=(i == 0),
            stop=(i == 15))), reads=[gb, ones_cb_b], writes=[bsx[1]])
        p.op("t", (lambda q, s=s2h, i=i: q.matmul(
            bsq[0][0:1, :], ones_cb_h[:, :], s[:, :], start=(i == 0),
            stop=(i == 15))), reads=[s2b, ones_cb_b], writes=[bsq[1]])
    p.op("v", (lambda q: q.tensor_scalar(rmu[0:1, :], bsx[0][0:1, :], 1.0 / 2048,
                                         None, ALU.mult)),
         reads=[bsx[1]], writes=[rmub])
    p.op("v", (lambda q: q.tensor_scalar(rex[0:1, :], bsq[0][0:1, :], 1.0 / 2048,
                                         None, ALU.mult)),
         reads=[bsq[1]], writes=[rexb])
    p.op("v", (lambda q: q.tensor_tensor(rm2[0:1, :], rmu[0:1, :], rmu[0:1, :],
                                         ALU.mult)), reads=[rmub], writes=[rm2b])
    p.op("v", (lambda q: q.memset(pad[0:1, :], 0.0)), writes=[padb])
    p.op("v", (lambda q: q.tensor_tensor(rvar[0:1, :], rex[0:1, :], rm2[0:1, :],
                                         ALU.subtract)), reads=[rexb, rm2b], writes=[rvarb])
    p.op("y", (lambda q: q.dma_start(out=rvL[0:1, :], in_=rvar[0:1, :])),
         reads=[rvarb], writes=[rvLb])
    p.op("s", (lambda q: q.activation(lnv[0:1, :], rvL[0:1, :], AF.Ln, bias=1e-5)),
         reads=[rvLb], writes=[lnvb])
    p.op("s", (lambda q: q.activation(pad[0:1, :], pad[0:1, :], AF.Identity)),
         reads=[padb], writes=[padb])
    p.op("s", (lambda q: q.activation(rstd[0:1, :], lnv[0:1, :], AF.Exp, scale=-0.5)),
         reads=[lnvb], writes=[rstdb])
    p.op("y", (lambda q: q.dma_start(out=rsL[0:1, :], in_=rstd[0:1, :])),
         reads=[rstdb], writes=[rsLb])
    p.op("v", (lambda q: q.scalar_tensor_tensor(rbnr[0:1, :], rmu[0:1, :], -1.0,
                                                rsL[0:1, :], ALU.mult, ALU.mult)),
         reads=[rmub, rsLb], writes=[rbnrb])
    p.op("y", (lambda q: q.dma_start(out=lnv[0:1, :], in_=rbnr[0:1, :])),
         reads=[rbnrb], writes=[lnvb])
    p.op("t", (lambda q: q.matmul(ba[0][:, :], ones_r[:, :], rsL[0:1, :],
                                  start=True, stop=True)),
         reads=[rsLb, ones_rb], writes=[ba[1]])
    p.op("t", (lambda q: q.matmul(bbn[0][:, :], ones_r[:, :], lnv[0:1, :],
                                  start=True, stop=True)),
         reads=[lnvb, ones_rb], writes=[bbn[1]])
    th, tb = gn["t1"]
    for i in range(16):
        gh, gb = yg[i]
        nh, nb = gn["nt"][i]
        p.op("v", (lambda q, s=gh: q.tensor_tensor(
            th[:, :], s[:, :], ba[0][:, :], ALU.mult)),
            reads=[gb, ba[1]], writes=[tb])
        p.op("v", (lambda q: q.tensor_tensor(th[:, :], th[:, :], bbn[0][:, :],
                                             ALU.add)),
             reads=[bbn[1], tb], writes=[tb])
        gc = b.col("L1_gamma", i)
        bc = b.col("L1_beta", i)
        p.op("s", (lambda q, d=nh, g2=gc[0], b2=bc[0]: q.activation(
            d[:, :], th[:, :], AF.Identity, scale=g2, bias=b2)),
            reads=[tb, gc[1], bc[1]], writes=[nb])
    p.paranoid = old_par




def emit_layernorm_v0(b, src_key):
    p = b.p
    x_d = b.dram[src_key]
    ones_c, ones_cb = b.ones_col
    ones_r, ones_rb = b.ones_row
    with p.phase():
        xt = [p.sb([128, FCW], F32) for _ in range(8)]
        sq = [p.sb([128, FCW], F32) for _ in range(2)]
        rws = [p.sb([1, FCW], F32) for _ in range(6)]
        (rmu, rmub), (rex, rexb), (rm2, rm2b), (rvar, rvarb), (rstd, rstdb), (rbn, rbnb) = rws
        bsx = p.banks[4]
        bsq = p.banks[5]
        ba = p.banks[6]
        bbc = p.banks[7]
        for fc in range(NFC):
            for i in range(8):
                h, bb = xt[i]
                src = x_d[i * 128:(i + 1) * 128, fc * FCW:(fc + 1) * FCW]
                p.op("y", (lambda q, d=h, s=src: q.dma_start(out=d[:, :], in_=s)),
                     reads=[p.db((src_key, i, fc))], writes=[bb])
            for i in range(8):
                h, bb = xt[i]
                sh, sb2 = sq[i % 2]
                p.op("s", (lambda q, d=sh, s=h: q.activation(
                    d[:, :], s[:, :], AF.Square)), reads=[bb], writes=[sb2])
                p.op("t", (lambda q, s=h, i=i: q.matmul(
                    bsx[0][0:1, :], ones_c[:, :], s[:, :], start=(i == 0),
                    stop=(i == 7))), reads=[bb, ones_cb], writes=[bsx[1]])
                p.op("t", (lambda q, s=sh, i=i: q.matmul(
                    bsq[0][0:1, :], ones_c[:, :], s[:, :], start=(i == 0),
                    stop=(i == 7))), reads=[sb2, ones_cb], writes=[bsq[1]])
            p.op("v", (lambda q: q.tensor_scalar(rmu[0:1, :], bsx[0][0:1, :],
                                                 1.0 / D, None, ALU.mult)),
                 reads=[bsx[1]], writes=[rmub])
            p.op("v", (lambda q: q.tensor_scalar(rex[0:1, :], bsq[0][0:1, :],
                                                 1.0 / D, None, ALU.mult)),
                 reads=[bsq[1]], writes=[rexb])
            p.op("s", (lambda q: q.activation(rm2[0:1, :], rmu[0:1, :], AF.Square)),
                 reads=[rmub], writes=[rm2b])
            p.op("v", (lambda q: q.tensor_tensor(rvar[0:1, :], rex[0:1, :],
                                                 rm2[0:1, :], ALU.subtract)),
                 reads=[rexb, rm2b], writes=[rvarb])
            p.op("s", (lambda q: q.activation(rvar[0:1, :], rvar[0:1, :], AF.Ln,
                                              bias=1e-5)), reads=[rvarb], writes=[rvarb])
            p.op("s", (lambda q: q.activation(rstd[0:1, :], rvar[0:1, :], AF.Exp,
                                              scale=-0.5)), reads=[rvarb], writes=[rstdb])
            p.op("v", (lambda q: q.scalar_tensor_tensor(
                rbn[0:1, :], rmu[0:1, :], -1.0, rstd[0:1, :], ALU.mult, ALU.mult)),
                reads=[rmub, rstdb], writes=[rbnb])
            p.op("t", (lambda q: q.matmul(ba[0][:, :], ones_r[:, :], rstd[0:1, :],
                                          start=True, stop=True)),
                 reads=[rstdb, ones_rb], writes=[ba[1]])
            p.op("t", (lambda q: q.matmul(bbc[0][:, :], ones_r[:, :], rbn[0:1, :],
                                          start=True, stop=True)),
                 reads=[rbnb, ones_rb], writes=[bbc[1]])
            for i in range(8):
                h, bb = xt[i]
                sh, sb2 = sq[i % 2]
                xh, xb = b.xn[i]
                p.op("v", (lambda q, s=h, d=sh: q.tensor_tensor(
                    d[:, :], s[:, :], ba[0][:, :], ALU.mult)),
                    reads=[bb, ba[1]], writes=[sb2])
                p.op("v", (lambda q, s=sh, d=xh, fc=fc: q.tensor_tensor(
                    d[:, fc * FCW:(fc + 1) * FCW], s[:, :], bbc[0][:, :],
                    ALU.add)), reads=[sb2, bbc[1]], writes=[xb])




def _gn_chunk_v0(b, gn, yg):
    p = b.p
    old_par = p.paranoid
    p.paranoid = True
    ones_cb_h, ones_cb_b = b.ones_col_bf
    ones_r, ones_rb = b.ones_row
    (rmu, rmub), (rex, rexb), (rm2, rm2b), (rvar, rvarb), (rstd, rstdb), (rbnr, rbnrb) = gn["rws"]
    bsx = p.banks[4]; bsq = p.banks[5]; ba = p.banks[6]; bbn = p.banks[7]
    sh, sb2 = gn["sq"][0]
    for i in range(16):
        gh, gb = yg[i]
        p.op("s", (lambda q, s=gh: q.activation(sh[:, :], s[:, :], AF.Square)),
             reads=[gb], writes=[sb2])
        p.op("t", (lambda q, s=gh, i=i: q.matmul(
            bsx[0][0:1, :], ones_cb_h[:, :], s[:, :], start=(i == 0),
            stop=(i == 15))), reads=[gb, ones_cb_b], writes=[bsx[1]])
        p.op("t", (lambda q, i=i: q.matmul(
            bsq[0][0:1, :], ones_cb_h[:, :], sh[:, :], start=(i == 0),
            stop=(i == 15))), reads=[sb2, ones_cb_b], writes=[bsq[1]])
    p.op("v", (lambda q: q.tensor_scalar(rmu[0:1, :], bsx[0][0:1, :], 1.0 / 2048,
                                         None, ALU.mult)),
         reads=[bsx[1]], writes=[rmub])
    p.op("v", (lambda q: q.tensor_scalar(rex[0:1, :], bsq[0][0:1, :], 1.0 / 2048,
                                         None, ALU.mult)),
         reads=[bsq[1]], writes=[rexb])
    p.op("s", (lambda q: q.activation(rm2[0:1, :], rmu[0:1, :], AF.Square)),
         reads=[rmub], writes=[rm2b])
    p.op("v", (lambda q: q.tensor_tensor(rvar[0:1, :], rex[0:1, :], rm2[0:1, :],
                                         ALU.subtract)), reads=[rexb, rm2b], writes=[rvarb])
    p.op("s", (lambda q: q.activation(rvar[0:1, :], rvar[0:1, :], AF.Ln, bias=1e-5)),
         reads=[rvarb], writes=[rvarb])
    p.op("s", (lambda q: q.activation(rstd[0:1, :], rvar[0:1, :], AF.Exp, scale=-0.5)),
         reads=[rvarb], writes=[rstdb])
    p.op("v", (lambda q: q.scalar_tensor_tensor(rbnr[0:1, :], rmu[0:1, :], -1.0,
                                                rstd[0:1, :], ALU.mult, ALU.mult)),
         reads=[rmub, rstdb], writes=[rbnrb])
    p.op("t", (lambda q: q.matmul(ba[0][:, :], ones_r[:, :], rstd[0:1, :],
                                  start=True, stop=True)),
         reads=[rstdb, ones_rb], writes=[ba[1]])
    p.op("t", (lambda q: q.matmul(bbn[0][:, :], ones_r[:, :], rbnr[0:1, :],
                                  start=True, stop=True)),
         reads=[rbnrb, ones_rb], writes=[bbn[1]])
    th, tb = gn["t1"]
    for i in range(16):
        gh, gb = yg[i]
        nh, nb = gn["nt"][i]
        p.op("v", (lambda q, s=gh: q.tensor_tensor(
            th[:, :], s[:, :], ba[0][:, :], ALU.mult)),
            reads=[gb, ba[1]], writes=[tb])
        p.op("v", (lambda q: q.tensor_tensor(th[:, :], th[:, :], bbn[0][:, :],
                                             ALU.add)),
             reads=[bbn[1], tb], writes=[tb])
        gc = b.col("L1_gamma", i)
        bc = b.col("L1_beta", i)
        p.op("s", (lambda q, d=nh, g2=gc[0], b2=bc[0]: q.activation(
            d[:, :], th[:, :], AF.Identity, scale=g2, bias=b2)),
            reads=[tb, gc[1], bc[1]], writes=[nb])
    p.paranoid = old_par




def emit_ffn_ternary(b, pref, src_key, dst_key):
    p = b.p
    lp = "f0" if pref == "L0" else "f2"
    with p.phase():
        ha = [p.sb([128, T], BF16) for _ in range(32)]
        t1e = [p.sb([128, FCW], BF16) for _ in range(2)]
        w1d, w2d = b.dram[lp + "_w1"], b.dram[lp + "_w2"]
        wt1 = [p.sb([128, 8 * 128], BF16) for _ in range(2)]
        wt2 = [p.sb([128, 8 * 128], BF16) for _ in range(2)]
        for j in range(32):
            b1h, b1b = wt1[j % 2]
            b2h, b2b = wt2[j % 2]
            for wi2, (wd, wh, wb2) in enumerate(((w1d, b1h, b1b),
                                                 (w2d, b2h, b2b))):
                for hf in range(2):
                    src = AP(tensor=wd.tensor,
                             offset=hf * 4 * 128 * 4096 + j * 128,
                             ap=[[4096, 128], [128 * 4096, 4], [1, 128]])
                    stage = None
                    sview = 4
                    b.wload(stage, wh[:, hf * 512:(hf + 1) * 512], wb2, src,
                            4 * 128, view=sview)
            for fc in range(NFC):
                p1h, p1b = p.banks[fc % 2]
                p2h, p2b = p.banks[2 + fc % 2]
                fsl = slice(fc * FCW, (fc + 1) * FCW)

                def emit1(q, wh=b1h, ph=p1h, fsl=fsl):
                    ins = None
                    for c in range(8):
                        ins = q.matmul(ph[:, :], wh[:, c * 128:(c + 1) * 128],
                                       b.xn[c][0][:, fsl], start=(c == 0),
                                       stop=(c == 7))
                    return ins

                def emit2(q, wh=b2h, ph=p2h, fsl=fsl):
                    ins = None
                    for c in range(8):
                        ins = q.matmul(ph[:, :], wh[:, c * 128:(c + 1) * 128],
                                       b.xn[c][0][:, fsl], start=(c == 0),
                                       stop=(c == 7))
                    return ins

                p.op("t", emit1, reads=[b1b] + [bb for (_, bb) in b.xn],
                     writes=[p1b])
                p.op("t", emit2, reads=[b2b] + [bb for (_, bb) in b.xn],
                     writes=[p2b])
                th, tb = t1e[fc % 2]
                s1 = b.col(pref + "_w1s", j)
                s2 = b.col(pref + "_w2s", j)
                p.op("s", (lambda q, d=th, ph=p1h, sc=s1[0]: q.activation(
                    d[:, :], ph[:, :], AF.Silu, scale=sc)),
                    reads=[p1b, s1[1]], writes=[tb])
                hh, hb = ha[j]
                p.op("v", (lambda q, d=hh, ph=p2h, s=th, sc=s2[0], fsl=fsl:
                           q.scalar_tensor_tensor(d[:, fsl], ph[:, :], sc,
                                                  s[:, :], ALU.mult, ALU.mult)),
                     reads=[p2b, s2[1], tb], writes=[hb])

        ha_in = [((lambda fc, h=h: h[:, fc * FCW:(fc + 1) * FCW]), bb)
                 for (h, bb) in ha]
        hb_ev = [p.sb([128, FCW], BF16) for _ in range(2)]

        def cb_h1(j, fc, ps_h, ps_b):
            h, bb = hb_ev[(j * NFC + fc) % 2]
            sc = b.col(pref + "_h1s", j)
            p.op("s", (lambda q, d=h, ps=ps_h, sc2=sc[0]: q.activation(
                d[:, :], ps[:, :], AF.Gelu, scale=sc2)),
                reads=[ps_b, sc[1]], writes=[bb])
            dst = b.dram["sp_hb"][j * 128:(j + 1) * 128,
                                  fc * FCW:(fc + 1) * FCW]
            p.op("y", (lambda q, s=h, d=dst: q.dma_start(out=d, in_=s[:, :])),
                 reads=[bb], writes=[p.db(("sp_hb", j, fc))])

        b.linear(lp + "_h1", ha_in, cb_h1, 4096, 4096, banks=(4, 5))

    hc_ev = [p.sb([128, FCW], BF16) for _ in range(2)]

    def cb_h2(j, fc, ps_h, ps_b):
        h, bb = hc_ev[(j * NFC + fc) % 2]
        sc = b.col(pref + "_h2s", j)
        b.p.op("s", (lambda q, d=h, ps=ps_h, sc2=sc[0]: q.activation(
            d[:, :], ps[:, :], AF.Gelu, scale=sc2)),
            reads=[ps_b, sc[1]], writes=[bb])
        dst = b.dram["sp_hc"][j * 128:(j + 1) * 128, fc * FCW:(fc + 1) * FCW]
        b.p.op("y", (lambda q, s=h, d=dst: q.dma_start(out=d, in_=s[:, :])),
               reads=[bb], writes=[b.p.db(("sp_hc", j, fc))])

    b.linear_stream(lp + "_h2", "sp_hb", cb_h2, 4096, 4096)

    def cb_w3(j, fc, ps_h, ps_b):
        sc = b.col(pref + "_w3s", j)
        b.merge(j, fc, ps_h, ps_b, scalar=sc[0], scalar_buf=sc[1],
                op0=ALU.mult, src_key=src_key, dst_key=dst_key)

    b.linear_stream(lp + "_w3", "sp_hc", cb_w3, 4096, 1024)


def emit_ffn_quant(b, src_key, dst_key):
    p = b.p
    with p.phase():
        hq = [p.sb([128, T], BF16) for _ in range(24)]
        t1e = [p.sb([128, FCW], BF16) for _ in range(2)]
        wgd, wud = b.dram["f1_wg"], b.dram["f1_wu"]
        wt1 = [p.sb([128, 8 * 128], BF16) for _ in range(2)]
        wt2 = [p.sb([128, 8 * 128], BF16) for _ in range(2)]
        for j in range(24):
            b1h, b1b = wt1[j % 2]
            b2h, b2b = wt2[j % 2]
            for wi2, (wd, wh, wb2) in enumerate(((wgd, b1h, b1b),
                                                 (wud, b2h, b2b))):
                for hf in range(2):
                    src = AP(tensor=wd.tensor,
                             offset=hf * 4 * 128 * 3072 + j * 128,
                             ap=[[3072, 128], [128 * 3072, 4], [1, 128]])
                    stage = None
                    sview = 4
                    b.wload(stage, wh[:, hf * 512:(hf + 1) * 512], wb2, src,
                            4 * 128, view=sview)
            for fc in range(NFC):
                p1h, p1b = p.banks[fc % 2]
                p2h, p2b = p.banks[2 + fc % 2]
                fsl = slice(fc * FCW, (fc + 1) * FCW)

                def emit1(q, wh=b1h, ph=p1h, fsl=fsl):
                    ins = None
                    for c in range(8):
                        ins = q.matmul(ph[:, :], wh[:, c * 128:(c + 1) * 128],
                                       b.xn[c][0][:, fsl], start=(c == 0),
                                       stop=(c == 7))
                    return ins

                def emit2(q, wh=b2h, ph=p2h, fsl=fsl):
                    ins = None
                    for c in range(8):
                        ins = q.matmul(ph[:, :], wh[:, c * 128:(c + 1) * 128],
                                       b.xn[c][0][:, fsl], start=(c == 0),
                                       stop=(c == 7))
                    return ins

                p.op("t", emit1, reads=[b1b] + [bb for (_, bb) in b.xn],
                     writes=[p1b])
                p.op("t", emit2, reads=[b2b] + [bb for (_, bb) in b.xn],
                     writes=[p2b])
                th, tb = t1e[fc % 2]
                p.op("s", (lambda q, d=th, ph=p1h: q.activation(
                    d[:, :], ph[:, :], AF.Silu, scale=1.0 / 127.0)),
                    reads=[p1b], writes=[tb])
                hh, hb = hq[j]
                p.op("v", (lambda q, d=hh, ph=p2h, s=th, fsl=fsl:
                           q.scalar_tensor_tensor(d[:, fsl], ph[:, :],
                                                  1.0 / 127.0, s[:, :],
                                                  ALU.mult, ALU.mult)),
                     reads=[p2b, tb], writes=[hb])

        hq_in = [((lambda fc, h=h: h[:, fc * FCW:(fc + 1) * FCW]), bb)
                 for (h, bb) in hq]

        def cb_wd(j, fc, ps_h, ps_b):
            b.merge(j, fc, ps_h, ps_b, scalar=1.0 / 127.0, op0=ALU.mult,
                    src_key=src_key, dst_key=dst_key)

        b.linear("f1_wd", hq_in, cb_wd, 3072, 1024, banks=(4, 5))



def _emit_attention(b, src_key, layer):
    p = b.p
    NH = 16
    with p.phase():
        qf = [p.sb([128, T], BF16) for _ in range(8)]
        kf = [p.sb([128, T], BF16) for _ in range(8)]
        vtok = [p.sb([128, NH * 65], BF16) for _ in range(16)]
        negc = p.sb([128, NH], F32)

        with p.phase():
            if layer == 2:
                ps_e = p.banks[4]
                re_sb = p.sb([32, 16], F32)
                oh_sb = p.sb([32, 4095], F32)
                eb_ev = p.sb([16, FCW], BF16)
                p.op("y", (lambda q: q.dma_start(out=re_sb[0][:, :],
                                                 in_=b.dram["rel_emb"])),
                     writes=[re_sb[1]])
                p.op("y", (lambda q: q.dma_start(out=oh_sb[0][:, :],
                                                 in_=b.dram["onehot"])),
                     writes=[oh_sb[1]])
                for fc in range(8):
                    w = min(FCW, 4095 - fc * FCW)
                    p.op("t", (lambda q, fc=fc, w=w: q.matmul(
                        ps_e[0][0:16, 0:w], re_sb[0][:, :],
                        oh_sb[0][:, fc * FCW:fc * FCW + w],
                        start=True, stop=True)),
                        reads=[re_sb[1], oh_sb[1]], writes=[ps_e[1]])
                    p.op("s", (lambda q, w=w: q.activation(
                        eb_ev[0][0:16, 0:w], ps_e[0][0:16, 0:w], AF.Exp)),
                        reads=[ps_e[1]], writes=[eb_ev[1]])
                    dst = b.dram["sp_expb"][:, fc * FCW:fc * FCW + w]
                    p.op("y", (lambda q, d=dst, w=w: q.dma_start(
                        out=d, in_=eb_ev[0][0:16, 0:w])),
                        reads=[eb_ev[1]], writes=[p.db(("sp_expb", fc))])

                def cb_q(j, fc, ps_h, ps_b):
                    h, bb = qf[j]
                    p.op("s", (lambda q, d=h, ps=ps_h, fc=fc: q.activation(
                        d[:, fc * FCW:(fc + 1) * FCW], ps[:, :], AF.Copy)),
                        reads=[ps_b], writes=[bb])

                def cb_k(j, fc, ps_h, ps_b):
                    h, bb = kf[j]
                    p.op("s", (lambda q, d=h, ps=ps_h, fc=fc: q.activation(
                        d[:, fc * FCW:(fc + 1) * FCW], ps[:, :], AF.Copy)),
                        reads=[ps_b], writes=[bb])

                b.linear("w2_q", b.xn_in(), cb_q, 1024, 1024)
                b.linear("w2_k", b.xn_in(), cb_k, 1024, 1024)
                wv_t = [p.sb([128, 1024], BF16) for _ in range(8)]
                for c in range(8):
                    h, bb = wv_t[c]
                    src = b.dram["w2_v"][c * 128:(c + 1) * 128, :]
                    b.wload(None, h[:, :], bb, src, 1024)
                _emit_vtok(b, vtok, wv_t, 64, 1024, 0)
            else:
                wq_re = [p.sb([128, 1024], BF16) for _ in range(8)]
                wk_re = [p.sb([128, 1024], BF16) for _ in range(8)]
                wv_re = [p.sb([128, 1024], BF16) for _ in range(8)]
                with p.phase():
                    wq_t = [p.sb([128, 3072], BF16) for _ in range(2)]
                    for c in range(8):
                        h, bb = wq_t[c % 2]
                        src = b.dram["w3_qkv"][c * 128:(c + 1) * 128, :]
                        b.wload(None, h[:, :], bb, src, 3072)
                        for off, re in ((0, wq_re), (64, wk_re),
                                        (128, wv_re)):
                            dh, db2 = re[c]
                            srcap = AP(tensor=h, offset=off,
                                       ap=[[3072, 128], [192, 16], [1, 64]])
                            p.op("v", (lambda q, d=dh, s=srcap:
                                       q.tensor_copy(d[:, :], s)),
                                 reads=[bb], writes=[db2])

                def cb_q3(j, fc, ps_h, ps_b):
                    h, bb = qf[j]
                    p.op("s", (lambda q, d=h, ps=ps_h, fc=fc: q.activation(
                        d[:, fc * FCW:(fc + 1) * FCW], ps[:, :], AF.Copy)),
                        reads=[ps_b], writes=[bb])

                def cb_k3(j, fc, ps_h, ps_b):
                    h, bb = kf[j]
                    p.op("s", (lambda q, d=h, ps=ps_h, fc=fc: q.activation(
                        d[:, fc * FCW:(fc + 1) * FCW], ps[:, :], AF.Copy)),
                        reads=[ps_b], writes=[bb])

                for re, cb in ((wq_re, cb_q3), (wk_re, cb_k3)):
                    for j in range(8):
                        for fc in range(NFC):
                            ph, pb = p.banks[fc % 2]

                            def emit(q, re=re, j=j, fc=fc, ph=ph):
                                ins = None
                                for c in range(8):
                                    ins = q.matmul(
                                        ph[:, :],
                                        re[c][0][:, j * 128:(j + 1) * 128],
                                        b.xn[c][0][:, fc * FCW:(fc + 1) * FCW],
                                        start=(c == 0), stop=(c == 7))
                                return ins
                            p.op("t", emit, reads=[t[1] for t in re] +
                                 [bb for (_, bb) in b.xn], writes=[pb])
                            cb(j, fc, ph, pb)
                _emit_vtok(b, vtok, wv_re, 64, 1024, 0)

            sqt = p.sb([128, FCW], F32)
            nrows = [p.sb([1, T], F32) for _ in range(2)]
            nreds = [p.sb([1, 1], F32) for _ in range(2)]
            ones_c, ones_cb = b.ones_col
            ones_r, ones_rb = b.ones_row
            bn = p.banks[5]
            bc2 = p.banks[6]
            for hh in range(NH):
                prow = (hh % 2) * 64
                for r, tiles in ((0, qf), (1, kf)):
                    th, tb2 = tiles[hh // 2]
                    nr, nrb = nrows[r]
                    for fc in range(NFC):
                        p.op("s", (lambda q, th=th, prow=prow, fc=fc:
                                   q.activation(sqt[0][0:64, :],
                                                th[prow:prow + 64,
                                                   fc * FCW:(fc + 1) * FCW],
                                                AF.Square)),
                             reads=[tb2], writes=[sqt[1]])
                        p.op("t", (lambda q: q.matmul(
                            bn[0][0:1, :], ones_c[0:64, :],
                            sqt[0][0:64, :], start=True, stop=True)),
                            reads=[sqt[1], ones_cb], writes=[bn[1]])
                        p.op("v", (lambda q, nr=nr, fc=fc: q.tensor_copy(
                            nr[0:1, fc * FCW:(fc + 1) * FCW],
                            bn[0][0:1, :])), reads=[bn[1]],
                            writes=[nrb])
                for r in range(2):
                    p.op("v", (lambda q, r=r: q.reduce_max(
                        nreds[r][0][0:1, :], nrows[r][0][0:1, :], axis=AX.X)),
                        reads=[nrows[r][1]], writes=[nreds[r][1]], tail=True)
                p.op("v", (lambda q: q.tensor_tensor(
                    nreds[0][0][0:1, :], nreds[0][0][0:1, :],
                    nreds[1][0][0:1, :], ALU.mult)),
                    reads=[nreds[0][1], nreds[1][1]], writes=[nreds[0][1]],
                    tail=True)
                p.op("s", (lambda q: q.activation(nreds[0][0][0:1, :],
                                                  nreds[0][0][0:1, :], AF.Sqrt)),
                     reads=[nreds[0][1]], writes=[nreds[0][1]])
                p.op("t", (lambda q: q.matmul(bc2[0][:, 0:1], ones_r[:, :],
                                              nreds[0][0][0:1, :], start=True,
                                              stop=True)),
                     reads=[nreds[0][1], ones_rb], writes=[bc2[1]])
                p.op("v", (lambda q, hh=hh: q.tensor_scalar_mul(
                    negc[0][:, hh:hh + 1], bc2[0][:, 0:1], -0.125)),
                    reads=[bc2[1]], writes=[negc[1]])

        with p.phase():
            pt = [p.sb([128, T], BF16) for _ in range(2)]
            ebt = [p.sb([128, 3968], BF16) for _ in range(2)]
            ctx_sc = p.sb([128, FCW], F32)
            ctx_bf = p.sb([128, FCW], BF16)
            for hh in range(NH):
                if layer == 2:
                    eh, ebb = ebt[hh % 2]
                    src = AP(tensor=b.dram["sp_expb"].tensor, offset=hh * 4095,
                             ap=[[1, 128], [1, 3968]])
                    p.op("y", (lambda q, d=eh, s=src: q.dma_start(
                        out=d[:, :], in_=s)),
                        reads=[p.db(("sp_expb", k)) for k in range(8)],
                        writes=[ebb])
                qt, qtb = qf[hh // 2]
                kt, ktb = kf[hh // 2]
                prow = (hh % 2) * 64
                for kb in range(16):
                    pth, ptb = pt[kb % 2]
                    vt, vtb = vtok[kb]
                    for qc in range(NFC):
                        sh, sb2 = p.banks[qc]
                        p.op("t", (lambda q, sh=sh, kt=kt, qt=qt, prow=prow,
                                   kb=kb, qc=qc: q.matmul(
                                       sh[:, :],
                                       kt[prow:prow + 64,
                                          kb * 128:(kb + 1) * 128],
                                       qt[prow:prow + 64,
                                          qc * FCW:(qc + 1) * FCW],
                                       start=True, stop=True)),
                             reads=[ktb, qtb], writes=[sb2])
                        p.op("s", (lambda q, sh=sh, pth=pth, qc=qc, hh=hh:
                                   q.activation(
                                       pth[:, qc * FCW:(qc + 1) * FCW],
                                       sh[:, :], AF.Exp, scale=0.125,
                                       bias=negc[0][:, hh:hh + 1])),
                             reads=[sb2, negc[1]], writes=[ptb])
                        if layer == 2:
                            rev = AP(tensor=ebt[hh % 2][0],
                                     offset=kb * 128 + 2047 - qc * FCW,
                                     ap=[[3968, 128], [-1, FCW]])
                            p.op("v", (lambda q, pth=pth, rev=rev, qc=qc:
                                       q.tensor_tensor(
                                           pth[:, qc * FCW:(qc + 1) * FCW],
                                           pth[:, qc * FCW:(qc + 1) * FCW],
                                           rev, ALU.mult)),
                                 reads=[ebt[hh % 2][1], ptb], writes=[ptb])
                    for qc in range(NFC):
                        ch, cb3 = p.banks[4 + qc]
                        p.op("t", (lambda q, ch=ch, vt=vt, pth=pth, hh=hh,
                                   kb=kb, qc=qc: q.matmul(
                                       ch[0:65, :], vt[:, hh * 65:(hh + 1) * 65],
                                       pth[:, qc * FCW:(qc + 1) * FCW],
                                       start=(kb == 0), stop=(kb == 15))),
                             reads=[vtb, ptb], writes=[cb3])
                for qc in range(NFC):
                    ch, cb3 = p.banks[4 + qc]
                    p.op("v", (lambda q, ch=ch: q.tensor_copy(
                        ctx_sc[0][0:65, :], ch[0:65, :])),
                        reads=[cb3], writes=[ctx_sc[1]])
                    dst = b.dram["sp_ctx"][hh * 64:(hh + 1) * 64,
                                           qc * FCW:(qc + 1) * FCW]
                    p.op("v", (lambda q: q.tensor_copy(
                        ctx_bf[0][0:64, :], ctx_sc[0][0:64, :])),
                        reads=[ctx_sc[1]], writes=[ctx_bf[1]])
                    p.op("y", (lambda q, d=dst: q.dma_start(
                        out=d, in_=ctx_bf[0][0:64, :])),
                        reads=[ctx_bf[1]], writes=[p.db(("sp_ctx", hh, qc))])
                    dst2 = b.dram["sp_den"][hh:hh + 1,
                                            qc * FCW:(qc + 1) * FCW]
                    p.op("y", (lambda q, d=dst2: q.dma_start(
                        out=d, in_=ctx_sc[0][64:65, :])),
                        reads=[ctx_sc[1]], writes=[p.db(("sp_den", hh, qc))])

    with p.phase():
        den = p.sb([16, T], F32)
        p.op("y", (lambda q: q.dma_start(out=den[0][:, :], in_=b.dram["sp_den"])),
             reads=[p.db(("sp_den", hh, qc)) for hh in range(16)
                    for qc in range(NFC)], writes=[den[1]])
        p.op("v", (lambda q: q.reciprocal(den[0][:, :], den[0][:, :])),
             reads=[den[1]], writes=[den[1]])
        p.op("y", (lambda q: q.dma_start(out=b.dram["sp_den"], in_=den[0][:, :])),
             reads=[den[1]], writes=[p.db("sp_rd")])

        ctx_t = [p.sb([128, T], BF16) for _ in range(8)]
        rdb = [p.sb([128, FCW], F32) for _ in range(2)]
        for cblk in range(8):
            h, bb = ctx_t[cblk]
            src = b.dram["sp_ctx"][cblk * 128:(cblk + 1) * 128, :]
            p.op("y", (lambda q, d=h, s=src: q.dma_start(out=d[:, :], in_=s)),
                 reads=[p.db(("sp_ctx", cblk * 2, qq)) for qq in range(NFC)] +
                       [p.db(("sp_ctx", cblk * 2 + 1, qq)) for qq in range(NFC)],
                 writes=[bb])
            for fc in range(NFC):
                rh, rb2 = rdb[fc % 2]
                s_up = b.dram["sp_den"][cblk * 2:cblk * 2 + 1,
                                        fc * FCW:(fc + 1) * FCW].broadcast_to(
                                            (64, FCW))
                s_dn = b.dram["sp_den"][cblk * 2 + 1:cblk * 2 + 2,
                                        fc * FCW:(fc + 1) * FCW].broadcast_to(
                                            (64, FCW))
                p.op("y", (lambda q, d=rh, s=s_up: q.dma_start(out=d[0:64, :],
                                                               in_=s)),
                     reads=[p.db("sp_rd")], writes=[rb2])
                p.op("y", (lambda q, d=rh, s=s_dn: q.dma_start(out=d[64:128, :],
                                                               in_=s)),
                     reads=[p.db("sp_rd")], writes=[rb2])
                p.op("v", (lambda q, h=h, rh=rh, fc=fc: q.tensor_tensor(
                    h[:, fc * FCW:(fc + 1) * FCW],
                    h[:, fc * FCW:(fc + 1) * FCW], rh[:, :], ALU.mult)),
                    reads=[rb2, bb], writes=[bb])

        if layer == 2:
            ctx_in = [((lambda fc, h=h: h[:, fc * FCW:(fc + 1) * FCW]), bb)
                      for (h, bb) in ctx_t]

            def cb_wo(j, fc, ps_h, ps_b):
                b.merge(j, fc, ps_h, ps_b, src_key=src_key, dst_key="xres")

            b.linear("w2_o", ctx_in, cb_wo, 1024, 1024)
        else:
            for j in range(8):
                h, bb = ctx_t[j]
                for fc in range(NFC):
                    b.merge_sb(j, fc, h[:, fc * FCW:(fc + 1) * FCW], bb,
                               src_key=src_key, dst_key="xres")


def _emit_vtok(b, vtok, w_tiles, head_stride, ncol, v_off):
    p = b.p
    for tb in range(16):
        vt, vtb = vtok[tb]
        p.op("v", (lambda q, vt=vt: q.memset(vt[:, :], 1.0)), writes=[vtb])
        for hg in range(2):
            ph, pb = p.banks[hg % 2]

            def emit(q, tb=tb, hg=hg, ph=ph):
                ins = None
                for c in range(8):
                    wt, _ = w_tiles[c]
                    src = AP(tensor=wt[0] if isinstance(wt, tuple) else wt,
                             offset=v_off + hg * 8 * head_stride,
                             ap=[[ncol, 128], [head_stride, 8], [1, 64]])
                    ins = q.matmul(ph[:, :],
                                   b.xn[c][0][:, tb * 128:(tb + 1) * 128],
                                   src, start=(c == 0), stop=(c == 7))
                return ins

            p.op("t", emit, reads=[wb for (_, wb) in w_tiles] +
                 [bb for (_, bb) in b.xn], writes=[pb])
            dst = AP(tensor=vt, offset=hg * 8 * 65,
                     ap=[[16 * 65, 128], [65, 8], [1, 64]])
            p.op("v", (lambda q, ph=ph, dst=dst: q.tensor_copy(
                dst, ph[:, :].rearrange("p (h d) -> p h d", h=8))),
                reads=[pb], writes=[vtb])



def emit_moe(b, src_key, dst_key):
    p = b.p
    idh, idb = b.ident
    ones_r, ones_rb = b.ones_row
    with p.phase():
        comb_bf = p.sb([10, T], BF16)
        with p.phase():
            wr = p.sb([128, 8 * 8], BF16)
            src = AP(tensor=b.dram["r3_w"].tensor, offset=0,
                     ap=[[8, 128], [128 * 8, 8], [1, 8]])
            p.op("g", (lambda q, d=wr[0][:, :].rearrange("p (c k) -> p c k", c=8),
                       s=src: q.dma_start(out=d, in_=s)), writes=[wr[1]])
            gates = p.sb([8, T], F32)
            rbc = b.col("L3_rb", rows=8)
            bg = p.banks[4]
            for fc in range(NFC):
                def emit(q, fc=fc):
                    ins = None
                    for c in range(8):
                        ins = q.matmul(bg[0][0:8, :], wr[0][:, c * 8:(c + 1) * 8],
                                       b.xn[c][0][:, fc * FCW:(fc + 1) * FCW],
                                       start=(c == 0), stop=(c == 7))
                    return ins
                p.op("t", emit, reads=[wr[1]] + [bb for (_, bb) in b.xn],
                     writes=[bg[1]])
                p.op("s", (lambda q, fc=fc: q.activation(
                    gates[0][0:8, fc * FCW:(fc + 1) * FCW], bg[0][0:8, :],
                    AF.Identity, bias=rbc[0])),
                    reads=[bg[1], rbc[1]], writes=[gates[1]])

            comb = p.sb([10, T], F32)
            comb_bf = p.sb([10, T], BF16)
            p.op("v", (lambda q: q.memset(comb[0][:, :], 0.5)), writes=[comb[1]])
            NT = 16
            gt = [p.sb([128, 8], F32) for _ in range(NT)]
            m1 = [p.sb([128, 1], F32) for _ in range(NT)]
            m2 = [p.sb([128, 1], F32) for _ in range(NT)]
            eq1 = [p.sb([128, 8], F32) for _ in range(NT)]
            eq2 = [p.sb([128, 8], F32) for _ in range(NT)]
            gm = [p.sb([128, 8], F32) for _ in range(NT)]
            e21 = [p.sb([128, 1], F32) for _ in range(NT)]
            p1c = [p.sb([128, 1], F32) for _ in range(NT)]
            p2c = [p.sb([128, 1], F32) for _ in range(NT)]
            btr = p.banks[5]
            btr2 = p.banks[6]
            for tb in range(NT):
                p.op("t", (lambda q, tb=tb: q.transpose(
                    btr[0][:, 0:8], gates[0][0:8, tb * 128:(tb + 1) * 128],
                    idh[0:8, 0:8])), reads=[gates[1], idb], writes=[btr[1]])
                p.op("v", (lambda q, tb=tb: q.tensor_copy(gt[tb][0][:, :],
                                                          btr[0][:, 0:8])),
                     reads=[btr[1]], writes=[gt[tb][1]])
            for tb in range(NT):
                p.op("v", (lambda q, tb=tb: q.reduce_max(
                    m1[tb][0][:, :], gt[tb][0][:, :], axis=AX.X)),
                    reads=[gt[tb][1]], writes=[m1[tb][1]])
            for tb in range(NT):
                p.op("v", (lambda q, tb=tb: q.tensor_tensor(
                    eq1[tb][0][:, :], gt[tb][0][:, :],
                    m1[tb][0][:, 0:1].broadcast_to((128, 8)), ALU.is_equal)),
                    reads=[gt[tb][1], m1[tb][1]], writes=[eq1[tb][1]])
            for tb in range(NT):
                p.op("v", (lambda q, tb=tb: q.scalar_tensor_tensor(
                    gm[tb][0][:, :], eq1[tb][0][:, :], -1e30, gt[tb][0][:, :],
                    ALU.mult, ALU.add)),
                    reads=[eq1[tb][1], gt[tb][1]], writes=[gm[tb][1]])
            for tb in range(NT):
                p.op("v", (lambda q, tb=tb: q.reduce_max(
                    m2[tb][0][:, :], gm[tb][0][:, :], axis=AX.X)),
                    reads=[gm[tb][1]], writes=[m2[tb][1]])
            for tb in range(NT):
                p.op("v", (lambda q, tb=tb: q.tensor_tensor(
                    eq2[tb][0][:, :], gm[tb][0][:, :],
                    m2[tb][0][:, 0:1].broadcast_to((128, 8)), ALU.is_equal)),
                    reads=[gm[tb][1], m2[tb][1]], writes=[eq2[tb][1]])
            for tb in range(NT):
                p.op("v", (lambda q, tb=tb: q.tensor_tensor(
                    e21[tb][0][:, :], m2[tb][0][:, :], m1[tb][0][:, :],
                    ALU.subtract)),
                    reads=[m1[tb][1], m2[tb][1]], writes=[e21[tb][1]])
            for tb in range(NT):
                p.op("s", (lambda q, tb=tb: q.activation(
                    e21[tb][0][:, :], e21[tb][0][:, :], AF.Exp)),
                    reads=[e21[tb][1]], writes=[e21[tb][1]])
            for tb in range(NT):
                p.op("v", (lambda q, tb=tb: q.tensor_scalar(
                    p1c[tb][0][:, :], e21[tb][0][:, :], 1.0, None, ALU.add)),
                    reads=[e21[tb][1]], writes=[p1c[tb][1]])
            for tb in range(NT):
                p.op("v", (lambda q, tb=tb: q.reciprocal(
                    p1c[tb][0][:, :], p1c[tb][0][:, :])),
                    reads=[p1c[tb][1]], writes=[p1c[tb][1]])
            for tb in range(NT):
                p.op("v", (lambda q, tb=tb: q.tensor_tensor(
                    p2c[tb][0][:, :], e21[tb][0][:, :], p1c[tb][0][:, :],
                    ALU.mult)),
                    reads=[e21[tb][1], p1c[tb][1]], writes=[p2c[tb][1]])
            for tb in range(NT):
                p.op("v", (lambda q, tb=tb: q.tensor_scalar(
                    eq1[tb][0][:, :], eq1[tb][0][:, :], p1c[tb][0][:, :], None,
                    ALU.mult)),
                    reads=[p1c[tb][1], eq1[tb][1]], writes=[eq1[tb][1]])
            for tb in range(NT):
                p.op("v", (lambda q, tb=tb: q.tensor_scalar(
                    eq2[tb][0][:, :], eq2[tb][0][:, :], p2c[tb][0][:, :], None,
                    ALU.mult)),
                    reads=[p2c[tb][1], eq2[tb][1]], writes=[eq2[tb][1]])
            for tb in range(NT):
                p.op("v", (lambda q, tb=tb: q.tensor_tensor(
                    eq1[tb][0][:, :], eq1[tb][0][:, :], eq2[tb][0][:, :],
                    ALU.add)),
                    reads=[eq2[tb][1], eq1[tb][1]], writes=[eq1[tb][1]])
            for tb in range(NT):
                p.op("t", (lambda q, tb=tb: q.transpose(
                    btr2[0][0:8, 0:128], eq1[tb][0][:, 0:8], idh[:, :])),
                    reads=[eq1[tb][1], idb], writes=[btr2[1]])
                p.op("v", (lambda q, tb=tb: q.tensor_copy(
                    comb[0][0:8, tb * 128:(tb + 1) * 128], btr2[0][0:8, 0:128])),
                    reads=[btr2[1]], writes=[comb[1]])
            p.op("v", (lambda q: q.tensor_copy(comb_bf[0][:, :], comb[0][:, :])),
                 reads=[comb[1]], writes=[comb_bf[1]], tail=True)
            p.op("y", (lambda q: q.dma_start(out=b.dram["sp_comb"],
                                             in_=comb_bf[0][:, :])),
                 reads=[comb_bf[1]], writes=[p.db("sp_comb")])

        b2sb = p.sb([10, 1024], BF16)
        comb_rl = [p.sb([10, FCW], BF16) for _ in range(2)]
        b.wload(None, b2sb[0][:, :], b2sb[1], b.dram["b2ext"], 1024)

        yacc = [p.sb([128, T], F32) for _ in range(8)]
        hbf = [p.sb([128, FCW], BF16) for _ in range(16)]
        gel = p.sb([128, FCW], BF16)
        cbb = [p.sb([128, FCW], BF16) for _ in range(2)]
        w1t = [p.sb([128, 8 * 128], BF16) for _ in range(16)]
        w2t = [p.sb([128, 16 * 128], BF16) for _ in range(8)]

        for e in range(11):
            if e < 8:
                w1n, w1o = "m3_w1", e * 1024 * 2048
                w2n, w2o = "m3_w2", e * 2048 * 1024
                b1n, b1k = "L3_b1", e * 16
            elif e < 10:
                w1n, w1o = "m3_sw1", (e - 8) * 1024 * 2048
                w2n, w2o = "m3_sw2", (e - 8) * 2048 * 1024
                b1n, b1k = "L3_sb1", (e - 8) * 16
            if e < 10:
                for j in range(16):
                    wh, wb2 = w1t[j]
                    for hf in range(2):
                        src = AP(tensor=b.dram[w1n].tensor,
                                 offset=w1o + hf * 4 * 128 * 2048 + j * 128,
                                 ap=[[2048, 128], [128 * 2048, 4], [1, 128]])
                        stage = None
                        sview = 4
                        b.wload(stage, wh[:, hf * 512:(hf + 1) * 512], wb2,
                                src, 4 * 128, view=sview)
                for pb2 in range(8):
                    wh, wb2 = w2t[pb2]
                    for cg in range(4):
                        src = AP(tensor=b.dram[w2n].tensor,
                                 offset=w2o + pb2 * 128 + cg * 4 * 128 * 1024,
                                 ap=[[1024, 128], [128 * 1024, 4], [1, 128]])
                        stage = None
                        sview = 4
                        b.wload(stage, wh[:, cg * 512:(cg + 1) * 512], wb2,
                                src, 4 * 128, view=sview)
            for tc in range(NFC):
                fsl = slice(tc * FCW, (tc + 1) * FCW)
                if e < 10:
                    ch, chb = cbb[tc % 2]
                    src = b.dram["sp_comb"][e:e + 1, fsl].broadcast_to(
                        (128, FCW))
                    p.op("y", (lambda q, d=ch, s=src: q.dma_start(
                        out=d[:, :], in_=s)), reads=[p.db("sp_comb")],
                        writes=[chb])
                    for j in range(16):
                        ph, pb = p.banks[j % 2]
                        wh, wb2 = w1t[j]

                        def emit(q, wh=wh, ph=ph, fsl=fsl):
                            ins = None
                            for c in range(8):
                                ins = q.matmul(ph[:, :],
                                               wh[:, c * 128:(c + 1) * 128],
                                               b.xn[c][0][:, fsl],
                                               start=(c == 0), stop=(c == 7))
                            return ins
                        p.op("t", emit, reads=[wb2] + [bb for (_, bb) in b.xn],
                             writes=[pb])
                        bcol = b.col(b1n, b1k + j)
                        p.op("s", (lambda q, ph=ph, bc=bcol[0]: q.activation(
                            gel[0][:, :], ph[:, :], AF.Gelu, bias=bc)),
                            reads=[pb, bcol[1]], writes=[gel[1]])
                        hh, hb = hbf[j]
                        p.op("v", (lambda q, hh=hh, ch=ch: q.tensor_tensor(
                            hh[:, :], gel[0][:, :], ch[:, :], ALU.mult)),
                            reads=[gel[1], chb], writes=[hb])
                for pblk in range(8):
                    ph2, pb3 = p.banks[2 + pblk % 2]
                    if e < 10:
                        wh, wb2 = w2t[pblk]

                        def emity(q, wh=wh, ph2=ph2):
                            ins = None
                            for c in range(16):
                                ins = q.matmul(ph2[:, :],
                                               wh[:, c * 128:(c + 1) * 128],
                                               hbf[c][0][:, :],
                                               start=(c == 0), stop=(c == 15))
                            return ins
                        p.op("t", emity, reads=[wb2] + [hb for (_, hb) in hbf],
                             writes=[pb3])
                    else:
                        cr_h, cr_b = comb_rl[tc % 2]
                        if pblk == 0:
                            src2 = b.dram["sp_comb"][:, fsl]
                            p.op("y", (lambda q, d=cr_h, s=src2: q.dma_start(
                                out=d[:, :], in_=s)),
                                reads=[p.db("sp_comb")], writes=[cr_b])

                        def emity(q, ph2=ph2, pblk=pblk, cr_h=cr_h):
                            return q.matmul(
                                ph2[:, :],
                                b2sb[0][:, pblk * 128:(pblk + 1) * 128],
                                cr_h[:, :], start=True, stop=True)
                        p.op("t", emity, reads=[b2sb[1], cr_b],
                             writes=[pb3])
                    yh, yb = yacc[pblk]
                    if e == 0:
                        p.op("v", (lambda q, yh=yh, ph2=ph2, fsl=fsl:
                                   q.tensor_copy(yh[:, fsl], ph2[:, :])),
                             reads=[pb3], writes=[yb])
                    else:
                        p.op("v", (lambda q, yh=yh, ph2=ph2, fsl=fsl:
                                   q.tensor_tensor(yh[:, fsl], yh[:, fsl],
                                                   ph2[:, :], ALU.add)),
                             reads=[pb3, yb], writes=[yb])

        for j in range(8):
            yh, yb = yacc[j]
            for fc in range(NFC):
                b.merge_sb(j, fc, yh[:, fc * FCW:(fc + 1) * FCW], yb,
                           src_key=src_key, dst_key=dst_key)



DBG_HOOK = None


def build_program(layers, warrs):
    nc = bass.Bass("TRN2", target_bir_lowering=False, debug=False)
    p = P(nc)
    dram = {}
    dram["xin"] = nc.dram_tensor("xin", [D, T], F32, kind="ExternalInput").ap()
    dram["xout"] = nc.dram_tensor("xout", [D, T], F32,
                                  kind="ExternalOutput").ap()
    dram["cols_all"] = nc.dram_tensor("cols_all", [128, NCOLS], F32,
                                      kind="ExternalInput").ap()
    dram["ident"] = nc.dram_tensor("ident", [128, 128], F32,
                                   kind="ExternalInput").ap()
    for L in layers:
        for name, _ in _wspec(L):
            arr = warrs[name]
            dt = I32 if arr.dtype == np.int32 else F32
            dram[name] = nc.dram_tensor(name, list(arr.shape), dt,
                                        kind="ExternalInput").ap()
    for k, (shape, dt) in {
        "xres": ([D, T], F32),
        "sp_u": ([2048, T], BF16),
        "sp_z": ([2048, T], BF16),
        "sp_dt": ([2048, T], F32),
        "sp_bc": ([32, T], BF16),
        "sp_hb": ([4096, T], BF16),
        "sp_hc": ([4096, T], BF16),
        "sp_ctx": ([D, T], BF16),
        "sp_den": ([16, T], F32),
        "sp_comb": ([10, T], BF16),
        "sp_expb": ([16, 4095], BF16),
    }.items():
        dram[k] = nc.dram_tensor(k, shape, dt, kind="Internal").ap()

    b = B(p, dram)
    b.cols = p.sb([128, NCOLS], F32, "cols")
    b.ident = p.sb([128, 128], F32, "identsb")
    b.ones_col = p.sb([128, 1], F32, "ones_col")
    b.ones_col_bf = p.sb([128, 1], BF16, "ones_col_bf")
    b.ones_row = p.sb([1, 128], F32, "ones_row")
    b.xn = [p.sb([128, T], BF16, f"xn{i}") for i in range(8)]
    b.mt = [p.sb([128, FCW], F32) for _ in range(2)]
    b.mt2 = [p.sb([128, FCW], F32) for _ in range(2)] if MERGE_ACC else None
    b.mti = 0

    p.op("y", lambda q: q.dma_start(out=b.cols[0][:, :], in_=dram["cols_all"]),
         writes=[b.cols[1]])
    p.op("y", lambda q: q.dma_start(out=b.ident[0][:, :], in_=dram["ident"]),
         writes=[b.ident[1]])
    p.op("v", lambda q: q.memset(b.ones_col[0][:, :], 1.0),
         writes=[b.ones_col[1]])
    p.op("v", lambda q: q.memset(b.ones_col_bf[0][:, :], 1.0),
         writes=[b.ones_col_bf[1]])
    p.op("v", lambda q: q.memset(b.ones_row[0][:, :], 1.0),
         writes=[b.ones_row[1]])
    for cv in (1e-5, -0.5, 0.125, 1.0 / 127.0, 2.0, 0.5, -1.0, 1.0):
        cth = p.sb([128, 1], F32)
        p.op("v", (lambda q, t=cth[0], cv=cv: q.memset(t[:, :], cv)),
             writes=[cth[1]])
        nc.const_aps.register(F32, cv, cth[0][:, :])
    for j in range(8 if MERGE_ACC else 0):
        for fc in range(NFC):
            srcp = dram["xin"][j * 128:(j + 1) * 128, fc * FCW:(fc + 1) * FCW]
            dstp = dram["xres"][j * 128:(j + 1) * 128, fc * FCW:(fc + 1) * FCW]
            p.op("y", (lambda q, s=srcp, d=dstp: q.dma_start(out=d, in_=s)),
                 reads=[p.db(("xin", j, fc))], writes=[p.db(("xres", j, fc))])
    p.fence()
    if 1 in layers:
        o1 = COLS["L1_bdt"][0]; o2 = COLS["L1_dtb"][0]; o3 = COLS["L1_dtsum"][0]
        ch, cb = b.cols
        p.op("v", lambda q: q.tensor_tensor(ch[:, o3:o3 + 16],
                                            ch[:, o1:o1 + 16],
                                            ch[:, o2:o2 + 16], ALU.add),
             reads=[cb], writes=[cb])

    first, last = layers[0], layers[-1]
    for L in layers:
        src_key = "xin" if L == first else "xres"
        dst_key = "xout" if L == last else "xres"
        if L == 0:
            p.paranoid = P.PARANOID_NOSELF
        (emit_layernorm if LN_HARD else emit_layernorm_v0)(b, src_key)
        if L == 0:
            emit_mamba0(b, src_key)
            p.paranoid = False
            emit_ffn_ternary(b, "L0", "xres", dst_key)
        elif L == 1:
            emit_mamba1(b, src_key)
            emit_ffn_quant(b, "xres", dst_key)
        elif L == 2:
            _emit_attention(b, src_key, 2)
            emit_ffn_ternary(b, "L2", "xres", dst_key)
        else:
            _emit_attention(b, src_key, 3)
            emit_moe(b, "xres", dst_key)

    for j in range(8 if MERGE_ACC else 0):
        for fc in range(NFC):
            srcp = dram["xres"][j * 128:(j + 1) * 128, fc * FCW:(fc + 1) * FCW]
            dstp = dram["xout"][j * 128:(j + 1) * 128, fc * FCW:(fc + 1) * FCW]
            p.op("y", (lambda q, s=srcp, d=dstp: q.dma_start(out=d, in_=s)),
                 reads=[p.db(("xres", j, fc))], writes=[p.db(("xout", j, fc))])
    if DBG_HOOK is not None:
        DBG_HOOK(p, dram, b)
    for k in P.ENG:
        p.raw_wait("y", k, p.cnt[k])
    with nc.Block() as block:
        p.finalize(block)
    return nc, p



def _host_prep(inputs):
    x = np.asarray(inputs["L0_attn_x"])
    xT = np.ascontiguousarray(x.transpose(0, 2, 1))
    cols = _build_cols(inputs)
    ident = np.eye(128, dtype=np.float32)
    warrs = {}
    for L in range(4):
        for name, fn in _wspec(L):
            a = np.asarray(fn(inputs))
            if a.dtype != np.int32:
                a = a.astype(np.float32)
            warrs[name] = np.ascontiguousarray(a)
    return xT, cols, ident, warrs


def kernel(**inputs):
    xT, cols, ident, warrs = _host_prep(inputs)
    groups = [[0, 1, 2, 3]] if FUSED else [[0], [1], [2], [3]]
    state = [xT[c] for c in range(8)]
    for layers in groups:
        nc, p = build_program(layers, warrs)
        in_maps = []
        for c in range(8):
            m = {"xin": state[c], "cols_all": cols, "ident": ident}
            for L in layers:
                for name, _ in _wspec(L):
                    m[name] = warrs[name]
            in_maps.append(m)
        res = run_bass_kernel_spmd(nc, in_maps, core_ids=list(range(8)))
        state = [np.asarray(r["xout"]) for r in res.results]
        p.root.close()
    out = np.stack([s.T for s in state], axis=0).astype(np.float32)
    return np.ascontiguousarray(out)


# version 17
# speedup vs baseline: 1.2370x; 1.0522x over previous
import math
from contextlib import ExitStack

import numpy as np
import ml_dtypes

import concourse.bass as bass
import concourse.mybir as mybir
from concourse.bass_utils import run_bass_kernel_spmd

F32 = mybir.dt.float32
BF16 = mybir.dt.bfloat16
I32 = mybir.dt.int32
AF = mybir.ActivationFunctionType
ALU = mybir.AluOpType
AX = mybir.AxisListType
AP = bass.AP

T = 2048
D = 1024
FCW = 512
NFC = T // FCW

FUSED = True
LN_HARD = False
GN_HARD = False
MERGE_ACC = False
YG_REUSE = True


def _t(a):
    return np.ascontiguousarray(np.asarray(a).T)


def _cols(vec, n128):
    return np.ascontiguousarray(np.asarray(vec).reshape(n128, 128).T)


def _acols(alog):
    a = np.asarray(alog).reshape(16, 128, 16).transpose(1, 0, 2).reshape(128, 256)
    return np.ascontiguousarray(a)


def _bucket_onehot():
    r = np.arange(4095) - 2047
    nb = 16
    rb = (r > 0).astype(np.int64) * nb
    ra = np.abs(r)
    me = 8
    rp_f = np.maximum(ra, 1).astype(np.float32)
    ril = me + (np.log(rp_f / me) / math.log(128 / me) * (nb - me)).astype(np.int32)
    ril = np.minimum(ril, nb - 1)
    b = rb + np.where(ra < me, ra, ril)
    oh = np.zeros((32, 4095), np.float32)
    oh[b, np.arange(4095)] = 1.0
    return oh


COLS = {}
_nxt = [0]
for _n, _w in [
    ("L0_b_in", 32), ("L0_b_xproj", 1), ("L0_b_dt", 16), ("L0_b_out", 8),
    ("L0_D", 16), ("L0_w1s", 32), ("L0_w2s", 32), ("L0_h1s", 32),
    ("L0_h2s", 32), ("L0_w3s", 8),
    ("L1_bdt", 16), ("L1_dtb", 16), ("L1_gamma", 16), ("L1_beta", 16),
    ("L2_w1s", 32), ("L2_w2s", 32), ("L2_h1s", 32), ("L2_h2s", 32),
    ("L2_w3s", 8),
    ("L3_rb", 1), ("L3_b1", 128), ("L3_sb1", 32),
    ("L1_dtsum", 16),
]:
    COLS[_n] = (_nxt[0], _w)
    _nxt[0] += _w
NCOLS = _nxt[0]


def _build_cols(inp):
    c = np.zeros((128, NCOLS), np.float32)

    def put(name, vec):
        o, w = COLS[name]
        v = np.asarray(vec).astype(np.float32).reshape(-1)
        if w == 1:
            c[: v.shape[0], o] = v
        else:
            c[:, o:o + w] = _cols(v, w)

    put("L0_b_in", inp["L0_attn_b_in"]); put("L0_b_xproj", inp["L0_attn_b_xproj"])
    put("L0_b_dt", inp["L0_attn_b_dt"]); put("L0_b_out", inp["L0_attn_b_out"])
    put("L0_D", inp["L0_attn_D_param"])
    put("L0_w1s", inp["L0_ffn_w1_s"]); put("L0_w2s", inp["L0_ffn_w2_s"])
    put("L0_h1s", inp["L0_ffn_h1_s"]); put("L0_h2s", inp["L0_ffn_h2_s"])
    put("L0_w3s", inp["L0_ffn_w3_s"])
    put("L1_bdt", inp["L1_attn_bdt"]); put("L1_dtb", inp["L1_attn_dt_bias"])
    put("L1_gamma", inp["L1_attn_gamma"]); put("L1_beta", inp["L1_attn_beta"])
    put("L2_w1s", inp["L2_ffn_w1_s"]); put("L2_w2s", inp["L2_ffn_w2_s"])
    put("L2_h1s", inp["L2_ffn_h1_s"]); put("L2_h2s", inp["L2_ffn_h2_s"])
    put("L2_w3s", inp["L2_ffn_w3_s"])
    put("L3_rb", inp["L3_ffn_router_b"]); put("L3_b1", inp["L3_ffn_b1"])
    put("L3_sb1", inp["L3_ffn_sb1"])
    return c


def _wspec(layer):
    if layer == 0:
        return [
            ("w0_in", lambda i: i["L0_attn_W_in"]),
            ("w0_xp", lambda i: i["L0_attn_W_xproj"]),
            ("w0_dt", lambda i: i["L0_attn_W_dt"]),
            ("w0_out", lambda i: i["L0_attn_W_out"]),
            ("a0", lambda i: _acols(i["L0_attn_A_log"])),
            ("f0_w1", lambda i: _t(i["L0_ffn_w1_w"])),
            ("f0_w2", lambda i: _t(i["L0_ffn_w2_w"])),
            ("f0_h1", lambda i: _t(i["L0_ffn_h1_w"])),
            ("f0_h2", lambda i: _t(i["L0_ffn_h2_w"])),
            ("f0_w3", lambda i: _t(i["L0_ffn_w3_w"])),
        ]
    if layer == 1:
        return [
            ("w1_in", lambda i: _t(i["L1_attn_Win"])),
            ("w1_bc", lambda i: np.concatenate(
                [_t(i["L1_attn_WB"]), _t(i["L1_attn_WC"])], axis=1)),
            ("w1_dt", lambda i: _t(i["L1_attn_Wdt"])),
            ("w1_out", lambda i: _t(i["L1_attn_Wout"])),
            ("a1", lambda i: _acols(i["L1_attn_A_log"])),
            ("f1_wg", lambda i: _t(i["L1_ffn_w_gate"])),
            ("f1_wu", lambda i: _t(i["L1_ffn_w_up"])),
            ("f1_wd", lambda i: _t(i["L1_ffn_w_down"])),
        ]
    if layer == 2:
        return [
            ("w2_q", lambda i: _t(i["L2_attn_Wq"])),
            ("w2_k", lambda i: _t(i["L2_attn_Wk"])),
            ("w2_v", lambda i: _t(i["L2_attn_Wv"])),
            ("w2_o", lambda i: _t(i["L2_attn_Wo"])),
            ("rel_emb", lambda i: i["L2_attn_rel_emb"]),
            ("onehot", lambda i: _bucket_onehot()),
            ("f2_w1", lambda i: _t(i["L2_ffn_w1_w"])),
            ("f2_w2", lambda i: _t(i["L2_ffn_w2_w"])),
            ("f2_h1", lambda i: _t(i["L2_ffn_h1_w"])),
            ("f2_h2", lambda i: _t(i["L2_ffn_h2_w"])),
            ("f2_w3", lambda i: _t(i["L2_ffn_w3_w"])),
        ]
    return [
        ("w3_qkv", lambda i: _t(i["L3_attn_W_qkv"])),
        ("r3_w", lambda i: i["L3_ffn_router_w"]),
        ("m3_w1", lambda i: i["L3_ffn_w1"]),
        ("m3_w2", lambda i: i["L3_ffn_w2"]),
        ("m3_sw1", lambda i: i["L3_ffn_sw1"]),
        ("m3_sw2", lambda i: i["L3_ffn_sw2"]),
        ("b2ext", lambda i: np.concatenate(
            [i["L3_ffn_b2"], i["L3_ffn_sb2"]], axis=0)),
    ]



class Buf:
    __slots__ = ("w", "r")

    def __init__(self):
        self.w = None
        self.r = []


_NULL = Buf()


class P:
    ENG = ("t", "v", "s", "g", "y")

    def __init__(self, nc):
        self.nc = nc
        self.root = ExitStack()
        self.stacks = [self.root]
        self.streams = {k: [] for k in self.ENG}
        self.cnt = dict.fromkeys(self.ENG, 0)
        self.seen = {k: dict.fromkeys(self.ENG, 0) for k in self.ENG}
        self.sems = {k: self.root.enter_context(nc.semaphore("sem_" + k))
                     for k in self.ENG}
        self.dbufs = {}
        self._uid = 0
        self.banks = []
        for i in range(8):
            h = self.root.enter_context(
                nc.psum_tensor(f"bank{i}", [128, FCW], F32))
            self.banks.append((h, Buf()))

    def sb(self, shape, dt, name=None):
        self._uid += 1
        h = self.stacks[-1].enter_context(
            self.nc.sbuf_tensor(name or f"sb{self._uid}", list(shape), dt))
        return (h, Buf())

    def db(self, key):
        b = self.dbufs.get(key)
        if b is None:
            b = self.dbufs[key] = Buf()
        return b

    paranoid = False
    PARANOID_FULL = {k: ("v", "s", "y", "g")
                     for k in ("t", "v", "s", "g", "y")}
    PARANOID_NOSELF = {"t": ("v", "s", "y", "g"), "v": ("s", "y", "g"),
                       "s": ("v", "y", "g"), "y": ("v", "s", "g"),
                       "g": ("v", "s", "y")}
    PARANOID_FAST = {"v": ("y", "g"), "s": ("v", "y", "g"),
                     "t": (), "y": (), "g": ()}

    def op(self, ek, emit, reads=(), writes=(), tail=False):
        need = {}
        if self.paranoid:
            pset = self.paranoid if isinstance(self.paranoid, dict) else                 P.PARANOID_FULL
            for sk in pset.get(ek, ()):
                if self.cnt[sk] > need.get(sk, 0):
                    need[sk] = self.cnt[sk]
        for b in reads:
            if b.w is not None:
                sk, v = b.w
                if (sk != ek or tail) and v > need.get(sk, 0):
                    need[sk] = v
        for b in writes:
            deps = ([b.w] if b.w is not None else []) + b.r
            for sk, v in deps:
                if sk != ek and v > need.get(sk, 0):
                    need[sk] = v
        st = self.streams[ek]
        for sk in self.ENG:
            v = need.get(sk, 0)
            if v > self.seen[ek].get(sk, 0):
                self.seen[ek][sk] = v
                st.append(("w", sk, v))
        n = 16 if ek in ("g", "y") else 1
        self.cnt[ek] += n
        tok = (ek, self.cnt[ek])
        st.append(("e", emit, ek, n))
        for b in reads:
            b.r.append(tok)
        for b in writes:
            b.w = tok
            b.r = []
        return tok

    def raw_wait(self, ek, sk, v):
        if v > self.seen[ek][sk]:
            self.seen[ek][sk] = v
            self.streams[ek].append(("w", sk, v))

    def fence(self):
        for ek in self.ENG:
            for sk in self.ENG:
                if sk != ek:
                    self.raw_wait(ek, sk, self.cnt[sk])

    class _Phase:
        def __init__(self, p):
            self.p = p

        def __enter__(self):
            self.p.stacks.append(ExitStack())
            return self

        def __exit__(self, *a):
            st = self.p.stacks.pop()
            st.close()
            self.p.fence()
            return False

    def phase(self):
        return P._Phase(self)

    def finalize(self, block):
        sems = self.sems
        names = {"t": "tensor", "v": "vector", "s": "scalar",
                 "g": "gpsimd", "y": "sync"}
        for k in self.ENG:
            items = self.streams[k]

            def make(items=items):
                def f(q):
                    for it in items:
                        if it[0] == "w":
                            q.wait_ge(sems[it[1]], it[2])
                        else:
                            it[1](q).then_inc(sems[it[2]], it[3])
                return f

            getattr(block, names[k])(make())



class B:
    def __init__(self, p, dram):
        self.p = p
        self.dram = dram

    def col(self, name, blk=0, rows=128):
        o, w = COLS[name]
        h, b = self.cols
        return (h[0:rows, o + blk:o + blk + 1], b)

    def launder(self, tiles, out_tiles):
        p = self.p
        for (sh, sb2), (dh, db2) in zip(tiles, out_tiles):
            p.op("y", (lambda q, s=sh, d=dh: q.dma_start(out=d[:, :],
                                                         in_=s[:, :])),
                 reads=[sb2], writes=[db2])
        return out_tiles

    def wload(self, stage, dst_ap, dst_buf, src_ap, cols, view=None):
        p = self.p
        dview = dst_ap if view is None else dst_ap.rearrange(
            "p (c k) -> p c k", c=int(view))
        p.op("g", (lambda q, d=dview, s=src_ap: q.dma_start(out=d, in_=s)),
             reads=[], writes=[dst_buf])

    def linear(self, wname, in_tiles, out_cb, in_dim, out_dim, banks=(0, 1),
               woff=0, wrowlen=None):
        p = self.p
        nci = in_dim // 128
        ncj = out_dim // 128
        wrl = wrowlen if wrowlen is not None else out_dim
        w_d = self.dram[wname]
        wb = [p.sb([128, nci * 128], BF16) for _ in range(2)]
        for j in range(ncj):
            wt, wbuf = wb[j % 2]
            for cgi, cg0 in enumerate(range(0, nci, 4)):
                cgn = min(4, nci - cg0)
                src = AP(tensor=w_d.tensor, offset=woff + cg0 * 128 * wrl + j * 128,
                         ap=[[wrl, 128], [128 * wrl, cgn], [1, 128]])
                stage = None
                sview = cgn
                dst = wt[:, cg0 * 128:(cg0 + cgn) * 128]
                self.wload(stage, dst, wbuf, src, cgn * 128, view=sview)
            for fc in range(NFC):
                bh, bb = self.p.banks[banks[(j * NFC + fc) % len(banks)]]

                def emit(q, wt=wt, bh=bh, fc=fc, nci=nci, its=in_tiles):
                    ins = None
                    for c in range(nci):
                        ins = q.matmul(bh[:, :], wt[:, c * 128:(c + 1) * 128],
                                       its[c][0](fc), start=(c == 0),
                                       stop=(c == nci - 1))
                    return ins

                p.op("t", emit, reads=[wbuf] + [t[1] for t in in_tiles],
                     writes=[bb])
                out_cb(j, fc, bh, bb)

    def linear_stream(self, wname, rhs_dram, out_cb, in_dim, out_dim,
                      banks=(0, 1), jg=8, rhs_bf=True):
        p = self.p
        nci = in_dim // 128
        ncj = out_dim // 128
        w_d = self.dram[wname]
        with p.phase():
            panels = [p.sb([128, nci * 128], BF16) for _ in range(jg)]
            rts = [p.sb([128, FCW], BF16 if rhs_bf else F32)
                   for _ in range(nci)]
            nst = 0
            for jg0 in range(0, ncj, jg):
                jn = min(jg, ncj - jg0)
                for ji in range(jn):
                    j = jg0 + ji
                    wt, wbuf = panels[ji]
                    for cg0 in range(0, nci, 4):
                        cgn = min(4, nci - cg0)
                        src = AP(tensor=w_d.tensor,
                                 offset=cg0 * 128 * out_dim + j * 128,
                                 ap=[[out_dim, 128], [128 * out_dim, cgn],
                                     [1, 128]])
                        stage = None
                        nst += 1
                        sview = cgn
                        dst = wt[:, cg0 * 128:(cg0 + cgn) * 128]
                        self.wload(stage, dst, wbuf, src, cgn * 128,
                                   view=sview)
                for fc in range(NFC):
                    for c in range(nci):
                        h, bb = rts[c]
                        src = self.dram[rhs_dram][c * 128:(c + 1) * 128,
                                                  fc * FCW:(fc + 1) * FCW]
                        p.op("y", (lambda q, d=h, s=src:
                                   q.dma_start(out=d[:, :], in_=s)),
                             reads=[p.db((rhs_dram, c, fc))], writes=[bb])
                    for ji in range(jn):
                        j = jg0 + ji
                        wt, wbuf = panels[ji]
                        bh, bb = p.banks[banks[(j * NFC + fc) % len(banks)]]

                        def emit(q, wt=wt, bh=bh, nci=nci):
                            ins = None
                            for c in range(nci):
                                ins = q.matmul(bh[:, :],
                                               wt[:, c * 128:(c + 1) * 128],
                                               rts[c][0][:, :], start=(c == 0),
                                               stop=(c == nci - 1))
                            return ins

                        p.op("t", emit, reads=[wbuf] + [t[1] for t in rts],
                             writes=[bb])
                        out_cb(j, fc, bh, bb)

    def merge(self, j, fc, ps_h, ps_b, scalar=1.0, scalar_buf=None,
              op0=ALU.mult, src_key="xres", dst_key="xres"):
        p = self.p
        if not MERGE_ACC:
            return self.merge_v0(j, fc, ps_h, ps_b, scalar, scalar_buf, op0,
                                 src_key, dst_key)
        h, b = self.mt[self.mti]
        h2, b2 = self.mt2[self.mti]
        self.mti = (self.mti + 1) % len(self.mt)
        dbuf = p.db(("xres", j, fc))
        dst = self.dram["xres"][j * 128:(j + 1) * 128, fc * FCW:(fc + 1) * FCW]
        sb_extra = [scalar_buf] if scalar_buf is not None else []
        p.op("v", (lambda q, d=h, ps=ps_h, sc=scalar, op0=op0:
                   q.tensor_scalar(d[:, :], ps[:, :], sc, None, op0)),
             reads=[ps_b] + sb_extra, writes=[b])
        p.op("y", (lambda q, s=h, d=h2: q.dma_start(out=d[:, :], in_=s[:, :])),
             reads=[b], writes=[b2])
        p.op("g", (lambda q, s=h2, d=dst: q.dma_start(
            out=d, in_=s[:, :], accum_op=ALU.add)),
             reads=[b2, dbuf], writes=[dbuf], tail=True)

    def merge_sb(self, j, fc, src_ap, src_buf, src_key="xres", dst_key="xres"):
        p = self.p
        if not MERGE_ACC:
            return self.merge_sb_v0(j, fc, src_ap, src_buf, src_key, dst_key)
        h2, b2 = self.mt2[self.mti]
        self.mti = (self.mti + 1) % len(self.mt)
        dbuf = p.db(("xres", j, fc))
        dst = self.dram["xres"][j * 128:(j + 1) * 128, fc * FCW:(fc + 1) * FCW]
        p.op("y", (lambda q, s=src_ap, d=h2: q.dma_start(out=d[:, :], in_=s)),
             reads=[src_buf], writes=[b2])
        p.op("g", (lambda q, s=h2, d=dst: q.dma_start(
            out=d, in_=s[:, :], accum_op=ALU.add)),
             reads=[b2, dbuf], writes=[dbuf], tail=True)

    def merge_v0(self, j, fc, ps_h, ps_b, scalar=1.0, scalar_buf=None,
                 op0=ALU.mult, src_key="xres", dst_key="xres"):
        p = self.p
        h, b = self.mt[(j + fc) % 2]
        sbuf = p.db((src_key, j, fc))
        dbuf = p.db((dst_key, j, fc))
        src = self.dram[src_key][j * 128:(j + 1) * 128, fc * FCW:(fc + 1) * FCW]
        dst = self.dram[dst_key][j * 128:(j + 1) * 128, fc * FCW:(fc + 1) * FCW]
        p.op("y", (lambda q, d=h, s=src: q.dma_start(out=d[:, :], in_=s)),
             reads=[sbuf], writes=[b])
        sb_extra = [scalar_buf] if scalar_buf is not None else []
        p.op("v", (lambda q, d=h, ps=ps_h, sc=scalar, op0=op0:
                   q.scalar_tensor_tensor(d[:, :], ps[:, :], sc, d[:, :],
                                          op0, ALU.add)),
             reads=[ps_b, b] + sb_extra, writes=[b])
        p.op("y", (lambda q, s=h, d=dst: q.dma_start(out=d, in_=s[:, :])),
             reads=[b], writes=[dbuf])

    def merge_sb_v0(self, j, fc, src_ap, src_buf, src_key="xres",
                    dst_key="xres"):
        p = self.p
        h, b = self.mt[(j + fc) % 2]
        sbuf = p.db((src_key, j, fc))
        dbuf = p.db((dst_key, j, fc))
        src = self.dram[src_key][j * 128:(j + 1) * 128, fc * FCW:(fc + 1) * FCW]
        dst = self.dram[dst_key][j * 128:(j + 1) * 128, fc * FCW:(fc + 1) * FCW]
        p.op("y", (lambda q, d=h, s=src: q.dma_start(out=d[:, :], in_=s)),
             reads=[sbuf], writes=[b])
        p.op("v", (lambda q, d=h, sa=src_ap: q.tensor_tensor(
            d[:, :], d[:, :], sa, ALU.add)), reads=[src_buf, b], writes=[b])
        p.op("y", (lambda q, s=h, d=dst: q.dma_start(out=d, in_=s[:, :])),
             reads=[b], writes=[dbuf])

    def xn_in(self):
        return [((lambda fc, h=h: h[:, fc * FCW:(fc + 1) * FCW]), bb)
                for (h, bb) in self.xn]



def emit_layernorm(b, src_key):
    p = b.p
    x_d = b.dram[src_key]
    ones_c, ones_cb = b.ones_col
    ones_r, ones_rb = b.ones_row
    with p.phase():
        xt = [p.sb([128, FCW], F32) for _ in range(8)]
        sq = [p.sb([128, FCW], F32) for _ in range(2)]
        sq2 = [p.sb([128, FCW], F32) for _ in range(2)]
        rws = [p.sb([1, FCW], F32) for _ in range(7)]
        (rmu, rmub), (rex, rexb), (rm2, rm2b), (rvar, rvarb), (rstd, rstdb), \
            (rbn, rbnb), (pad, padb) = rws
        rvL, rvLb = rm2, rm2b
        lnv, lnvb = rex, rexb
        rsL, rsLb = rvar, rvarb
        bsx = p.banks[4]
        bsq = p.banks[5]
        ba = p.banks[6]
        bbc = p.banks[7]
        for fc in range(NFC):
            for i in range(8):
                h, bb = xt[i]
                src = x_d[i * 128:(i + 1) * 128, fc * FCW:(fc + 1) * FCW]
                p.op("y", (lambda q, d=h, s=src: q.dma_start(out=d[:, :], in_=s)),
                     reads=[p.db((src_key, i, fc))], writes=[bb])
            for i in range(8):
                h, bb = xt[i]
                sh, sb2 = sq[i % 2]
                s2h, s2b = sq2[i % 2]
                p.op("s", (lambda q, d=sh, s=h: q.activation(
                    d[:, :], s[:, :], AF.Square)), reads=[bb], writes=[sb2])
                p.op("y", (lambda q, s=sh, d=s2h: q.dma_start(
                    out=d[:, :], in_=s[:, :])), reads=[sb2], writes=[s2b])
                p.op("t", (lambda q, s=h, i=i: q.matmul(
                    bsx[0][0:1, :], ones_c[:, :], s[:, :], start=(i == 0),
                    stop=(i == 7))), reads=[bb, ones_cb], writes=[bsx[1]])
                p.op("t", (lambda q, s=s2h, i=i: q.matmul(
                    bsq[0][0:1, :], ones_c[:, :], s[:, :], start=(i == 0),
                    stop=(i == 7))), reads=[s2b, ones_cb], writes=[bsq[1]])
            p.op("v", (lambda q: q.tensor_scalar(rmu[0:1, :], bsx[0][0:1, :],
                                                 1.0 / D, None, ALU.mult)),
                 reads=[bsx[1]], writes=[rmub])
            p.op("v", (lambda q: q.tensor_scalar(rex[0:1, :], bsq[0][0:1, :],
                                                 1.0 / D, None, ALU.mult)),
                 reads=[bsq[1]], writes=[rexb])
            p.op("v", (lambda q: q.tensor_tensor(rm2[0:1, :], rmu[0:1, :],
                                                 rmu[0:1, :], ALU.mult)),
                 reads=[rmub], writes=[rm2b])
            p.op("v", (lambda q: q.memset(pad[0:1, :], 0.0)), writes=[padb])
            p.op("v", (lambda q: q.tensor_tensor(rvar[0:1, :], rex[0:1, :],
                                                 rm2[0:1, :], ALU.subtract)),
                 reads=[rexb, rm2b], writes=[rvarb])
            p.op("y", (lambda q: q.dma_start(out=rvL[0:1, :], in_=rvar[0:1, :])),
                 reads=[rvarb], writes=[rvLb])
            p.op("s", (lambda q: q.activation(lnv[0:1, :], rvL[0:1, :], AF.Ln,
                                              bias=1e-5)), reads=[rvLb], writes=[lnvb])
            p.op("s", (lambda q: q.activation(pad[0:1, :], pad[0:1, :],
                                              AF.Identity)), reads=[padb],
                 writes=[padb])
            p.op("s", (lambda q: q.activation(rstd[0:1, :], lnv[0:1, :], AF.Exp,
                                              scale=-0.5)), reads=[lnvb], writes=[rstdb])
            p.op("y", (lambda q: q.dma_start(out=rsL[0:1, :], in_=rstd[0:1, :])),
                 reads=[rstdb], writes=[rsLb])
            p.op("v", (lambda q: q.scalar_tensor_tensor(
                rbn[0:1, :], rmu[0:1, :], -1.0, rsL[0:1, :], ALU.mult, ALU.mult)),
                reads=[rmub, rsLb], writes=[rbnb])
            p.op("y", (lambda q: q.dma_start(out=lnv[0:1, :], in_=rbn[0:1, :])),
                 reads=[rbnb], writes=[lnvb])
            p.op("t", (lambda q: q.matmul(ba[0][:, :], ones_r[:, :], rsL[0:1, :],
                                          start=True, stop=True)),
                 reads=[rsLb, ones_rb], writes=[ba[1]])
            p.op("t", (lambda q: q.matmul(bbc[0][:, :], ones_r[:, :], lnv[0:1, :],
                                          start=True, stop=True)),
                 reads=[lnvb, ones_rb], writes=[bbc[1]])
            for i in range(8):
                h, bb = xt[i]
                sh, sb2 = sq[i % 2]
                xh, xb = b.xn[i]
                p.op("v", (lambda q, s=h, d=sh: q.tensor_tensor(
                    d[:, :], s[:, :], ba[0][:, :], ALU.mult)),
                    reads=[bb, ba[1]], writes=[sb2])
                p.op("v", (lambda q, s=sh, d=xh, fc=fc: q.tensor_tensor(
                    d[:, fc * FCW:(fc + 1) * FCW], s[:, :], bbc[0][:, :],
                    ALU.add)), reads=[sb2, bbc[1]], writes=[xb])



def _softplus_cb(b, tiles, bias_name):
    p = b.p

    def cb(j, fc, ps_h, ps_b):
        (v, vb), (a, ab), (l, lb), (r, rb2) = tiles
        bias = b.col(bias_name, j)
        p.op("s", (lambda q, ps=ps_h, bc=bias[0]: q.activation(
            v[:, :], ps[:, :], AF.Identity, bias=bc)),
            reads=[ps_b, bias[1]], writes=[vb])
        p.op("s", (lambda q: q.activation(a[:, :], v[:, :], AF.Abs)),
             reads=[vb], writes=[ab])
        p.op("s", (lambda q: q.activation(a[:, :], a[:, :], AF.Exp, scale=-1.0)),
             reads=[ab], writes=[ab])
        p.op("s", (lambda q: q.activation(l[:, :], a[:, :], AF.Ln, bias=1.0)),
             reads=[ab], writes=[lb])
        p.op("v", (lambda q: q.tensor_scalar_max(r[:, :], v[:, :], 0.0)),
             reads=[vb], writes=[rb2])
        p.op("v", (lambda q: q.tensor_tensor(r[:, :], r[:, :], l[:, :],
                                             ALU.add)), reads=[lb, rb2],
             writes=[rb2])
        dst = b.dram["sp_dt"][j * 128:(j + 1) * 128, fc * FCW:(fc + 1) * FCW]
        p.op("y", (lambda q, s=r, d=dst: q.dma_start(out=d, in_=s[:, :])),
             reads=[rb2], writes=[p.db(("sp_dt", j, fc))])

    return cb



def emit_mamba0(b, src_key):
    p = b.p
    with p.phase():
        ev = [p.sb([128, FCW], BF16) for _ in range(2)]

        def cb_win(j, fc, ps_h, ps_b):
            h, bb = ev[(j * NFC + fc) % 2]
            bias = b.col("L0_b_in", j)
            p.op("s", (lambda q, d=h, ps=ps_h, bc=bias[0]: q.activation(
                d[:, :], ps[:, :], AF.Silu, bias=bc)),
                reads=[ps_b, bias[1]], writes=[bb])
            tgt = "sp_u" if j < 16 else "sp_z"
            jj = j if j < 16 else j - 16
            dst = b.dram[tgt][jj * 128:(jj + 1) * 128, fc * FCW:(fc + 1) * FCW]
            p.op("y", (lambda q, s=h, d=dst: q.dma_start(out=d, in_=s[:, :])),
                 reads=[bb], writes=[p.db((tgt, jj, fc))])

        b.linear("w0_in", b.xn_in(), cb_win, 1024, 4096)

    with p.phase():
        u_t = [p.sb([128, T], BF16) for _ in range(16)]
        for c in range(16):
            h, bb = u_t[c]
            src = b.dram["sp_u"][c * 128:(c + 1) * 128, :]
            p.op("y", (lambda q, d=h, s=src: q.dma_start(out=d[:, :], in_=s)),
                 reads=[p.db(("sp_u", c, fc2)) for fc2 in range(NFC)],
                 writes=[bb])
        u_in = [((lambda fc, h=h: h[:, fc * FCW:(fc + 1) * FCW]), bb)
                for (h, bb) in u_t]

        wxp = p.sb([128, 16 * 32], BF16)
        w_d = b.dram["w0_xp"]
        for gi, cg0 in enumerate((0, 8)):
            src = AP(tensor=w_d.tensor, offset=cg0 * 128 * 32,
                     ap=[[32, 128], [128 * 32, 8], [1, 32]])
            stage = None
            sview = 8
            b.wload(stage, wxp[0][:, cg0 * 32:(cg0 + 8) * 32], wxp[1], src,
                    8 * 32, view=sview)
        xd_ev = p.sb([32, FCW], BF16)
        bxp = p.banks[4]
        for fc in range(NFC):
            def emit(q, fc=fc):
                ins = None
                for c in range(16):
                    ins = q.matmul(bxp[0][0:32, :], wxp[0][:, c * 32:(c + 1) * 32],
                                   u_t[c][0][:, fc * FCW:(fc + 1) * FCW],
                                   start=(c == 0), stop=(c == 15))
                return ins
            p.op("t", emit, reads=[wxp[1]] + [bb for (_, bb) in u_t],
                 writes=[bxp[1]])
            h, bb = xd_ev
            bias = b.col("L0_b_xproj", rows=32)
            p.op("s", (lambda q, d=h, bc=bias[0]: q.activation(
                d[:, :], bxp[0][0:32, :], AF.Identity, bias=bc)),
                reads=[bxp[1], bias[1]], writes=[bb])
            dst = b.dram["sp_bc"][:, fc * FCW:(fc + 1) * FCW]
            p.op("y", (lambda q, s=h, d=dst: q.dma_start(out=d, in_=s[:, :])),
                 reads=[bb], writes=[p.db(("sp_bc", fc))])

        sp_tiles = [p.sb([128, FCW], F32) for _ in range(4)]
        b.linear("w0_dt", u_in, _softplus_cb(b, sp_tiles, "L0_b_dt"),
                 2048, 2048)

    emit_scan(b, mode=0, aname="a0", src_key=src_key)


def emit_mamba1(b, src_key):
    p = b.p
    with p.phase():
        ev = [p.sb([128, FCW], BF16) for _ in range(2)]

        def cb_win(j, fc, ps_h, ps_b):
            h, bb = ev[(j * NFC + fc) % 2]
            if j < 16:
                p.op("s", (lambda q, d=h, ps=ps_h: q.activation(
                    d[:, :], ps[:, :], AF.Copy)), reads=[ps_b], writes=[bb])
                tgt, jj = "sp_u", j
            else:
                p.op("s", (lambda q, d=h, ps=ps_h: q.activation(
                    d[:, :], ps[:, :], AF.Silu)), reads=[ps_b], writes=[bb])
                tgt, jj = "sp_z", j - 16
            dst = b.dram[tgt][jj * 128:(jj + 1) * 128, fc * FCW:(fc + 1) * FCW]
            p.op("y", (lambda q, s=h, d=dst: q.dma_start(out=d, in_=s[:, :])),
                 reads=[bb], writes=[p.db((tgt, jj, fc))])

        b.linear("w1_in", b.xn_in(), cb_win, 1024, 4096)

    with p.phase():
        wbc = p.sb([128, 8 * 32], BF16)
        w_d = b.dram["w1_bc"]
        src = AP(tensor=w_d.tensor, offset=0, ap=[[32, 128], [128 * 32, 8], [1, 32]])
        b.wload(None, wbc[0][:, :], wbc[1], src, 8 * 32,
                view=8)
        xd_ev = p.sb([32, FCW], BF16)
        bxp = p.banks[4]
        for fc in range(NFC):
            def emit(q, fc=fc):
                ins = None
                for c in range(8):
                    ins = q.matmul(bxp[0][0:32, :], wbc[0][:, c * 32:(c + 1) * 32],
                                   b.xn[c][0][:, fc * FCW:(fc + 1) * FCW],
                                   start=(c == 0), stop=(c == 7))
                return ins
            p.op("t", emit, reads=[wbc[1]] + [bb for (_, bb) in b.xn],
                 writes=[bxp[1]])
            h, bb = xd_ev
            p.op("s", (lambda q, d=h: q.activation(d[:, :], bxp[0][0:32, :],
                                                   AF.Copy)),
                 reads=[bxp[1]], writes=[bb])
            dst = b.dram["sp_bc"][:, fc * FCW:(fc + 1) * FCW]
            p.op("y", (lambda q, s=h, d=dst: q.dma_start(out=d, in_=s[:, :])),
                 reads=[bb], writes=[p.db(("sp_bc", fc))])

        sp_tiles = [p.sb([128, FCW], F32) for _ in range(4)]
        b.linear("w1_dt", b.xn_in(), _softplus_cb(b, sp_tiles, "L1_dtsum"),
                 1024, 2048)

    emit_scan(b, mode=1, aname="a1", src_key=src_key)


def emit_scan(b, mode, aname, src_key):
    p = b.p
    p.paranoid = P.PARANOID_NOSELF
    with p.phase():
        a_sb = p.sb([128, 256], F32)
        p.op("y", (lambda q: q.dma_start(out=a_sb[0][:, :], in_=b.dram[aname])),
             writes=[a_sb[1]])
        p.op("s", (lambda q: q.activation(a_sb[0][:, :], a_sb[0][:, :], AF.Exp)),
             reads=[a_sb[1]], writes=[a_sb[1]])
        p.op("s", (lambda q, m=(-1.0 if mode == 0 else 0.5):
                   q.activation(a_sb[0][:, :], a_sb[0][:, :], AF.Identity,
                                scale=m)),
             reads=[a_sb[1]], writes=[a_sb[1]])

        wname = "w0_out" if mode == 0 else "w1_out"
        w_d = b.dram[wname]
        wout = [p.sb([128, 16 * 128], BF16) for _ in range(8)]
        for j in range(8):
            for cg0 in (0, 4, 8, 12):
                wsrc = AP(tensor=w_d.tensor,
                          offset=cg0 * 128 * 1024 + j * 128,
                          ap=[[1024, 128], [128 * 1024, 4], [1, 128]])
                b.wload(None, wout[j][0][:, cg0 * 128:(cg0 + 4) * 128],
                        wout[j][1], wsrc, 4 * 128, view=4)

        hstate = [p.sb([128, 16], F32) for _ in range(16)]
        for i in range(16):
            p.op("v", (lambda q, h=hstate[i][0]: q.memset(h[:, :], 0.0)),
                 writes=[hstate[i][1]])

        u_c = [p.sb([128, FCW], BF16) for _ in range(2)]
        dt_c = [p.sb([128, FCW], F32) for _ in range(2)]
        z_c = [p.sb([128, FCW], BF16) for _ in range(2)]
        bbc = [p.sb([128, FCW], BF16) for _ in range(16)]
        cbc = [p.sb([128, FCW], BF16) for _ in range(16)]
        dA = [[p.sb([128, FCW], F32) for _ in range(2)] for _ in range(2)]
        bt = [[p.sb([128, FCW], F32) for _ in range(2)] for _ in range(2)]
        ht = [[p.sb([128, FCW], F32) for _ in range(2)] for _ in range(2)]
        if mode == 0:
            dA2 = [[p.sb([128, FCW], F32) for _ in range(2)]
                   for _ in range(2)]
            rr = None
        else:
            dA2 = None
            rr = [[p.sb([128, FCW], F32) for _ in range(2)] for _ in range(2)]
        tmp = [p.sb([128, FCW], F32) for _ in range(2)]
        du = [p.sb([128, FCW], F32) for _ in range(2)]
        yac = [p.sb([128, FCW], F32) for _ in range(2)]
        yg = [p.sb([128, FCW], BF16) for _ in range(16)]
        yg2 = [p.sb([128, FCW], BF16) for _ in range(16)] if mode == 0 \
            else None

        gn = None
        if mode == 1:
            gn = dict(
                sq=[p.sb([128, FCW], BF16) for _ in range(2 if GN_HARD else 1)],
                sq2=([p.sb([128, FCW], BF16) for _ in range(2)]
                     if GN_HARD else None),
                rws=[p.sb([1, FCW], F32) for _ in range(7 if GN_HARD else 6)],
                nt=[p.sb([128, FCW], BF16) for _ in range(16)],
                t1=p.sb([128, FCW], F32),
            )

        for tc in range(NFC):
            fsl = slice(tc * FCW, (tc + 1) * FCW)
            for s_i in range(16):
                for (tiles, row) in ((bbc, s_i), (cbc, 16 + s_i)):
                    h, bb = tiles[s_i]
                    src = b.dram["sp_bc"][row:row + 1, fsl].broadcast_to(
                        (128, FCW))
                    p.op("y", (lambda q, d=h, s=src: q.dma_start(
                        out=d[:, :], in_=s)),
                        reads=[p.db(("sp_bc", tc))], writes=[bb])
            for ip in range(8):
                ii = (2 * ip, 2 * ip + 1)
                for k in range(2):
                    i = ii[k]
                    for (tt, dname) in ((u_c[k], "sp_u"), (dt_c[k], "sp_dt"),
                                        (z_c[k], "sp_z")):
                        h2, bb2 = tt
                        src = b.dram[dname][i * 128:(i + 1) * 128, fsl]
                        p.op("y", (lambda q, d=h2, s=src: q.dma_start(
                            out=d[:, :], in_=s)),
                            reads=[p.db((dname, i, tc))], writes=[bb2])
                for k in range(2):
                    p.op("v", (lambda q, k=k: q.tensor_tensor(
                        du[k][0][:, :], dt_c[k][0][:, :], u_c[k][0][:, :],
                        ALU.mult)),
                        reads=[u_c[k][1], dt_c[k][1]], writes=[du[k][1]])
                for k in range(2):
                    p.op("v", (lambda q, k=k: q.memset(yac[k][0][:, :], 0.0)),
                         writes=[yac[k][1]])
                if mode == 0:
                    for k in range(2):
                        a2 = a_sb[0][:, ii[k] * 16:ii[k] * 16 + 1]
                        p.op("s", (lambda q, k=k, a2=a2: q.activation(
                            dA[k][0][0][:, :], dt_c[k][0][:, :], AF.Exp,
                            scale=a2)),
                            reads=[dt_c[k][1], a_sb[1]], writes=[dA[k][0][1]])
                    for k in range(2):
                        p.op("y", (lambda q, k=k: q.dma_start(
                            out=dA2[k][0][0][:, :], in_=dA[k][0][0][:, :])),
                            reads=[dA[k][0][1]], writes=[dA2[k][0][1]])
                for s_i in range(16):
                    sp = s_i % 2
                    ac = [a_sb[0][:, ii[k] * 16 + s_i:ii[k] * 16 + s_i + 1]
                          for k in range(2)]
                    if mode == 0:
                        if s_i + 1 < 16:
                            spn = (s_i + 1) % 2
                            for k in range(2):
                                a2 = a_sb[0][:, ii[k] * 16 + s_i + 1:
                                             ii[k] * 16 + s_i + 2]
                                p.op("s", (lambda q, k=k, spn=spn, a2=a2:
                                           q.activation(dA[k][spn][0][:, :],
                                                        dt_c[k][0][:, :],
                                                        AF.Exp, scale=a2)),
                                     reads=[dt_c[k][1], a_sb[1]],
                                     writes=[dA[k][spn][1]])
                            for k in range(2):
                                p.op("y", (lambda q, k=k, spn=spn: q.dma_start(
                                    out=dA2[k][spn][0][:, :],
                                    in_=dA[k][spn][0][:, :])),
                                    reads=[dA[k][spn][1]],
                                    writes=[dA2[k][spn][1]])
                        for k in range(2):
                            p.op("v", (lambda q, k=k, sp=sp, s1=bbc[s_i][0]:
                                       q.tensor_tensor(bt[k][sp][0][:, :],
                                                       du[k][0][:, :],
                                                       s1[:, :], ALU.mult)),
                                 reads=[du[k][1], bbc[s_i][1]],
                                 writes=[bt[k][sp][1]])
                    else:
                        for k in range(2):
                            p.op("s", (lambda q, k=k, sp=sp, a2=ac[k]:
                                       q.activation(rr[k][sp][0][:, :],
                                                    dt_c[k][0][:, :], AF.Ln,
                                                    scale=a2, bias=1.0)),
                                 reads=[dt_c[k][1], a_sb[1]],
                                 writes=[rr[k][sp][1]])
                        for k in range(2):
                            p.op("s", (lambda q, k=k, sp=sp: q.activation(
                                rr[k][sp][0][:, :], rr[k][sp][0][:, :],
                                AF.Exp, scale=-1.0)),
                                reads=[rr[k][sp][1]], writes=[rr[k][sp][1]])
                        for k in range(2):
                            p.op("v", (lambda q, k=k, sp=sp: q.tensor_scalar(
                                dA[k][sp][0][:, :], rr[k][sp][0][:, :], 2.0,
                                -1.0, ALU.mult, ALU.add)),
                                reads=[rr[k][sp][1]], writes=[dA[k][sp][1]])
                        for k in range(2):
                            p.op("v", (lambda q, k=k, sp=sp, s1=bbc[s_i][0]:
                                       q.tensor_tensor(bt[k][sp][0][:, :],
                                                       du[k][0][:, :],
                                                       s1[:, :], ALU.mult)),
                                 reads=[du[k][1], bbc[s_i][1]],
                                 writes=[bt[k][sp][1]])
                        for k in range(2):
                            p.op("v", (lambda q, k=k, sp=sp: q.tensor_tensor(
                                bt[k][sp][0][:, :], bt[k][sp][0][:, :],
                                rr[k][sp][0][:, :], ALU.mult)),
                                reads=[rr[k][sp][1], bt[k][sp][1]],
                                writes=[bt[k][sp][1]])
                    for k in range(2):
                        hs_h, hs_b = hstate[ii[k]]
                        da_use = dA2 if mode == 0 else dA
                        p.op("v", (lambda q, k=k, sp=sp, hs=hs_h, s_i=s_i,
                                   du2=da_use: q.tensor_tensor_scan(
                                       ht[k][sp][0][:, :], du2[k][sp][0][:, :],
                                       bt[k][sp][0][:, :],
                                       hs[:, s_i:s_i + 1], ALU.mult, ALU.add)),
                             reads=[da_use[k][sp][1], bt[k][sp][1], hs_b],
                             writes=[ht[k][sp][1]])
                    for k in range(2):
                        hs_h, hs_b = hstate[ii[k]]
                        p.op("s", (lambda q, k=k, sp=sp, hs=hs_h, s_i=s_i:
                                   q.copy(hs[:, s_i:s_i + 1],
                                          ht[k][sp][0][:, FCW - 1:FCW])),
                             reads=[ht[k][sp][1]], writes=[hs_b])
                    for k in range(2):
                        p.op("v", (lambda q, k=k, sp=sp, c1=cbc[s_i][0]:
                                   q.tensor_tensor(tmp[k][0][:, :],
                                                   ht[k][sp][0][:, :],
                                                   c1[:, :], ALU.mult)),
                             reads=[ht[k][sp][1], cbc[s_i][1]],
                             writes=[tmp[k][1]])
                    for k in range(2):
                        p.op("v", (lambda q, k=k: q.tensor_tensor(
                            yac[k][0][:, :], yac[k][0][:, :], tmp[k][0][:, :],
                            ALU.add)),
                            reads=[tmp[k][1], yac[k][1]], writes=[yac[k][1]])
                if mode == 0:
                    for k in range(2):
                        dcol = b.col("L0_D", ii[k])
                        p.op("v", (lambda q, k=k, dc=dcol[0]:
                                   q.scalar_tensor_tensor(
                                       yac[k][0][:, :], u_c[k][0][:, :], dc,
                                       yac[k][0][:, :], ALU.mult, ALU.add)),
                             reads=[u_c[k][1], dcol[1], yac[k][1]],
                             writes=[yac[k][1]])
                for k in range(2):
                    gh, gb = yg[ii[k]]
                    p.op("v", (lambda q, k=k, gh=gh: q.tensor_tensor(
                        gh[:, :], yac[k][0][:, :], z_c[k][0][:, :], ALU.mult)),
                        reads=[yac[k][1], z_c[k][1]], writes=[gb])
            if mode == 1:
                (_gn_chunk if GN_HARD else _gn_chunk_v0)(b, gn, yg)
            if mode == 0:
                src_tiles = b.launder(yg, yg2)
            else:
                src_tiles = b.launder(gn["nt"], yg)
            for j in range(8):
                wt, wbuf = wout[j]
                bh, bb = p.banks[6 + j % 2]

                def emit(q, j=j, bh=bh, wt=wt, st=src_tiles):
                    ins = None
                    for c in range(16):
                        ins = q.matmul(bh[:, :],
                                       wt[:, c * 128:(c + 1) * 128],
                                       st[c][0][:, :], start=(c == 0),
                                       stop=(c == 15))
                    return ins
                p.op("t", emit, reads=[wbuf] + [t[1] for t in src_tiles],
                     writes=[bb])
                if mode == 0:
                    bias = b.col("L0_b_out", j)
                    b.merge(j, tc, bh, bb, scalar=bias[0], scalar_buf=bias[1],
                            op0=ALU.add, src_key=src_key)
                else:
                    b.merge(j, tc, bh, bb, src_key=src_key)
        p.paranoid = False


def _gn_chunk(b, gn, yg):
    p = b.p
    old_par = p.paranoid
    p.paranoid = True
    ones_cb_h, ones_cb_b = b.ones_col_bf
    ones_r, ones_rb = b.ones_row
    (rmu, rmub), (rex, rexb), (rm2, rm2b), (rvar, rvarb), (rstd, rstdb), \
        (rbnr, rbnrb), (pad, padb) = gn["rws"]
    rvL, rvLb = rm2, rm2b
    lnv, lnvb = rex, rexb
    rsL, rsLb = rvar, rvarb
    bsx = p.banks[4]; bsq = p.banks[5]; ba = p.banks[6]; bbn = p.banks[7]
    for i in range(16):
        gh, gb = yg[i]
        sh, sb2 = gn["sq"][i % 2]
        s2h, s2b = gn["sq2"][i % 2]
        p.op("s", (lambda q, d=sh, s=gh: q.activation(d[:, :], s[:, :],
                                                      AF.Square)),
             reads=[gb], writes=[sb2])
        p.op("y", (lambda q, s=sh, d=s2h: q.dma_start(out=d[:, :], in_=s[:, :])),
             reads=[sb2], writes=[s2b])
        p.op("t", (lambda q, s=gh, i=i: q.matmul(
            bsx[0][0:1, :], ones_cb_h[:, :], s[:, :], start=(i == 0),
            stop=(i == 15))), reads=[gb, ones_cb_b], writes=[bsx[1]])
        p.op("t", (lambda q, s=s2h, i=i: q.matmul(
            bsq[0][0:1, :], ones_cb_h[:, :], s[:, :], start=(i == 0),
            stop=(i == 15))), reads=[s2b, ones_cb_b], writes=[bsq[1]])
    p.op("v", (lambda q: q.tensor_scalar(rmu[0:1, :], bsx[0][0:1, :], 1.0 / 2048,
                                         None, ALU.mult)),
         reads=[bsx[1]], writes=[rmub])
    p.op("v", (lambda q: q.tensor_scalar(rex[0:1, :], bsq[0][0:1, :], 1.0 / 2048,
                                         None, ALU.mult)),
         reads=[bsq[1]], writes=[rexb])
    p.op("v", (lambda q: q.tensor_tensor(rm2[0:1, :], rmu[0:1, :], rmu[0:1, :],
                                         ALU.mult)), reads=[rmub], writes=[rm2b])
    p.op("v", (lambda q: q.memset(pad[0:1, :], 0.0)), writes=[padb])
    p.op("v", (lambda q: q.tensor_tensor(rvar[0:1, :], rex[0:1, :], rm2[0:1, :],
                                         ALU.subtract)), reads=[rexb, rm2b], writes=[rvarb])
    p.op("y", (lambda q: q.dma_start(out=rvL[0:1, :], in_=rvar[0:1, :])),
         reads=[rvarb], writes=[rvLb])
    p.op("s", (lambda q: q.activation(lnv[0:1, :], rvL[0:1, :], AF.Ln, bias=1e-5)),
         reads=[rvLb], writes=[lnvb])
    p.op("s", (lambda q: q.activation(pad[0:1, :], pad[0:1, :], AF.Identity)),
         reads=[padb], writes=[padb])
    p.op("s", (lambda q: q.activation(rstd[0:1, :], lnv[0:1, :], AF.Exp, scale=-0.5)),
         reads=[lnvb], writes=[rstdb])
    p.op("y", (lambda q: q.dma_start(out=rsL[0:1, :], in_=rstd[0:1, :])),
         reads=[rstdb], writes=[rsLb])
    p.op("v", (lambda q: q.scalar_tensor_tensor(rbnr[0:1, :], rmu[0:1, :], -1.0,
                                                rsL[0:1, :], ALU.mult, ALU.mult)),
         reads=[rmub, rsLb], writes=[rbnrb])
    p.op("y", (lambda q: q.dma_start(out=lnv[0:1, :], in_=rbnr[0:1, :])),
         reads=[rbnrb], writes=[lnvb])
    p.op("t", (lambda q: q.matmul(ba[0][:, :], ones_r[:, :], rsL[0:1, :],
                                  start=True, stop=True)),
         reads=[rsLb, ones_rb], writes=[ba[1]])
    p.op("t", (lambda q: q.matmul(bbn[0][:, :], ones_r[:, :], lnv[0:1, :],
                                  start=True, stop=True)),
         reads=[lnvb, ones_rb], writes=[bbn[1]])
    th, tb = gn["t1"]
    for i in range(16):
        gh, gb = yg[i]
        nh, nb = gn["nt"][i]
        p.op("v", (lambda q, s=gh: q.tensor_tensor(
            th[:, :], s[:, :], ba[0][:, :], ALU.mult)),
            reads=[gb, ba[1]], writes=[tb])
        p.op("v", (lambda q: q.tensor_tensor(th[:, :], th[:, :], bbn[0][:, :],
                                             ALU.add)),
             reads=[bbn[1], tb], writes=[tb])
        gc = b.col("L1_gamma", i)
        bc = b.col("L1_beta", i)
        p.op("s", (lambda q, d=nh, g2=gc[0], b2=bc[0]: q.activation(
            d[:, :], th[:, :], AF.Identity, scale=g2, bias=b2)),
            reads=[tb, gc[1], bc[1]], writes=[nb])
    p.paranoid = old_par




def emit_layernorm_v0(b, src_key):
    p = b.p
    x_d = b.dram[src_key]
    ones_c, ones_cb = b.ones_col
    ones_r, ones_rb = b.ones_row
    with p.phase():
        xt = [p.sb([128, FCW], F32) for _ in range(8)]
        sq = [p.sb([128, FCW], F32) for _ in range(2)]
        rws = [p.sb([1, FCW], F32) for _ in range(6)]
        (rmu, rmub), (rex, rexb), (rm2, rm2b), (rvar, rvarb), (rstd, rstdb), (rbn, rbnb) = rws
        bsx = p.banks[4]
        bsq = p.banks[5]
        ba = p.banks[6]
        bbc = p.banks[7]
        for fc in range(NFC):
            for i in range(8):
                h, bb = xt[i]
                src = x_d[i * 128:(i + 1) * 128, fc * FCW:(fc + 1) * FCW]
                p.op("y", (lambda q, d=h, s=src: q.dma_start(out=d[:, :], in_=s)),
                     reads=[p.db((src_key, i, fc))], writes=[bb])
            for i in range(8):
                h, bb = xt[i]
                sh, sb2 = sq[i % 2]
                p.op("s", (lambda q, d=sh, s=h: q.activation(
                    d[:, :], s[:, :], AF.Square)), reads=[bb], writes=[sb2])
                p.op("t", (lambda q, s=h, i=i: q.matmul(
                    bsx[0][0:1, :], ones_c[:, :], s[:, :], start=(i == 0),
                    stop=(i == 7))), reads=[bb, ones_cb], writes=[bsx[1]])
                p.op("t", (lambda q, s=sh, i=i: q.matmul(
                    bsq[0][0:1, :], ones_c[:, :], s[:, :], start=(i == 0),
                    stop=(i == 7))), reads=[sb2, ones_cb], writes=[bsq[1]])
            p.op("v", (lambda q: q.tensor_scalar(rmu[0:1, :], bsx[0][0:1, :],
                                                 1.0 / D, None, ALU.mult)),
                 reads=[bsx[1]], writes=[rmub])
            p.op("v", (lambda q: q.tensor_scalar(rex[0:1, :], bsq[0][0:1, :],
                                                 1.0 / D, None, ALU.mult)),
                 reads=[bsq[1]], writes=[rexb])
            p.op("s", (lambda q: q.activation(rm2[0:1, :], rmu[0:1, :], AF.Square)),
                 reads=[rmub], writes=[rm2b])
            p.op("v", (lambda q: q.tensor_tensor(rvar[0:1, :], rex[0:1, :],
                                                 rm2[0:1, :], ALU.subtract)),
                 reads=[rexb, rm2b], writes=[rvarb])
            p.op("s", (lambda q: q.activation(rvar[0:1, :], rvar[0:1, :], AF.Ln,
                                              bias=1e-5)), reads=[rvarb], writes=[rvarb])
            p.op("s", (lambda q: q.activation(rstd[0:1, :], rvar[0:1, :], AF.Exp,
                                              scale=-0.5)), reads=[rvarb], writes=[rstdb])
            p.op("v", (lambda q: q.scalar_tensor_tensor(
                rbn[0:1, :], rmu[0:1, :], -1.0, rstd[0:1, :], ALU.mult, ALU.mult)),
                reads=[rmub, rstdb], writes=[rbnb])
            p.op("t", (lambda q: q.matmul(ba[0][:, :], ones_r[:, :], rstd[0:1, :],
                                          start=True, stop=True)),
                 reads=[rstdb, ones_rb], writes=[ba[1]])
            p.op("t", (lambda q: q.matmul(bbc[0][:, :], ones_r[:, :], rbn[0:1, :],
                                          start=True, stop=True)),
                 reads=[rbnb, ones_rb], writes=[bbc[1]])
            for i in range(8):
                h, bb = xt[i]
                sh, sb2 = sq[i % 2]
                xh, xb = b.xn[i]
                p.op("v", (lambda q, s=h, d=sh: q.tensor_tensor(
                    d[:, :], s[:, :], ba[0][:, :], ALU.mult)),
                    reads=[bb, ba[1]], writes=[sb2])
                p.op("v", (lambda q, s=sh, d=xh, fc=fc: q.tensor_tensor(
                    d[:, fc * FCW:(fc + 1) * FCW], s[:, :], bbc[0][:, :],
                    ALU.add)), reads=[sb2, bbc[1]], writes=[xb])




def _gn_chunk_v0(b, gn, yg):
    p = b.p
    old_par = p.paranoid
    p.paranoid = P.PARANOID_NOSELF
    ones_cb_h, ones_cb_b = b.ones_col_bf
    ones_r, ones_rb = b.ones_row
    (rmu, rmub), (rex, rexb), (rm2, rm2b), (rvar, rvarb), (rstd, rstdb), (rbnr, rbnrb) = gn["rws"]
    bsx = p.banks[4]; bsq = p.banks[5]; ba = p.banks[6]; bbn = p.banks[7]
    sh, sb2 = gn["sq"][0]
    for i in range(16):
        gh, gb = yg[i]
        p.op("s", (lambda q, s=gh: q.activation(sh[:, :], s[:, :], AF.Square)),
             reads=[gb], writes=[sb2])
        p.op("t", (lambda q, s=gh, i=i: q.matmul(
            bsx[0][0:1, :], ones_cb_h[:, :], s[:, :], start=(i == 0),
            stop=(i == 15))), reads=[gb, ones_cb_b], writes=[bsx[1]])
        p.op("t", (lambda q, i=i: q.matmul(
            bsq[0][0:1, :], ones_cb_h[:, :], sh[:, :], start=(i == 0),
            stop=(i == 15))), reads=[sb2, ones_cb_b], writes=[bsq[1]])
    p.op("v", (lambda q: q.tensor_scalar(rmu[0:1, :], bsx[0][0:1, :], 1.0 / 2048,
                                         None, ALU.mult)),
         reads=[bsx[1]], writes=[rmub])
    p.op("v", (lambda q: q.tensor_scalar(rex[0:1, :], bsq[0][0:1, :], 1.0 / 2048,
                                         None, ALU.mult)),
         reads=[bsq[1]], writes=[rexb])
    p.op("s", (lambda q: q.activation(rm2[0:1, :], rmu[0:1, :], AF.Square)),
         reads=[rmub], writes=[rm2b])
    p.op("v", (lambda q: q.tensor_tensor(rvar[0:1, :], rex[0:1, :], rm2[0:1, :],
                                         ALU.subtract)), reads=[rexb, rm2b], writes=[rvarb])
    p.op("s", (lambda q: q.activation(rvar[0:1, :], rvar[0:1, :], AF.Ln, bias=1e-5)),
         reads=[rvarb], writes=[rvarb])
    p.op("s", (lambda q: q.activation(rstd[0:1, :], rvar[0:1, :], AF.Exp, scale=-0.5)),
         reads=[rvarb], writes=[rstdb])
    p.op("v", (lambda q: q.scalar_tensor_tensor(rbnr[0:1, :], rmu[0:1, :], -1.0,
                                                rstd[0:1, :], ALU.mult, ALU.mult)),
         reads=[rmub, rstdb], writes=[rbnrb])
    p.op("t", (lambda q: q.matmul(ba[0][:, :], ones_r[:, :], rstd[0:1, :],
                                  start=True, stop=True)),
         reads=[rstdb, ones_rb], writes=[ba[1]])
    p.op("t", (lambda q: q.matmul(bbn[0][:, :], ones_r[:, :], rbnr[0:1, :],
                                  start=True, stop=True)),
         reads=[rbnrb, ones_rb], writes=[bbn[1]])
    th, tb = gn["t1"]
    for i in range(16):
        gh, gb = yg[i]
        nh, nb = gn["nt"][i]
        p.op("v", (lambda q, s=gh: q.tensor_tensor(
            th[:, :], s[:, :], ba[0][:, :], ALU.mult)),
            reads=[gb, ba[1]], writes=[tb])
        p.op("v", (lambda q: q.tensor_tensor(th[:, :], th[:, :], bbn[0][:, :],
                                             ALU.add)),
             reads=[bbn[1], tb], writes=[tb])
        gc = b.col("L1_gamma", i)
        bc = b.col("L1_beta", i)
        p.op("s", (lambda q, d=nh, g2=gc[0], b2=bc[0]: q.activation(
            d[:, :], th[:, :], AF.Identity, scale=g2, bias=b2)),
            reads=[tb, gc[1], bc[1]], writes=[nb])
    p.paranoid = old_par




def emit_ffn_ternary(b, pref, src_key, dst_key):
    p = b.p
    lp = "f0" if pref == "L0" else "f2"
    with p.phase():
        ha = [p.sb([128, T], BF16) for _ in range(32)]
        t1e = [p.sb([128, FCW], BF16) for _ in range(2)]
        w1d, w2d = b.dram[lp + "_w1"], b.dram[lp + "_w2"]
        wt1 = [p.sb([128, 8 * 128], BF16) for _ in range(2)]
        wt2 = [p.sb([128, 8 * 128], BF16) for _ in range(2)]
        for j in range(32):
            b1h, b1b = wt1[j % 2]
            b2h, b2b = wt2[j % 2]
            for wi2, (wd, wh, wb2) in enumerate(((w1d, b1h, b1b),
                                                 (w2d, b2h, b2b))):
                for hf in range(2):
                    src = AP(tensor=wd.tensor,
                             offset=hf * 4 * 128 * 4096 + j * 128,
                             ap=[[4096, 128], [128 * 4096, 4], [1, 128]])
                    stage = None
                    sview = 4
                    b.wload(stage, wh[:, hf * 512:(hf + 1) * 512], wb2, src,
                            4 * 128, view=sview)
            for fc in range(NFC):
                p1h, p1b = p.banks[fc % 2]
                p2h, p2b = p.banks[2 + fc % 2]
                fsl = slice(fc * FCW, (fc + 1) * FCW)

                def emit1(q, wh=b1h, ph=p1h, fsl=fsl):
                    ins = None
                    for c in range(8):
                        ins = q.matmul(ph[:, :], wh[:, c * 128:(c + 1) * 128],
                                       b.xn[c][0][:, fsl], start=(c == 0),
                                       stop=(c == 7))
                    return ins

                def emit2(q, wh=b2h, ph=p2h, fsl=fsl):
                    ins = None
                    for c in range(8):
                        ins = q.matmul(ph[:, :], wh[:, c * 128:(c + 1) * 128],
                                       b.xn[c][0][:, fsl], start=(c == 0),
                                       stop=(c == 7))
                    return ins

                p.op("t", emit1, reads=[b1b] + [bb for (_, bb) in b.xn],
                     writes=[p1b])
                p.op("t", emit2, reads=[b2b] + [bb for (_, bb) in b.xn],
                     writes=[p2b])
                th, tb = t1e[fc % 2]
                s1 = b.col(pref + "_w1s", j)
                s2 = b.col(pref + "_w2s", j)
                p.op("s", (lambda q, d=th, ph=p1h, sc=s1[0]: q.activation(
                    d[:, :], ph[:, :], AF.Silu, scale=sc)),
                    reads=[p1b, s1[1]], writes=[tb])
                hh, hb = ha[j]
                p.op("v", (lambda q, d=hh, ph=p2h, s=th, sc=s2[0], fsl=fsl:
                           q.scalar_tensor_tensor(d[:, fsl], ph[:, :], sc,
                                                  s[:, :], ALU.mult, ALU.mult)),
                     reads=[p2b, s2[1], tb], writes=[hb])

        ha_in = [((lambda fc, h=h: h[:, fc * FCW:(fc + 1) * FCW]), bb)
                 for (h, bb) in ha]
        hb_ev = [p.sb([128, FCW], BF16) for _ in range(2)]

        def cb_h1(j, fc, ps_h, ps_b):
            h, bb = hb_ev[(j * NFC + fc) % 2]
            sc = b.col(pref + "_h1s", j)
            p.op("s", (lambda q, d=h, ps=ps_h, sc2=sc[0]: q.activation(
                d[:, :], ps[:, :], AF.Gelu, scale=sc2)),
                reads=[ps_b, sc[1]], writes=[bb])
            dst = b.dram["sp_hb"][j * 128:(j + 1) * 128,
                                  fc * FCW:(fc + 1) * FCW]
            p.op("y", (lambda q, s=h, d=dst: q.dma_start(out=d, in_=s[:, :])),
                 reads=[bb], writes=[p.db(("sp_hb", j, fc))])

        b.linear(lp + "_h1", ha_in, cb_h1, 4096, 4096, banks=(4, 5))

    hc_ev = [p.sb([128, FCW], BF16) for _ in range(2)]

    def cb_h2(j, fc, ps_h, ps_b):
        h, bb = hc_ev[(j * NFC + fc) % 2]
        sc = b.col(pref + "_h2s", j)
        b.p.op("s", (lambda q, d=h, ps=ps_h, sc2=sc[0]: q.activation(
            d[:, :], ps[:, :], AF.Gelu, scale=sc2)),
            reads=[ps_b, sc[1]], writes=[bb])
        dst = b.dram["sp_hc"][j * 128:(j + 1) * 128, fc * FCW:(fc + 1) * FCW]
        b.p.op("y", (lambda q, s=h, d=dst: q.dma_start(out=d, in_=s[:, :])),
               reads=[bb], writes=[b.p.db(("sp_hc", j, fc))])

    b.linear_stream(lp + "_h2", "sp_hb", cb_h2, 4096, 4096)

    def cb_w3(j, fc, ps_h, ps_b):
        sc = b.col(pref + "_w3s", j)
        b.merge(j, fc, ps_h, ps_b, scalar=sc[0], scalar_buf=sc[1],
                op0=ALU.mult, src_key=src_key, dst_key=dst_key)

    b.linear_stream(lp + "_w3", "sp_hc", cb_w3, 4096, 1024)


def emit_ffn_quant(b, src_key, dst_key):
    p = b.p
    with p.phase():
        hq = [p.sb([128, T], BF16) for _ in range(24)]
        t1e = [p.sb([128, FCW], BF16) for _ in range(2)]
        wgd, wud = b.dram["f1_wg"], b.dram["f1_wu"]
        wt1 = [p.sb([128, 8 * 128], BF16) for _ in range(2)]
        wt2 = [p.sb([128, 8 * 128], BF16) for _ in range(2)]
        for j in range(24):
            b1h, b1b = wt1[j % 2]
            b2h, b2b = wt2[j % 2]
            for wi2, (wd, wh, wb2) in enumerate(((wgd, b1h, b1b),
                                                 (wud, b2h, b2b))):
                for hf in range(2):
                    src = AP(tensor=wd.tensor,
                             offset=hf * 4 * 128 * 3072 + j * 128,
                             ap=[[3072, 128], [128 * 3072, 4], [1, 128]])
                    stage = None
                    sview = 4
                    b.wload(stage, wh[:, hf * 512:(hf + 1) * 512], wb2, src,
                            4 * 128, view=sview)
            for fc in range(NFC):
                p1h, p1b = p.banks[fc % 2]
                p2h, p2b = p.banks[2 + fc % 2]
                fsl = slice(fc * FCW, (fc + 1) * FCW)

                def emit1(q, wh=b1h, ph=p1h, fsl=fsl):
                    ins = None
                    for c in range(8):
                        ins = q.matmul(ph[:, :], wh[:, c * 128:(c + 1) * 128],
                                       b.xn[c][0][:, fsl], start=(c == 0),
                                       stop=(c == 7))
                    return ins

                def emit2(q, wh=b2h, ph=p2h, fsl=fsl):
                    ins = None
                    for c in range(8):
                        ins = q.matmul(ph[:, :], wh[:, c * 128:(c + 1) * 128],
                                       b.xn[c][0][:, fsl], start=(c == 0),
                                       stop=(c == 7))
                    return ins

                p.op("t", emit1, reads=[b1b] + [bb for (_, bb) in b.xn],
                     writes=[p1b])
                p.op("t", emit2, reads=[b2b] + [bb for (_, bb) in b.xn],
                     writes=[p2b])
                th, tb = t1e[fc % 2]
                p.op("s", (lambda q, d=th, ph=p1h: q.activation(
                    d[:, :], ph[:, :], AF.Silu, scale=1.0 / 127.0)),
                    reads=[p1b], writes=[tb])
                hh, hb = hq[j]
                p.op("v", (lambda q, d=hh, ph=p2h, s=th, fsl=fsl:
                           q.scalar_tensor_tensor(d[:, fsl], ph[:, :],
                                                  1.0 / 127.0, s[:, :],
                                                  ALU.mult, ALU.mult)),
                     reads=[p2b, tb], writes=[hb])

        hq_in = [((lambda fc, h=h: h[:, fc * FCW:(fc + 1) * FCW]), bb)
                 for (h, bb) in hq]

        def cb_wd(j, fc, ps_h, ps_b):
            b.merge(j, fc, ps_h, ps_b, scalar=1.0 / 127.0, op0=ALU.mult,
                    src_key=src_key, dst_key=dst_key)

        b.linear("f1_wd", hq_in, cb_wd, 3072, 1024, banks=(4, 5))



def _emit_attention(b, src_key, layer):
    p = b.p
    NH = 16
    with p.phase():
        qf = [p.sb([128, T], BF16) for _ in range(8)]
        kf = [p.sb([128, T], BF16) for _ in range(8)]
        vtok = [p.sb([128, NH * 65], BF16) for _ in range(16)]
        negc = p.sb([128, NH], F32)

        with p.phase():
            if layer == 2:
                ps_e = p.banks[4]
                re_sb = p.sb([32, 16], F32)
                oh_sb = p.sb([32, 4095], F32)
                eb_ev = p.sb([16, FCW], BF16)
                p.op("y", (lambda q: q.dma_start(out=re_sb[0][:, :],
                                                 in_=b.dram["rel_emb"])),
                     writes=[re_sb[1]])
                p.op("y", (lambda q: q.dma_start(out=oh_sb[0][:, :],
                                                 in_=b.dram["onehot"])),
                     writes=[oh_sb[1]])
                for fc in range(8):
                    w = min(FCW, 4095 - fc * FCW)
                    p.op("t", (lambda q, fc=fc, w=w: q.matmul(
                        ps_e[0][0:16, 0:w], re_sb[0][:, :],
                        oh_sb[0][:, fc * FCW:fc * FCW + w],
                        start=True, stop=True)),
                        reads=[re_sb[1], oh_sb[1]], writes=[ps_e[1]])
                    p.op("s", (lambda q, w=w: q.activation(
                        eb_ev[0][0:16, 0:w], ps_e[0][0:16, 0:w], AF.Exp)),
                        reads=[ps_e[1]], writes=[eb_ev[1]])
                    dst = b.dram["sp_expb"][:, fc * FCW:fc * FCW + w]
                    p.op("y", (lambda q, d=dst, w=w: q.dma_start(
                        out=d, in_=eb_ev[0][0:16, 0:w])),
                        reads=[eb_ev[1]], writes=[p.db(("sp_expb", fc))])

                def cb_q(j, fc, ps_h, ps_b):
                    h, bb = qf[j]
                    p.op("s", (lambda q, d=h, ps=ps_h, fc=fc: q.activation(
                        d[:, fc * FCW:(fc + 1) * FCW], ps[:, :], AF.Copy)),
                        reads=[ps_b], writes=[bb])

                def cb_k(j, fc, ps_h, ps_b):
                    h, bb = kf[j]
                    p.op("s", (lambda q, d=h, ps=ps_h, fc=fc: q.activation(
                        d[:, fc * FCW:(fc + 1) * FCW], ps[:, :], AF.Copy)),
                        reads=[ps_b], writes=[bb])

                b.linear("w2_q", b.xn_in(), cb_q, 1024, 1024)
                b.linear("w2_k", b.xn_in(), cb_k, 1024, 1024)
                wv_t = [p.sb([128, 1024], BF16) for _ in range(8)]
                for c in range(8):
                    h, bb = wv_t[c]
                    src = b.dram["w2_v"][c * 128:(c + 1) * 128, :]
                    b.wload(None, h[:, :], bb, src, 1024)
                _emit_vtok(b, vtok, wv_t, 64, 1024, 0)
            else:
                wq_re = [p.sb([128, 1024], BF16) for _ in range(8)]
                wk_re = [p.sb([128, 1024], BF16) for _ in range(8)]
                wv_re = [p.sb([128, 1024], BF16) for _ in range(8)]
                with p.phase():
                    wq_t = [p.sb([128, 3072], BF16) for _ in range(2)]
                    for c in range(8):
                        h, bb = wq_t[c % 2]
                        src = b.dram["w3_qkv"][c * 128:(c + 1) * 128, :]
                        b.wload(None, h[:, :], bb, src, 3072)
                        for off, re in ((0, wq_re), (64, wk_re),
                                        (128, wv_re)):
                            dh, db2 = re[c]
                            srcap = AP(tensor=h, offset=off,
                                       ap=[[3072, 128], [192, 16], [1, 64]])
                            p.op("v", (lambda q, d=dh, s=srcap:
                                       q.tensor_copy(d[:, :], s)),
                                 reads=[bb], writes=[db2])

                def cb_q3(j, fc, ps_h, ps_b):
                    h, bb = qf[j]
                    p.op("s", (lambda q, d=h, ps=ps_h, fc=fc: q.activation(
                        d[:, fc * FCW:(fc + 1) * FCW], ps[:, :], AF.Copy)),
                        reads=[ps_b], writes=[bb])

                def cb_k3(j, fc, ps_h, ps_b):
                    h, bb = kf[j]
                    p.op("s", (lambda q, d=h, ps=ps_h, fc=fc: q.activation(
                        d[:, fc * FCW:(fc + 1) * FCW], ps[:, :], AF.Copy)),
                        reads=[ps_b], writes=[bb])

                for re, cb in ((wq_re, cb_q3), (wk_re, cb_k3)):
                    for j in range(8):
                        for fc in range(NFC):
                            ph, pb = p.banks[fc % 2]

                            def emit(q, re=re, j=j, fc=fc, ph=ph):
                                ins = None
                                for c in range(8):
                                    ins = q.matmul(
                                        ph[:, :],
                                        re[c][0][:, j * 128:(j + 1) * 128],
                                        b.xn[c][0][:, fc * FCW:(fc + 1) * FCW],
                                        start=(c == 0), stop=(c == 7))
                                return ins
                            p.op("t", emit, reads=[t[1] for t in re] +
                                 [bb for (_, bb) in b.xn], writes=[pb])
                            cb(j, fc, ph, pb)
                _emit_vtok(b, vtok, wv_re, 64, 1024, 0)

            sqt = p.sb([128, FCW], F32)
            nrows = [p.sb([1, T], F32) for _ in range(2)]
            nreds = [p.sb([1, 1], F32) for _ in range(2)]
            ones_c, ones_cb = b.ones_col
            ones_r, ones_rb = b.ones_row
            bn = p.banks[5]
            bc2 = p.banks[6]
            for hh in range(NH):
                prow = (hh % 2) * 64
                for r, tiles in ((0, qf), (1, kf)):
                    th, tb2 = tiles[hh // 2]
                    nr, nrb = nrows[r]
                    for fc in range(NFC):
                        p.op("s", (lambda q, th=th, prow=prow, fc=fc:
                                   q.activation(sqt[0][0:64, :],
                                                th[prow:prow + 64,
                                                   fc * FCW:(fc + 1) * FCW],
                                                AF.Square)),
                             reads=[tb2], writes=[sqt[1]])
                        p.op("t", (lambda q: q.matmul(
                            bn[0][0:1, :], ones_c[0:64, :],
                            sqt[0][0:64, :], start=True, stop=True)),
                            reads=[sqt[1], ones_cb], writes=[bn[1]])
                        p.op("v", (lambda q, nr=nr, fc=fc: q.tensor_copy(
                            nr[0:1, fc * FCW:(fc + 1) * FCW],
                            bn[0][0:1, :])), reads=[bn[1]],
                            writes=[nrb])
                for r in range(2):
                    p.op("v", (lambda q, r=r: q.reduce_max(
                        nreds[r][0][0:1, :], nrows[r][0][0:1, :], axis=AX.X)),
                        reads=[nrows[r][1]], writes=[nreds[r][1]], tail=True)
                p.op("v", (lambda q: q.tensor_tensor(
                    nreds[0][0][0:1, :], nreds[0][0][0:1, :],
                    nreds[1][0][0:1, :], ALU.mult)),
                    reads=[nreds[0][1], nreds[1][1]], writes=[nreds[0][1]],
                    tail=True)
                p.op("s", (lambda q: q.activation(nreds[0][0][0:1, :],
                                                  nreds[0][0][0:1, :], AF.Sqrt)),
                     reads=[nreds[0][1]], writes=[nreds[0][1]])
                p.op("t", (lambda q: q.matmul(bc2[0][:, 0:1], ones_r[:, :],
                                              nreds[0][0][0:1, :], start=True,
                                              stop=True)),
                     reads=[nreds[0][1], ones_rb], writes=[bc2[1]])
                p.op("v", (lambda q, hh=hh: q.tensor_scalar_mul(
                    negc[0][:, hh:hh + 1], bc2[0][:, 0:1], -0.125)),
                    reads=[bc2[1]], writes=[negc[1]])

        with p.phase():
            pt = [p.sb([128, T], BF16) for _ in range(2)]
            ebt = [p.sb([128, 3968], BF16) for _ in range(2)]
            ctx_sc = p.sb([128, FCW], F32)
            ctx_bf = p.sb([128, FCW], BF16)
            for hh in range(NH):
                if layer == 2:
                    eh, ebb = ebt[hh % 2]
                    src = AP(tensor=b.dram["sp_expb"].tensor, offset=hh * 4095,
                             ap=[[1, 128], [1, 3968]])
                    p.op("y", (lambda q, d=eh, s=src: q.dma_start(
                        out=d[:, :], in_=s)),
                        reads=[p.db(("sp_expb", k)) for k in range(8)],
                        writes=[ebb])
                qt, qtb = qf[hh // 2]
                kt, ktb = kf[hh // 2]
                prow = (hh % 2) * 64
                for kb in range(16):
                    pth, ptb = pt[kb % 2]
                    vt, vtb = vtok[kb]
                    for qc in range(NFC):
                        sh, sb2 = p.banks[qc]
                        p.op("t", (lambda q, sh=sh, kt=kt, qt=qt, prow=prow,
                                   kb=kb, qc=qc: q.matmul(
                                       sh[:, :],
                                       kt[prow:prow + 64,
                                          kb * 128:(kb + 1) * 128],
                                       qt[prow:prow + 64,
                                          qc * FCW:(qc + 1) * FCW],
                                       start=True, stop=True)),
                             reads=[ktb, qtb], writes=[sb2])
                        p.op("s", (lambda q, sh=sh, pth=pth, qc=qc, hh=hh:
                                   q.activation(
                                       pth[:, qc * FCW:(qc + 1) * FCW],
                                       sh[:, :], AF.Exp, scale=0.125,
                                       bias=negc[0][:, hh:hh + 1])),
                             reads=[sb2, negc[1]], writes=[ptb])
                        if layer == 2:
                            rev = AP(tensor=ebt[hh % 2][0],
                                     offset=kb * 128 + 2047 - qc * FCW,
                                     ap=[[3968, 128], [-1, FCW]])
                            p.op("v", (lambda q, pth=pth, rev=rev, qc=qc:
                                       q.tensor_tensor(
                                           pth[:, qc * FCW:(qc + 1) * FCW],
                                           pth[:, qc * FCW:(qc + 1) * FCW],
                                           rev, ALU.mult)),
                                 reads=[ebt[hh % 2][1], ptb], writes=[ptb])
                    for qc in range(NFC):
                        ch, cb3 = p.banks[4 + qc]
                        p.op("t", (lambda q, ch=ch, vt=vt, pth=pth, hh=hh,
                                   kb=kb, qc=qc: q.matmul(
                                       ch[0:65, :], vt[:, hh * 65:(hh + 1) * 65],
                                       pth[:, qc * FCW:(qc + 1) * FCW],
                                       start=(kb == 0), stop=(kb == 15))),
                             reads=[vtb, ptb], writes=[cb3])
                for qc in range(NFC):
                    ch, cb3 = p.banks[4 + qc]
                    p.op("v", (lambda q, ch=ch: q.tensor_copy(
                        ctx_sc[0][0:65, :], ch[0:65, :])),
                        reads=[cb3], writes=[ctx_sc[1]])
                    dst = b.dram["sp_ctx"][hh * 64:(hh + 1) * 64,
                                           qc * FCW:(qc + 1) * FCW]
                    p.op("v", (lambda q: q.tensor_copy(
                        ctx_bf[0][0:64, :], ctx_sc[0][0:64, :])),
                        reads=[ctx_sc[1]], writes=[ctx_bf[1]])
                    p.op("y", (lambda q, d=dst: q.dma_start(
                        out=d, in_=ctx_bf[0][0:64, :])),
                        reads=[ctx_bf[1]], writes=[p.db(("sp_ctx", hh, qc))])
                    dst2 = b.dram["sp_den"][hh:hh + 1,
                                            qc * FCW:(qc + 1) * FCW]
                    p.op("y", (lambda q, d=dst2: q.dma_start(
                        out=d, in_=ctx_sc[0][64:65, :])),
                        reads=[ctx_sc[1]], writes=[p.db(("sp_den", hh, qc))])

    with p.phase():
        den = p.sb([16, T], F32)
        p.op("y", (lambda q: q.dma_start(out=den[0][:, :], in_=b.dram["sp_den"])),
             reads=[p.db(("sp_den", hh, qc)) for hh in range(16)
                    for qc in range(NFC)], writes=[den[1]])
        p.op("v", (lambda q: q.reciprocal(den[0][:, :], den[0][:, :])),
             reads=[den[1]], writes=[den[1]])
        p.op("y", (lambda q: q.dma_start(out=b.dram["sp_den"], in_=den[0][:, :])),
             reads=[den[1]], writes=[p.db("sp_rd")])

        ctx_t = [p.sb([128, T], BF16) for _ in range(8)]
        rdb = [p.sb([128, FCW], F32) for _ in range(2)]
        for cblk in range(8):
            h, bb = ctx_t[cblk]
            src = b.dram["sp_ctx"][cblk * 128:(cblk + 1) * 128, :]
            p.op("y", (lambda q, d=h, s=src: q.dma_start(out=d[:, :], in_=s)),
                 reads=[p.db(("sp_ctx", cblk * 2, qq)) for qq in range(NFC)] +
                       [p.db(("sp_ctx", cblk * 2 + 1, qq)) for qq in range(NFC)],
                 writes=[bb])
            for fc in range(NFC):
                rh, rb2 = rdb[fc % 2]
                s_up = b.dram["sp_den"][cblk * 2:cblk * 2 + 1,
                                        fc * FCW:(fc + 1) * FCW].broadcast_to(
                                            (64, FCW))
                s_dn = b.dram["sp_den"][cblk * 2 + 1:cblk * 2 + 2,
                                        fc * FCW:(fc + 1) * FCW].broadcast_to(
                                            (64, FCW))
                p.op("y", (lambda q, d=rh, s=s_up: q.dma_start(out=d[0:64, :],
                                                               in_=s)),
                     reads=[p.db("sp_rd")], writes=[rb2])
                p.op("y", (lambda q, d=rh, s=s_dn: q.dma_start(out=d[64:128, :],
                                                               in_=s)),
                     reads=[p.db("sp_rd")], writes=[rb2])
                p.op("v", (lambda q, h=h, rh=rh, fc=fc: q.tensor_tensor(
                    h[:, fc * FCW:(fc + 1) * FCW],
                    h[:, fc * FCW:(fc + 1) * FCW], rh[:, :], ALU.mult)),
                    reads=[rb2, bb], writes=[bb])

        if layer == 2:
            ctx_in = [((lambda fc, h=h: h[:, fc * FCW:(fc + 1) * FCW]), bb)
                      for (h, bb) in ctx_t]

            def cb_wo(j, fc, ps_h, ps_b):
                b.merge(j, fc, ps_h, ps_b, src_key=src_key, dst_key="xres")

            b.linear("w2_o", ctx_in, cb_wo, 1024, 1024)
        else:
            for j in range(8):
                h, bb = ctx_t[j]
                for fc in range(NFC):
                    b.merge_sb(j, fc, h[:, fc * FCW:(fc + 1) * FCW], bb,
                               src_key=src_key, dst_key="xres")


def _emit_vtok(b, vtok, w_tiles, head_stride, ncol, v_off):
    p = b.p
    for tb in range(16):
        vt, vtb = vtok[tb]
        p.op("v", (lambda q, vt=vt: q.memset(vt[:, :], 1.0)), writes=[vtb])
        for hg in range(2):
            ph, pb = p.banks[hg % 2]

            def emit(q, tb=tb, hg=hg, ph=ph):
                ins = None
                for c in range(8):
                    wt, _ = w_tiles[c]
                    src = AP(tensor=wt[0] if isinstance(wt, tuple) else wt,
                             offset=v_off + hg * 8 * head_stride,
                             ap=[[ncol, 128], [head_stride, 8], [1, 64]])
                    ins = q.matmul(ph[:, :],
                                   b.xn[c][0][:, tb * 128:(tb + 1) * 128],
                                   src, start=(c == 0), stop=(c == 7))
                return ins

            p.op("t", emit, reads=[wb for (_, wb) in w_tiles] +
                 [bb for (_, bb) in b.xn], writes=[pb])
            dst = AP(tensor=vt, offset=hg * 8 * 65,
                     ap=[[16 * 65, 128], [65, 8], [1, 64]])
            p.op("v", (lambda q, ph=ph, dst=dst: q.tensor_copy(
                dst, ph[:, :].rearrange("p (h d) -> p h d", h=8))),
                reads=[pb], writes=[vtb])



def emit_moe(b, src_key, dst_key):
    p = b.p
    idh, idb = b.ident
    ones_r, ones_rb = b.ones_row
    with p.phase():
        comb_bf = p.sb([10, T], BF16)
        with p.phase():
            wr = p.sb([128, 8 * 8], BF16)
            src = AP(tensor=b.dram["r3_w"].tensor, offset=0,
                     ap=[[8, 128], [128 * 8, 8], [1, 8]])
            p.op("g", (lambda q, d=wr[0][:, :].rearrange("p (c k) -> p c k", c=8),
                       s=src: q.dma_start(out=d, in_=s)), writes=[wr[1]])
            gates = p.sb([8, T], F32)
            rbc = b.col("L3_rb", rows=8)
            bg = p.banks[4]
            for fc in range(NFC):
                def emit(q, fc=fc):
                    ins = None
                    for c in range(8):
                        ins = q.matmul(bg[0][0:8, :], wr[0][:, c * 8:(c + 1) * 8],
                                       b.xn[c][0][:, fc * FCW:(fc + 1) * FCW],
                                       start=(c == 0), stop=(c == 7))
                    return ins
                p.op("t", emit, reads=[wr[1]] + [bb for (_, bb) in b.xn],
                     writes=[bg[1]])
                p.op("s", (lambda q, fc=fc: q.activation(
                    gates[0][0:8, fc * FCW:(fc + 1) * FCW], bg[0][0:8, :],
                    AF.Identity, bias=rbc[0])),
                    reads=[bg[1], rbc[1]], writes=[gates[1]])

            comb = p.sb([10, T], F32)
            comb_bf = p.sb([10, T], BF16)
            p.op("v", (lambda q: q.memset(comb[0][:, :], 0.5)), writes=[comb[1]])
            NT = 16
            gt = [p.sb([128, 8], F32) for _ in range(NT)]
            m1 = [p.sb([128, 1], F32) for _ in range(NT)]
            m2 = [p.sb([128, 1], F32) for _ in range(NT)]
            eq1 = [p.sb([128, 8], F32) for _ in range(NT)]
            eq2 = [p.sb([128, 8], F32) for _ in range(NT)]
            gm = [p.sb([128, 8], F32) for _ in range(NT)]
            e21 = [p.sb([128, 1], F32) for _ in range(NT)]
            p1c = [p.sb([128, 1], F32) for _ in range(NT)]
            p2c = [p.sb([128, 1], F32) for _ in range(NT)]
            btr = p.banks[5]
            btr2 = p.banks[6]
            for tb in range(NT):
                p.op("t", (lambda q, tb=tb: q.transpose(
                    btr[0][:, 0:8], gates[0][0:8, tb * 128:(tb + 1) * 128],
                    idh[0:8, 0:8])), reads=[gates[1], idb], writes=[btr[1]])
                p.op("v", (lambda q, tb=tb: q.tensor_copy(gt[tb][0][:, :],
                                                          btr[0][:, 0:8])),
                     reads=[btr[1]], writes=[gt[tb][1]])
            for tb in range(NT):
                p.op("v", (lambda q, tb=tb: q.reduce_max(
                    m1[tb][0][:, :], gt[tb][0][:, :], axis=AX.X)),
                    reads=[gt[tb][1]], writes=[m1[tb][1]])
            for tb in range(NT):
                p.op("v", (lambda q, tb=tb: q.tensor_tensor(
                    eq1[tb][0][:, :], gt[tb][0][:, :],
                    m1[tb][0][:, 0:1].broadcast_to((128, 8)), ALU.is_equal)),
                    reads=[gt[tb][1], m1[tb][1]], writes=[eq1[tb][1]])
            for tb in range(NT):
                p.op("v", (lambda q, tb=tb: q.scalar_tensor_tensor(
                    gm[tb][0][:, :], eq1[tb][0][:, :], -1e30, gt[tb][0][:, :],
                    ALU.mult, ALU.add)),
                    reads=[eq1[tb][1], gt[tb][1]], writes=[gm[tb][1]])
            for tb in range(NT):
                p.op("v", (lambda q, tb=tb: q.reduce_max(
                    m2[tb][0][:, :], gm[tb][0][:, :], axis=AX.X)),
                    reads=[gm[tb][1]], writes=[m2[tb][1]])
            for tb in range(NT):
                p.op("v", (lambda q, tb=tb: q.tensor_tensor(
                    eq2[tb][0][:, :], gm[tb][0][:, :],
                    m2[tb][0][:, 0:1].broadcast_to((128, 8)), ALU.is_equal)),
                    reads=[gm[tb][1], m2[tb][1]], writes=[eq2[tb][1]])
            for tb in range(NT):
                p.op("v", (lambda q, tb=tb: q.tensor_tensor(
                    e21[tb][0][:, :], m2[tb][0][:, :], m1[tb][0][:, :],
                    ALU.subtract)),
                    reads=[m1[tb][1], m2[tb][1]], writes=[e21[tb][1]])
            for tb in range(NT):
                p.op("s", (lambda q, tb=tb: q.activation(
                    e21[tb][0][:, :], e21[tb][0][:, :], AF.Exp)),
                    reads=[e21[tb][1]], writes=[e21[tb][1]])
            for tb in range(NT):
                p.op("v", (lambda q, tb=tb: q.tensor_scalar(
                    p1c[tb][0][:, :], e21[tb][0][:, :], 1.0, None, ALU.add)),
                    reads=[e21[tb][1]], writes=[p1c[tb][1]])
            for tb in range(NT):
                p.op("v", (lambda q, tb=tb: q.reciprocal(
                    p1c[tb][0][:, :], p1c[tb][0][:, :])),
                    reads=[p1c[tb][1]], writes=[p1c[tb][1]])
            for tb in range(NT):
                p.op("v", (lambda q, tb=tb: q.tensor_tensor(
                    p2c[tb][0][:, :], e21[tb][0][:, :], p1c[tb][0][:, :],
                    ALU.mult)),
                    reads=[e21[tb][1], p1c[tb][1]], writes=[p2c[tb][1]])
            for tb in range(NT):
                p.op("v", (lambda q, tb=tb: q.tensor_scalar(
                    eq1[tb][0][:, :], eq1[tb][0][:, :], p1c[tb][0][:, :], None,
                    ALU.mult)),
                    reads=[p1c[tb][1], eq1[tb][1]], writes=[eq1[tb][1]])
            for tb in range(NT):
                p.op("v", (lambda q, tb=tb: q.tensor_scalar(
                    eq2[tb][0][:, :], eq2[tb][0][:, :], p2c[tb][0][:, :], None,
                    ALU.mult)),
                    reads=[p2c[tb][1], eq2[tb][1]], writes=[eq2[tb][1]])
            for tb in range(NT):
                p.op("v", (lambda q, tb=tb: q.tensor_tensor(
                    eq1[tb][0][:, :], eq1[tb][0][:, :], eq2[tb][0][:, :],
                    ALU.add)),
                    reads=[eq2[tb][1], eq1[tb][1]], writes=[eq1[tb][1]])
            for tb in range(NT):
                p.op("t", (lambda q, tb=tb: q.transpose(
                    btr2[0][0:8, 0:128], eq1[tb][0][:, 0:8], idh[:, :])),
                    reads=[eq1[tb][1], idb], writes=[btr2[1]])
                p.op("v", (lambda q, tb=tb: q.tensor_copy(
                    comb[0][0:8, tb * 128:(tb + 1) * 128], btr2[0][0:8, 0:128])),
                    reads=[btr2[1]], writes=[comb[1]])
            p.op("v", (lambda q: q.tensor_copy(comb_bf[0][:, :], comb[0][:, :])),
                 reads=[comb[1]], writes=[comb_bf[1]], tail=True)
            p.op("y", (lambda q: q.dma_start(out=b.dram["sp_comb"],
                                             in_=comb_bf[0][:, :])),
                 reads=[comb_bf[1]], writes=[p.db("sp_comb")])

        b2sb = p.sb([10, 1024], BF16)
        comb_rl = [p.sb([10, FCW], BF16) for _ in range(2)]
        b.wload(None, b2sb[0][:, :], b2sb[1], b.dram["b2ext"], 1024)

        yacc = [p.sb([128, T], F32) for _ in range(8)]
        hbf = [p.sb([128, FCW], BF16) for _ in range(16)]
        gel = p.sb([128, FCW], BF16)
        cbb = [p.sb([128, FCW], BF16) for _ in range(2)]
        w1t = [p.sb([128, 8 * 128], BF16) for _ in range(16)]
        w2t = [p.sb([128, 16 * 128], BF16) for _ in range(8)]

        for e in range(11):
            if e < 8:
                w1n, w1o = "m3_w1", e * 1024 * 2048
                w2n, w2o = "m3_w2", e * 2048 * 1024
                b1n, b1k = "L3_b1", e * 16
            elif e < 10:
                w1n, w1o = "m3_sw1", (e - 8) * 1024 * 2048
                w2n, w2o = "m3_sw2", (e - 8) * 2048 * 1024
                b1n, b1k = "L3_sb1", (e - 8) * 16
            if e < 10:
                for j in range(16):
                    wh, wb2 = w1t[j]
                    for hf in range(2):
                        src = AP(tensor=b.dram[w1n].tensor,
                                 offset=w1o + hf * 4 * 128 * 2048 + j * 128,
                                 ap=[[2048, 128], [128 * 2048, 4], [1, 128]])
                        stage = None
                        sview = 4
                        b.wload(stage, wh[:, hf * 512:(hf + 1) * 512], wb2,
                                src, 4 * 128, view=sview)
                for pb2 in range(8):
                    wh, wb2 = w2t[pb2]
                    for cg in range(4):
                        src = AP(tensor=b.dram[w2n].tensor,
                                 offset=w2o + pb2 * 128 + cg * 4 * 128 * 1024,
                                 ap=[[1024, 128], [128 * 1024, 4], [1, 128]])
                        stage = None
                        sview = 4
                        b.wload(stage, wh[:, cg * 512:(cg + 1) * 512], wb2,
                                src, 4 * 128, view=sview)
            for tc in range(NFC):
                fsl = slice(tc * FCW, (tc + 1) * FCW)
                if e < 10:
                    ch, chb = cbb[tc % 2]
                    src = b.dram["sp_comb"][e:e + 1, fsl].broadcast_to(
                        (128, FCW))
                    p.op("y", (lambda q, d=ch, s=src: q.dma_start(
                        out=d[:, :], in_=s)), reads=[p.db("sp_comb")],
                        writes=[chb])
                    for j in range(16):
                        ph, pb = p.banks[j % 2]
                        wh, wb2 = w1t[j]

                        def emit(q, wh=wh, ph=ph, fsl=fsl):
                            ins = None
                            for c in range(8):
                                ins = q.matmul(ph[:, :],
                                               wh[:, c * 128:(c + 1) * 128],
                                               b.xn[c][0][:, fsl],
                                               start=(c == 0), stop=(c == 7))
                            return ins
                        p.op("t", emit, reads=[wb2] + [bb for (_, bb) in b.xn],
                             writes=[pb])
                        bcol = b.col(b1n, b1k + j)
                        p.op("s", (lambda q, ph=ph, bc=bcol[0]: q.activation(
                            gel[0][:, :], ph[:, :], AF.Gelu, bias=bc)),
                            reads=[pb, bcol[1]], writes=[gel[1]])
                        hh, hb = hbf[j]
                        p.op("v", (lambda q, hh=hh, ch=ch: q.tensor_tensor(
                            hh[:, :], gel[0][:, :], ch[:, :], ALU.mult)),
                            reads=[gel[1], chb], writes=[hb])
                for pblk in range(8):
                    ph2, pb3 = p.banks[2 + pblk % 2]
                    if e < 10:
                        wh, wb2 = w2t[pblk]

                        def emity(q, wh=wh, ph2=ph2):
                            ins = None
                            for c in range(16):
                                ins = q.matmul(ph2[:, :],
                                               wh[:, c * 128:(c + 1) * 128],
                                               hbf[c][0][:, :],
                                               start=(c == 0), stop=(c == 15))
                            return ins
                        p.op("t", emity, reads=[wb2] + [hb for (_, hb) in hbf],
                             writes=[pb3])
                    else:
                        cr_h, cr_b = comb_rl[tc % 2]
                        if pblk == 0:
                            src2 = b.dram["sp_comb"][:, fsl]
                            p.op("y", (lambda q, d=cr_h, s=src2: q.dma_start(
                                out=d[:, :], in_=s)),
                                reads=[p.db("sp_comb")], writes=[cr_b])

                        def emity(q, ph2=ph2, pblk=pblk, cr_h=cr_h):
                            return q.matmul(
                                ph2[:, :],
                                b2sb[0][:, pblk * 128:(pblk + 1) * 128],
                                cr_h[:, :], start=True, stop=True)
                        p.op("t", emity, reads=[b2sb[1], cr_b],
                             writes=[pb3])
                    yh, yb = yacc[pblk]
                    if e == 0:
                        p.op("v", (lambda q, yh=yh, ph2=ph2, fsl=fsl:
                                   q.tensor_copy(yh[:, fsl], ph2[:, :])),
                             reads=[pb3], writes=[yb])
                    else:
                        p.op("v", (lambda q, yh=yh, ph2=ph2, fsl=fsl:
                                   q.tensor_tensor(yh[:, fsl], yh[:, fsl],
                                                   ph2[:, :], ALU.add)),
                             reads=[pb3, yb], writes=[yb])

        for j in range(8):
            yh, yb = yacc[j]
            for fc in range(NFC):
                b.merge_sb(j, fc, yh[:, fc * FCW:(fc + 1) * FCW], yb,
                           src_key=src_key, dst_key=dst_key)



DBG_HOOK = None


def build_program(layers, warrs):
    nc = bass.Bass("TRN2", target_bir_lowering=False, debug=False)
    p = P(nc)
    dram = {}
    dram["xin"] = nc.dram_tensor("xin", [D, T], F32, kind="ExternalInput").ap()
    dram["xout"] = nc.dram_tensor("xout", [D, T], F32,
                                  kind="ExternalOutput").ap()
    dram["cols_all"] = nc.dram_tensor("cols_all", [128, NCOLS], F32,
                                      kind="ExternalInput").ap()
    dram["ident"] = nc.dram_tensor("ident", [128, 128], F32,
                                   kind="ExternalInput").ap()
    for L in layers:
        for name, _ in _wspec(L):
            arr = warrs[name]
            dt = I32 if arr.dtype == np.int32 else F32
            dram[name] = nc.dram_tensor(name, list(arr.shape), dt,
                                        kind="ExternalInput").ap()
    for k, (shape, dt) in {
        "xres": ([D, T], F32),
        "sp_u": ([2048, T], BF16),
        "sp_z": ([2048, T], BF16),
        "sp_dt": ([2048, T], F32),
        "sp_bc": ([32, T], BF16),
        "sp_hb": ([4096, T], BF16),
        "sp_hc": ([4096, T], BF16),
        "sp_ctx": ([D, T], BF16),
        "sp_den": ([16, T], F32),
        "sp_comb": ([10, T], BF16),
        "sp_expb": ([16, 4095], BF16),
    }.items():
        dram[k] = nc.dram_tensor(k, shape, dt, kind="Internal").ap()

    b = B(p, dram)
    b.cols = p.sb([128, NCOLS], F32, "cols")
    b.ident = p.sb([128, 128], F32, "identsb")
    b.ones_col = p.sb([128, 1], F32, "ones_col")
    b.ones_col_bf = p.sb([128, 1], BF16, "ones_col_bf")
    b.ones_row = p.sb([1, 128], F32, "ones_row")
    b.xn = [p.sb([128, T], BF16, f"xn{i}") for i in range(8)]
    b.mt = [p.sb([128, FCW], F32) for _ in range(2)]
    b.mt2 = [p.sb([128, FCW], F32) for _ in range(2)] if MERGE_ACC else None
    b.mti = 0

    p.op("y", lambda q: q.dma_start(out=b.cols[0][:, :], in_=dram["cols_all"]),
         writes=[b.cols[1]])
    p.op("y", lambda q: q.dma_start(out=b.ident[0][:, :], in_=dram["ident"]),
         writes=[b.ident[1]])
    p.op("v", lambda q: q.memset(b.ones_col[0][:, :], 1.0),
         writes=[b.ones_col[1]])
    p.op("v", lambda q: q.memset(b.ones_col_bf[0][:, :], 1.0),
         writes=[b.ones_col_bf[1]])
    p.op("v", lambda q: q.memset(b.ones_row[0][:, :], 1.0),
         writes=[b.ones_row[1]])
    for cv in (1e-5, -0.5, 0.125, 1.0 / 127.0, 2.0, 0.5, -1.0, 1.0):
        cth = p.sb([128, 1], F32)
        p.op("v", (lambda q, t=cth[0], cv=cv: q.memset(t[:, :], cv)),
             writes=[cth[1]])
        nc.const_aps.register(F32, cv, cth[0][:, :])
    for j in range(8 if MERGE_ACC else 0):
        for fc in range(NFC):
            srcp = dram["xin"][j * 128:(j + 1) * 128, fc * FCW:(fc + 1) * FCW]
            dstp = dram["xres"][j * 128:(j + 1) * 128, fc * FCW:(fc + 1) * FCW]
            p.op("y", (lambda q, s=srcp, d=dstp: q.dma_start(out=d, in_=s)),
                 reads=[p.db(("xin", j, fc))], writes=[p.db(("xres", j, fc))])
    p.fence()
    if 1 in layers:
        o1 = COLS["L1_bdt"][0]; o2 = COLS["L1_dtb"][0]; o3 = COLS["L1_dtsum"][0]
        ch, cb = b.cols
        p.op("v", lambda q: q.tensor_tensor(ch[:, o3:o3 + 16],
                                            ch[:, o1:o1 + 16],
                                            ch[:, o2:o2 + 16], ALU.add),
             reads=[cb], writes=[cb])

    first, last = layers[0], layers[-1]
    for L in layers:
        src_key = "xin" if L == first else "xres"
        dst_key = "xout" if L == last else "xres"
        if L == 0:
            p.paranoid = P.PARANOID_NOSELF
        (emit_layernorm if LN_HARD else emit_layernorm_v0)(b, src_key)
        p.paranoid = False
        if L == 0:
            emit_mamba0(b, src_key)
            emit_ffn_ternary(b, "L0", "xres", dst_key)
        elif L == 1:
            emit_mamba1(b, src_key)
            emit_ffn_quant(b, "xres", dst_key)
        elif L == 2:
            _emit_attention(b, src_key, 2)
            emit_ffn_ternary(b, "L2", "xres", dst_key)
        else:
            _emit_attention(b, src_key, 3)
            emit_moe(b, "xres", dst_key)

    for j in range(8 if MERGE_ACC else 0):
        for fc in range(NFC):
            srcp = dram["xres"][j * 128:(j + 1) * 128, fc * FCW:(fc + 1) * FCW]
            dstp = dram["xout"][j * 128:(j + 1) * 128, fc * FCW:(fc + 1) * FCW]
            p.op("y", (lambda q, s=srcp, d=dstp: q.dma_start(out=d, in_=s)),
                 reads=[p.db(("xres", j, fc))], writes=[p.db(("xout", j, fc))])
    if DBG_HOOK is not None:
        DBG_HOOK(p, dram, b)
    for k in P.ENG:
        p.raw_wait("y", k, p.cnt[k])
    with nc.Block() as block:
        p.finalize(block)
    return nc, p



def _host_prep(inputs):
    x = np.asarray(inputs["L0_attn_x"])
    xT = np.ascontiguousarray(x.transpose(0, 2, 1))
    cols = _build_cols(inputs)
    ident = np.eye(128, dtype=np.float32)
    warrs = {}
    for L in range(4):
        for name, fn in _wspec(L):
            a = np.asarray(fn(inputs))
            if a.dtype != np.int32:
                a = a.astype(np.float32)
            warrs[name] = np.ascontiguousarray(a)
    return xT, cols, ident, warrs


def kernel(**inputs):
    xT, cols, ident, warrs = _host_prep(inputs)
    groups = [[0, 1, 2, 3]] if FUSED else [[0], [1], [2], [3]]
    state = [xT[c] for c in range(8)]
    for layers in groups:
        nc, p = build_program(layers, warrs)
        in_maps = []
        for c in range(8):
            m = {"xin": state[c], "cols_all": cols, "ident": ident}
            for L in layers:
                for name, _ in _wspec(L):
                    m[name] = warrs[name]
            in_maps.append(m)
        res = run_bass_kernel_spmd(nc, in_maps, core_ids=list(range(8)))
        state = [np.asarray(r["xout"]) for r in res.results]
        p.root.close()
    out = np.stack([s.T for s in state], axis=0).astype(np.float32)
    return np.ascontiguousarray(out)
